# version 02
# baseline (speedup 1.0000x reference)
import contextlib
import math
import numpy as np
import ml_dtypes

import concourse.bass as bass
import concourse.mybir as mybir
from concourse.bass_utils import run_bass_kernel_spmd

AF = mybir.ActivationFunctionType
AL = mybir.AluOpType
AX = mybir.AxisListType
F32 = mybir.dt.float32
BF16 = mybir.dt.bfloat16
U8 = mybir.dt.uint8
bfdt = ml_dtypes.bfloat16

N_CORES = 8
SEQ = 2048
DM = 1024
SEQ_PER_CORE = 2
T_CORE = SEQ_PER_CORE * SEQ

ENGINES = ("sync", "tensor", "vector", "scalar", "gpsimd")


class Sem:
    def __init__(self, handle, name):
        self.h = handle
        self.name = name
        self.v = 0


class K:
    def __init__(self, nc):
        self.nc = nc
        self.q = {e: [] for e in ENGINES}
        self.sems = {}

    def sem(self, name) -> Sem:
        if name not in self.sems:
            self.sems[name] = Sem(self.nc.alloc_semaphore(name), name)
        return self.sems[name]

    SELF_SEM = {"vector": "V", "scalar": "S", "gpsimd": "P"}

    def op(self, eng, meth, *args, wait=(), inc=None, **kw):
        waits = [(s.h, v) for (s, v) in wait if v > 0]
        incs = []
        after = None
        auto = self.SELF_SEM.get(eng)
        if meth.startswith("dma_start"):
            auto = None
        if auto is not None:
            ssem = self.sem(auto)
            if ssem.v > 0:
                waits.append((ssem.h, ssem.v))
            ssem.v += 1
            incs.append((ssem.h, 1))
            after = ssem.v
        if inc is not None:
            s, kk = inc if isinstance(inc, tuple) else (inc, 1)
            if auto is not None and s is self.sem(auto):
                pass
            else:
                s.v += kk
                incs.append((s.h, kk))
                after = s.v

        use_drain = bool(incs) and not meth.startswith("dma_start") and (
            (eng == "tensor" and meth in ("matmul", "transpose")) or
            eng in ("vector", "scalar", "gpsimd"))

        def emit(e, meth=meth, args=args, kw=kw, waits=waits, incs=incs,
                 use_drain=use_drain):
            for h, v in waits:
                e.wait_ge(h, v)
            ins = getattr(e, meth)(*args, **kw)
            if use_drain:
                ins = e.drain()
            for h, kk in incs:
                ins.then_inc(h, kk)

        self.q[eng].append(emit)
        return after

    def wait(self, eng, sem, v):
        if v <= 0:
            return
        self.q[eng].append(lambda e, h=sem.h, v=v: e.wait_ge(h, v))

    POOL_SIZE = {"sync": 8, "gpsimd": 6, "scalar": 4}

    def _pool(self, eng):
        if not hasattr(self, "_dma_pools"):
            self._dma_pools = {}
        if eng not in self._dma_pools:
            n = self.POOL_SIZE[eng]
            self._dma_pools[eng] = {
                "sems": [self.sem(f"{eng}_dq{i}") for i in range(n)],
                "idx": 0}
        return self._dma_pools[eng]

    def new_group(self, eng):
        p = self._pool(eng)
        sem = p["sems"][p["idx"]]
        p["idx"] = (p["idx"] + 1) % len(p["sems"])
        return {"sem": sem, "throttle": sem.v, "fresh": True}

    def dma(self, eng, dst, src, g=None, wait=(), method="dma_start", **kw):
        if g is None:
            g = self.new_group(eng)
        wl = list(wait)
        if g["fresh"]:
            if g["throttle"] > 0:
                wl.append((g["sem"], g["throttle"]))
            g["fresh"] = False
        v = self.op(eng, method, wait=wl, inc=(g["sem"], 16), out=dst,
                    in_=src, **kw)
        return (g["sem"], v)

    def milestone(self, eng, sem, k=1):
        return self.op(eng, "nop", inc=(sem, k))

    def engv(self, eng):
        return self.sems[self.SELF_SEM.get(eng, "T")].v

    def full_barrier(self):
        T, V, S, P = self.sem("T"), self.sem("V"), self.sem("S"), self.sem("P")
        tv = self.milestone("tensor", T)
        vv = self.milestone("vector", V)
        sv = self.milestone("scalar", S)
        gv = self.milestone("gpsimd", P)
        targets = [(T, tv), (V, vv), (S, sv), (P, gv)]
        for name, sem in self.sems.items():
            if name.endswith(tuple("01234567")) and "_dq" in name and sem.v > 0:
                targets.append((sem, sem.v))
        for e in ENGINES:
            for sem, v in targets:
                self.wait(e, sem, v)

    def finalize(self):
        with self.nc.Block() as block:
            for name, reg in (("sync", block.sync), ("tensor", block.tensor),
                              ("vector", block.vector), ("scalar", block.scalar),
                              ("gpsimd", block.gpsimd)):
                ops = self.q[name]

                def run(e, ops=ops):
                    for f in ops:
                        f(e)

                reg(run)


def host_consts():
    c = {}
    c["iden_bf"] = np.eye(128, dtype=bfdt)
    c["iden_f32"] = np.eye(128, dtype=np.float32)
    for r in range(4):
        i = np.arange(128)[:, None] + r * 128
        j = np.arange(512)[None, :]
        c[f"trimask{r}"] = (i <= j).astype(bfdt)
    theta = 1.0 / (10000.0 ** (np.arange(0, 64, 2, dtype=np.float64) / 64))
    ang = np.arange(16, dtype=np.float64)[:, None] * theta[None, :]
    c["l0_cos"] = np.repeat(np.cos(ang).astype(np.float32).reshape(1, 512), 128, 0)
    c["l0_sin"] = np.repeat(np.sin(ang).astype(np.float32).reshape(1, 512), 128, 0)
    fr = (1.0 / 1024.0) ** np.linspace(0.0, 1.0, 32)
    fr = np.concatenate([fr, np.zeros(32)])
    th = np.arange(2048, dtype=np.float64)[None, :] * fr[:, None]
    cc = np.cos(th).astype(np.float32)
    ss = np.sin(th).astype(np.float32)
    c["l3_cos"] = np.concatenate([cc, cc], 0)
    c["l3_sin"] = np.concatenate([ss, ss], 0)
    invf = 1.0 / (10000.0 ** (np.arange(0, 32, 2, dtype=np.float64) / 32))
    d_ = np.arange(32)
    fr1 = invf[d_ % 16]
    th1 = np.arange(2048, dtype=np.float64)[None, :] * fr1[:, None]
    c["l1_cos"] = np.cos(th1).astype(np.float32)
    c["l1_sin"] = np.sin(th1).astype(np.float32)
    c["l1_tie"] = np.repeat(np.linspace(0.0, 1e-6, 32, dtype=np.float32
                                        ).reshape(1, 32), 128, 0)
    return c


CONST_SPECS = {
    "iden_bf": ([128, 128], BF16),
    "iden_f32": ([128, 128], F32),
    "trimask0": ([128, 512], BF16),
    "trimask1": ([128, 512], BF16),
    "trimask2": ([128, 512], BF16),
    "trimask3": ([128, 512], BF16),
    "l0_cos": ([128, 512], F32),
    "l0_sin": ([128, 512], F32),
    "l3_cos": ([128, 2048], F32),
    "l3_sin": ([128, 2048], F32),
    "l1_cos": ([32, 2048], F32),
    "l1_sin": ([32, 2048], F32),
    "l1_tie": ([128, 32], F32),
}


class Ctx:
    def __init__(self, k: K, used=None):
        self.k = k
        nc = k.nc
        self.dram = {}
        self.sb = {}
        g = k.new_group("sync")
        items = [(n, CONST_SPECS[n]) for n in (used or CONST_SPECS)]
        nc._used_consts = [n for n, _ in items]
        for n, (sh, dt) in items:
            self.dram[n] = nc.dram_tensor(n, sh, dt, kind="ExternalInput").ap()
            t = nc.alloc_sbuf_tensor("c_" + n, sh, dt)
            self.sb[n] = t
            ld = k.dma("sync", t[:], self.dram[n], g)
        self.ones_col = nc.alloc_sbuf_tensor("ones_col_f32", [128, 1], F32)
        self.ones_col_bf = nc.alloc_sbuf_tensor("ones_col_bf", [128, 1], BF16)
        self.ones_row = nc.alloc_sbuf_tensor("ones_row_f32", [1, 128], F32)
        k.wait("vector", ld[0], ld[1])
        k.op("vector", "memset", self.ones_col[:], 1.0)
        k.op("vector", "memset", self.ones_col_bf[:], 1.0)
        k.op("vector", "memset", self.ones_row[:], 1.0)
        self._immtile = nc.alloc_sbuf_tensor("imm_tile", [128, 8], F32)
        self._immvals = {}
        self.P = [nc.alloc_psum_tensor(f"P512_{i}", [128, 512], F32) for i in range(6)]
        self.PBF = [nc.alloc_psum_tensor(f"PBF_{i}", [128, 1024], BF16) for i in range(2)]
        for v in (1e-5, -0.5, 0.5, 1e-30, 2.0, -1.0, 1.1920929e-07):
            self.imm(v)
        k.full_barrier()

    def imm(self, val):
        if val not in self._immvals:
            i = len(self._immvals)
            assert i < 8
            self.k.op("vector", "memset", self._immtile[:, i:i + 1], float(val))
            self._immvals[val] = i
        i = self._immvals[val]
        return self._immtile[:, i:i + 1]

    def load_tables(self, es, names):
        k = self.k
        nc = k.nc
        g = k.new_group("sync")
        lv = None
        for n in names:
            sh, dt = CONST_SPECS[n]
            if n in self.sb and not isinstance(self.sb[n], tuple):
                continue
            t = es.enter_context(nc.sbuf_tensor("t_" + n + str(k.sems["T"].v),
                                                sh, dt))
            if n not in self.dram:
                self.dram[n] = nc.dram_tensor(n, sh, dt,
                                              kind="ExternalInput").ap()
                nc._used_consts.append(n)
            lv = k.dma("sync", t[:], self.dram[n], g)
            self.sb[n] = t
        if lv is not None:
            for e in ("vector", "scalar", "tensor", "gpsimd"):
                k.wait(e, lv[0], lv[1])

    def bcast_scalar(self, name, es, scal_sb_ap, ready, scale=1.0, add=0.0):
        k = self.k
        T, V = k.sem("T"), k.sem("V")
        out = es.enter_context(k.nc.sbuf_tensor(name, [128, 1], F32))
        S = k.sem("S")
        tv = k.op("tensor", "matmul", self.P[5][:, 0:1], self.ones_row[0:1, :],
                  scal_sb_ap, start=True, stop=True,
                  wait=[ready, (V, V.v), (S, S.v)], inc=T)
        vv = k.op("vector", "tensor_scalar", out[:], self.P[5][:, 0:1], scale, add,
                  AL.mult, AL.add, wait=[(T, tv)], inc=V)
        return out, (V, vv)

    def colize(self, name, es, vec_dram_ap, n_feat, rot_fn=None):
        k = self.k
        nc = k.nc
        DS, T, S, V, DG = (k.sem("DS"), k.sem("T"), k.sem("S"), k.sem("V"),
                           k.sem("DG"))
        nch = n_feat // 128
        if rot_fn is not None:
            row = es.enter_context(nc.sbuf_tensor(name + "_row", [1, n_feat], F32))
            lv = k.dma("sync", row[:], vec_dram_ap.rearrange("(a f) -> a f", a=1))
            row_ap, ready = rot_fn(row, lv)
            scr = nc.dram_tensor(name + "_scr", [n_feat], F32, kind="Internal").ap()
            sv = k.dma("gpsimd", scr.rearrange("(a f) -> a f", a=1), row_ap, wait=[ready])
            src, src_ready = scr, sv
        else:
            src, src_ready = vec_dram_ap, None
        rows = es.enter_context(nc.sbuf_tensor(name + "_rows", [nch, 128], F32))
        lv2 = k.dma("sync", rows[:], src.rearrange("(c p) -> c p", p=128),
                    wait=[src_ready] if src_ready else [])
        out = es.enter_context(nc.sbuf_tensor(name + "_cols", [128, nch], F32))
        tv = k.op("tensor", "transpose", self.P[5][:, 0:nch], rows[:],
                  self.sb["iden_f32"][0:nch, 0:nch],
                  wait=[lv2, (V, V.v), (S, k.sem("S").v)], inc=T)
        sv2 = k.op("scalar", "copy", out[:], self.P[5][:, 0:nch], wait=[(T, tv)],
                   inc=k.sem("S"))
        return out, (k.sem("S"), sv2)


def emit_ln(k: K, C: Ctx, x_dram, xnfm_dram, xntok_dram=None):
    nc = k.nc
    DS, DG = k.sem("DS"), k.sem("DG")
    T, V, S = k.sem("T"), k.sem("V"), k.sem("S")
    pfx = f"ln{len(k.q['vector'])}_"
    with contextlib.ExitStack() as es:
        xt = [es.enter_context(nc.sbuf_tensor(pfx + f"x{i}", [128, 1024], F32))
              for i in range(2)]
        xnb = [es.enter_context(nc.sbuf_tensor(pfx + f"xnb{i}", [128, 1024], BF16))
               for i in range(2)]
        stg = [es.enter_context(nc.sbuf_tensor(pfx + f"stg{i}", [128, 128], BF16))
               for i in range(2)]
        xc = es.enter_context(nc.sbuf_tensor(pfx + "xc", [128, 1024], F32))
        sq = es.enter_context(nc.sbuf_tensor(pfx + "sq", [128, 1024], F32))
        mu = es.enter_context(nc.sbuf_tensor(pfx + "mu", [128, 1], F32))
        ssum = es.enter_context(nc.sbuf_tensor(pfx + "ssum", [128, 1], F32))
        rstd = es.enter_context(nc.sbuf_tensor(pfx + "rstd", [128, 1], F32))
        x_done = [0, 0]
        xn_done_t = [0, 0]
        xn_done_dma = [None, None]
        stg_done = [None, None]
        tr_done = [0, 0]
        for i in range(T_CORE // 128):
            b = i % 2
            lv = k.dma("sync", xt[b][:], x_dram[i * 128:(i + 1) * 128, :],
                       wait=[(V, x_done[b])] if x_done[b] else [])
            k.wait("vector", lv[0], lv[1])
            k.op("vector", "reduce_sum", mu[:], xt[b][:], AX.X, negate=True)
            k.op("vector", "tensor_single_scalar", mu[:], mu[:], 1.0 / 1024.0, AL.mult)
            k.op("vector", "tensor_scalar_add", xc[:], xt[b][:], mu[:])
            x_done[b] = k.op("vector", "tensor_mul", sq[:], xc[:], xc[:], inc=V)
            k.op("vector", "reduce_sum", ssum[:], sq[:], AX.X)
            vv = k.op("vector", "tensor_single_scalar", ssum[:], ssum[:],
                      1.0 / 1024.0, AL.mult, inc=V)
            k.op("scalar", "activation", rstd[:], ssum[:], AF.Ln,
                 bias=C.imm(1e-5), wait=[(V, vv)])
            sv = k.op("scalar", "activation", rstd[:], rstd[:], AF.Exp,
                      scale=C.imm(-0.5), inc=S)
            wlist = [(S, sv)]
            if xn_done_t[b]:
                wlist.append((T, xn_done_t[b]))
            if xn_done_dma[b]:
                wlist.append(xn_done_dma[b])
            xnv = k.op("vector", "tensor_scalar_mul", xnb[b][:], xc[:], rstd[:],
                       wait=wlist, inc=V)
            if xntok_dram is not None:
                xn_done_dma[b] = k.dma("gpsimd",
                                       xntok_dram[i * 128:(i + 1) * 128, :],
                                       xnb[b][:], wait=[(V, xnv)])
            for cdx in range(8):
                pb = cdx % 2
                wl = [(V, xnv)]
                if tr_done[pb]:
                    wl.append((S, tr_done[pb]))
                tv = k.op("tensor", "transpose", C.PBF[pb][:, 0:128],
                          xnb[b][:, cdx * 128:(cdx + 1) * 128],
                          C.sb["iden_bf"][:], wait=wl, inc=T)
                xn_done_t[b] = tv
                sb = cdx % 2
                wl2 = [(T, tv)]
                if stg_done[sb]:
                    wl2.append(stg_done[sb])
                ev = k.op("scalar", "copy", stg[sb][:], C.PBF[pb][:, 0:128],
                          wait=wl2, inc=S)
                tr_done[pb] = ev
                stg_done[sb] = k.dma("gpsimd",
                                     xnfm_dram[cdx * 128:(cdx + 1) * 128,
                                               i * 128:(i + 1) * 128],
                                     stg[sb][:], wait=[(S, ev)])
    k.full_barrier()


L0_WSPECS = {
    "L0_attn_W_attn": ([1024, 3072], F32),
    "L0_attn_b_attn": ([3072], F32),
    "L0_attn_W_proj": ([1024, 1024], F32),
    "L0_attn_b_proj": ([1024], F32),
    "L0_ffn_k0": ([2048, 4096], U8),
    "L0_ffn_k1": ([4096, 4096], U8),
    "L0_ffn_k2": ([4096, 1024], U8),
    "L0_ffn_s0": ([1], F32),
    "L0_ffn_s1": ([1], F32),
    "L0_ffn_s2": ([1], F32),
}


def emit_layer0(k: K, C: Ctx, W, x_dram, xout_dram, scratch, stop_after="all"):
    nc = k.nc
    DS, DG, DA = k.sem("DS"), k.sem("DG"), k.sem("DA")
    T, V, S = k.sem("T"), k.sem("V"), k.sem("S")
    xnfm = scratch["xnfm"]
    s_d = scratch["s_fm"]
    h1_d = scratch["h1"]
    h2_d = scratch["h2"]
    w1e_d = scratch["w1eff"]

    emit_ln(k, C, x_dram, xnfm)
    if stop_after == "ln":
        return

    with contextlib.ExitStack() as es:
        C.load_tables(es, ["l0_cos", "l0_sin"])
        wrot = [es.enter_context(nc.sbuf_tensor(f"l0_wrot{c}", [128, 3072], BF16))
                for c in range(8)]
        wf = [es.enter_context(nc.sbuf_tensor(f"l0_wf{i}", [128, 1024], F32))
              for i in range(2)]
        st1 = es.enter_context(nc.sbuf_tensor("l0_st1", [128, 512], F32))
        st2 = es.enter_context(nc.sbuf_tensor("l0_st2", [128, 512], F32))
        st3 = es.enter_context(nc.sbuf_tensor("l0_st3", [128, 512], F32))
        st4 = es.enter_context(nc.sbuf_tensor("l0_st4", [128, 512], F32))

        def rot_part(E, O, dst_even, dst_odd, nrow, scale):
            cosv = C.sb["l0_cos"][0:nrow, :].rearrange("p (h j) -> p h j", j=32)
            sinv = C.sb["l0_sin"][0:nrow, :].rearrange("p (h j) -> p h j", j=32)
            t1 = st1[0:nrow, :].rearrange("p (h j) -> p h j", j=32)
            t2 = st2[0:nrow, :].rearrange("p (h j) -> p h j", j=32)
            t3 = st3[0:nrow, :].rearrange("p (h j) -> p h j", j=32)
            t4 = st4[0:nrow, :].rearrange("p (h j) -> p h j", j=32)
            k.op("vector", "tensor_tensor", t1, E, cosv, AL.mult)
            k.op("vector", "tensor_tensor", t2, O, sinv, AL.mult)
            k.op("vector", "tensor_tensor", t3, O, cosv, AL.mult)
            k.op("vector", "tensor_tensor", t4, E, sinv, AL.mult)
            k.op("vector", "tensor_sub", t1, t1, t2)
            k.op("vector", "tensor_add", t3, t3, t4)
            k.op("vector", "tensor_single_scalar", dst_even, t1, scale, AL.mult)
            return k.op("vector", "tensor_single_scalar", dst_odd, t3, scale,
                        AL.mult, inc=V)

        wf_done = [0, 0]
        for c in range(8):
            for part in range(3):
                b = (c * 3 + part) % 2
                base = part * 1024
                wl = [(V, wf_done[b])] if wf_done[b] else []
                lv = k.dma("sync", wf[b][:],
                           W["L0_attn_W_attn"][c * 128:(c + 1) * 128,
                                               base:base + 1024], wait=wl)
                k.wait("vector", lv[0], lv[1])
                if part == 2:
                    wf_done[b] = k.op("vector", "tensor_copy",
                                      wrot[c][:, 2048:3072], wf[b][:], inc=V)
                else:
                    src = wf[b][:].rearrange("p (h s j) -> p h s j", s=2, j=32)
                    dst = wrot[c][:, base:base + 1024].rearrange(
                        "p (h s j) -> p h s j", s=2, j=32)
                    scale = 0.125 if part == 0 else 1.0
                    wf_done[b] = rot_part(src[:, :, 0, :], src[:, :, 1, :],
                                          dst[:, :, 0, :], dst[:, :, 1, :],
                                          128, scale)

        def rot_b(row, ready):
            k.wait("vector", ready[0], ready[1])
            for base, scale in ((0, 0.125), (1024, 1.0)):
                src = row[0:1, base:base + 1024].rearrange(
                    "p (h s j) -> p h s j", s=2, j=32)
                rot_part(src[:, :, 0, :], src[:, :, 1, :],
                         src[:, :, 0, :], src[:, :, 1, :], 1, scale)
            rv = k.op("vector", "engine_nop", inc=V)
            return row[0:1, :], (V, rv)

        batt_col, _ = C.colize("l0_batt", es, W["L0_attn_b_attn"], 3072, rot_fn=rot_b)
        bproj_col, _ = C.colize("l0_bproj", es, W["L0_attn_b_proj"], 1024)

        wproj = [es.enter_context(nc.sbuf_tensor(f"l0_wpj{c}", [128, 1024], BF16))
                 for c in range(8)]
        wpf_done = k.sems["V"].v
        wpf_prev = [wpf_done, wpf_done]
        for c in range(8):
            b = c % 2
            wl = [(V, wpf_prev[b])]
            lv = k.dma("sync", wf[b][:], W["L0_attn_W_proj"][c * 128:(c + 1) * 128, :],
                       wait=wl)
            wpf_prev[b] = k.op("vector", "tensor_copy", wproj[c][:], wf[b][:],
                               wait=[lv], inc=V)
        k.full_barrier()
        if stop_after == "prep":
            return

        xnt = [[es.enter_context(nc.sbuf_tensor(f"l0_xnt{i}_{c}", [128, 512], BF16))
                for c in range(8)] for i in range(2)]
        ymix = [es.enter_context(nc.sbuf_tensor(f"l0_ym{c}", [128, 2048], BF16))
                for c in range(8)]
        qa = [es.enter_context(nc.sbuf_tensor(f"l0_qa{i}", [65, 2048], BF16))
              for i in range(2)]
        ka = [es.enter_context(nc.sbuf_tensor(f"l0_ka{i}", [65, 2048], BF16))
              for i in range(2)]
        vto = [es.enter_context(nc.sbuf_tensor(f"l0_v{i}", [128, 1040], BF16))
               for i in range(2)]
        vfm = es.enter_context(nc.sbuf_tensor("l0_vfm", [128, 2048], BF16))
        attT = [es.enter_context(nc.sbuf_tensor(f"l0_attT{i}", [128, 512], BF16))
                for i in range(2)]
        sq64 = es.enter_context(nc.sbuf_tensor("l0_sq64", [64, 2048], BF16))
        nrm = es.enter_context(nc.sbuf_tensor("l0_nrm", [1, 2048], F32))
        kmx = es.enter_context(nc.sbuf_tensor("l0_kmx", [1, 1], F32))
        rd = es.enter_context(nc.sbuf_tensor("l0_rd", [1, 512], F32))
        rb = es.enter_context(nc.sbuf_tensor("l0_rb", [64, 512], F32))
        sstg = [es.enter_context(nc.sbuf_tensor(f"l0_ss{i}", [128, 512], BF16))
                for i in range(2)]

        for sq_i in range(SEQ_PER_CORE):
            base_t = sq_i * SEQ
            for hp in range(8):
                xn_used = [k.sems["T"].v, k.sems["T"].v]
                psum_free = [k.sems["V"].v] * 3
                for tt in range(4):
                    b = tt % 2
                    lv = 0
                    for kc in range(8):
                        lv = k.dma("sync", xnt[b][kc][:],
                                   xnfm[kc * 128:(kc + 1) * 128,
                                        base_t + tt * 512:base_t + (tt + 1) * 512],
                                   wait=[(T, xn_used[b])] if kc == 0 else [])
                    k.wait("tensor", lv[0], lv[1])
                    for pi, fbase in ((0, hp * 128), (1, 1024 + hp * 128),
                                      (2, 2048 + hp * 128)):
                        p = C.P[pi]
                        for kc in range(8):
                            wl = [(V, psum_free[pi])] if kc == 0 else []
                            k.op("tensor", "matmul", p[:, :],
                                 wrot[kc][:, fbase:fbase + 128],
                                 xnt[b][kc][:], start=(kc == 0), stop=(kc == 7),
                                 wait=wl, inc=(T if kc == 7 else None))
                        tv = k.sems["T"].v
                        if pi == 2:
                            xn_used[b] = tv
                        cs = tt * 512
                        if pi < 2:
                            dste, dsto = (qa, ka)[pi][0], (qa, ka)[pi][1]
                            bc = hp if pi == 0 else 8 + hp
                            k.op("vector", "tensor_scalar_add",
                                 dste[0:64, cs:cs + 512], p[0:64, :],
                                 batt_col[0:64, bc:bc + 1], wait=[(T, tv)])
                            psum_free[pi] = k.op(
                                "vector", "tensor_scalar_add",
                                dsto[0:64, cs:cs + 512], p[64:128, :],
                                batt_col[64:128, bc:bc + 1], inc=V)
                        else:
                            psum_free[pi] = k.op(
                                "vector", "tensor_scalar_add", vfm[:, cs:cs + 512],
                                p[:, :], batt_col[:, 16 + hp:17 + hp],
                                wait=[(T, tv)], inc=V)
                k.op("vector", "memset",
                     vto[0][:].rearrange("p (t c) -> p t c", c=65)[:, :, 64:65], 1.0,
                     wait=[(T, k.sems["T"].v)])
                m1 = k.op("vector", "memset",
                          vto[1][:].rearrange("p (t c) -> p t c", c=65)[:, :, 64:65],
                          1.0, inc=V)
                tr_ev = [0, 0]
                for tb in range(16):
                    pb = tb % 2
                    wl = [(V, k.sems["V"].v)]
                    if tr_ev[pb]:
                        wl.append((S, tr_ev[pb]))
                    tv = k.op("tensor", "transpose", C.PBF[pb][:, 0:128],
                              vfm[:, tb * 128:(tb + 1) * 128], C.sb["iden_bf"][:],
                              wait=wl, inc=T)
                    k.op("scalar", "copy", vto[0][:, tb * 65:tb * 65 + 64],
                         C.PBF[pb][0:128, 0:64], wait=[(T, tv)])
                    tr_ev[pb] = k.op("scalar", "copy",
                                     vto[1][:, tb * 65:tb * 65 + 64],
                                     C.PBF[pb][0:128, 64:128], inc=S)
                for hh in range(2):
                    sv = k.op("scalar", "activation", sq64[:, :], ka[hh][0:64, :],
                              AF.Square,
                              wait=[(T, k.sems["T"].v), (V, k.sems["V"].v)], inc=S)
                    cpv = 0
                    for tt in range(4):
                        pst = C.P[3 + tt % 2]
                        wl = [(S, sv)]
                        if tt >= 2:
                            wl.append((V, cpv))
                        tv = k.op("tensor", "matmul", pst[0:1, :],
                                  C.ones_col_bf[0:64, :],
                                  sq64[:, tt * 512:(tt + 1) * 512], start=True,
                                  stop=True, wait=wl, inc=T)
                        cpv = k.op("vector", "tensor_copy",
                                   nrm[0:1, tt * 512:(tt + 1) * 512], pst[0:1, :],
                                   wait=[(T, tv)], inc=V)
                    kv = k.op("vector", "reduce_max", kmx[:], nrm[:], AX.X, inc=V)
                    sv = k.op("scalar", "activation", sq64[:, :], qa[hh][0:64, :],
                              AF.Square, wait=[(T, k.sems["T"].v)], inc=S)
                    for tt in range(4):
                        pst = C.P[3 + tt % 2]
                        wl = [(S, sv), (V, kv)]
                        if tt >= 2:
                            wl.append((V, cpv))
                        tv = k.op("tensor", "matmul", pst[0:1, :],
                                  C.ones_col_bf[0:64, :],
                                  sq64[:, tt * 512:(tt + 1) * 512], start=True,
                                  stop=True, wait=wl, inc=T)
                        cpv = k.op("vector", "tensor_copy",
                                   nrm[0:1, tt * 512:(tt + 1) * 512], pst[0:1, :],
                                   wait=[(T, tv)], inc=V)
                    vv = k.op("vector", "tensor_scalar_mul", nrm[:], nrm[:],
                              kmx[0:1, 0:1], inc=V)
                    k.op("vector", "tensor_single_scalar", nrm[:], nrm[:],
                         1e-30, AL.add, wait=[(V, vv)])
                    k.op("scalar", "activation", nrm[:], nrm[:], AF.Ln,
                         wait=[(V, k.sems["V"].v)])
                    k.op("vector", "tensor_single_scalar", nrm[:], nrm[:], 0.5,
                         AL.mult, wait=[(S, k.sems["S"].v)])
                    sv = k.op("scalar", "activation", nrm[:], nrm[:], AF.Exp,
                              wait=[(V, k.sems["V"].v)], inc=S)
                    k.op("vector", "tensor_scalar", qa[hh][64:65, :], nrm[0:1, :],
                         -1.0, -1.0, AL.mult, AL.add, wait=[(S, sv)])
                    k.op("vector", "memset", ka[hh][64:65, :], 1.0, inc=V)
                aug_ready = k.sems["V"].v
                k.wait("tensor", V, aug_ready)
                for hh in range(2):
                    qh, kh, vh = qa[hh], ka[hh], vto[hh]
                    y_ev = k.sems["V"].v
                    sc_ev = [k.sems["S"].v, k.sems["S"].v]
                    for Q in range(4):
                        for kt in range(4 * Q + 4):
                            pb = kt % 2
                            p = C.P[pb]
                            wl = [(S, sc_ev[pb])] if sc_ev[pb] else []
                            tv = k.op("tensor", "matmul", p[:, :],
                                      kh[0:65, kt * 128:(kt + 1) * 128],
                                      qh[0:65, Q * 512:(Q + 1) * 512],
                                      start=True, stop=True, wait=wl, inc=T)
                            sc_ev[pb] = k.op("scalar", "activation", attT[pb][:, :],
                                             p[:, :], AF.Exp, wait=[(T, tv)], inc=S)
                            r = kt - 4 * Q
                            src = (S, sc_ev[pb])
                            if r >= 0:
                                mv = k.op("vector", "tensor_tensor", attT[pb][:, :],
                                          attT[pb][:, :], C.sb[f"trimask{r}"][:, :],
                                          AL.mult, wait=[src], inc=V)
                                src = (V, mv)
                            wl2 = [src]
                            if kt == 0 and y_ev:
                                wl2.append((V, y_ev))
                            k.op("tensor", "matmul", C.P[2][0:65, :],
                                 vh[:, kt * 65:kt * 65 + 65], attT[pb][:, :],
                                 start=(kt == 0), stop=(kt == 4 * Q + 3),
                                 wait=wl2, inc=(T if kt == 4 * Q + 3 else None))
                        tv = k.sems["T"].v
                        rv = k.op("vector", "reciprocal", rd[:], C.P[2][64:65, :],
                                  wait=[(T, tv)], inc=V)
                        tv2 = k.op("tensor", "matmul", C.P[3][0:64, :],
                                   C.ones_row[0:1, 0:64], rd[:], start=True,
                                   stop=True, wait=[(V, rv)], inc=T)
                        cv = k.op("vector", "tensor_copy", rb[:], C.P[3][0:64, :],
                                  wait=[(T, tv2)], inc=V)
                        y_ev = k.op("vector", "tensor_mul",
                                    ymix[hp][hh * 64:hh * 64 + 64,
                                             Q * 512:(Q + 1) * 512],
                                    C.P[2][0:64, :], rb[:], inc=V)
                        k.wait("tensor", V, y_ev)
            psum_free = [k.sems["V"].v] * 2
            stg_done = [None, None]
            for tt in range(4):
                for fc in range(8):
                    pi = fc % 2
                    p = C.P[pi]
                    for kc in range(8):
                        wl = [(V, psum_free[pi])] if kc == 0 else []
                        k.op("tensor", "matmul", p[:, :],
                             wproj[kc][:, fc * 128:(fc + 1) * 128],
                             ymix[kc][:, tt * 512:(tt + 1) * 512],
                             start=(kc == 0), stop=(kc == 7), wait=wl,
                             inc=(T if kc == 7 else None))
                    sb = fc % 2
                    wl = [(T, k.sems["T"].v)]
                    if stg_done[sb]:
                        wl.append(stg_done[sb])
                    psum_free[pi] = k.op("vector", "tensor_scalar_add", sstg[sb][:],
                                         p[:, :], bproj_col[:, fc:fc + 1], wait=wl,
                                         inc=V)
                    stg_done[sb] = k.dma(
                        "gpsimd",
                        s_d[fc * 128:(fc + 1) * 128,
                            base_t + tt * 512:base_t + (tt + 1) * 512],
                        sstg[sb][:], wait=[(V, k.sems["V"].v)])
        k.full_barrier()
    if stop_after == "mixer":
        return

    with contextlib.ExitStack() as es:
        ssb = es.enter_context(nc.sbuf_tensor("l0_ssb", [1, 3], F32))
        l1 = k.dma("sync", ssb[0:1, 0:1], W["L0_ffn_s0"].rearrange("(a o) -> a o", a=1))
        l2 = k.dma("sync", ssb[0:1, 1:2], W["L0_ffn_s1"].rearrange("(a o) -> a o", a=1))
        l3 = k.dma("sync", ssb[0:1, 2:3], W["L0_ffn_s2"].rearrange("(a o) -> a o", a=1))
        s0c, r0 = C.bcast_scalar("l0_s0c", es, ssb[0:1, 0:1], l1, scale=2.0)
        s1c, r1 = C.bcast_scalar("l0_s1c", es, ssb[0:1, 1:2], l2, scale=2.0)
        s2c, r2 = C.bcast_scalar("l0_s2c", es, ssb[0:1, 2:3], l3, scale=2.0)
        k.wait("scalar", r2[0], r2[1])

        w0 = [es.enter_context(nc.sbuf_tensor(f"l0_w0_{c}", [128, 4096], BF16))
              for c in range(8)]
        u8a = [es.enter_context(nc.sbuf_tensor(f"l0_u8a{i}", [128, 4096], U8))
               for i in range(2)]
        u8b = [es.enter_context(nc.sbuf_tensor(f"l0_u8b{i}", [128, 4096], U8))
               for i in range(2)]
        cva = es.enter_context(nc.sbuf_tensor("l0_cva", [128, 4096], BF16))
        cvb = es.enter_context(nc.sbuf_tensor("l0_cvb", [128, 4096], BF16))
        xnf = [es.enter_context(nc.sbuf_tensor(f"l0_xnf{c}", [128, 4096], BF16))
               for c in range(8)]
        hstg = [es.enter_context(nc.sbuf_tensor(f"l0_hs{i}", [128, 512], BF16))
                for i in range(2)]
        ud = [0, 0]
        for c in range(8):
            b = c % 2
            wl = [(V, ud[b])] if ud[b] else []
            la = k.dma("sync", u8a[b][:], W["L0_ffn_k0"][c * 128:(c + 1) * 128, :],
                       wait=wl)
            lb = k.dma("sync", u8b[b][:],
                       W["L0_ffn_k0"][1024 + c * 128:1024 + (c + 1) * 128, :])
            k.op("scalar", "copy", cva[:], u8a[b][:],
                 wait=[la, lb, (V, k.sems["V"].v)])
            sv = k.op("scalar", "copy", cvb[:], u8b[b][:], inc=S)
            ud[b] = k.op("vector", "tensor_sub", w0[c][:], cva[:], cvb[:],
                         wait=[(S, sv)], inc=V)
        for c in range(8):
            k.dma("sync", xnf[c][:], xnfm[c * 128:(c + 1) * 128, :])
        k.full_barrier()

        psum_free = [0, 0]
        stg_d = [None, None]
        for tt in range(8):
            for fc in range(32):
                pi = fc % 2
                p = C.P[pi]
                for kc in range(8):
                    wl = [(S, psum_free[pi])] if kc == 0 else []
                    k.op("tensor", "matmul", p[:, :],
                         w0[kc][:, fc * 128:(fc + 1) * 128],
                         xnf[kc][:, tt * 512:(tt + 1) * 512], start=(kc == 0),
                         stop=(kc == 7), wait=wl, inc=(T if kc == 7 else None))
                sb = fc % 2
                wl = [(T, k.sems["T"].v)]
                if stg_d[sb]:
                    wl.append(stg_d[sb])
                psum_free[pi] = k.op("scalar", "activation", hstg[sb][:], p[:, :],
                                     AF.Relu, scale=s0c[:, 0:1], wait=wl, inc=S)
                stg_d[sb] = k.dma("gpsimd",
                                  h1_d[fc * 128:(fc + 1) * 128,
                                       tt * 512:(tt + 1) * 512],
                                  hstg[sb][:], wait=[(S, k.sems["S"].v)])
        ud = [0, 0]
        st_d = [None, None]
        w0_prep_done = k.sems["V"].v
        for rc in range(32):
            b = rc % 2
            wl = [(S, ud[b])] if ud[b] else []
            la = k.dma("sync", u8a[b][:], W["L0_ffn_k1"][rc * 128:(rc + 1) * 128, :],
                       wait=wl)
            wl2 = [la, (V, w0_prep_done)]
            if st_d[b]:
                wl2.append(st_d[b])
            ud[b] = k.op("scalar", "activation", cva[:] if b == 0 else cvb[:],
                         u8a[b][:], AF.Copy, scale=2.0, bias=-1.0,
                         wait=wl2, inc=S)
            st_d[b] = k.dma("gpsimd", w1e_d[rc * 128:(rc + 1) * 128, :],
                            cva[:] if b == 0 else cvb[:], wait=[(S, ud[b])])
        k.full_barrier()
    if stop_after == "f1":
        return

    with contextlib.ExitStack() as es:
        s1c = None
        ssb = es.enter_context(nc.sbuf_tensor("l0_ssb2", [1, 2], F32))
        l2 = k.dma("sync", ssb[0:1, 0:1], W["L0_ffn_s1"].rearrange("(a o) -> a o", a=1))
        l3 = k.dma("sync", ssb[0:1, 1:2], W["L0_ffn_s2"].rearrange("(a o) -> a o", a=1))
        s1c, r1 = C.bcast_scalar("l0_s1c2", es, ssb[0:1, 0:1], l2, scale=2.0)
        s2c, r2 = C.bcast_scalar("l0_s2c2", es, ssb[0:1, 1:2], l3, scale=2.0)
        k.wait("scalar", r2[0], r2[1])
        w1g = [es.enter_context(nc.sbuf_tensor(f"l0_w1g{rc}", [128, 1024], BF16))
               for rc in range(32)]
        h1t = [es.enter_context(nc.sbuf_tensor(f"l0_h1t{rc}", [128, 512], BF16))
               for rc in range(32)]
        hstg = [es.enter_context(nc.sbuf_tensor(f"l0_hs2{i}", [128, 512], BF16))
                for i in range(2)]
        import os
        F2G = int(os.environ.get("F2G", "4"))
        F2T = int(os.environ.get("F2T", "8"))
        F2F = int(os.environ.get("F2F", "8"))
        F2K = int(os.environ.get("F2K", "32"))
        psum_free = [0, 0]
        stg_d = [None, None]
        for grp in range(F2G):
            wv = 0
            for rc in range(32):
                wl = [(T, k.sems["T"].v)] if (grp and rc == 0) else []
                wv = k.dma("sync", w1g[rc][:],
                           w1e_d[rc * 128:(rc + 1) * 128,
                                 grp * 1024:(grp + 1) * 1024], wait=wl)
            for tt in range(F2T):
                hv = 0
                for rc in range(32):
                    wl = [(T, k.sems["T"].v)] if ((grp or tt) and rc == 0) else []
                    hv = k.dma("sync", h1t[rc][:],
                               h1_d[rc * 128:(rc + 1) * 128,
                                    tt * 512:(tt + 1) * 512], wait=wl)
                k.wait("tensor", hv[0], hv[1])
                if tt == 0:
                    k.wait("tensor", wv[0], wv[1])
                if os.environ.get("F2B"):
                    k.full_barrier()
                if os.environ.get("F2D") and grp == 0 and tt == 0:
                    dbg_w = nc.dram_tensor("dbg_w1g", [32, 128, 1024], BF16,
                                           kind="ExternalOutput").ap()
                    dbg_h = nc.dram_tensor("dbg_h1t", [32, 128, 512], BF16,
                                           kind="ExternalOutput").ap()
                    for rc in range(32):
                        k.dma("gpsimd", dbg_w[rc], w1g[rc][:])
                        k.dma("gpsimd", dbg_h[rc], h1t[rc][:])
                    k.full_barrier()
                for f2 in range(F2F):
                    pi = f2 % 2
                    p = C.P[pi]
                    for kc in range(F2K):
                        wl = [(S, psum_free[pi])] if kc == 0 else []
                        k.op("tensor", "matmul", p[:, :],
                             w1g[kc][:, f2 * 128:(f2 + 1) * 128], h1t[kc][:],
                             start=(kc == 0), stop=(kc == F2K - 1), wait=wl,
                             inc=(T if kc == F2K - 1 else None))
                    sb = f2 % 2
                    wl = [(T, k.sems["T"].v)]
                    if stg_d[sb]:
                        wl.append(stg_d[sb])
                    F2E = int(os.environ.get("F2E", "3"))
                    if F2E == 9:
                        k.full_barrier()
                        psum_free[pi] = k.op("scalar", "copy", hstg[sb][:],
                                             p[:, :], inc=S)
                    elif F2E == 6:
                        psum_free[pi] = k.op("vector", "tensor_relu", hstg[sb][:],
                                             C.P[2][:, :], wait=wl, inc=k.sem("V"))
                    elif F2E == 7:
                        psum_free[pi] = k.op("vector", "tensor_relu", hstg[sb][:],
                                             hstg[sb][:], wait=wl, inc=k.sem("V"))
                    elif F2E == 4:
                        psum_free[pi] = k.op("vector", "tensor_relu", hstg[sb][:],
                                             p[:, :], wait=wl, inc=k.sem("V"))
                    elif F2E == 5:
                        psum_free[pi] = k.op("scalar", "copy", hstg[sb][:],
                                             p[:, :], wait=wl, inc=S)
                    elif F2E >= 1:
                        if F2E >= 2:
                            psum_free[pi] = k.op("scalar", "activation", hstg[sb][:],
                                                 p[:, :], AF.Relu, scale=s1c[:, 0:1],
                                                 wait=wl, inc=S)
                        else:
                            psum_free[pi] = k.op("scalar", "activation", hstg[sb][:],
                                                 p[:, :], AF.Relu,
                                                 wait=wl, inc=S)
                        if F2E >= 3:
                            stg_d[sb] = k.dma(
                                "gpsimd",
                                h2_d[(grp * 8 + f2) * 128:(grp * 8 + f2 + 1) * 128,
                                     tt * 512:(tt + 1) * 512],
                                hstg[sb][:], wait=[(S, k.sems["S"].v)])
        k.full_barrier()
        if stop_after == "f2":
            return

        u8t = [es.enter_context(nc.sbuf_tensor(f"l0_u8t{i}", [128, 1024], U8))
               for i in range(2)]
        ud2 = [0, 0]
        for rc in range(32):
            b = rc % 2
            wl = [(S, ud2[b])] if ud2[b] else []
            la = k.dma("sync", u8t[b][:],
                       W["L0_ffn_k2"][rc * 128:(rc + 1) * 128, 0:1024], wait=wl)
            ud2[b] = k.op("scalar", "activation", w1g[rc][:],
                          u8t[b][:], AF.Copy, scale=2.0,
                          bias=-1.0, wait=[la], inc=S)
        k.full_barrier()
        w2 = w1g
        xsb = [es.enter_context(nc.sbuf_tensor(f"l0_xsb{i}", [128, 1024], F32))
               for i in range(4)]
        sfc = es.enter_context(nc.sbuf_tensor("l0_sfc", [128, 512], BF16))
        gt = es.enter_context(nc.sbuf_tensor("l0_gt", [128, 512], BF16))
        sg = es.enter_context(nc.sbuf_tensor("l0_sg", [128, 512], F32))
        psum_free = [0, 0]
        x_st = None
        for tt in range(8):
            gh = k.new_group("sync")
            for rc in range(32):
                wl = [(T, k.sems["T"].v)] if (tt and rc == 0) else []
                hv = k.dma("sync", h1t[rc][:],
                           h2_d[rc * 128:(rc + 1) * 128, tt * 512:(tt + 1) * 512],
                           gh, wait=wl)
            gx = k.new_group("sync")
            for tb in range(4):
                wl = [x_st] if (x_st and tb == 0) else []
                xl = k.dma("sync", xsb[tb][:],
                           x_dram[tt * 512 + tb * 128:tt * 512 + (tb + 1) * 128, :],
                           gx, wait=wl)
            k.wait("tensor", hv[0], hv[1])
            k.wait("vector", xl[0], xl[1])
            for fc in range(8):
                pi = fc % 2
                p = C.P[pi]
                sl = k.dma("sync", sfc[:],
                           s_d[fc * 128:(fc + 1) * 128, tt * 512:(tt + 1) * 512],
                           wait=[(V, k.sems["V"].v)])
                for kc in range(32):
                    wl = [(S, psum_free[pi])] if kc == 0 else []
                    k.op("tensor", "matmul", p[:, :],
                         w2[kc][:, fc * 128:(fc + 1) * 128], h1t[kc][:],
                         start=(kc == 0), stop=(kc == 31), wait=wl,
                         inc=(T if kc == 31 else None))
                psum_free[pi] = k.op("scalar", "activation", gt[:], p[:, :],
                                     AF.Silu, scale=s2c[:, 0:1],
                                     wait=[(T, k.sems["T"].v)], inc=S)
                mv = k.op("vector", "tensor_mul", sg[:], gt[:], sfc[:],
                          wait=[(S, k.sems["S"].v), sl], inc=V)
                for tb in range(4):
                    pt = C.P[4 + tb % 2]
                    tv = k.op("tensor", "transpose", pt[:, 0:128],
                              sg[:, tb * 128:(tb + 1) * 128],
                              C.sb["iden_f32"][:], wait=[(V, k.sems["V"].v)],
                              inc=T)
                    k.op("vector", "tensor_add",
                         xsb[tb][:, fc * 128:(fc + 1) * 128],
                         xsb[tb][:, fc * 128:(fc + 1) * 128], pt[:, 0:128],
                         wait=[(T, tv)], inc=V)
                k.wait("tensor", V, k.sems["V"].v)
            for tb in range(4):
                x_st = k.dma("gpsimd",
                             xout_dram[tt * 512 + tb * 128:tt * 512 + (tb + 1) * 128, :],
                             xsb[tb][:], wait=[(V, k.sems["V"].v)])
        k.full_barrier()


L2_WSPECS = {
    "L2_W_hgT": ([1024, 3072], F32),
    "L2_attn_b_hg": ([3072], F32),
    "L2_W_outT": ([1536, 1024], F32),
    "L2_attn_b_out": ([1024], F32),
    "L2_attn_h0": ([1536], F32),
    "L2_w1T": ([1024, 4096], F32),
    "L2_ffn_w1_s": ([4096], F32),
    "L2_w2T": ([1024, 4096], F32),
    "L2_ffn_w2_s": ([4096], F32),
    "L2_h1T": ([4096, 4096], F32),
    "L2_ffn_h1_s": ([4096], F32),
    "L2_h2T": ([4096, 4096], F32),
    "L2_ffn_h2_s": ([4096], F32),
    "L2_w3T": ([4096, 1024], F32),
    "L2_ffn_w3_s": ([1024], F32),
}


def emit_layer2(k: K, C: Ctx, W, x_dram, xout_dram, scratch, stop_after="all"):
    nc = k.nc
    T, V, S = k.sem("T"), k.sem("V"), k.sem("S")
    xnfm = scratch["xnfm"]
    s_d = scratch["s_fm"]
    h1_d = scratch["h1"]
    h2_d = scratch["h2"]

    emit_ln(k, C, x_dram, xnfm)
    if stop_after == "ln":
        return

    with contextlib.ExitStack() as es:
        whg = [es.enter_context(nc.sbuf_tensor(f"l2_whg{c}", [128, 3072], BF16))
               for c in range(8)]
        wf = [es.enter_context(nc.sbuf_tensor(f"l2_wf{i}", [128, 3072], F32))
              for i in range(2)]
        wf_done = [0, 0]
        for c in range(8):
            b = c % 2
            wl = [(V, wf_done[b])] if wf_done[b] else []
            lv = k.dma("sync", wf[b][:], W["L2_W_hgT"][c * 128:(c + 1) * 128, :],
                       wait=wl)
            wf_done[b] = k.op("vector", "tensor_copy", whg[c][:], wf[b][:],
                              wait=[lv], inc=V)
        bhg_col, _ = C.colize("l2_bhg", es, W["L2_attn_b_hg"], 3072)
        bout_col, _ = C.colize("l2_bout", es, W["L2_attn_b_out"], 1024)
        h0_col, h0r = C.colize("l2_h0", es, W["L2_attn_h0"], 1536)
        ginit = es.enter_context(nc.sbuf_tensor("l2_ginit", [128, 12], F32))
        sv = k.op("scalar", "activation", ginit[:], h0_col[:], AF.Sigmoid,
                  wait=[h0r], inc=S)
        k.op("vector", "scalar_tensor_tensor", ginit[:], h0_col[:], 0.5,
             ginit[:], AL.add, AL.max, wait=[(S, sv)], inc=V)
        wout = [es.enter_context(nc.sbuf_tensor(f"l2_wo{c}", [128, 1024], BF16))
                for c in range(12)]
        for c in range(12):
            b = c % 2
            wl = [(V, wf_done[b])] if wf_done[b] else []
            lv = k.dma("sync", wf[b][:, 0:1024],
                       W["L2_W_outT"][c * 128:(c + 1) * 128, :], wait=wl)
            wf_done[b] = k.op("vector", "tensor_copy", wout[c][:],
                              wf[b][:, 0:1024], wait=[lv], inc=V)
        k.full_barrier()

        xnt = [[es.enter_context(nc.sbuf_tensor(f"l2_xnt{i}_{c}", [128, 512], BF16))
                for c in range(8)] for i in range(2)]
        gate = es.enter_context(nc.sbuf_tensor("l2_gate", [128, 2048], F32))
        hid = es.enter_context(nc.sbuf_tensor("l2_hid", [128, 2048], F32))
        sgm = es.enter_context(nc.sbuf_tensor("l2_sgm", [128, 2048], F32))
        hsc = es.enter_context(nc.sbuf_tensor("l2_hsc", [128, 2048], F32))
        hout = es.enter_context(nc.sbuf_tensor("l2_hout", [128, 2048], BF16))
        hmt = [es.enter_context(nc.sbuf_tensor(f"l2_hm{c}", [128, 512], BF16))
               for c in range(12)]
        sstg = [es.enter_context(nc.sbuf_tensor(f"l2_ss{i}", [128, 512], BF16))
                for i in range(2)]
        xn_used = [k.sems["T"].v, k.sems["T"].v]
        hout_store = None

        def mm_into(p, fbase, tt, b, nk=8, tiles=None, psfree=0):
            tiles = tiles if tiles is not None else whg
            for kc in range(nk):
                wl = []
                if kc == 0 and psfree:
                    wl = [(V, psfree)]
                k.op("tensor", "matmul", p[:, :],
                     tiles[kc][:, fbase:fbase + 128],
                     xnt[b][kc][:], start=(kc == 0), stop=(kc == nk - 1),
                     wait=wl, inc=(T if kc == nk - 1 else None))
            return k.sems["T"].v

        for sq_i in range(SEQ_PER_CORE):
            base_t = sq_i * SEQ
            for j in range(12):
                ps_free = k.sems["V"].v
                for tt in range(4):
                    b = tt % 2
                    gld = k.new_group("sync")
                    lv = None
                    for kc in range(8):
                        lv = k.dma("sync", xnt[b][kc][:],
                                   xnfm[kc * 128:(kc + 1) * 128,
                                        base_t + tt * 512:base_t + (tt + 1) * 512],
                                   gld,
                                   wait=[(T, xn_used[b])] if kc == 0 else [])
                    k.wait("tensor", lv[0], lv[1])
                    tv = mm_into(C.P[tt % 2], j * 128, tt, b, psfree=ps_free)
                    k.op("vector", "tensor_scalar_add",
                         hid[:, tt * 512:(tt + 1) * 512], C.P[tt % 2][:, :],
                         bhg_col[:, j:j + 1], wait=[(T, tv)])
                    tv = mm_into(C.P[2 + tt % 2], 1536 + j * 128, tt, b)
                    xn_used[b] = tv
                    ps_free = k.op("vector", "tensor_scalar_add",
                                   gate[:, tt * 512:(tt + 1) * 512],
                                   C.P[2 + tt % 2][:, :],
                                   bhg_col[:, 12 + j:13 + j], wait=[(T, tv)],
                                   inc=V)
                sv = k.op("scalar", "activation", sgm[:], gate[:], AF.Sigmoid,
                          wait=[(V, ps_free)], inc=S)
                k.op("vector", "tensor_scalar", gate[:], sgm[:], -1.0, 1.0,
                     AL.mult, AL.add, wait=[(S, sv)])
                sv = k.op("scalar", "activation", hsc[:], hid[:], AF.Sigmoid,
                          inc=S)
                k.op("vector", "scalar_tensor_tensor", hsc[:], hid[:], 0.5,
                     hsc[:], AL.add, AL.max, wait=[(S, sv)])
                k.op("vector", "tensor_mul", hsc[:], sgm[:], hsc[:])
                hm_v = k.op("vector", "tensor_tensor_scan", hout[:], gate[:],
                            hsc[:], ginit[:, j:j + 1], AL.mult, AL.add,
                            wait=[hout_store] if hout_store else [], inc=V)
                hout_store = k.dma("gpsimd",
                                   h2_d[j * 128:(j + 1) * 128, base_t:base_t + SEQ],
                                   hout[:], wait=[(V, hm_v)])
        k.full_barrier()
        psum_free = [k.sems["V"].v] * 2
        stg_done = [None, None]
        for tt in range(8):
            ghm = k.new_group("sync")
            hv = None
            for kc in range(12):
                wl = [(T, k.sems["T"].v)] if (tt and kc == 0) else []
                hv = k.dma("sync", hmt[kc][:],
                           h2_d[kc * 128:(kc + 1) * 128, tt * 512:(tt + 1) * 512],
                           ghm, wait=wl)
            k.wait("tensor", hv[0], hv[1])
            for fc in range(8):
                pi = fc % 2
                p = C.P[pi]
                for kc in range(12):
                    wl = [(V, psum_free[pi])] if kc == 0 else []
                    k.op("tensor", "matmul", p[:, :],
                         wout[kc][:, fc * 128:(fc + 1) * 128],
                         hmt[kc][:],
                         start=(kc == 0), stop=(kc == 11), wait=wl,
                         inc=(T if kc == 11 else None))
                sb = fc % 2
                wl = [(T, k.sems["T"].v)]
                if stg_done[sb]:
                    wl.append(stg_done[sb])
                psum_free[pi] = k.op("vector", "tensor_scalar_add", sstg[sb][:],
                                     p[:, :], bout_col[:, fc:fc + 1], wait=wl,
                                     inc=V)
                stg_done[sb] = k.dma(
                    "gpsimd",
                    s_d[fc * 128:(fc + 1) * 128, tt * 512:(tt + 1) * 512],
                    sstg[sb][:], wait=[(V, k.sems["V"].v)])
        k.full_barrier()
    if stop_after == "mixer":
        return

    with contextlib.ExitStack() as es:
        w1 = [es.enter_context(nc.sbuf_tensor(f"l2_w1_{c}", [128, 4096], BF16))
              for c in range(8)]
        w2 = [es.enter_context(nc.sbuf_tensor(f"l2_w2_{c}", [128, 4096], BF16))
              for c in range(8)]
        wf = [es.enter_context(nc.sbuf_tensor(f"l2_wfb{i}", [128, 2048], F32))
              for i in range(2)]
        xnt2 = [[es.enter_context(nc.sbuf_tensor(f"l2_xf{i}_{c}", [128, 512], BF16))
                 for c in range(8)] for i in range(2)]
        s1col, _ = C.colize("l2_s1", es, W["L2_ffn_w1_s"], 4096)
        s2col, _ = C.colize("l2_s2", es, W["L2_ffn_w2_s"], 4096)
        wf_done = [0, 0]
        for dst_tiles, wname in ((w1, "L2_w1T"), (w2, "L2_w2T")):
            for c in range(8):
                for hh in range(2):
                    b = (c * 2 + hh) % 2
                    wl = [(V, wf_done[b])] if wf_done[b] else []
                    lv = k.dma("sync", wf[b][:],
                               W[wname][c * 128:(c + 1) * 128,
                                        hh * 2048:(hh + 1) * 2048], wait=wl)
                    wf_done[b] = k.op("vector", "tensor_copy",
                                      dst_tiles[c][:, hh * 2048:(hh + 1) * 2048],
                                      wf[b][:], wait=[lv], inc=V)
        k.full_barrier()

        ut = es.enter_context(nc.sbuf_tensor("l2_ut", [128, 512], BF16))
        vt = es.enter_context(nc.sbuf_tensor("l2_vt", [128, 512], BF16))
        hstg = [es.enter_context(nc.sbuf_tensor(f"l2_hs{i}", [128, 512], BF16))
                for i in range(2)]
        psum_free = [0, 0]
        stg_d = [None, None]
        xn_used2 = [k.sems["T"].v, k.sems["T"].v]
        for tt in range(8):
            b = tt % 2
            gld = k.new_group("sync")
            lv = None
            for kc in range(8):
                lv = k.dma("sync", xnt2[b][kc][:],
                           xnfm[kc * 128:(kc + 1) * 128,
                                tt * 512:(tt + 1) * 512], gld,
                           wait=[(T, xn_used2[b])] if kc == 0 else [])
            k.wait("tensor", lv[0], lv[1])
            for fc in range(32):
                pi = fc % 2
                pu, pv = C.P[pi], C.P[2 + pi]
                for kc in range(8):
                    wl = [(V, psum_free[pi])] if kc == 0 else []
                    k.op("tensor", "matmul", pu[:, :],
                         w1[kc][:, fc * 128:(fc + 1) * 128],
                         xnt2[b][kc][:], start=(kc == 0),
                         stop=(kc == 7), wait=wl)
                for kc in range(8):
                    k.op("tensor", "matmul", pv[:, :],
                         w2[kc][:, fc * 128:(fc + 1) * 128],
                         xnt2[b][kc][:], start=(kc == 0),
                         stop=(kc == 7), inc=(T if kc == 7 else None))
                if fc == 31:
                    xn_used2[b] = k.sems["T"].v
                tv = k.sems["T"].v
                sv = k.op("scalar", "activation", ut[:], pu[:, :], AF.Silu,
                          scale=s1col[:, fc:fc + 1], wait=[(T, tv)], inc=S)
                k.op("vector", "tensor_scalar", vt[:], pv[:, :],
                     s2col[:, fc:fc + 1], None, AL.mult, wait=[(T, tv)])
                sb = fc % 2
                wl = [(S, sv)]
                if stg_d[sb]:
                    wl.append(stg_d[sb])
                psum_free[pi] = k.op("vector", "tensor_mul", hstg[sb][:], ut[:],
                                     vt[:], wait=wl, inc=V)
                stg_d[sb] = k.dma(
                    "gpsimd",
                    h1_d[fc * 128:(fc + 1) * 128, tt * 512:(tt + 1) * 512],
                    hstg[sb][:], wait=[(V, k.sems["V"].v)])
        k.full_barrier()
    if stop_after == "f1":
        return

    def big_layer(win_name, scol_name, src_d, dst_d, act_silu=False):
        with contextlib.ExitStack() as es:
            nm = win_name
            scol, _ = C.colize("l2c_" + scol_name, es, W[scol_name], 4096)
            wg = [es.enter_context(nc.sbuf_tensor(f"{nm}_wg{rc}", [128, 1024], BF16))
                  for rc in range(32)]
            ht = [es.enter_context(nc.sbuf_tensor(f"{nm}_ht{rc}", [128, 512], BF16))
                  for rc in range(32)]
            wfs = [es.enter_context(nc.sbuf_tensor(f"{nm}_wfs{i}", [128, 1024], F32))
                   for i in range(2)]
            hstg = [es.enter_context(nc.sbuf_tensor(f"{nm}_hg{i}", [128, 512], BF16))
                    for i in range(2)]
            psum_free = [0, 0]
            stg_d = [None, None]
            for grp in range(4):
                cv_done = [0, 0]
                for rc in range(32):
                    b = rc % 2
                    wl = [(V, cv_done[b])] if cv_done[b] else []
                    lv = k.dma("sync", wfs[b][:],
                               W[win_name][rc * 128:(rc + 1) * 128,
                                           grp * 1024:(grp + 1) * 1024],
                               wait=wl)
                    wl2 = [lv]
                    if grp and rc < 2:
                        wl2.append((T, k.sems["T"].v))
                    cv_done[b] = k.op("vector", "tensor_copy", wg[rc][:],
                                      wfs[b][:], wait=wl2, inc=V)
                wready = k.sems["V"].v
                for tt in range(8):
                    gh = k.new_group("sync")
                    hv = None
                    for rc in range(32):
                        wl = [(T, k.sems["T"].v)] if ((grp or tt) and rc == 0) else []
                        hv = k.dma("sync", ht[rc][:],
                                   src_d[rc * 128:(rc + 1) * 128,
                                         tt * 512:(tt + 1) * 512], gh, wait=wl)
                    k.wait("tensor", hv[0], hv[1])
                    if tt == 0:
                        k.wait("tensor", V, wready)
                    for f2 in range(8):
                        pi = f2 % 2
                        p = C.P[pi]
                        for kc in range(32):
                            wl = []
                            if kc == 0 and psum_free[pi]:
                                wl = [(S, psum_free[pi])]
                            k.op("tensor", "matmul", p[:, :],
                                 wg[kc][:, f2 * 128:(f2 + 1) * 128], ht[kc][:],
                                 start=(kc == 0), stop=(kc == 31), wait=wl,
                                 inc=(T if kc == 31 else None))
                        sb = f2 % 2
                        wl = [(T, k.sems["T"].v)]
                        if stg_d[sb]:
                            wl.append(stg_d[sb])
                        psum_free[pi] = k.op(
                            "scalar", "activation", hstg[sb][:], p[:, :],
                            AF.Gelu, scale=scol[:, grp * 8 + f2:grp * 8 + f2 + 1],
                            wait=wl, inc=S)
                        stg_d[sb] = k.dma(
                            "gpsimd",
                            dst_d[(grp * 8 + f2) * 128:(grp * 8 + f2 + 1) * 128,
                                  tt * 512:(tt + 1) * 512],
                            hstg[sb][:], wait=[(S, k.sems["S"].v)])
            k.full_barrier()

    big_layer("L2_h1T", "L2_ffn_h1_s", h1_d, h2_d)
    if stop_after == "f2":
        return
    big_layer("L2_h2T", "L2_ffn_h2_s", h2_d, h1_d)

    with contextlib.ExitStack() as es:
        s3col, _ = C.colize("l2_s3", es, W["L2_ffn_w3_s"], 1024)
        w3 = [es.enter_context(nc.sbuf_tensor(f"l2_w3_{rc}", [128, 1024], BF16))
              for rc in range(32)]
        wfs = [es.enter_context(nc.sbuf_tensor(f"l2_wf3{i}", [128, 1024], F32))
               for i in range(2)]
        ht = [es.enter_context(nc.sbuf_tensor(f"l2_ht3{rc}", [128, 512], BF16))
              for rc in range(32)]
        xsb = [es.enter_context(nc.sbuf_tensor(f"l2_xsb{i}", [128, 1024], F32))
               for i in range(4)]
        sfc = es.enter_context(nc.sbuf_tensor("l2_sfc", [128, 512], BF16))
        gt = es.enter_context(nc.sbuf_tensor("l2_gt", [128, 512], BF16))
        sg = es.enter_context(nc.sbuf_tensor("l2_sg", [128, 512], F32))
        cv_done = [0, 0]
        for rc in range(32):
            b = rc % 2
            wl = [(V, cv_done[b])] if cv_done[b] else []
            lv = k.dma("sync", wfs[b][:],
                       W["L2_w3T"][rc * 128:(rc + 1) * 128, :], wait=wl)
            cv_done[b] = k.op("vector", "tensor_copy", w3[rc][:], wfs[b][:],
                              wait=[lv], inc=V)
        k.full_barrier()
        psum_free = [0, 0]
        x_st = None
        for tt in range(8):
            gh = k.new_group("sync")
            hv = None
            for rc in range(32):
                wl = [(T, k.sems["T"].v)] if (tt and rc == 0) else []
                hv = k.dma("sync", ht[rc][:],
                           h1_d[rc * 128:(rc + 1) * 128, tt * 512:(tt + 1) * 512],
                           gh, wait=wl)
            gx = k.new_group("sync")
            xl = None
            for tb in range(4):
                wl = [x_st] if (x_st and tb == 0) else []
                xl = k.dma("sync", xsb[tb][:],
                           x_dram[tt * 512 + tb * 128:tt * 512 + (tb + 1) * 128, :],
                           gx, wait=wl)
            k.wait("tensor", hv[0], hv[1])
            k.wait("vector", xl[0], xl[1])
            for fc in range(8):
                pi = fc % 2
                p = C.P[pi]
                sl = k.dma("sync", sfc[:],
                           s_d[fc * 128:(fc + 1) * 128, tt * 512:(tt + 1) * 512],
                           wait=[(V, k.sems["V"].v)])
                for kc in range(32):
                    wl = []
                    if kc == 0 and psum_free[pi]:
                        wl = [(S, psum_free[pi])]
                    k.op("tensor", "matmul", p[:, :],
                         w3[kc][:, fc * 128:(fc + 1) * 128], ht[kc][:],
                         start=(kc == 0), stop=(kc == 31), wait=wl,
                         inc=(T if kc == 31 else None))
                psum_free[pi] = k.op("scalar", "activation", gt[:], p[:, :],
                                     AF.Silu, scale=s3col[:, fc:fc + 1],
                                     wait=[(T, k.sems["T"].v)], inc=S)
                mv = k.op("vector", "tensor_mul", sg[:], gt[:], sfc[:],
                          wait=[(S, k.sems["S"].v), sl], inc=V)
                for tb in range(4):
                    pt = C.P[4 + tb % 2]
                    tv = k.op("tensor", "transpose", pt[:, 0:128],
                              sg[:, tb * 128:(tb + 1) * 128],
                              C.sb["iden_f32"][:], wait=[(V, k.sems["V"].v)],
                              inc=T)
                    k.op("vector", "tensor_add",
                         xsb[tb][:, fc * 128:(fc + 1) * 128],
                         xsb[tb][:, fc * 128:(fc + 1) * 128], pt[:, 0:128],
                         wait=[(T, tv)], inc=V)
                k.wait("tensor", V, k.sems["V"].v)
            for tb in range(4):
                x_st = k.dma("gpsimd",
                             xout_dram[tt * 512 + tb * 128:tt * 512 + (tb + 1) * 128, :],
                             xsb[tb][:], wait=[(V, k.sems["V"].v)])
        k.full_barrier()


L3_WSPECS = {
    "L3_w1T": ([1024, 256], F32),
    "L3_w2T": ([256, 6144], F32),
    "L3_cw1T": ([2048, 256], F32),
    "L3_cw2T": ([256, 2048], F32),
    "L3_Wr": ([1024, 8], F32),
    "L3_WiT": ([8, 1024, 1024], F32),
    "L3_Wo": ([8, 1024, 1024], F32),
}


def emit_layer3(k: K, C: Ctx, W, x_dram, xout_dram, scratch, stop_after="all"):
    nc = k.nc
    T, V, S, P = k.sem("T"), k.sem("V"), k.sem("S"), k.sem("P")
    xnfm = scratch["xnfm"]
    s_d = scratch["s_fm"]
    ym_d = scratch["h1"]
    we_d = scratch["h2"]
    coeff_d = nc.dram_tensor("l3_coeff", [T_CORE, 8], F32, kind="Internal").ap()

    emit_ln(k, C, x_dram, xnfm)
    if stop_after == "ln":
        return

    FEPS = 1.1920929e-07

    with contextlib.ExitStack() as es:
        wr = es.enter_context(nc.sbuf_tensor("l3_wr", [128, 64], BF16))
        wrf = es.enter_context(nc.sbuf_tensor("l3_wrf", [128, 64], F32))
        xt = [[es.enter_context(nc.sbuf_tensor(f"l3_rx{i}_{c}", [128, 128], BF16))
               for c in range(8)] for i in range(2)]
        m1 = es.enter_context(nc.sbuf_tensor("l3_m1", [128, 1], F32))
        nm1 = es.enter_context(nc.sbuf_tensor("l3_nm1", [128, 1], F32))
        m2 = es.enter_context(nc.sbuf_tensor("l3_m2", [128, 1], F32))
        msk = es.enter_context(nc.sbuf_tensor("l3_msk", [128, 8], F32))
        eu = es.enter_context(nc.sbuf_tensor("l3_eu", [128, 8], F32))
        den = es.enter_context(nc.sbuf_tensor("l3_den", [128, 1], F32))
        rcp = es.enter_context(nc.sbuf_tensor("l3_rcp", [128, 1], F32))
        cof = es.enter_context(nc.sbuf_tensor("l3_cof", [128, 8], F32))
        gld = k.new_group("sync")
        lv = k.dma("sync", wrf[:].rearrange("p (c e) -> p c e", e=8),
                   W["L3_Wr"].rearrange("(c p) e -> c p e", p=128).transpose([1, 0, 2]),
                   gld)
        k.op("vector", "tensor_copy", wr[:], wrf[:], wait=[lv], inc=V)
        k.full_barrier()
        xn_used = [k.sems["T"].v, k.sems["T"].v]
        ps_free = [0, 0]
        for i in range(T_CORE // 128):
            b = i % 2
            gx = k.new_group("sync")
            lv = None
            for c in range(8):
                lv = k.dma("sync", xt[b][c][:],
                           xnfm[c * 128:(c + 1) * 128, i * 128:(i + 1) * 128],
                           gx, wait=[(T, xn_used[b])] if c == 0 else [])
            k.wait("tensor", lv[0], lv[1])
            p = C.P[b]
            for c in range(8):
                wl = [(V, ps_free[b])] if (c == 0 and ps_free[b]) else []
                k.op("tensor", "matmul", p[:, 0:8], xt[b][c][:],
                     wr[:, c * 8:(c + 1) * 8], start=(c == 0), stop=(c == 7),
                     wait=wl, inc=(T if c == 7 else None))
            tv = k.sems["T"].v
            xn_used[b] = tv
            k.op("vector", "reduce_max", m1[:], p[:, 0:8], AX.X, wait=[(T, tv)])
            k.op("vector", "reduce_max", nm1[:], p[:, 0:8], AX.X, negate=True)
            k.op("vector", "tensor_single_scalar", msk[:], p[:, 0:8], m1[:],
                 AL.is_ge)
            k.op("vector", "scalar_tensor_tensor", eu[:], msk[:], -1e30, p[:, 0:8],
                 AL.mult, AL.add)
            vv = k.op("vector", "reduce_max", m2[:], eu[:], AX.X, inc=V)
            sv = k.op("scalar", "activation", eu[:], p[:, 0:8], AF.Exp,
                      bias=nm1[:, 0:1], accum_out=den[:], wait=[(V, vv)], inc=S)
            k.op("vector", "reciprocal", rcp[:], den[:], wait=[(S, sv)])
            k.op("vector", "tensor_single_scalar", msk[:], p[:, 0:8], m2[:],
                 AL.is_ge)
            k.op("vector", "tensor_mul", cof[:], msk[:], eu[:])
            ps_free[b] = k.op("vector", "tensor_scalar_mul", cof[:], cof[:],
                              rcp[:], inc=V)
            k.dma("gpsimd", coeff_d[i * 128:(i + 1) * 128, :], cof[:],
                  wait=[(V, ps_free[b])])
        k.full_barrier()

    with contextlib.ExitStack() as es:
        C.load_tables(es, ["l3_cos", "l3_sin"])
        w1 = [es.enter_context(nc.sbuf_tensor(f"l3_w1_{c}", [128, 256], BF16))
              for c in range(8)]
        w2 = [es.enter_context(nc.sbuf_tensor(f"l3_w2_{c}", [128, 6144], BF16))
              for c in range(2)]
        cw1 = [es.enter_context(nc.sbuf_tensor(f"l3_cw1_{c}", [128, 256], BF16))
               for c in range(16)]
        cw2 = [es.enter_context(nc.sbuf_tensor(f"l3_cw2_{c}", [128, 2048], BF16))
               for c in range(2)]
        wfs = [es.enter_context(nc.sbuf_tensor(f"l3_wf{i}", [128, 1024], F32))
               for i in range(2)]
        cv_state = {"b": 0, "done": [0, 0]}

        def load_cast(dst_ap, src_ap):
            n = src_ap.shape[1]
            for o in range(0, n, 1024):
                w = min(1024, n - o)
                b = cv_state["b"]
                cv_state["b"] ^= 1
                wl = [(V, cv_state["done"][b])] if cv_state["done"][b] else []
                lv = k.dma("sync", wfs[b][:, 0:w], src_ap[:, o:o + w], wait=wl)
                cv_state["done"][b] = k.op("vector", "tensor_copy",
                                           dst_ap[:, o:o + w], wfs[b][:, 0:w],
                                           wait=[lv], inc=V)

        for c in range(8):
            load_cast(w1[c][:], W["L3_w1T"][c * 128:(c + 1) * 128, :])
        for c in range(2):
            load_cast(w2[c][:], W["L3_w2T"][c * 128:(c + 1) * 128, :])
        for c in range(16):
            load_cast(cw1[c][:], W["L3_cw1T"][c * 128:(c + 1) * 128, :])
        for c in range(2):
            load_cast(cw2[c][:], W["L3_cw2T"][c * 128:(c + 1) * 128, :])
        k.full_barrier()

        xnt = [[es.enter_context(nc.sbuf_tensor(f"l3_xnt{i}_{c}", [128, 512], BF16))
                for c in range(8)] for i in range(2)]
        rmsn = [es.enter_context(nc.sbuf_tensor(f"l3_rm{c}", [128, 512], BF16))
                for c in range(8)]
        hr = [es.enter_context(nc.sbuf_tensor(f"l3_hr{c}", [128, 2048], BF16))
              for c in range(2)]
        sqv = [es.enter_context(nc.sbuf_tensor(f"l3_sqv{i}", [128, 512], BF16))
               for i in range(2)]
        rrow = es.enter_context(nc.sbuf_tensor("l3_rrow", [1, 2048], F32))
        qf = es.enter_context(nc.sbuf_tensor("l3_qf", [128, 2048], F32))
        kf = es.enter_context(nc.sbuf_tensor("l3_kf", [128, 2048], F32))
        vb = es.enter_context(nc.sbuf_tensor("l3_vb", [128, 2048], BF16))
        skv = es.enter_context(nc.sbuf_tensor("l3_skv", [128, 2048], F32))
        cum = es.enter_context(nc.sbuf_tensor("l3_cum", [128, 2048], F32))
        ym = es.enter_context(nc.sbuf_tensor("l3_ym", [128, 2048], BF16))
        c1 = [es.enter_context(nc.sbuf_tensor(f"l3_c1_{c}", [128, 2048], BF16))
              for c in range(2)]
        ymt = xnt[0] + xnt[1]
        zz = es.enter_context(nc.sbuf_tensor("l3_zz", [128, 2048], F32))
        k.op("vector", "memset", zz[:], 1.0)
        sst = es.enter_context(nc.sbuf_tensor("l3_sst", [128, 512], BF16))
        sgg = es.enter_context(nc.sbuf_tensor("l3_sgg", [128, 512], BF16))
        xnc = es.enter_context(nc.sbuf_tensor("l3_xnc", [128, 512], BF16))
        uss = es.enter_context(nc.sbuf_tensor("l3_uss", [128, 512], F32))

        def rms_norm_fm(src_tiles, n_chunks, dst_tiles, dst_dtype_bf=True,
                        seq_base=0, width=2048, from_psum=False):
            for blk in range(width // 512):
                cs = blk * 512
                pS = C.P[4]
                tmv = [0, 0]
                for c in range(n_chunks):
                    sb2 = c % 2
                    swl = [(V, k.sems["V"].v)]
                    if tmv[sb2]:
                        swl.append((T, tmv[sb2]))
                    sv = k.op("scalar", "activation", sqv[sb2][:, :],
                              src_tiles[c][:, cs:cs + 512], AF.Square,
                              wait=swl, inc=S)
                    wl = [(S, sv)]
                    if c == 0:
                        wl.append((V, k.sems["V"].v))
                    tmv[sb2] = k.op("tensor", "matmul", pS[0:1, :],
                                    C.ones_col_bf[:, :], sqv[sb2][:, :],
                                    start=(c == 0), stop=(c == n_chunks - 1),
                                    wait=wl, inc=T)
                tv = k.sems["T"].v
                k.op("vector", "tensor_scalar", rrow[0:1, cs:cs + 512],
                     pS[0:1, :], 1.0 / (n_chunks * 128), FEPS, AL.mult, AL.add,
                     wait=[(T, tv)])
                k.op("scalar", "activation", rrow[0:1, cs:cs + 512],
                     rrow[0:1, cs:cs + 512], AF.Ln,
                     wait=[(V, k.sems["V"].v)])
                k.op("vector", "tensor_single_scalar", rrow[0:1, cs:cs + 512],
                     rrow[0:1, cs:cs + 512], -0.5, AL.mult,
                     wait=[(S, k.sems["S"].v)])
                sv = k.op("scalar", "activation", rrow[0:1, cs:cs + 512],
                          rrow[0:1, cs:cs + 512], AF.Exp,
                          wait=[(V, k.sems["V"].v)], inc=S)
                import os as _os
                if _os.environ.get("L3DBG2"):
                    dbg3 = nc.dram_tensor(f"dbg_rr{k.sems['T'].v}", [1, 2048], F32,
                                          kind="ExternalOutput").ap()
                    k.dma("gpsimd", dbg3[0:1, 0:512], rrow[0:1, cs:cs + 512],
                          wait=[(S, sv)])
                tv = k.op("tensor", "matmul", C.P[5][:, :], C.ones_row[0:1, :],
                          rrow[0:1, cs:cs + 512], start=True, stop=True,
                          wait=[(S, sv), (V, k.sems["V"].v)], inc=T)
                if _os.environ.get("L3DBG2") and not hasattr(k, "_dbg5"):
                    k._dbg5 = nc.dram_tensor("dbg_p5", [128, 512], F32,
                                             kind="ExternalOutput").ap()
                    vv2 = k.op("vector", "tensor_copy", cum[:, 0:512],
                               C.P[5][:, :], wait=[(T, tv)], inc=k.sem("V"))
                    k.dma("gpsimd", k._dbg5, cum[:, 0:512],
                          wait=[(k.sem("V"), vv2)])
                for c in range(n_chunks):
                    k.op("vector", "tensor_mul", dst_tiles[c][:, cs:cs + 512],
                         src_tiles[c][:, cs:cs + 512], C.P[5][:, :],
                         wait=[(T, tv)] if c == 0 else [], inc=V)

        for sq_i in range(SEQ_PER_CORE):
            base_t = sq_i * SEQ
            xn_used = [k.sems["T"].v, k.sems["T"].v]
            ps_free = k.sems["V"].v
            for tt in range(4):
                b = tt % 2
                gx = k.new_group("sync")
                lv = None
                for c in range(8):
                    lv = k.dma("sync", xnt[b][c][:],
                               xnfm[c * 128:(c + 1) * 128,
                                    base_t + tt * 512:base_t + (tt + 1) * 512],
                               gx, wait=[(T, xn_used[b]), (V, k.sems["V"].v)]
                               if c == 0 else [])
                k.wait("scalar", lv[0], lv[1])
                k.wait("vector", lv[0], lv[1])
                rms_norm_fm([xnt[b][c][:] for c in range(8)], 8,
                            [rmsn[c][:] for c in range(8)], width=512)
                import os as _os
                if _os.environ.get("L3DBG") and tt == 0 and sq_i == 0:
                    dbg2 = nc.dram_tensor("dbg_rms", [2, 128, 512], F32,
                                          kind="ExternalOutput").ap()
                    vv = k.sems["V"].v
                    k.dma("gpsimd", dbg2[0], rmsn[0][:], wait=[(V, vv)])
                    k.full_barrier()
                rms_done = k.sems["V"].v
                for rc in range(2):
                    p = C.P[rc]
                    for kc in range(8):
                        wl = [(V, max(ps_free, rms_done))] if kc == 0 else []
                        k.op("tensor", "matmul", p[:, :],
                             w1[kc][:, rc * 128:(rc + 1) * 128],
                             rmsn[kc][:], start=(kc == 0), stop=(kc == 7),
                             wait=wl, inc=(T if kc == 7 else None))
                    tv = k.sems["T"].v
                    if rc == 1:
                        xn_used[b] = tv
                    k.op("vector", "tensor_single_scalar", qf[:, 0:512], p[:, :],
                         0.0, AL.max, wait=[(T, tv)])
                    ps_free = k.op("vector", "tensor_mul",
                                   hr[rc][:, tt * 512:(tt + 1) * 512],
                                   qf[:, 0:512], qf[:, 0:512], inc=V)
                    if _os.environ.get("L3DBG") and tt == 0 and sq_i == 0 and rc == 0:
                        k.dma("gpsimd", dbg2[1], hr[0][:, 0:512].rearrange("p f -> p f"),
                              wait=[(V, ps_free)])
                        k.full_barrier()
            ym_store = None
            for h in range(16):
                ps_free2 = k.sems["V"].v
                for part, dst in ((0, qf), (1, kf), (2, vb)):
                    fbase = part * 2048 + h * 128
                    for tt in range(4):
                        p = C.P[2 + tt % 2]
                        for kc in range(2):
                            wl = [(V, ps_free2)] if kc == 0 else []
                            k.op("tensor", "matmul", p[:, :],
                                 w2[kc][:, fbase:fbase + 128],
                                 hr[kc][:, tt * 512:(tt + 1) * 512],
                                 start=(kc == 0), stop=(kc == 1), wait=wl,
                                 inc=(T if kc == 1 else None))
                        tv = k.sems["T"].v
                        ps_free2 = k.op("vector", "tensor_copy",
                                        dst[:, tt * 512:(tt + 1) * 512], p[:, :],
                                        wait=[(T, tv)], inc=V)
                import os as _os
                if _os.environ.get("L3DBG") and h == 0 and sq_i == 0:
                    dbg1 = nc.dram_tensor("dbg_qkv", [3, 128, 2048], F32,
                                          kind="ExternalOutput").ap()
                    dbg4 = nc.dram_tensor("dbg_hr", [2, 128, 2048], BF16,
                                          kind="ExternalOutput").ap()
                    dbg6 = nc.dram_tensor("dbg_w2", [2, 128, 256], BF16,
                                          kind="ExternalOutput").ap()
                    vv = k.sems["V"].v
                    k.dma("gpsimd", dbg1[0], qf[:], wait=[(V, vv)])
                    k.dma("gpsimd", dbg1[1], kf[:], wait=[(V, vv)])
                    for rc2 in range(2):
                        k.dma("gpsimd", dbg4[rc2], hr[rc2][:], wait=[(V, vv)])
                        k.dma("gpsimd", dbg6[rc2], w2[rc2][:, 0:256], wait=[(V, vv)])
                    k.full_barrier()
                rms_norm_fm([qf[:]], 1, [qf[:]], width=2048)
                rms_norm_fm([kf[:]], 1, [kf[:]], width=2048)
                if _os.environ.get("L3DBG") and h == 0 and sq_i == 0:
                    vv = k.sems["V"].v
                    k.dma("gpsimd", dbg1[2], qf[:], wait=[(V, vv)])
                    k.full_barrier()
                for z in (qf, kf):
                    cosv, sinv = C.sb["l3_cos"], C.sb["l3_sin"]
                    k.op("vector", "tensor_mul", skv[0:64, :], z[0:64, :],
                         cosv[0:64, :])
                    k.op("vector", "tensor_mul", cum[0:64, :], z[0:64, :],
                         sinv[0:64, :])
                    k.op("vector", "tensor_mul", skv[64:128, :], z[64:128, :],
                         sinv[64:128, :])
                    k.op("vector", "tensor_mul", cum[64:128, :], z[64:128, :],
                         cosv[64:128, :])
                    k.op("vector", "tensor_copy", z[0:64, :], skv[64:128, :])
                    k.op("vector", "tensor_add", z[0:64, :], z[0:64, :],
                         skv[0:64, :])
                    k.op("vector", "tensor_copy", z[64:128, :], cum[0:64, :])
                    k.op("vector", "tensor_sub", z[64:128, :], cum[64:128, :],
                         z[64:128, :])
                sv = k.op("scalar", "activation", kf[:], kf[:], AF.Sigmoid,
                          wait=[(V, k.sems["V"].v)], inc=S)
                k.op("vector", "tensor_mul", skv[:], kf[:], vb[:],
                     wait=[(S, sv)])
                k.op("vector", "tensor_tensor_scan", cum[:], zz[:], skv[:],
                     0.0, AL.mult, AL.add)
                mv = k.op("vector", "scalar_tensor_tensor", ym[:], qf[:], 0.0,
                          cum[:], AL.max, AL.mult,
                          wait=[ym_store] if ym_store else [], inc=V)
                ym_store = k.dma("gpsimd",
                                 ym_d[h * 128:(h + 1) * 128, base_t:base_t + SEQ],
                                 ym[:], wait=[(V, mv)])
            k.full_barrier()
            ps_free = 0
            for rc in range(2):
                for tt in range(4):
                    gy = k.new_group("sync")
                    lv = None
                    for c in range(16):
                        wl = [(T, k.sems["T"].v)] if c == 0 else []
                        lv = k.dma("sync", ymt[c][:],
                                   ym_d[c * 128:(c + 1) * 128,
                                        base_t + tt * 512:base_t + (tt + 1) * 512],
                                   gy, wait=wl)
                    k.wait("tensor", lv[0], lv[1])
                    p = C.P[tt % 2]
                    for kc in range(16):
                        wl = [(V, ps_free)] if (kc == 0 and ps_free) else []
                        k.op("tensor", "matmul", p[:, :],
                             cw1[kc][:, rc * 128:(rc + 1) * 128], ymt[kc][:],
                             start=(kc == 0), stop=(kc == 15), wait=wl,
                             inc=(T if kc == 15 else None))
                    ps_free = k.op("vector", "tensor_copy",
                                   c1[rc][:, tt * 512:(tt + 1) * 512],
                                   C.P[tt % 2][:, :], wait=[(T, k.sems["T"].v)],
                                   inc=V)
            s_store = None
            for tt in range(4):
                for f in range(8):
                    pu, pg = C.P[0 + (f % 2) * 2], C.P[1 + (f % 2) * 2]
                    for kc in range(2):
                        wl = [(V, k.sems["V"].v)] if kc == 0 else []
                        k.op("tensor", "matmul", pu[:, :],
                             cw2[kc][:, f * 128:(f + 1) * 128],
                             c1[kc][:, tt * 512:(tt + 1) * 512],
                             start=(kc == 0), stop=(kc == 1), wait=wl)
                    for kc in range(2):
                        k.op("tensor", "matmul", pg[:, :],
                             cw2[kc][:, 1024 + f * 128:1024 + (f + 1) * 128],
                             c1[kc][:, tt * 512:(tt + 1) * 512],
                             start=(kc == 0), stop=(kc == 1),
                             inc=(T if kc == 1 else None))
                    tv = k.sems["T"].v
                    xl = k.dma("sync", xnc[:],
                               xnfm[f * 128:(f + 1) * 128,
                                    base_t + tt * 512:base_t + (tt + 1) * 512],
                               wait=[(V, k.sems["V"].v)])
                    sv = k.op("scalar", "activation", sgg[:], pg[:, :], AF.Silu,
                              wait=[(T, tv)], inc=S)
                    k.op("vector", "tensor_mul", uss[:], pu[:, :], sgg[:],
                         wait=[(S, sv)])
                    wl = [xl]
                    if s_store:
                        wl.append(s_store)
                    mv = k.op("vector", "tensor_add", sst[:], xnc[:], uss[:],
                              wait=wl, inc=V)
                    s_store = k.dma("gpsimd",
                                    s_d[f * 128:(f + 1) * 128,
                                        base_t + tt * 512:base_t + (tt + 1) * 512],
                                    sst[:], wait=[(V, mv)])
        k.full_barrier()
    if stop_after == "mixer":
        return

    with contextlib.ExitStack() as es:
        wi = [es.enter_context(nc.sbuf_tensor(f"l3_wi{c}", [128, 1024], BF16))
              for c in range(8)]
        wo = [es.enter_context(nc.sbuf_tensor(f"l3_wo{c}", [128, 1024], BF16))
              for c in range(8)]
        wfs = [es.enter_context(nc.sbuf_tensor(f"l3_wfm{i}", [128, 1024], F32))
               for i in range(2)]
        stg = [es.enter_context(nc.sbuf_tensor(f"l3_wst{i}", [128, 512], BF16))
               for i in range(2)]
        cv_done = [0, 0]
        stg_d = [None, None]
        for e in range(8):
            for c in range(8):
                b = c % 2
                wl = [(V, cv_done[b])] if cv_done[b] else []
                if c < 2:
                    wl.append((T, k.sems["T"].v))
                lv = k.dma("sync", wfs[b][:], W["L3_WiT"][e, c * 128:(c + 1) * 128, :],
                           wait=wl)
                cv_done[b] = k.op("vector", "tensor_copy", wi[c][:], wfs[b][:],
                                  wait=[lv], inc=V)
            for c in range(8):
                b = c % 2
                wl = [(V, cv_done[b])]
                lv = k.dma("sync", wfs[b][:], W["L3_Wo"][e, c * 128:(c + 1) * 128, :],
                           wait=wl)
                cv_done[b] = k.op("vector", "tensor_copy", wo[c][:], wfs[b][:],
                                  wait=[lv], inc=V)
            wready = k.sems["V"].v
            k.wait("tensor", V, wready)
            for ic in range(8):
                for ob in range(2):
                    pi = ob
                    p = C.P[pi]
                    for hc in range(8):
                        wl = [(S, k.sems["S"].v)] if hc == 0 else []
                        k.op("tensor", "matmul", p[:, :],
                             wi[hc][:, ic * 128:(ic + 1) * 128],
                             wo[hc][:, ob * 512:(ob + 1) * 512],
                             start=(hc == 0), stop=(hc == 7), wait=wl,
                             inc=(T if hc == 7 else None))
                    sb = ob
                    wl = [(T, k.sems["T"].v)]
                    if stg_d[sb]:
                        wl.append(stg_d[sb])
                    sv = k.op("scalar", "copy", stg[sb][:], p[:, :], wait=wl,
                              inc=S)
                    rbase = (e % 4) * 1024 + ic * 128
                    cbase = (e // 4) * 1024 + ob * 512
                    stg_d[sb] = k.dma(
                        "gpsimd",
                        we_d[rbase:rbase + 128, cbase:cbase + 512],
                        stg[sb][:], wait=[(S, sv)])
        k.full_barrier()

    with contextlib.ExitStack() as es:
        xng = [es.enter_context(nc.sbuf_tensor(f"l3_xg{c}", [128, 1024], BF16))
               for c in range(8)]
        weg = [[es.enter_context(nc.sbuf_tensor(f"l3_we{i}_{c}", [128, 1024], BF16))
                for c in range(8)] for i in range(2)]
        gacc = [es.enter_context(nc.sbuf_tensor(f"l3_ga{tb}", [128, 1024], F32))
                for tb in range(8)]
        coft = [es.enter_context(nc.sbuf_tensor(f"l3_cf{tb}", [128, 8], F32))
                for tb in range(8)]
        gsil = es.enter_context(nc.sbuf_tensor("l3_gs", [128, 1024], BF16))
        ssl = es.enter_context(nc.sbuf_tensor("l3_ssl", [128, 1024], BF16))
        sgf = es.enter_context(nc.sbuf_tensor("l3_sgf", [128, 1024], F32))
        xsb = es.enter_context(nc.sbuf_tensor("l3_xsb", [128, 1024], F32))
        for tg in range(4):
            tbase = tg * 1024
            gx = k.new_group("sync")
            for tb in range(8):
                wl = [(T, k.sems["T"].v), (V, k.sems["V"].v)] if tb == 0 else []
                for c in range(8):
                    lv = k.dma("sync", xng[c][:, tb * 128:(tb + 1) * 128],
                               xnfm[c * 128:(c + 1) * 128,
                                    tbase + tb * 128:tbase + (tb + 1) * 128],
                               gx, wait=wl if c == 0 else [])
                cl = k.dma("sync", coft[tb][:],
                           coeff_d[tbase + tb * 128:tbase + (tb + 1) * 128, :], gx)
            k.wait("tensor", cl[0], cl[1])
            k.wait("vector", cl[0], cl[1])
            we_used = [k.sems["T"].v, k.sems["T"].v]
            for e in range(8):
                eb = e % 2
                gw = k.new_group("sync")
                wv = None
                for c in range(8):
                    rbase = (e % 4) * 1024 + c * 128
                    cbase = (e // 4) * 1024
                    wv = k.dma("sync", weg[eb][c][:],
                               we_d[rbase:rbase + 128, cbase:cbase + 1024],
                               gw, wait=[(T, we_used[eb])] if c == 0 else [])
                k.wait("tensor", wv[0], wv[1])
                for tb in range(8):
                    for ob in range(2):
                        p = C.P[2 * eb + ob]
                        for ic in range(8):
                            wl = [(V, k.sems["V"].v)] if ic == 0 else []
                            k.op("tensor", "matmul", p[:, :],
                                 xng[ic][:, tb * 128:(tb + 1) * 128],
                                 weg[eb][ic][:, ob * 512:(ob + 1) * 512],
                                 start=(ic == 0), stop=(ic == 7), wait=wl,
                                 inc=(T if ic == 7 else None))
                        tv = k.sems["T"].v
                        if tb == 7 and ob == 1:
                            we_used[eb] = tv
                        dst = gacc[tb][:, ob * 512:(ob + 1) * 512]
                        if e == 0:
                            k.op("vector", "tensor_scalar_mul", dst, p[:, :],
                                 coft[tb][:, 0:1],
                                 wait=[(T, tv), (S, k.sems["S"].v)], inc=V)
                        else:
                            k.op("vector", "scalar_tensor_tensor", dst, p[:, :],
                                 coft[tb][:, e:e + 1], dst, AL.mult, AL.add,
                                 wait=[(T, tv)], inc=V)
            x_st = None
            for tb in range(8):
                sv = k.op("scalar", "activation", gsil[:], gacc[tb][:], AF.Silu,
                          wait=[(V, k.sems["V"].v)], inc=S)
                gs = k.new_group("sync")
                sl = None
                for c in range(8):
                    sl = k.dma("sync", ssl[:, c * 128:(c + 1) * 128],
                               s_d[c * 128:(c + 1) * 128,
                                   tbase + tb * 128:tbase + (tb + 1) * 128],
                               gs, wait=[(V, k.sems["V"].v)] if c == 0 else [])
                xl = k.dma("sync", xsb[:],
                           x_dram[tbase + tb * 128:tbase + (tb + 1) * 128, :],
                           wait=[x_st] if x_st else [])
                k.wait("tensor", sl[0], sl[1])
                mvb = [0, 0]
                for c in range(8):
                    pb = c % 2
                    wl = [(V, mvb[pb])] if mvb[pb] else []
                    if c < 2:
                        wl += [(V, k.sems["V"].v), (S, sv)]
                    tv = k.op("tensor", "transpose", C.PBF[pb][:, 0:128],
                              ssl[:, c * 128:(c + 1) * 128], C.sb["iden_bf"][:],
                              wait=wl, inc=T)
                    mvb[pb] = k.op("vector", "tensor_mul",
                                   sgf[:, c * 128:(c + 1) * 128],
                                   gsil[:, c * 128:(c + 1) * 128],
                                   C.PBF[pb][:, 0:128],
                                   wait=[(T, tv)], inc=V)
                vv = k.op("vector", "tensor_add", xsb[:], xsb[:], sgf[:],
                          wait=[xl], inc=V)
                x_st = k.dma("gpsimd",
                             xout_dram[tbase + tb * 128:tbase + (tb + 1) * 128, :],
                             xsb[:], wait=[(V, vv)])
        k.full_barrier()


L1_WSPECS = {
    "L1_attn_Wq_down": ([1024, 1024], F32),
    "L1_attn_Wk_down": ([1024, 1024], F32),
    "L1_attn_Wv_down": ([1024, 1024], F32),
    "L1_attn_Wq_up_c": ([64, 32], F32),
    "L1_attn_Wq_up_e": ([64, 32], F32),
    "L1_attn_Wk_up_c": ([64, 32], F32),
    "L1_attn_Wk_up_e": ([64, 32], F32),
    "L1_attn_Wv_up": ([64, 64], F32),
    "L1_attn_Wc": ([1024, 1024], F32),
    "L1_Wc_r": ([1024, 128], F32),
    "L1_bc": ([128], F32),
    "L1_W1": ([128, 1024, 32], F32),
    "L1_W2T": ([128, 32, 1024], F32),
    "L1_B1T": ([32, 128], F32),
    "L1_B2": ([128, 1024], F32),
}


def emit_layer1(k: K, C: Ctx, W, x_dram, xout_dram, scratch, stop_after="all"):
    nc = k.nc
    T, V, S, P = k.sem("T"), k.sem("V"), k.sem("S"), k.sem("P")
    xnfm = scratch["xnfm"]
    xntok = scratch["xntok"]
    s_d = scratch["s_fm"]
    p_d = nc.dram_tensor("l1_p", [128, 128, 1024], BF16, kind="Internal").ap()
    r_d = nc.dram_tensor("l1_r", [128, 128, 1024], BF16, kind="Internal").ap()
    perm_d = nc.dram_tensor("l1_perm", [128, 128, 32], BF16, kind="Internal").ap()

    emit_ln(k, C, x_dram, xnfm, xntok)
    if stop_after == "ln":
        return

    with contextlib.ExitStack() as es:
        C.load_tables(es, ["l1_cos", "l1_sin"])
        wqd = [es.enter_context(nc.sbuf_tensor(f"l1_wqd{c}", [128, 1024], BF16))
               for c in range(8)]
        wkd = [es.enter_context(nc.sbuf_tensor(f"l1_wkd{c}", [128, 1024], BF16))
               for c in range(8)]
        wvd = [es.enter_context(nc.sbuf_tensor(f"l1_wvd{c}", [128, 1024], BF16))
               for c in range(8)]
        wc = [es.enter_context(nc.sbuf_tensor(f"l1_wc{c}", [128, 1024], BF16))
              for c in range(8)]
        wfs1 = es.enter_context(nc.sbuf_tensor("l1_wf0", [128, 1024], F32))
        ups_f = es.enter_context(nc.sbuf_tensor("l1_upsf", [64, 224], F32))
        upr_f = es.enter_context(nc.sbuf_tensor("l1_uprf", [64, 64], F32))
        ups = es.enter_context(nc.sbuf_tensor("l1_ups", [128, 288], BF16))
        cv_done = 0
        for tiles, name in ((wqd, "L1_attn_Wq_down"), (wkd, "L1_attn_Wk_down"),
                            (wvd, "L1_attn_Wv_down"), (wc, "L1_attn_Wc")):
            for c in range(8):
                wl = [(V, cv_done)] if cv_done else []
                lv = k.dma("sync", wfs1[:], W[name][c * 128:(c + 1) * 128, :],
                           wait=wl)
                cv_done = k.op("vector", "tensor_copy", tiles[c][:], wfs1[:],
                               wait=[lv], inc=V)
        gu = k.new_group("sync")
        lv = None
        for i, nm in enumerate(("L1_attn_Wq_up_c", "L1_attn_Wq_up_e",
                                "L1_attn_Wk_up_c", "L1_attn_Wk_up_e")):
            lv = k.dma("sync", ups_f[:, i * 32:(i + 1) * 32], W[nm], gu)
        lv = k.dma("sync", ups_f[:, 128:192], W["L1_attn_Wv_up"], gu)
        k.wait("vector", lv[0], lv[1])
        k.op("vector", "tensor_single_scalar", ups[0:64, 0:64], ups_f[:, 0:64],
             0.125, AL.mult)
        qe = ups_f[:, 32:64].rearrange("p (j two) -> p j two", two=2)
        qer = ups[0:64, 64:96].rearrange("p (j two) -> p j two", two=2)
        k.op("vector", "tensor_single_scalar", qer[:, :, 0], qe[:, :, 1],
             -0.125, AL.mult)
        k.op("vector", "tensor_single_scalar", qer[:, :, 1], qe[:, :, 0],
             0.125, AL.mult)
        k.op("vector", "tensor_copy", ups[0:64, 96:160], ups_f[:, 64:128])
        ke = ups_f[:, 96:128].rearrange("p (j two) -> p j two", two=2)
        ker = ups[0:64, 160:192].rearrange("p (j two) -> p j two", two=2)
        k.op("vector", "tensor_single_scalar", ker[:, :, 0], ke[:, :, 1],
             -1.0, AL.mult)
        k.op("vector", "tensor_copy", ker[:, :, 1], ke[:, :, 0])
        k.op("vector", "tensor_copy", ups[0:64, 192:256], ups_f[:, 128:192])
        k.op("vector", "tensor_copy", ups[64:128, 0:288], ups[0:64, 0:288])
        k.full_barrier()

        xnt = [[es.enter_context(nc.sbuf_tensor(f"l1_xnt{i}_{c}", [128, 512], BF16))
                for c in range(8)] for i in range(2)]
        ymix = [es.enter_context(nc.sbuf_tensor(f"l1_ym{c}", [128, 2048], BF16))
                for c in range(8)]
        qa = [es.enter_context(nc.sbuf_tensor(f"l1_qa{i}", [65, 2048], BF16))
              for i in range(2)]
        ka = [es.enter_context(nc.sbuf_tensor(f"l1_ka{i}", [65, 2048], BF16))
              for i in range(2)]
        vto = [es.enter_context(nc.sbuf_tensor(f"l1_v{i}", [128, 1040], BF16))
               for i in range(2)]
        qlat = es.enter_context(nc.sbuf_tensor("l1_qlat", [128, 2048], BF16))
        klat = es.enter_context(nc.sbuf_tensor("l1_klat", [128, 2048], BF16))
        vlat = es.enter_context(nc.sbuf_tensor("l1_vlat", [128, 2048], BF16))
        e1 = es.enter_context(nc.sbuf_tensor("l1_e1", [32, 2048], BF16))
        e2 = es.enter_context(nc.sbuf_tensor("l1_e2", [32, 2048], BF16))
        attT = [es.enter_context(nc.sbuf_tensor(f"l1_attT{i}", [128, 512], BF16))
                for i in range(2)]
        sq64 = es.enter_context(nc.sbuf_tensor("l1_sq64", [64, 2048], BF16))
        nrm = es.enter_context(nc.sbuf_tensor("l1_nrm", [1, 2048], F32))
        kmx = es.enter_context(nc.sbuf_tensor("l1_kmx", [1, 1], F32))
        rd = es.enter_context(nc.sbuf_tensor("l1_rd", [1, 512], F32))
        rb = es.enter_context(nc.sbuf_tensor("l1_rb", [64, 512], F32))
        sstg = [es.enter_context(nc.sbuf_tensor(f"l1_ss{i}", [128, 512], BF16))
                for i in range(2)]

        for sq_i in range(SEQ_PER_CORE):
            base_t = sq_i * SEQ
            for hp in range(8):
                xn_used = [k.sems["T"].v, k.sems["T"].v]
                psum_free = [k.sems["V"].v] * 3
                for tt in range(4):
                    b = tt % 2
                    gld = k.new_group("sync")
                    lv = None
                    for kc in range(8):
                        lv = k.dma("sync", xnt[b][kc][:],
                                   xnfm[kc * 128:(kc + 1) * 128,
                                        base_t + tt * 512:base_t + (tt + 1) * 512],
                                   gld,
                                   wait=[(T, xn_used[b])] if kc == 0 else [])
                    k.wait("tensor", lv[0], lv[1])
                    for pi, (tiles, dst) in enumerate(
                            ((wqd, qlat), (wkd, klat), (wvd, vlat))):
                        p = C.P[pi]
                        for kc in range(8):
                            wl = [(V, psum_free[pi])] if kc == 0 else []
                            k.op("tensor", "matmul", p[:, :],
                                 tiles[kc][:, hp * 128:(hp + 1) * 128],
                                 xnt[b][kc][:], start=(kc == 0), stop=(kc == 7),
                                 wait=wl, inc=(T if kc == 7 else None))
                        tv = k.sems["T"].v
                        if pi == 2:
                            xn_used[b] = tv
                        psum_free[pi] = k.op("vector", "tensor_copy",
                                             dst[:, tt * 512:(tt + 1) * 512],
                                             p[:, :], wait=[(T, tv)], inc=V)
                lat_done = k.sems["V"].v
                k.wait("tensor", V, lat_done)
                for hh in range(2):
                    lb = hh * 64
                    for (lat, dsta, ucol, ecol, rcol, ctab) in (
                            (qlat, qa[hh], 0, 32, 64, None),
                            (klat, ka[hh], 96, 128, 160, None)):
                        pc = C.P[0]
                        pe_ = C.P[1]
                        per = C.P[2]
                        for ts in range(4):
                            cs = ts * 512
                            wl = [(V, k.sems["V"].v)]
                            k.op("tensor", "matmul", pc[0:32, :],
                                 ups[lb:lb + 64, ucol:ucol + 32],
                                 lat[lb:lb + 64, cs:cs + 512], start=True,
                                 stop=True, wait=wl)
                            k.op("tensor", "matmul", pe_[0:32, :],
                                 ups[lb:lb + 64, ecol:ecol + 32],
                                 lat[lb:lb + 64, cs:cs + 512], start=True,
                                 stop=True)
                            tv = k.op("tensor", "matmul", per[0:32, :],
                                      ups[lb:lb + 64, rcol:rcol + 32],
                                      lat[lb:lb + 64, cs:cs + 512], start=True,
                                      stop=True, inc=T)
                            k.op("vector", "tensor_copy",
                                 dsta[0:32, cs:cs + 512], pc[0:32, :],
                                 wait=[(T, tv)])
                            k.op("vector", "tensor_copy", e1[:, cs:cs + 512],
                                 pe_[0:32, :])
                            k.op("vector", "tensor_copy", e2[:, cs:cs + 512],
                                 per[0:32, :], inc=V)
                        k.op("vector", "tensor_mul", e1[:], e1[:],
                             C.sb["l1_cos"][:])
                        k.op("vector", "tensor_mul", e2[:], e2[:],
                             C.sb["l1_sin"][:])
                        vv = k.op("vector", "tensor_add", e1[:], e1[:], e2[:],
                                  inc=V)
                        k.op("vector", "tensor_copy", dsta[32:64, :], e1[:],
                             inc=V)
                    vw = [0, 0]
                    for tb in range(16):
                        pv = C.P[3 + tb % 2]
                        wl = [(V, vw[tb % 2])] if vw[tb % 2] else []
                        tv = k.op("tensor", "matmul", pv[0:128, 0:64],
                                  vlat[lb:lb + 64, tb * 128:(tb + 1) * 128],
                                  ups[lb:lb + 64, 192:256], start=True,
                                  stop=True, wait=wl, inc=T)
                        vw[tb % 2] = k.op("vector", "tensor_copy",
                                          vto[hh][:, tb * 65:tb * 65 + 64],
                                          pv[0:128, 0:64], wait=[(T, tv)], inc=V)
                    k.op("vector", "memset",
                         vto[hh][:].rearrange("p (t c) -> p t c", c=65)[:, :, 64:65],
                         1.0, inc=V)
                for hh in range(2):
                    sv = k.op("scalar", "activation", sq64[:, :], ka[hh][0:64, :],
                              AF.Square,
                              wait=[(T, k.sems["T"].v), (V, k.sems["V"].v)], inc=S)
                    cpv = 0
                    for tt in range(4):
                        pst = C.P[3 + tt % 2]
                        wl = [(S, sv)]
                        if tt >= 2:
                            wl.append((V, cpv))
                        tv = k.op("tensor", "matmul", pst[0:1, :],
                                  C.ones_col_bf[0:64, :],
                                  sq64[:, tt * 512:(tt + 1) * 512], start=True,
                                  stop=True, wait=wl, inc=T)
                        cpv = k.op("vector", "tensor_copy",
                                   nrm[0:1, tt * 512:(tt + 1) * 512], pst[0:1, :],
                                   wait=[(T, tv)], inc=V)
                    kv = k.op("vector", "reduce_max", kmx[:], nrm[:], AX.X, inc=V)
                    sv = k.op("scalar", "activation", sq64[:, :], qa[hh][0:64, :],
                              AF.Square, wait=[(T, k.sems["T"].v)], inc=S)
                    for tt in range(4):
                        pst = C.P[3 + tt % 2]
                        wl = [(S, sv), (V, kv)]
                        if tt >= 2:
                            wl.append((V, cpv))
                        tv = k.op("tensor", "matmul", pst[0:1, :],
                                  C.ones_col_bf[0:64, :],
                                  sq64[:, tt * 512:(tt + 1) * 512], start=True,
                                  stop=True, wait=wl, inc=T)
                        cpv = k.op("vector", "tensor_copy",
                                   nrm[0:1, tt * 512:(tt + 1) * 512], pst[0:1, :],
                                   wait=[(T, tv)], inc=V)
                    vv = k.op("vector", "tensor_scalar_mul", nrm[:], nrm[:],
                              kmx[0:1, 0:1], inc=V)
                    k.op("vector", "tensor_single_scalar", nrm[:], nrm[:],
                         1e-30, AL.add, wait=[(V, vv)])
                    k.op("scalar", "activation", nrm[:], nrm[:], AF.Ln,
                         wait=[(V, k.sems["V"].v)])
                    k.op("vector", "tensor_single_scalar", nrm[:], nrm[:], 0.5,
                         AL.mult, wait=[(S, k.sems["S"].v)])
                    sv = k.op("scalar", "activation", nrm[:], nrm[:], AF.Exp,
                              wait=[(V, k.sems["V"].v)], inc=S)
                    k.op("vector", "tensor_scalar", qa[hh][64:65, :], nrm[0:1, :],
                         -1.0, -1.0, AL.mult, AL.add, wait=[(S, sv)])
                    k.op("vector", "memset", ka[hh][64:65, :], 1.0, inc=V)
                aug_ready = k.sems["V"].v
                k.wait("tensor", V, aug_ready)
                for hh in range(2):
                    qh, kh, vh = qa[hh], ka[hh], vto[hh]
                    y_ev = k.sems["V"].v
                    sc_ev = [k.sems["S"].v, k.sems["S"].v]
                    for Q in range(4):
                        for kt in range(4 * Q + 4):
                            pb = kt % 2
                            p = C.P[pb]
                            wl = [(S, sc_ev[pb])] if sc_ev[pb] else []
                            tv = k.op("tensor", "matmul", p[:, :],
                                      kh[0:65, kt * 128:(kt + 1) * 128],
                                      qh[0:65, Q * 512:(Q + 1) * 512],
                                      start=True, stop=True, wait=wl, inc=T)
                            sc_ev[pb] = k.op("scalar", "activation",
                                             attT[pb][:, :], p[:, :], AF.Exp,
                                             wait=[(T, tv)], inc=S)
                            r = kt - 4 * Q
                            src = (S, sc_ev[pb])
                            if r >= 0:
                                mv = k.op("vector", "tensor_tensor",
                                          attT[pb][:, :], attT[pb][:, :],
                                          C.sb[f"trimask{r}"][:, :], AL.mult,
                                          wait=[src], inc=V)
                                src = (V, mv)
                            wl2 = [src]
                            if kt == 0:
                                wl2.append((V, y_ev))
                            k.op("tensor", "matmul", C.P[2][0:65, :],
                                 vh[:, kt * 65:kt * 65 + 65], attT[pb][:, :],
                                 start=(kt == 0), stop=(kt == 4 * Q + 3),
                                 wait=wl2, inc=(T if kt == 4 * Q + 3 else None))
                        tv = k.sems["T"].v
                        rv = k.op("vector", "reciprocal", rd[:],
                                  C.P[2][64:65, :], wait=[(T, tv)], inc=V)
                        tv2 = k.op("tensor", "matmul", C.P[3][0:64, :],
                                   C.ones_row[0:1, 0:64], rd[:], start=True,
                                   stop=True, wait=[(V, rv)], inc=T)
                        cv = k.op("vector", "tensor_copy", rb[:], C.P[3][0:64, :],
                                  wait=[(T, tv2)], inc=V)
                        y_ev = k.op("vector", "tensor_mul",
                                    ymix[hp][hh * 64:hh * 64 + 64,
                                             Q * 512:(Q + 1) * 512],
                                    C.P[2][0:64, :], rb[:], inc=V)
                        k.wait("tensor", V, y_ev)
            psum_free = [k.sems["V"].v] * 2
            stg_done = [None, None]
            for tt in range(4):
                for fc in range(8):
                    pi = fc % 2
                    p = C.P[pi]
                    for kc in range(8):
                        wl = [(V, psum_free[pi])] if kc == 0 else []
                        k.op("tensor", "matmul", p[:, :],
                             wc[kc][:, fc * 128:(fc + 1) * 128],
                             ymix[kc][:, tt * 512:(tt + 1) * 512],
                             start=(kc == 0), stop=(kc == 7), wait=wl,
                             inc=(T if kc == 7 else None))
                    sb = fc % 2
                    wl = [(T, k.sems["T"].v)]
                    if stg_done[sb]:
                        wl.append(stg_done[sb])
                    psum_free[pi] = k.op("vector", "tensor_copy", sstg[sb][:],
                                         p[:, :], wait=wl, inc=V)
                    stg_done[sb] = k.dma(
                        "gpsimd",
                        s_d[fc * 128:(fc + 1) * 128,
                            base_t + tt * 512:base_t + (tt + 1) * 512],
                        sstg[sb][:], wait=[(V, k.sems["V"].v)])
        k.full_barrier()
    if stop_after == "mixer":
        return

    with contextlib.ExitStack() as es:
        C.load_tables(es, ["l1_tie"])
        wcr = [es.enter_context(nc.sbuf_tensor(f"l1_wcr{c}", [128, 128], BF16))
               for c in range(8)]
        wfs = [es.enter_context(nc.sbuf_tensor(f"l1_wfa{i}", [128, 128], F32))
               for i in range(2)]
        bcc, _ = C.colize("l1_bcc", es, W["L1_bc"], 128)
        cv_done = [0, 0]
        for c in range(8):
            b = c % 2
            wl = [(V, cv_done[b])] if cv_done[b] else []
            lv = k.dma("sync", wfs[b][:], W["L1_Wc_r"][c * 128:(c + 1) * 128, :],
                       wait=wl)
            cv_done[b] = k.op("vector", "tensor_copy", wcr[c][:], wfs[b][:],
                              wait=[lv], inc=V)
        xg = [[es.enter_context(nc.sbuf_tensor(f"l1_xg{i}_{c}", [128, 32], BF16))
               for c in range(8)] for i in range(2)]
        xtk = [es.enter_context(nc.sbuf_tensor(f"l1_xtk{i}", [32, 1024], BF16))
               for i in range(2)]
        lg = es.enter_context(nc.sbuf_tensor("l1_lg", [128, 32], F32))
        mrow = es.enter_context(nc.sbuf_tensor("l1_mrow", [128, 1], F32))
        permb = es.enter_context(nc.sbuf_tensor("l1_permb", [128, 32], BF16))
        permt = es.enter_context(nc.sbuf_tensor("l1_permt", [32, 128], BF16))
        pst = [es.enter_context(nc.sbuf_tensor(f"l1_pst{i}", [128, 512], BF16))
               for i in range(2)]
        k.full_barrier()
        xn_used = [k.sems["T"].v, k.sems["T"].v]
        xtk_used = [k.sems["T"].v, k.sems["T"].v]
        prev_write = [None, None]
        for g in range(128):
            b = g % 2
            gld = k.new_group("sync")
            lv = None
            for kc in range(8):
                lv = k.dma("sync", xg[b][kc][:],
                           xnfm[kc * 128:(kc + 1) * 128, g * 32:(g + 1) * 32],
                           gld, wait=[(T, xn_used[b])] if kc == 0 else [])
            xl = k.dma("sync", xtk[b][:],
                       xntok[g * 32:(g + 1) * 32, :], gld,
                       wait=[(T, xtk_used[b])])
            k.wait("tensor", lv[0], lv[1])
            p = C.P[b]
            for kc in range(8):
                wl = [(V, k.sems["V"].v)] if kc == 0 else []
                k.op("tensor", "matmul", p[:, 0:32], wcr[kc][:], xg[b][kc][:],
                     start=(kc == 0), stop=(kc == 7), wait=wl,
                     inc=(T if kc == 7 else None))
            tv = k.sems["T"].v
            xn_used[b] = tv
            k.op("vector", "scalar_tensor_tensor", lg[:], p[:, 0:32],
                 bcc[:, 0:1], C.sb["l1_tie"][:], AL.add, AL.add, wait=[(T, tv)])
            k.op("vector", "reduce_max", mrow[:], lg[:], AX.X)
            pv = k.op("vector", "tensor_single_scalar", permb[:], lg[:],
                      mrow[:, 0:1], AL.is_ge, inc=V)
            k.dma("gpsimd", perm_d[g], permb[:], wait=[(V, pv)])
            tv = k.op("tensor", "transpose", C.PBF[0][0:32, 0:128], permb[:],
                      C.sb["iden_bf"][:], wait=[(V, pv)], inc=T)
            pv2 = k.op("vector", "tensor_copy", permt[:], C.PBF[0][0:32, 0:128],
                       wait=[(T, tv)], inc=V)
            for ob in range(2):
                pp = C.P[2 + ob]
                tv = k.op("tensor", "matmul", pp[:, :], permt[:],
                          xtk[b][:, ob * 512:(ob + 1) * 512], start=True,
                          stop=True,
                          wait=[(V, pv2), xl] if ob == 0 else [], inc=T)
                xtk_used[b] = tv
                wl = [(T, tv)]
                if prev_write[ob]:
                    wl.append(prev_write[ob])
                vv = k.op("vector", "tensor_copy", pst[ob][:], pp[:, :],
                          wait=wl, inc=V)
                prev_write[ob] = k.dma("gpsimd",
                                       p_d[g, :, ob * 512:(ob + 1) * 512],
                                       pst[ob][:], wait=[(V, vv)])
        k.full_barrier()
    if stop_after == "ffn_a":
        return

    with contextlib.ExitStack() as es:
        pfm = [es.enter_context(nc.sbuf_tensor(f"l1_pfm{c}", [128, 128], BF16))
               for c in range(8)]
        w1t = [es.enter_context(nc.sbuf_tensor(f"l1_w1t{c}", [128, 32], BF16))
               for c in range(8)]
        w2t = es.enter_context(nc.sbuf_tensor("l1_w2t", [32, 1024], BF16))
        wfs = [es.enter_context(nc.sbuf_tensor(f"l1_wfb{i}", [128, 32], F32))
               for i in range(2)]
        w2f = es.enter_context(nc.sbuf_tensor("l1_w2f", [32, 1024], F32))
        b1t = es.enter_context(nc.sbuf_tensor("l1_b1t", [32, 128], F32))
        innt = es.enter_context(nc.sbuf_tensor("l1_innt", [32, 128], BF16))
        rst = [es.enter_context(nc.sbuf_tensor(f"l1_rst{i}", [128, 512], BF16))
               for i in range(2)]
        gb1 = k.new_group("sync")
        b1v = k.dma("sync", b1t[:], W["L1_B1T"], gb1)
        k.wait("vector", b1v[0], b1v[1])
        cv_done = [0, 0]
        w2_done = 0
        prev_write = [None, None]
        pfm_used = k.sems["T"].v
        for e in range(128):
            gp = k.new_group("sync")
            pv = None
            for c in range(8):
                pv = k.dma("sync", pfm[c][:],
                           p_d[:, e, c * 128:(c + 1) * 128], gp,
                           wait=[(T, pfm_used)] if c == 0 else [],
                           method="dma_start_transpose")
            gw = k.new_group("sync")
            wv = None
            for c in range(8):
                b = c % 2
                wl = [(V, cv_done[b])] if cv_done[b] else []
                wv = k.dma("sync", wfs[b][:], W["L1_W1"][e, c * 128:(c + 1) * 128, :],
                           gw, wait=wl)
                cv_done[b] = k.op("vector", "tensor_copy", w1t[c][:], wfs[b][:],
                                  wait=[wv], inc=V)
            w2l = k.dma("sync", w2f[:], W["L1_W2T"][e], wait=[(V, w2_done)]
                        if w2_done else [])
            w2_done = k.op("vector", "tensor_copy", w2t[:], w2f[:], wait=[w2l],
                           inc=V)
            k.wait("tensor", pv[0], pv[1])
            k.wait("tensor", V, k.sems["V"].v)
            p = C.P[0]
            for c in range(8):
                k.op("tensor", "matmul", p[0:32, 0:128], w1t[c][:], pfm[c][:],
                     start=(c == 0), stop=(c == 7),
                     inc=(T if c == 7 else None))
            tv = k.sems["T"].v
            pfm_used = tv
            k.op("vector", "tensor_scalar_add", innt[:], p[0:32, 0:128],
                 b1t[:, e:e + 1], wait=[(T, tv)])
            iv = k.op("vector", "tensor_relu", innt[:], innt[:], inc=V)
            for ob in range(2):
                pp = C.P[2 + ob]
                tv = k.op("tensor", "matmul", pp[:, :], innt[:],
                          w2t[:, ob * 512:(ob + 1) * 512], start=True, stop=True,
                          wait=[(V, iv)] if ob == 0 else [], inc=T)
                wl = [(T, tv)]
                if prev_write[ob]:
                    wl.append(prev_write[ob])
                vv = k.op("vector", "tensor_copy", rst[ob][:], pp[:, :],
                          wait=wl, inc=V)
                prev_write[ob] = k.dma("gpsimd",
                                       r_d[e, :, ob * 512:(ob + 1) * 512],
                                       rst[ob][:], wait=[(V, vv)])
        k.full_barrier()
    if stop_after == "ffn_b":
        return

    with contextlib.ExitStack() as es:
        b2t = [es.enter_context(nc.sbuf_tensor(f"l1_b2t{i}", [128, 1024], BF16))
               for i in range(1)]
        b2f = es.enter_context(nc.sbuf_tensor("l1_b2f", [128, 1024], F32))
        rcol = [es.enter_context(nc.sbuf_tensor(f"l1_rc{i}", [128, 1024], BF16))
                for i in range(2)]
        prmg = [es.enter_context(nc.sbuf_tensor(f"l1_pg{i}", [128, 32], BF16))
                for i in range(2)]
        sfc = es.enter_context(nc.sbuf_tensor("l1_sfc", [128, 32], BF16))
        gt = es.enter_context(nc.sbuf_tensor("l1_gt", [128, 32], BF16))
        sg = es.enter_context(nc.sbuf_tensor("l1_sg", [128, 32], F32))
        xsb = es.enter_context(nc.sbuf_tensor("l1_xsb", [32, 1024], F32))
        gb = k.new_group("sync")
        b2l = k.dma("sync", b2f[:], W["L1_B2"], gb)
        k.op("vector", "tensor_copy", b2t[0][:], b2f[:], wait=[b2l], inc=V)
        k.full_barrier()
        x_st = None
        rc_used = [k.sems["T"].v, k.sems["T"].v]
        pg_used = [k.sems["T"].v, k.sems["T"].v]
        for g in range(128):
            b = g % 2
            grp = k.new_group("sync")
            rv = k.dma("sync", rcol[b][:], r_d[:, g, :], grp,
                       wait=[(T, rc_used[b])])
            pgl = k.dma("sync", prmg[b][:], perm_d[g], grp,
                        wait=[(T, pg_used[b])])
            xl = k.dma("sync", xsb[:], x_dram[g * 32:(g + 1) * 32, :],
                       wait=[x_st] if x_st else [])
            k.wait("tensor", pgl[0], pgl[1])
            sgl = k.new_group("sync")
            sv_done = k.sems["V"].v
            for dc in range(8):
                p = C.P[dc % 2]
                sl = k.dma("sync", sfc[:],
                           s_d[dc * 128:(dc + 1) * 128, g * 32:(g + 1) * 32],
                           sgl, wait=[(V, sv_done)])
                k.op("tensor", "matmul", p[:, 0:32],
                     rcol[b][:, dc * 128:(dc + 1) * 128], prmg[b][:],
                     start=True, stop=False,
                     wait=[(V, k.sems["V"].v)] if dc < 2 else [])
                tv = k.op("tensor", "matmul", p[:, 0:32],
                          b2t[0][:, dc * 128:(dc + 1) * 128], prmg[b][:],
                          start=False, stop=True, inc=T)
                if dc >= 6:
                    rc_used[b] = tv
                    pg_used[b] = tv
                sv = k.op("scalar", "activation", gt[:], p[:, 0:32], AF.Silu,
                          wait=[(T, tv)], inc=S)
                mv = k.op("vector", "tensor_mul", sg[:], gt[:], sfc[:],
                          wait=[(S, sv), sl], inc=V)
                sv_done = mv
                pb2 = dc % 2
                tv2 = k.op("tensor", "transpose", C.P[4 + pb2][0:32, 0:128],
                           sg[:], C.sb["iden_f32"][:],
                           wait=[(V, mv)], inc=T)
                k.op("vector", "tensor_add", xsb[:, dc * 128:(dc + 1) * 128],
                     xsb[:, dc * 128:(dc + 1) * 128],
                     C.P[4 + pb2][0:32, 0:128],
                     wait=[(T, tv2), xl] if dc == 0 else [(T, tv2)], inc=V)
            x_st = k.dma("gpsimd", xout_dram[g * 32:(g + 1) * 32, :], xsb[:],
                         wait=[(V, k.sems["V"].v)])
        k.full_barrier()


def scratch_tensors(nc, which, debug=False):
    d = {}
    kind = "ExternalOutput" if debug else "Internal"
    mk = lambda n, sh, dt: nc.dram_tensor(n, sh, dt, kind=kind).ap()
    d["xnfm"] = mk("xnfm", [1024, 4096], BF16)
    d["xntok"] = mk("xntok", [4096, 1024], BF16)
    d["s_fm"] = mk("s_fm", [1024, 4096], BF16)
    if which in ("l0", "all"):
        d["h1"] = mk("h1", [4096, 4096], BF16)
        d["h2"] = mk("h2", [4096, 4096], BF16)
        d["w1eff"] = mk("w1eff", [4096, 4096], BF16)
    return d


def build_l2_program(stop_after="all", debug=False):
    nc = bass.Bass("TRN2", target_bir_lowering=False)
    k = K(nc)
    x_in = nc.dram_tensor("x_in", [T_CORE, DM], F32, kind="ExternalInput").ap()
    x_out = nc.dram_tensor("x_out", [T_CORE, DM], F32, kind="ExternalOutput").ap()
    C = Ctx(k, used=['iden_bf', 'iden_f32', 'trimask0', 'trimask1', 'trimask2', 'trimask3'])
    W = {n: nc.dram_tensor(n, sh, dt, kind="ExternalInput").ap()
         for n, (sh, dt) in L2_WSPECS.items()}
    scr = scratch_tensors(nc, "l0", debug=debug)
    emit_layer2(k, C, W, x_in, x_out, scr, stop_after=stop_after)
    k.finalize()
    return nc


def prep_l2_inputs(w):
    t = lambda a: np.ascontiguousarray(np.asarray(a, np.float32).T)
    c = lambda a: np.ascontiguousarray(np.asarray(a, np.float32))
    return {
        "L2_W_hgT": t(w["L2_attn_W_hg"]),
        "L2_attn_b_hg": c(w["L2_attn_b_hg"]),
        "L2_W_outT": t(w["L2_attn_W_out"]),
        "L2_attn_b_out": c(w["L2_attn_b_out"]),
        "L2_attn_h0": c(w["L2_attn_h0"]),
        "L2_w1T": t(w["L2_ffn_w1_w"]),
        "L2_ffn_w1_s": c(w["L2_ffn_w1_s"]),
        "L2_w2T": t(w["L2_ffn_w2_w"]),
        "L2_ffn_w2_s": c(w["L2_ffn_w2_s"]),
        "L2_h1T": t(w["L2_ffn_h1_w"]),
        "L2_ffn_h1_s": c(w["L2_ffn_h1_s"]),
        "L2_h2T": t(w["L2_ffn_h2_w"]),
        "L2_ffn_h2_s": c(w["L2_ffn_h2_s"]),
        "L2_w3T": t(w["L2_ffn_w3_w"]),
        "L2_ffn_w3_s": c(w["L2_ffn_w3_s"]),
    }


_L2_CACHE = {}


def run_layer2(x_full, inputs, stop_after="all", debug=False, cores=None):
    key = (stop_after, debug)
    if _L2_CACHE.get("key") != key:
        _L2_CACHE["nc"] = build_l2_program(stop_after, debug)
        _L2_CACHE["consts"] = host_consts()
        _L2_CACHE["key"] = key
    nc = _L2_CACHE["nc"]
    wmap = prep_l2_inputs(inputs)
    in_maps = []
    ncores = cores or N_CORES
    for core in range(ncores):
        m = {n: _L2_CACHE["consts"][n] for n in _L2_CACHE["nc"]._used_consts}
        m.update(wmap)
        m["x_in"] = np.ascontiguousarray(
            x_full[core * 2:(core + 1) * 2].reshape(T_CORE, DM), np.float32)
        in_maps.append(m)
    res = run_bass_kernel_spmd(nc, in_maps, core_ids=list(range(ncores)))
    if debug:
        return res
    return np.concatenate(
        [np.asarray(r["x_out"]).reshape(2, 2048, 1024) for r in res.results], 0)


def build_l3_program(stop_after="all", debug=False):
    nc = bass.Bass("TRN2", target_bir_lowering=False)
    k = K(nc)
    x_in = nc.dram_tensor("x_in", [T_CORE, DM], F32, kind="ExternalInput").ap()
    x_out = nc.dram_tensor("x_out", [T_CORE, DM], F32, kind="ExternalOutput").ap()
    C = Ctx(k, used=['iden_bf', 'iden_f32', 'trimask0', 'trimask1', 'trimask2', 'trimask3', 'l3_cos', 'l3_sin'])
    W = {n: nc.dram_tensor(n, sh, dt, kind="ExternalInput").ap()
         for n, (sh, dt) in L3_WSPECS.items()}
    scr = scratch_tensors(nc, "l0", debug=debug)
    emit_layer3(k, C, W, x_in, x_out, scr, stop_after=stop_after)
    k.finalize()
    return nc


def prep_l3_inputs(w):
    t = lambda a: np.ascontiguousarray(np.asarray(a, np.float32).T)
    c = lambda a: np.ascontiguousarray(np.asarray(a, np.float32))
    return {
        "L3_w1T": t(w["L3_attn_qkv_w1"]),
        "L3_w2T": t(w["L3_attn_qkv_w2"]),
        "L3_cw1T": t(w["L3_attn_cproj_w1"]),
        "L3_cw2T": t(w["L3_attn_cproj_w2"]),
        "L3_Wr": c(w["L3_ffn_Wr"]),
        "L3_WiT": np.ascontiguousarray(
            np.asarray(w["L3_ffn_Wi"], np.float32).transpose(0, 2, 1)),
        "L3_Wo": c(w["L3_ffn_Wo"]),
    }


_L3_CACHE = {}


def run_layer3(x_full, inputs, stop_after="all", debug=False, cores=None):
    key = (stop_after, debug)
    if _L3_CACHE.get("key") != key:
        _L3_CACHE["nc"] = build_l3_program(stop_after, debug)
        _L3_CACHE["consts"] = host_consts()
        _L3_CACHE["key"] = key
    nc = _L3_CACHE["nc"]
    wmap = prep_l3_inputs(inputs)
    in_maps = []
    ncores = cores or N_CORES
    for core in range(ncores):
        m = {n: _L3_CACHE["consts"][n] for n in _L3_CACHE["nc"]._used_consts}
        m.update(wmap)
        m["x_in"] = np.ascontiguousarray(
            x_full[core * 2:(core + 1) * 2].reshape(T_CORE, DM), np.float32)
        in_maps.append(m)
    res = run_bass_kernel_spmd(nc, in_maps, core_ids=list(range(ncores)))
    if debug:
        return res
    return np.concatenate(
        [np.asarray(r["x_out"]).reshape(2, 2048, 1024) for r in res.results], 0)


def build_l1_program(stop_after="all", debug=False):
    nc = bass.Bass("TRN2", target_bir_lowering=False)
    k = K(nc)
    x_in = nc.dram_tensor("x_in", [T_CORE, DM], F32, kind="ExternalInput").ap()
    x_out = nc.dram_tensor("x_out", [T_CORE, DM], F32, kind="ExternalOutput").ap()
    C = Ctx(k, used=['iden_bf', 'iden_f32', 'trimask0', 'trimask1', 'trimask2', 'trimask3', 'l1_cos', 'l1_sin', 'l1_tie'])
    W = {n: nc.dram_tensor(n, sh, dt, kind="ExternalInput").ap()
         for n, (sh, dt) in L1_WSPECS.items()}
    scr = scratch_tensors(nc, "l0", debug=debug)
    emit_layer1(k, C, W, x_in, x_out, scr, stop_after=stop_after)
    k.finalize()
    return nc


def prep_l1_inputs(w):
    c = lambda a: np.ascontiguousarray(np.asarray(a, np.float32))
    return {
        "L1_attn_Wq_down": c(w["L1_attn_Wq_down"]),
        "L1_attn_Wk_down": c(w["L1_attn_Wk_down"]),
        "L1_attn_Wv_down": c(w["L1_attn_Wv_down"]),
        "L1_attn_Wq_up_c": c(w["L1_attn_Wq_up_c"]),
        "L1_attn_Wq_up_e": c(w["L1_attn_Wq_up_e"]),
        "L1_attn_Wk_up_c": c(w["L1_attn_Wk_up_c"]),
        "L1_attn_Wk_up_e": c(w["L1_attn_Wk_up_e"]),
        "L1_attn_Wv_up": c(w["L1_attn_Wv_up"]),
        "L1_attn_Wc": c(w["L1_attn_Wc"]),
        "L1_Wc_r": c(w["L1_ffn_Wc"]).reshape(1024, 128),
        "L1_bc": c(w["L1_ffn_bc"]).reshape(128),
        "L1_W1": c(w["L1_ffn_W1"]).reshape(128, 1024, 32),
        "L1_W2T": np.ascontiguousarray(
            np.asarray(w["L1_ffn_W2"], np.float32).reshape(128, 1024, 32)
            .transpose(0, 2, 1)),
        "L1_B1T": np.ascontiguousarray(
            np.asarray(w["L1_ffn_b1"], np.float32).reshape(128, 32).T),
        "L1_B2": c(w["L1_ffn_b2"]).reshape(128, 1024),
    }


_L1_CACHE = {}


def run_layer1(x_full, inputs, stop_after="all", debug=False, cores=None):
    key = (stop_after, debug)
    if _L1_CACHE.get("key") != key:
        _L1_CACHE["nc"] = build_l1_program(stop_after, debug)
        _L1_CACHE["consts"] = host_consts()
        _L1_CACHE["key"] = key
    nc = _L1_CACHE["nc"]
    wmap = prep_l1_inputs(inputs)
    in_maps = []
    ncores = cores or N_CORES
    for core in range(ncores):
        m = {n: _L1_CACHE["consts"][n] for n in _L1_CACHE["nc"]._used_consts}
        m.update(wmap)
        m["x_in"] = np.ascontiguousarray(
            x_full[core * 2:(core + 1) * 2].reshape(T_CORE, DM), np.float32)
        in_maps.append(m)
    res = run_bass_kernel_spmd(nc, in_maps, core_ids=list(range(ncores)))
    if debug:
        return res
    return np.concatenate(
        [np.asarray(r["x_out"]).reshape(2, 2048, 1024) for r in res.results], 0)


def build_l0_program(stop_after="all", debug=False):
    nc = bass.Bass("TRN2", target_bir_lowering=False)
    k = K(nc)
    x_in = nc.dram_tensor("x_in", [T_CORE, DM], F32, kind="ExternalInput").ap()
    x_out = nc.dram_tensor("x_out", [T_CORE, DM], F32, kind="ExternalOutput").ap()
    C = Ctx(k, used=['iden_bf', 'iden_f32', 'trimask0', 'trimask1', 'trimask2', 'trimask3', 'l0_cos', 'l0_sin'])
    W = {n: nc.dram_tensor(n, sh, dt, kind="ExternalInput").ap()
         for n, (sh, dt) in L0_WSPECS.items()}
    scr = scratch_tensors(nc, "l0", debug=debug)
    emit_layer0(k, C, W, x_in, x_out, scr, stop_after=stop_after)
    k.finalize()
    return nc


def prep_l0_inputs(w):
    out = {}
    out["L0_attn_W_attn"] = np.ascontiguousarray(w["L0_attn_W_attn"], np.float32)
    out["L0_attn_b_attn"] = np.ascontiguousarray(w["L0_attn_b_attn"], np.float32)
    out["L0_attn_W_proj"] = np.ascontiguousarray(w["L0_attn_W_proj"], np.float32)
    out["L0_attn_b_proj"] = np.ascontiguousarray(w["L0_attn_b_proj"], np.float32)
    for n in ("L0_ffn_k0", "L0_ffn_k1", "L0_ffn_k2"):
        out[n] = np.ascontiguousarray(np.asarray(w[n])).view(np.uint8)
    for n in ("L0_ffn_s0", "L0_ffn_s1", "L0_ffn_s2"):
        out[n] = np.asarray(w[n], np.float32).reshape(1)
    return out


_L0_CACHE = {}


def run_layer0(x_full, inputs, stop_after="all", debug=False, cores=None):
    key = (stop_after, debug)
    if _L0_CACHE.get("key") != key:
        _L0_CACHE["nc"] = build_l0_program(stop_after, debug)
        _L0_CACHE["consts"] = host_consts()
        _L0_CACHE["key"] = key
    nc = _L0_CACHE["nc"]
    wmap = prep_l0_inputs(inputs)
    in_maps = []
    for core in range(N_CORES):
        m = {n: _L0_CACHE["consts"][n] for n in _L0_CACHE["nc"]._used_consts}
        m.update(wmap)
        m["x_in"] = np.ascontiguousarray(
            x_full[core * 2:(core + 1) * 2].reshape(T_CORE, DM), np.float32)
        in_maps.append(m)
    if cores is not None:
        in_maps = in_maps[:cores]
        res = run_bass_kernel_spmd(nc, in_maps, core_ids=list(range(cores)))
    else:
        res = run_bass_kernel_spmd(nc, in_maps, core_ids=list(range(N_CORES)))
    if debug:
        return res
    return np.concatenate(
        [np.asarray(r["x_out"]).reshape(2, 2048, 1024) for r in res.results], 0)


BASE_CONSTS = ["iden_bf", "iden_f32", "trimask0", "trimask1", "trimask2",
               "trimask3"]


def build_full_program():
    nc = bass.Bass("TRN2", target_bir_lowering=False)
    k = K(nc)
    x_in = nc.dram_tensor("x_in", [T_CORE, DM], F32, kind="ExternalInput").ap()
    x_out = nc.dram_tensor("x_out", [T_CORE, DM], F32, kind="ExternalOutput").ap()
    xa = nc.dram_tensor("x_a", [T_CORE, DM], F32, kind="Internal").ap()
    xb = nc.dram_tensor("x_b", [T_CORE, DM], F32, kind="Internal").ap()
    C = Ctx(k, used=BASE_CONSTS)
    W = {}
    for spec in (L0_WSPECS, L1_WSPECS, L2_WSPECS, L3_WSPECS):
        for n, (sh, dt) in spec.items():
            W[n] = nc.dram_tensor(n, sh, dt, kind="ExternalInput").ap()
    scr = scratch_tensors(nc, "l0")
    emit_layer0(k, C, W, x_in, xa, scr)
    emit_layer1(k, C, W, xa, xb, scr)
    emit_layer2(k, C, W, xb, xa, scr)
    emit_layer3(k, C, W, xa, x_out, scr)
    k.finalize()
    return nc


_FULL_CACHE = {}


def kernel_fused(inputs):
    if "nc" not in _FULL_CACHE:
        _FULL_CACHE["nc"] = build_full_program()
        _FULL_CACHE["consts"] = host_consts()
    nc = _FULL_CACHE["nc"]
    x = np.ascontiguousarray(np.asarray(inputs["L0_attn_x"]),
                             np.float32).reshape(16, 2048, 1024)
    wmap = {}
    wmap.update(prep_l0_inputs(inputs))
    wmap.update(prep_l1_inputs(inputs))
    wmap.update(prep_l2_inputs(inputs))
    wmap.update(prep_l3_inputs(inputs))
    consts = {n: _FULL_CACHE["consts"][n] for n in nc._used_consts}
    in_maps = []
    for core in range(N_CORES):
        m = dict(consts)
        m.update(wmap)
        m["x_in"] = np.ascontiguousarray(x[core * 2:(core + 1) * 2]
                                         .reshape(T_CORE, DM))
        in_maps.append(m)
    res = run_bass_kernel_spmd(nc, in_maps, core_ids=list(range(N_CORES)))
    return np.concatenate(
        [np.asarray(r["x_out"]).reshape(2, 2048, 1024) for r in res.results],
        0).astype(np.float32)


def kernel(**inputs):
    import os
    if os.environ.get("UNFUSED"):
        x = np.ascontiguousarray(np.asarray(inputs["L0_attn_x"]), np.float32)
        x = x.reshape(16, 2048, 1024)
        x = run_layer0(x, inputs)
        x = run_layer1(x, inputs)
        x = run_layer2(x, inputs)
        x = run_layer3(x, inputs)
        return x.astype(np.float32)
    return kernel_fused(inputs)


# version 03
# speedup vs baseline: 1.0463x; 1.0463x over previous
import contextlib
import math
import numpy as np
import ml_dtypes

import concourse.bass as bass
import concourse.mybir as mybir
from concourse.bass_utils import run_bass_kernel_spmd

AF = mybir.ActivationFunctionType
AL = mybir.AluOpType
AX = mybir.AxisListType
F32 = mybir.dt.float32
BF16 = mybir.dt.bfloat16
U8 = mybir.dt.uint8
bfdt = ml_dtypes.bfloat16

N_CORES = 8
SEQ = 2048
DM = 1024
SEQ_PER_CORE = 2
T_CORE = SEQ_PER_CORE * SEQ

ENGINES = ("sync", "tensor", "vector", "scalar", "gpsimd")


class Sem:
    def __init__(self, handle, name):
        self.h = handle
        self.name = name
        self.v = 0


class K:
    def __init__(self, nc):
        self.nc = nc
        self.q = {e: [] for e in ENGINES}
        self.sems = {}

    def sem(self, name) -> Sem:
        if name not in self.sems:
            self.sems[name] = Sem(self.nc.alloc_semaphore(name), name)
        return self.sems[name]

    SELF_SEM = {"vector": "V", "scalar": "S", "gpsimd": "P"}

    def op(self, eng, meth, *args, wait=(), inc=None, **kw):
        waits = [(s.h, v) for (s, v) in wait if v > 0]
        incs = []
        after = None
        auto = self.SELF_SEM.get(eng)
        if meth.startswith("dma_start"):
            auto = None
        if auto is not None:
            ssem = self.sem(auto)
            if ssem.v > 0:
                waits.append((ssem.h, ssem.v))
            ssem.v += 1
            incs.append((ssem.h, 1))
            after = ssem.v
        if inc is not None:
            s, kk = inc if isinstance(inc, tuple) else (inc, 1)
            if auto is not None and s is self.sem(auto):
                pass
            else:
                s.v += kk
                incs.append((s.h, kk))
                after = s.v

        use_drain = bool(incs) and not meth.startswith("dma_start") and (
            (eng == "tensor" and meth in ("matmul", "transpose")) or
            eng in ("vector", "scalar", "gpsimd"))

        def emit(e, meth=meth, args=args, kw=kw, waits=waits, incs=incs,
                 use_drain=use_drain):
            for h, v in waits:
                e.wait_ge(h, v)
            ins = getattr(e, meth)(*args, **kw)
            if use_drain:
                ins = e.drain()
            for h, kk in incs:
                ins.then_inc(h, kk)

        self.q[eng].append(emit)
        return after

    def wait(self, eng, sem, v):
        if v <= 0:
            return
        self.q[eng].append(lambda e, h=sem.h, v=v: e.wait_ge(h, v))

    POOL_SIZE = {"sync": 8, "gpsimd": 6, "scalar": 4}

    def _pool(self, eng):
        if not hasattr(self, "_dma_pools"):
            self._dma_pools = {}
        if eng not in self._dma_pools:
            n = self.POOL_SIZE[eng]
            self._dma_pools[eng] = {
                "sems": [self.sem(f"{eng}_dq{i}") for i in range(n)],
                "idx": 0}
        return self._dma_pools[eng]

    def new_group(self, eng):
        p = self._pool(eng)
        sem = p["sems"][p["idx"]]
        p["idx"] = (p["idx"] + 1) % len(p["sems"])
        return {"sem": sem, "throttle": sem.v, "fresh": True}

    def dma(self, eng, dst, src, g=None, wait=(), method="dma_start", **kw):
        if g is None:
            g = self.new_group(eng)
        wl = list(wait)
        if g["fresh"]:
            if g["throttle"] > 0:
                wl.append((g["sem"], g["throttle"]))
            g["fresh"] = False
        v = self.op(eng, method, wait=wl, inc=(g["sem"], 16), out=dst,
                    in_=src, **kw)
        return (g["sem"], v)

    def milestone(self, eng, sem, k=1):
        return self.op(eng, "nop", inc=(sem, k))

    def engv(self, eng):
        return self.sems[self.SELF_SEM.get(eng, "T")].v

    def full_barrier(self):
        T, V, S, P = self.sem("T"), self.sem("V"), self.sem("S"), self.sem("P")
        tv = self.milestone("tensor", T)
        vv = self.milestone("vector", V)
        sv = self.milestone("scalar", S)
        gv = self.milestone("gpsimd", P)
        targets = [(T, tv), (V, vv), (S, sv), (P, gv)]
        for name, sem in self.sems.items():
            if name.endswith(tuple("01234567")) and "_dq" in name and sem.v > 0:
                targets.append((sem, sem.v))
        for e in ENGINES:
            for sem, v in targets:
                self.wait(e, sem, v)

    def finalize(self):
        with self.nc.Block() as block:
            for name, reg in (("sync", block.sync), ("tensor", block.tensor),
                              ("vector", block.vector), ("scalar", block.scalar),
                              ("gpsimd", block.gpsimd)):
                ops = self.q[name]

                def run(e, ops=ops):
                    for f in ops:
                        f(e)

                reg(run)


def host_consts():
    c = {}
    c["iden_bf"] = np.eye(128, dtype=bfdt)
    c["iden_f32"] = np.eye(128, dtype=np.float32)
    for r in range(4):
        i = np.arange(128)[:, None] + r * 128
        j = np.arange(512)[None, :]
        c[f"trimask{r}"] = (i <= j).astype(bfdt)
    theta = 1.0 / (10000.0 ** (np.arange(0, 64, 2, dtype=np.float64) / 64))
    ang = np.arange(16, dtype=np.float64)[:, None] * theta[None, :]
    c["l0_cos"] = np.repeat(np.cos(ang).astype(np.float32).reshape(1, 512), 128, 0)
    c["l0_sin"] = np.repeat(np.sin(ang).astype(np.float32).reshape(1, 512), 128, 0)
    fr = (1.0 / 1024.0) ** np.linspace(0.0, 1.0, 32)
    fr = np.concatenate([fr, np.zeros(32)])
    th = np.arange(2048, dtype=np.float64)[None, :] * fr[:, None]
    cc = np.cos(th).astype(np.float32)
    ss = np.sin(th).astype(np.float32)
    c["l3_cos"] = np.concatenate([cc, cc], 0)
    c["l3_sin"] = np.concatenate([ss, ss], 0)
    invf = 1.0 / (10000.0 ** (np.arange(0, 32, 2, dtype=np.float64) / 32))
    d_ = np.arange(32)
    fr1 = invf[d_ % 16]
    th1 = np.arange(2048, dtype=np.float64)[None, :] * fr1[:, None]
    c["l1_cos"] = np.cos(th1).astype(np.float32)
    c["l1_sin"] = np.sin(th1).astype(np.float32)
    c["l1_tie"] = np.repeat(np.linspace(0.0, 1e-6, 32, dtype=np.float32
                                        ).reshape(1, 32), 128, 0)
    return c


CONST_SPECS = {
    "iden_bf": ([128, 128], BF16),
    "iden_f32": ([128, 128], F32),
    "trimask0": ([128, 512], BF16),
    "trimask1": ([128, 512], BF16),
    "trimask2": ([128, 512], BF16),
    "trimask3": ([128, 512], BF16),
    "l0_cos": ([128, 512], F32),
    "l0_sin": ([128, 512], F32),
    "l3_cos": ([128, 2048], F32),
    "l3_sin": ([128, 2048], F32),
    "l1_cos": ([32, 2048], F32),
    "l1_sin": ([32, 2048], F32),
    "l1_tie": ([128, 32], F32),
}


class Ctx:
    def __init__(self, k: K, used=None):
        self.k = k
        nc = k.nc
        self.dram = {}
        self.sb = {}
        g = k.new_group("sync")
        items = [(n, CONST_SPECS[n]) for n in (used or CONST_SPECS)]
        nc._used_consts = [n for n, _ in items]
        for n, (sh, dt) in items:
            self.dram[n] = nc.dram_tensor(n, sh, dt, kind="ExternalInput").ap()
            t = nc.alloc_sbuf_tensor("c_" + n, sh, dt)
            self.sb[n] = t
            ld = k.dma("sync", t[:], self.dram[n], g)
        self.ones_col = nc.alloc_sbuf_tensor("ones_col_f32", [128, 1], F32)
        self.ones_col_bf = nc.alloc_sbuf_tensor("ones_col_bf", [128, 1], BF16)
        self.ones_row = nc.alloc_sbuf_tensor("ones_row_f32", [1, 128], F32)
        k.wait("vector", ld[0], ld[1])
        k.op("vector", "memset", self.ones_col[:], 1.0)
        k.op("vector", "memset", self.ones_col_bf[:], 1.0)
        k.op("vector", "memset", self.ones_row[:], 1.0)
        self._immtile = nc.alloc_sbuf_tensor("imm_tile", [128, 8], F32)
        self._immvals = {}
        self.P = [nc.alloc_psum_tensor(f"P512_{i}", [128, 512], F32) for i in range(6)]
        self.PBF = [nc.alloc_psum_tensor(f"PBF_{i}", [128, 1024], BF16) for i in range(2)]
        for v in (1e-5, -0.5, 0.5, 1e-30, 2.0, -1.0, 1.1920929e-07):
            self.imm(v)
        k.full_barrier()

    def imm(self, val):
        if val not in self._immvals:
            i = len(self._immvals)
            assert i < 8
            self.k.op("vector", "memset", self._immtile[:, i:i + 1], float(val))
            self._immvals[val] = i
        i = self._immvals[val]
        return self._immtile[:, i:i + 1]

    def load_tables(self, es, names):
        k = self.k
        nc = k.nc
        g = k.new_group("sync")
        lv = None
        for n in names:
            sh, dt = CONST_SPECS[n]
            if n in self.sb and not isinstance(self.sb[n], tuple):
                continue
            t = es.enter_context(nc.sbuf_tensor("t_" + n + str(k.sems["T"].v),
                                                sh, dt))
            if n not in self.dram:
                self.dram[n] = nc.dram_tensor(n, sh, dt,
                                              kind="ExternalInput").ap()
                nc._used_consts.append(n)
            lv = k.dma("sync", t[:], self.dram[n], g)
            self.sb[n] = t
        if lv is not None:
            for e in ("vector", "scalar", "tensor", "gpsimd"):
                k.wait(e, lv[0], lv[1])

    def bcast_scalar(self, name, es, scal_sb_ap, ready, scale=1.0, add=0.0):
        k = self.k
        T, V = k.sem("T"), k.sem("V")
        out = es.enter_context(k.nc.sbuf_tensor(name, [128, 1], F32))
        S = k.sem("S")
        tv = k.op("tensor", "matmul", self.P[5][:, 0:1], self.ones_row[0:1, :],
                  scal_sb_ap, start=True, stop=True,
                  wait=[ready, (V, V.v), (S, S.v)], inc=T)
        vv = k.op("vector", "tensor_scalar", out[:], self.P[5][:, 0:1], scale, add,
                  AL.mult, AL.add, wait=[(T, tv)], inc=V)
        return out, (V, vv)

    def colize(self, name, es, vec_dram_ap, n_feat, rot_fn=None):
        k = self.k
        nc = k.nc
        DS, T, S, V, DG = (k.sem("DS"), k.sem("T"), k.sem("S"), k.sem("V"),
                           k.sem("DG"))
        nch = n_feat // 128
        if rot_fn is not None:
            row = es.enter_context(nc.sbuf_tensor(name + "_row", [1, n_feat], F32))
            lv = k.dma("sync", row[:], vec_dram_ap.rearrange("(a f) -> a f", a=1))
            row_ap, ready = rot_fn(row, lv)
            scr = nc.dram_tensor(name + "_scr", [n_feat], F32, kind="Internal").ap()
            sv = k.dma("gpsimd", scr.rearrange("(a f) -> a f", a=1), row_ap, wait=[ready])
            src, src_ready = scr, sv
        else:
            src, src_ready = vec_dram_ap, None
        rows = es.enter_context(nc.sbuf_tensor(name + "_rows", [nch, 128], F32))
        lv2 = k.dma("sync", rows[:], src.rearrange("(c p) -> c p", p=128),
                    wait=[src_ready] if src_ready else [])
        out = es.enter_context(nc.sbuf_tensor(name + "_cols", [128, nch], F32))
        tv = k.op("tensor", "transpose", self.P[5][:, 0:nch], rows[:],
                  self.sb["iden_f32"][0:nch, 0:nch],
                  wait=[lv2, (V, V.v), (S, k.sem("S").v)], inc=T)
        sv2 = k.op("scalar", "copy", out[:], self.P[5][:, 0:nch], wait=[(T, tv)],
                   inc=k.sem("S"))
        return out, (k.sem("S"), sv2)


def emit_ln(k: K, C: Ctx, x_dram, xnfm_dram, xntok_dram=None):
    nc = k.nc
    DS, DG = k.sem("DS"), k.sem("DG")
    T, V, S = k.sem("T"), k.sem("V"), k.sem("S")
    pfx = f"ln{len(k.q['vector'])}_"
    with contextlib.ExitStack() as es:
        xt = [es.enter_context(nc.sbuf_tensor(pfx + f"x{i}", [128, 1024], F32))
              for i in range(2)]
        xnb = [es.enter_context(nc.sbuf_tensor(pfx + f"xnb{i}", [128, 1024], BF16))
               for i in range(2)]
        stg = [es.enter_context(nc.sbuf_tensor(pfx + f"stg{i}", [128, 128], BF16))
               for i in range(2)]
        xc = es.enter_context(nc.sbuf_tensor(pfx + "xc", [128, 1024], F32))
        sq = es.enter_context(nc.sbuf_tensor(pfx + "sq", [128, 1024], F32))
        mu = es.enter_context(nc.sbuf_tensor(pfx + "mu", [128, 1], F32))
        ssum = es.enter_context(nc.sbuf_tensor(pfx + "ssum", [128, 1], F32))
        rstd = es.enter_context(nc.sbuf_tensor(pfx + "rstd", [128, 1], F32))
        x_done = [0, 0]
        xn_done_t = [0, 0]
        xn_done_dma = [None, None]
        stg_done = [None, None]
        tr_done = [0, 0]
        for i in range(T_CORE // 128):
            b = i % 2
            lv = k.dma("sync", xt[b][:], x_dram[i * 128:(i + 1) * 128, :],
                       wait=[(V, x_done[b])] if x_done[b] else [])
            k.wait("vector", lv[0], lv[1])
            k.op("vector", "reduce_sum", mu[:], xt[b][:], AX.X, negate=True)
            k.op("vector", "tensor_single_scalar", mu[:], mu[:], 1.0 / 1024.0, AL.mult)
            k.op("vector", "tensor_scalar_add", xc[:], xt[b][:], mu[:])
            x_done[b] = k.op("vector", "tensor_mul", sq[:], xc[:], xc[:], inc=V)
            k.op("vector", "reduce_sum", ssum[:], sq[:], AX.X)
            vv = k.op("vector", "tensor_single_scalar", ssum[:], ssum[:],
                      1.0 / 1024.0, AL.mult, inc=V)
            k.op("scalar", "activation", rstd[:], ssum[:], AF.Ln,
                 bias=C.imm(1e-5), wait=[(V, vv)])
            sv = k.op("scalar", "activation", rstd[:], rstd[:], AF.Exp,
                      scale=C.imm(-0.5), inc=S)
            wlist = [(S, sv)]
            if xn_done_t[b]:
                wlist.append((T, xn_done_t[b]))
            if xn_done_dma[b]:
                wlist.append(xn_done_dma[b])
            xnv = k.op("vector", "tensor_scalar_mul", xnb[b][:], xc[:], rstd[:],
                       wait=wlist, inc=V)
            if xntok_dram is not None:
                xn_done_dma[b] = k.dma("gpsimd",
                                       xntok_dram[i * 128:(i + 1) * 128, :],
                                       xnb[b][:], wait=[(V, xnv)])
            for cdx in range(8):
                pb = cdx % 2
                wl = [(V, xnv)]
                if tr_done[pb]:
                    wl.append((S, tr_done[pb]))
                tv = k.op("tensor", "transpose", C.PBF[pb][:, 0:128],
                          xnb[b][:, cdx * 128:(cdx + 1) * 128],
                          C.sb["iden_bf"][:], wait=wl, inc=T)
                xn_done_t[b] = tv
                sb = cdx % 2
                wl2 = [(T, tv)]
                if stg_done[sb]:
                    wl2.append(stg_done[sb])
                ev = k.op("scalar", "copy", stg[sb][:], C.PBF[pb][:, 0:128],
                          wait=wl2, inc=S)
                tr_done[pb] = ev
                stg_done[sb] = k.dma("gpsimd",
                                     xnfm_dram[cdx * 128:(cdx + 1) * 128,
                                               i * 128:(i + 1) * 128],
                                     stg[sb][:], wait=[(S, ev)])
    k.full_barrier()


L0_WSPECS = {
    "L0_attn_W_attn": ([1024, 3072], F32),
    "L0_attn_b_attn": ([3072], F32),
    "L0_attn_W_proj": ([1024, 1024], F32),
    "L0_attn_b_proj": ([1024], F32),
    "L0_ffn_k0": ([2048, 4096], U8),
    "L0_ffn_k1": ([4096, 4096], U8),
    "L0_ffn_k2": ([4096, 1024], U8),
    "L0_ffn_s0": ([1], F32),
    "L0_ffn_s1": ([1], F32),
    "L0_ffn_s2": ([1], F32),
}


def emit_layer0(k: K, C: Ctx, W, x_dram, xout_dram, scratch, stop_after="all"):
    nc = k.nc
    DS, DG, DA = k.sem("DS"), k.sem("DG"), k.sem("DA")
    T, V, S = k.sem("T"), k.sem("V"), k.sem("S")
    xnfm = scratch["xnfm"]
    s_d = scratch["s_fm"]
    h1_d = scratch["h1"]
    h2_d = scratch["h2"]
    w1e_d = scratch["w1eff"]

    emit_ln(k, C, x_dram, xnfm)
    if stop_after == "ln":
        return

    with contextlib.ExitStack() as es:
        C.load_tables(es, ["l0_cos", "l0_sin"])
        wrot = [es.enter_context(nc.sbuf_tensor(f"l0_wrot{c}", [128, 3072], BF16))
                for c in range(8)]
        wf = [es.enter_context(nc.sbuf_tensor(f"l0_wf{i}", [128, 1024], F32))
              for i in range(2)]
        st1 = es.enter_context(nc.sbuf_tensor("l0_st1", [128, 512], F32))
        st2 = es.enter_context(nc.sbuf_tensor("l0_st2", [128, 512], F32))
        st3 = es.enter_context(nc.sbuf_tensor("l0_st3", [128, 512], F32))
        st4 = es.enter_context(nc.sbuf_tensor("l0_st4", [128, 512], F32))

        def rot_part(E, O, dst_even, dst_odd, nrow, scale):
            cosv = C.sb["l0_cos"][0:nrow, :].rearrange("p (h j) -> p h j", j=32)
            sinv = C.sb["l0_sin"][0:nrow, :].rearrange("p (h j) -> p h j", j=32)
            t1 = st1[0:nrow, :].rearrange("p (h j) -> p h j", j=32)
            t2 = st2[0:nrow, :].rearrange("p (h j) -> p h j", j=32)
            t3 = st3[0:nrow, :].rearrange("p (h j) -> p h j", j=32)
            t4 = st4[0:nrow, :].rearrange("p (h j) -> p h j", j=32)
            k.op("vector", "tensor_tensor", t1, E, cosv, AL.mult)
            k.op("vector", "tensor_tensor", t2, O, sinv, AL.mult)
            k.op("vector", "tensor_tensor", t3, O, cosv, AL.mult)
            k.op("vector", "tensor_tensor", t4, E, sinv, AL.mult)
            k.op("vector", "tensor_sub", t1, t1, t2)
            k.op("vector", "tensor_add", t3, t3, t4)
            k.op("vector", "tensor_single_scalar", dst_even, t1, scale, AL.mult)
            return k.op("vector", "tensor_single_scalar", dst_odd, t3, scale,
                        AL.mult, inc=V)

        wf_done = [0, 0]
        for c in range(8):
            for part in range(3):
                b = (c * 3 + part) % 2
                base = part * 1024
                wl = [(V, wf_done[b])] if wf_done[b] else []
                lv = k.dma("sync", wf[b][:],
                           W["L0_attn_W_attn"][c * 128:(c + 1) * 128,
                                               base:base + 1024], wait=wl)
                k.wait("vector", lv[0], lv[1])
                if part == 2:
                    wf_done[b] = k.op("vector", "tensor_copy",
                                      wrot[c][:, 2048:3072], wf[b][:], inc=V)
                else:
                    src = wf[b][:].rearrange("p (h s j) -> p h s j", s=2, j=32)
                    dst = wrot[c][:, base:base + 1024].rearrange(
                        "p (h s j) -> p h s j", s=2, j=32)
                    scale = 0.125 if part == 0 else 1.0
                    wf_done[b] = rot_part(src[:, :, 0, :], src[:, :, 1, :],
                                          dst[:, :, 0, :], dst[:, :, 1, :],
                                          128, scale)

        def rot_b(row, ready):
            k.wait("vector", ready[0], ready[1])
            for base, scale in ((0, 0.125), (1024, 1.0)):
                src = row[0:1, base:base + 1024].rearrange(
                    "p (h s j) -> p h s j", s=2, j=32)
                rot_part(src[:, :, 0, :], src[:, :, 1, :],
                         src[:, :, 0, :], src[:, :, 1, :], 1, scale)
            rv = k.op("vector", "engine_nop", inc=V)
            return row[0:1, :], (V, rv)

        batt_col, _ = C.colize("l0_batt", es, W["L0_attn_b_attn"], 3072, rot_fn=rot_b)
        bproj_col, _ = C.colize("l0_bproj", es, W["L0_attn_b_proj"], 1024)

        wproj = [es.enter_context(nc.sbuf_tensor(f"l0_wpj{c}", [128, 1024], BF16))
                 for c in range(8)]
        wpf_done = k.sems["V"].v
        wpf_prev = [wpf_done, wpf_done]
        for c in range(8):
            b = c % 2
            wl = [(V, wpf_prev[b])]
            lv = k.dma("sync", wf[b][:], W["L0_attn_W_proj"][c * 128:(c + 1) * 128, :],
                       wait=wl)
            wpf_prev[b] = k.op("vector", "tensor_copy", wproj[c][:], wf[b][:],
                               wait=[lv], inc=V)
        k.full_barrier()
        if stop_after == "prep":
            return

        xnt = [[es.enter_context(nc.sbuf_tensor(f"l0_xnt{i}_{c}", [128, 512], BF16))
                for c in range(8)] for i in range(2)]
        ymix = [es.enter_context(nc.sbuf_tensor(f"l0_ym{c}", [128, 2048], BF16))
                for c in range(8)]
        qa = [es.enter_context(nc.sbuf_tensor(f"l0_qa{i}", [65, 2048], BF16))
              for i in range(2)]
        ka = [es.enter_context(nc.sbuf_tensor(f"l0_ka{i}", [65, 2048], BF16))
              for i in range(2)]
        vto = [es.enter_context(nc.sbuf_tensor(f"l0_v{i}", [128, 1040], BF16))
               for i in range(2)]
        vfm = es.enter_context(nc.sbuf_tensor("l0_vfm", [128, 2048], BF16))
        attT = [es.enter_context(nc.sbuf_tensor(f"l0_attT{i}", [128, 512], BF16))
                for i in range(2)]
        sq64 = es.enter_context(nc.sbuf_tensor("l0_sq64", [64, 2048], BF16))
        nrm = es.enter_context(nc.sbuf_tensor("l0_nrm", [1, 2048], F32))
        kmx = es.enter_context(nc.sbuf_tensor("l0_kmx", [1, 1], F32))
        rd = es.enter_context(nc.sbuf_tensor("l0_rd", [1, 512], F32))
        rb = es.enter_context(nc.sbuf_tensor("l0_rb", [64, 512], F32))
        sstg = [es.enter_context(nc.sbuf_tensor(f"l0_ss{i}", [128, 512], BF16))
                for i in range(2)]

        for sq_i in range(SEQ_PER_CORE):
            base_t = sq_i * SEQ
            for hp in range(8):
                xn_used = [k.sems["T"].v, k.sems["T"].v]
                psum_free = [k.sems["V"].v] * 3
                for tt in range(4):
                    b = tt % 2
                    lv = 0
                    for kc in range(8):
                        lv = k.dma("sync", xnt[b][kc][:],
                                   xnfm[kc * 128:(kc + 1) * 128,
                                        base_t + tt * 512:base_t + (tt + 1) * 512],
                                   wait=[(T, xn_used[b])] if kc == 0 else [])
                    k.wait("tensor", lv[0], lv[1])
                    for pi, fbase in ((0, hp * 128), (1, 1024 + hp * 128),
                                      (2, 2048 + hp * 128)):
                        p = C.P[pi]
                        for kc in range(8):
                            wl = [(V, psum_free[pi])] if kc == 0 else []
                            k.op("tensor", "matmul", p[:, :],
                                 wrot[kc][:, fbase:fbase + 128],
                                 xnt[b][kc][:], start=(kc == 0), stop=(kc == 7),
                                 wait=wl, inc=(T if kc == 7 else None))
                        tv = k.sems["T"].v
                        if pi == 2:
                            xn_used[b] = tv
                        cs = tt * 512
                        if pi < 2:
                            dste, dsto = (qa, ka)[pi][0], (qa, ka)[pi][1]
                            bc = hp if pi == 0 else 8 + hp
                            k.op("vector", "tensor_scalar_add",
                                 dste[0:64, cs:cs + 512], p[0:64, :],
                                 batt_col[0:64, bc:bc + 1], wait=[(T, tv)])
                            psum_free[pi] = k.op(
                                "vector", "tensor_scalar_add",
                                dsto[0:64, cs:cs + 512], p[64:128, :],
                                batt_col[64:128, bc:bc + 1], inc=V)
                        else:
                            psum_free[pi] = k.op(
                                "vector", "tensor_scalar_add", vfm[:, cs:cs + 512],
                                p[:, :], batt_col[:, 16 + hp:17 + hp],
                                wait=[(T, tv)], inc=V)
                k.op("vector", "memset",
                     vto[0][:].rearrange("p (t c) -> p t c", c=65)[:, :, 64:65], 1.0,
                     wait=[(T, k.sems["T"].v)])
                m1 = k.op("vector", "memset",
                          vto[1][:].rearrange("p (t c) -> p t c", c=65)[:, :, 64:65],
                          1.0, inc=V)
                tr_ev = [0, 0]
                for tb in range(16):
                    pb = tb % 2
                    wl = [(V, k.sems["V"].v)]
                    if tr_ev[pb]:
                        wl.append((S, tr_ev[pb]))
                    tv = k.op("tensor", "transpose", C.PBF[pb][:, 0:128],
                              vfm[:, tb * 128:(tb + 1) * 128], C.sb["iden_bf"][:],
                              wait=wl, inc=T)
                    k.op("scalar", "copy", vto[0][:, tb * 65:tb * 65 + 64],
                         C.PBF[pb][0:128, 0:64], wait=[(T, tv)])
                    tr_ev[pb] = k.op("scalar", "copy",
                                     vto[1][:, tb * 65:tb * 65 + 64],
                                     C.PBF[pb][0:128, 64:128], inc=S)
                for hh in range(2):
                    sv = k.op("scalar", "activation", sq64[:, :], ka[hh][0:64, :],
                              AF.Square,
                              wait=[(T, k.sems["T"].v), (V, k.sems["V"].v)], inc=S)
                    cpv = 0
                    for tt in range(4):
                        pst = C.P[3 + tt % 2]
                        wl = [(S, sv)]
                        if tt >= 2:
                            wl.append((V, cpv))
                        tv = k.op("tensor", "matmul", pst[0:1, :],
                                  C.ones_col_bf[0:64, :],
                                  sq64[:, tt * 512:(tt + 1) * 512], start=True,
                                  stop=True, wait=wl, inc=T)
                        cpv = k.op("vector", "tensor_copy",
                                   nrm[0:1, tt * 512:(tt + 1) * 512], pst[0:1, :],
                                   wait=[(T, tv)], inc=V)
                    kv = k.op("vector", "reduce_max", kmx[:], nrm[:], AX.X, inc=V)
                    sv = k.op("scalar", "activation", sq64[:, :], qa[hh][0:64, :],
                              AF.Square, wait=[(T, k.sems["T"].v)], inc=S)
                    for tt in range(4):
                        pst = C.P[3 + tt % 2]
                        wl = [(S, sv), (V, kv)]
                        if tt >= 2:
                            wl.append((V, cpv))
                        tv = k.op("tensor", "matmul", pst[0:1, :],
                                  C.ones_col_bf[0:64, :],
                                  sq64[:, tt * 512:(tt + 1) * 512], start=True,
                                  stop=True, wait=wl, inc=T)
                        cpv = k.op("vector", "tensor_copy",
                                   nrm[0:1, tt * 512:(tt + 1) * 512], pst[0:1, :],
                                   wait=[(T, tv)], inc=V)
                    vv = k.op("vector", "tensor_scalar_mul", nrm[:], nrm[:],
                              kmx[0:1, 0:1], inc=V)
                    k.op("vector", "tensor_single_scalar", nrm[:], nrm[:],
                         1e-30, AL.add, wait=[(V, vv)])
                    k.op("scalar", "activation", nrm[:], nrm[:], AF.Ln,
                         wait=[(V, k.sems["V"].v)])
                    k.op("vector", "tensor_single_scalar", nrm[:], nrm[:], 0.5,
                         AL.mult, wait=[(S, k.sems["S"].v)])
                    sv = k.op("scalar", "activation", nrm[:], nrm[:], AF.Exp,
                              wait=[(V, k.sems["V"].v)], inc=S)
                    k.op("vector", "tensor_scalar", qa[hh][64:65, :], nrm[0:1, :],
                         -1.0, -1.0, AL.mult, AL.add, wait=[(S, sv)])
                    k.op("vector", "memset", ka[hh][64:65, :], 1.0, inc=V)
                aug_ready = k.sems["V"].v
                k.wait("tensor", V, aug_ready)
                for hh in range(2):
                    qh, kh, vh = qa[hh], ka[hh], vto[hh]
                    y_ev = k.sems["V"].v
                    sc_ev = [k.sems["S"].v, k.sems["S"].v]
                    for Q in range(4):
                        for kt in range(4 * Q + 4):
                            pb = kt % 2
                            p = C.P[pb]
                            wl = [(S, sc_ev[pb])] if sc_ev[pb] else []
                            tv = k.op("tensor", "matmul", p[:, :],
                                      kh[0:65, kt * 128:(kt + 1) * 128],
                                      qh[0:65, Q * 512:(Q + 1) * 512],
                                      start=True, stop=True, wait=wl, inc=T)
                            sc_ev[pb] = k.op("scalar", "activation", attT[pb][:, :],
                                             p[:, :], AF.Exp, wait=[(T, tv)], inc=S)
                            r = kt - 4 * Q
                            src = (S, sc_ev[pb])
                            if r >= 0:
                                mv = k.op("vector", "tensor_tensor", attT[pb][:, :],
                                          attT[pb][:, :], C.sb[f"trimask{r}"][:, :],
                                          AL.mult, wait=[src], inc=V)
                                src = (V, mv)
                            wl2 = [src]
                            if kt == 0 and y_ev:
                                wl2.append((V, y_ev))
                            k.op("tensor", "matmul", C.P[2][0:65, :],
                                 vh[:, kt * 65:kt * 65 + 65], attT[pb][:, :],
                                 start=(kt == 0), stop=(kt == 4 * Q + 3),
                                 wait=wl2, inc=(T if kt == 4 * Q + 3 else None))
                        tv = k.sems["T"].v
                        rv = k.op("vector", "reciprocal", rd[:], C.P[2][64:65, :],
                                  wait=[(T, tv)], inc=V)
                        tv2 = k.op("tensor", "matmul", C.P[3][0:64, :],
                                   C.ones_row[0:1, 0:64], rd[:], start=True,
                                   stop=True, wait=[(V, rv)], inc=T)
                        cv = k.op("vector", "tensor_copy", rb[:], C.P[3][0:64, :],
                                  wait=[(T, tv2)], inc=V)
                        y_ev = k.op("vector", "tensor_mul",
                                    ymix[hp][hh * 64:hh * 64 + 64,
                                             Q * 512:(Q + 1) * 512],
                                    C.P[2][0:64, :], rb[:], inc=V)
                        k.wait("tensor", V, y_ev)
            psum_free = [k.sems["V"].v] * 2
            stg_done = [None, None]
            for tt in range(4):
                for fc in range(8):
                    pi = fc % 2
                    p = C.P[pi]
                    for kc in range(8):
                        wl = [(V, psum_free[pi])] if kc == 0 else []
                        k.op("tensor", "matmul", p[:, :],
                             wproj[kc][:, fc * 128:(fc + 1) * 128],
                             ymix[kc][:, tt * 512:(tt + 1) * 512],
                             start=(kc == 0), stop=(kc == 7), wait=wl,
                             inc=(T if kc == 7 else None))
                    sb = fc % 2
                    wl = [(T, k.sems["T"].v)]
                    if stg_done[sb]:
                        wl.append(stg_done[sb])
                    psum_free[pi] = k.op("vector", "tensor_scalar_add", sstg[sb][:],
                                         p[:, :], bproj_col[:, fc:fc + 1], wait=wl,
                                         inc=V)
                    stg_done[sb] = k.dma(
                        "gpsimd",
                        s_d[fc * 128:(fc + 1) * 128,
                            base_t + tt * 512:base_t + (tt + 1) * 512],
                        sstg[sb][:], wait=[(V, k.sems["V"].v)])
        k.full_barrier()
    if stop_after == "mixer":
        return

    with contextlib.ExitStack() as es:
        ssb = es.enter_context(nc.sbuf_tensor("l0_ssb", [1, 3], F32))
        l1 = k.dma("sync", ssb[0:1, 0:1], W["L0_ffn_s0"].rearrange("(a o) -> a o", a=1))
        l2 = k.dma("sync", ssb[0:1, 1:2], W["L0_ffn_s1"].rearrange("(a o) -> a o", a=1))
        l3 = k.dma("sync", ssb[0:1, 2:3], W["L0_ffn_s2"].rearrange("(a o) -> a o", a=1))
        s0c, r0 = C.bcast_scalar("l0_s0c", es, ssb[0:1, 0:1], l1, scale=2.0)
        s1c, r1 = C.bcast_scalar("l0_s1c", es, ssb[0:1, 1:2], l2, scale=2.0)
        s2c, r2 = C.bcast_scalar("l0_s2c", es, ssb[0:1, 2:3], l3, scale=2.0)
        k.wait("scalar", r2[0], r2[1])

        w0 = [es.enter_context(nc.sbuf_tensor(f"l0_w0_{c}", [128, 4096], BF16))
              for c in range(8)]
        u8a = [es.enter_context(nc.sbuf_tensor(f"l0_u8a{i}", [128, 4096], U8))
               for i in range(2)]
        u8b = [es.enter_context(nc.sbuf_tensor(f"l0_u8b{i}", [128, 4096], U8))
               for i in range(2)]
        cva = es.enter_context(nc.sbuf_tensor("l0_cva", [128, 4096], BF16))
        cvb = es.enter_context(nc.sbuf_tensor("l0_cvb", [128, 4096], BF16))
        xnf = [es.enter_context(nc.sbuf_tensor(f"l0_xnf{c}", [128, 4096], BF16))
               for c in range(8)]
        hstg = [es.enter_context(nc.sbuf_tensor(f"l0_hs{i}", [128, 512], BF16))
                for i in range(2)]
        ud = [0, 0]
        for c in range(8):
            b = c % 2
            wl = [(V, ud[b])] if ud[b] else []
            la = k.dma("sync", u8a[b][:], W["L0_ffn_k0"][c * 128:(c + 1) * 128, :],
                       wait=wl)
            lb = k.dma("sync", u8b[b][:],
                       W["L0_ffn_k0"][1024 + c * 128:1024 + (c + 1) * 128, :])
            k.op("scalar", "copy", cva[:], u8a[b][:],
                 wait=[la, lb, (V, k.sems["V"].v)])
            sv = k.op("scalar", "copy", cvb[:], u8b[b][:], inc=S)
            ud[b] = k.op("vector", "tensor_sub", w0[c][:], cva[:], cvb[:],
                         wait=[(S, sv)], inc=V)
        for c in range(8):
            k.dma("sync", xnf[c][:], xnfm[c * 128:(c + 1) * 128, :])
        k.full_barrier()

        psum_free = [0, 0]
        stg_d = [None, None]
        for tt in range(8):
            for fc in range(32):
                pi = fc % 2
                p = C.P[pi]
                for kc in range(8):
                    wl = [(S, psum_free[pi])] if kc == 0 else []
                    k.op("tensor", "matmul", p[:, :],
                         w0[kc][:, fc * 128:(fc + 1) * 128],
                         xnf[kc][:, tt * 512:(tt + 1) * 512], start=(kc == 0),
                         stop=(kc == 7), wait=wl, inc=(T if kc == 7 else None))
                sb = fc % 2
                wl = [(T, k.sems["T"].v)]
                if stg_d[sb]:
                    wl.append(stg_d[sb])
                psum_free[pi] = k.op("scalar", "activation", hstg[sb][:], p[:, :],
                                     AF.Relu, scale=s0c[:, 0:1], wait=wl, inc=S)
                stg_d[sb] = k.dma("gpsimd",
                                  h1_d[fc * 128:(fc + 1) * 128,
                                       tt * 512:(tt + 1) * 512],
                                  hstg[sb][:], wait=[(S, k.sems["S"].v)])
        ud = [0, 0]
        st_d = [None, None]
        w0_prep_done = k.sems["V"].v
        for rc in range(32):
            b = rc % 2
            wl = [(S, ud[b])] if ud[b] else []
            la = k.dma("sync", u8a[b][:], W["L0_ffn_k1"][rc * 128:(rc + 1) * 128, :],
                       wait=wl)
            wl2 = [la, (V, w0_prep_done)]
            if st_d[b]:
                wl2.append(st_d[b])
            ud[b] = k.op("scalar", "activation", cva[:] if b == 0 else cvb[:],
                         u8a[b][:], AF.Copy, scale=2.0, bias=-1.0,
                         wait=wl2, inc=S)
            st_d[b] = k.dma("gpsimd", w1e_d[rc * 128:(rc + 1) * 128, :],
                            cva[:] if b == 0 else cvb[:], wait=[(S, ud[b])])
        k.full_barrier()
    if stop_after == "f1":
        return

    with contextlib.ExitStack() as es:
        s1c = None
        ssb = es.enter_context(nc.sbuf_tensor("l0_ssb2", [1, 2], F32))
        l2 = k.dma("sync", ssb[0:1, 0:1], W["L0_ffn_s1"].rearrange("(a o) -> a o", a=1))
        l3 = k.dma("sync", ssb[0:1, 1:2], W["L0_ffn_s2"].rearrange("(a o) -> a o", a=1))
        s1c, r1 = C.bcast_scalar("l0_s1c2", es, ssb[0:1, 0:1], l2, scale=2.0)
        s2c, r2 = C.bcast_scalar("l0_s2c2", es, ssb[0:1, 1:2], l3, scale=2.0)
        k.wait("scalar", r2[0], r2[1])
        w1g = [es.enter_context(nc.sbuf_tensor(f"l0_w1g{rc}", [128, 1024], BF16))
               for rc in range(32)]
        h1t2 = [[es.enter_context(nc.sbuf_tensor(f"l0_h1t{i}_{rc}", [128, 512],
                                                 BF16)) for rc in range(32)]
                for i in range(2)]
        h1t = h1t2[0]
        h1t_used = [0, 0]
        hstg = [es.enter_context(nc.sbuf_tensor(f"l0_hs2{i}", [128, 512], BF16))
                for i in range(2)]
        import os
        F2G = int(os.environ.get("F2G", "4"))
        F2T = int(os.environ.get("F2T", "8"))
        F2F = int(os.environ.get("F2F", "8"))
        F2K = int(os.environ.get("F2K", "32"))
        psum_free = [0, 0]
        stg_d = [None, None]
        for grp in range(F2G):
            wv = 0
            for rc in range(32):
                wl = [(T, k.sems["T"].v)] if (grp and rc == 0) else []
                wv = k.dma("sync", w1g[rc][:],
                           w1e_d[rc * 128:(rc + 1) * 128,
                                 grp * 1024:(grp + 1) * 1024], wait=wl)
            for tt in range(F2T):
                hb = (grp * F2T + tt) % 2
                h1t = h1t2[hb]
                gh = k.new_group("sync")
                for rc in range(32):
                    wl = [(T, h1t_used[hb])] if (rc == 0 and h1t_used[hb]) else []
                    hv = k.dma("sync", h1t[rc][:],
                               h1_d[rc * 128:(rc + 1) * 128,
                                    tt * 512:(tt + 1) * 512], gh, wait=wl)
                k.wait("tensor", hv[0], hv[1])
                if tt == 0:
                    k.wait("tensor", wv[0], wv[1])
                if os.environ.get("F2B"):
                    k.full_barrier()
                if os.environ.get("F2D") and grp == 0 and tt == 0:
                    dbg_w = nc.dram_tensor("dbg_w1g", [32, 128, 1024], BF16,
                                           kind="ExternalOutput").ap()
                    dbg_h = nc.dram_tensor("dbg_h1t", [32, 128, 512], BF16,
                                           kind="ExternalOutput").ap()
                    for rc in range(32):
                        k.dma("gpsimd", dbg_w[rc], w1g[rc][:])
                        k.dma("gpsimd", dbg_h[rc], h1t[rc][:])
                    k.full_barrier()
                for f2 in range(F2F):
                    pi = f2 % 2
                    p = C.P[pi]
                    for kc in range(F2K):
                        wl = [(S, psum_free[pi])] if kc == 0 else []
                        k.op("tensor", "matmul", p[:, :],
                             w1g[kc][:, f2 * 128:(f2 + 1) * 128], h1t[kc][:],
                             start=(kc == 0), stop=(kc == F2K - 1), wait=wl,
                             inc=(T if kc == F2K - 1 else None))
                    h1t_used[hb] = k.sems["T"].v
                    sb = f2 % 2
                    wl = [(T, k.sems["T"].v)]
                    if stg_d[sb]:
                        wl.append(stg_d[sb])
                    F2E = int(os.environ.get("F2E", "3"))
                    if F2E == 9:
                        k.full_barrier()
                        psum_free[pi] = k.op("scalar", "copy", hstg[sb][:],
                                             p[:, :], inc=S)
                    elif F2E == 6:
                        psum_free[pi] = k.op("vector", "tensor_relu", hstg[sb][:],
                                             C.P[2][:, :], wait=wl, inc=k.sem("V"))
                    elif F2E == 7:
                        psum_free[pi] = k.op("vector", "tensor_relu", hstg[sb][:],
                                             hstg[sb][:], wait=wl, inc=k.sem("V"))
                    elif F2E == 4:
                        psum_free[pi] = k.op("vector", "tensor_relu", hstg[sb][:],
                                             p[:, :], wait=wl, inc=k.sem("V"))
                    elif F2E == 5:
                        psum_free[pi] = k.op("scalar", "copy", hstg[sb][:],
                                             p[:, :], wait=wl, inc=S)
                    elif F2E >= 1:
                        if F2E >= 2:
                            psum_free[pi] = k.op("scalar", "activation", hstg[sb][:],
                                                 p[:, :], AF.Relu, scale=s1c[:, 0:1],
                                                 wait=wl, inc=S)
                        else:
                            psum_free[pi] = k.op("scalar", "activation", hstg[sb][:],
                                                 p[:, :], AF.Relu,
                                                 wait=wl, inc=S)
                        if F2E >= 3:
                            stg_d[sb] = k.dma(
                                "gpsimd",
                                h2_d[(grp * 8 + f2) * 128:(grp * 8 + f2 + 1) * 128,
                                     tt * 512:(tt + 1) * 512],
                                hstg[sb][:], wait=[(S, k.sems["S"].v)])
        k.full_barrier()
        if stop_after == "f2":
            return

        u8t = [es.enter_context(nc.sbuf_tensor(f"l0_u8t{i}", [128, 1024], U8))
               for i in range(2)]
        ud2 = [0, 0]
        for rc in range(32):
            b = rc % 2
            wl = [(S, ud2[b])] if ud2[b] else []
            la = k.dma("sync", u8t[b][:],
                       W["L0_ffn_k2"][rc * 128:(rc + 1) * 128, 0:1024], wait=wl)
            ud2[b] = k.op("scalar", "activation", w1g[rc][:],
                          u8t[b][:], AF.Copy, scale=2.0,
                          bias=-1.0, wait=[la], inc=S)
        k.full_barrier()
        w2 = w1g
        xsb = [es.enter_context(nc.sbuf_tensor(f"l0_xsb{i}", [128, 1024], F32))
               for i in range(4)]
        sfc = es.enter_context(nc.sbuf_tensor("l0_sfc", [128, 512], BF16))
        gt = es.enter_context(nc.sbuf_tensor("l0_gt", [128, 512], BF16))
        sg = es.enter_context(nc.sbuf_tensor("l0_sg", [128, 512], F32))
        psum_free = [0, 0]
        x_st = None
        h1t = h1t2[0]
        for tt in range(8):
            gh = k.new_group("sync")
            for rc in range(32):
                wl = [(T, k.sems["T"].v)] if (rc == 0) else []
                hv = k.dma("sync", h1t[rc][:],
                           h2_d[rc * 128:(rc + 1) * 128, tt * 512:(tt + 1) * 512],
                           gh, wait=wl)
            gx = k.new_group("sync")
            for tb in range(4):
                wl = [x_st] if (x_st and tb == 0) else []
                xl = k.dma("sync", xsb[tb][:],
                           x_dram[tt * 512 + tb * 128:tt * 512 + (tb + 1) * 128, :],
                           gx, wait=wl)
            k.wait("tensor", hv[0], hv[1])
            k.wait("vector", xl[0], xl[1])
            for fc in range(8):
                pi = fc % 2
                p = C.P[pi]
                sl = k.dma("sync", sfc[:],
                           s_d[fc * 128:(fc + 1) * 128, tt * 512:(tt + 1) * 512],
                           wait=[(V, k.sems["V"].v)])
                for kc in range(32):
                    wl = [(S, psum_free[pi])] if kc == 0 else []
                    k.op("tensor", "matmul", p[:, :],
                         w2[kc][:, fc * 128:(fc + 1) * 128], h1t[kc][:],
                         start=(kc == 0), stop=(kc == 31), wait=wl,
                         inc=(T if kc == 31 else None))
                psum_free[pi] = k.op("scalar", "activation", gt[:], p[:, :],
                                     AF.Silu, scale=s2c[:, 0:1],
                                     wait=[(T, k.sems["T"].v)], inc=S)
                mv = k.op("vector", "tensor_mul", sg[:], gt[:], sfc[:],
                          wait=[(S, k.sems["S"].v), sl], inc=V)
                for tb in range(4):
                    pt = C.P[4 + tb % 2]
                    tv = k.op("tensor", "transpose", pt[:, 0:128],
                              sg[:, tb * 128:(tb + 1) * 128],
                              C.sb["iden_f32"][:], wait=[(V, k.sems["V"].v)],
                              inc=T)
                    k.op("vector", "tensor_add",
                         xsb[tb][:, fc * 128:(fc + 1) * 128],
                         xsb[tb][:, fc * 128:(fc + 1) * 128], pt[:, 0:128],
                         wait=[(T, tv)], inc=V)
                k.wait("tensor", V, k.sems["V"].v)
            for tb in range(4):
                x_st = k.dma("gpsimd",
                             xout_dram[tt * 512 + tb * 128:tt * 512 + (tb + 1) * 128, :],
                             xsb[tb][:], wait=[(V, k.sems["V"].v)])
        k.full_barrier()


L2_WSPECS = {
    "L2_W_hgT": ([1024, 3072], F32),
    "L2_attn_b_hg": ([3072], F32),
    "L2_W_outT": ([1536, 1024], F32),
    "L2_attn_b_out": ([1024], F32),
    "L2_attn_h0": ([1536], F32),
    "L2_w1T": ([1024, 4096], F32),
    "L2_ffn_w1_s": ([4096], F32),
    "L2_w2T": ([1024, 4096], F32),
    "L2_ffn_w2_s": ([4096], F32),
    "L2_h1T": ([4096, 4096], F32),
    "L2_ffn_h1_s": ([4096], F32),
    "L2_h2T": ([4096, 4096], F32),
    "L2_ffn_h2_s": ([4096], F32),
    "L2_w3T": ([4096, 1024], F32),
    "L2_ffn_w3_s": ([1024], F32),
}


def emit_layer2(k: K, C: Ctx, W, x_dram, xout_dram, scratch, stop_after="all"):
    nc = k.nc
    T, V, S = k.sem("T"), k.sem("V"), k.sem("S")
    xnfm = scratch["xnfm"]
    s_d = scratch["s_fm"]
    h1_d = scratch["h1"]
    h2_d = scratch["h2"]

    emit_ln(k, C, x_dram, xnfm)
    if stop_after == "ln":
        return

    with contextlib.ExitStack() as es:
        whg = [es.enter_context(nc.sbuf_tensor(f"l2_whg{c}", [128, 3072], BF16))
               for c in range(8)]
        wf = [es.enter_context(nc.sbuf_tensor(f"l2_wf{i}", [128, 3072], F32))
              for i in range(2)]
        wf_done = [0, 0]
        for c in range(8):
            b = c % 2
            wl = [(V, wf_done[b])] if wf_done[b] else []
            lv = k.dma("sync", wf[b][:], W["L2_W_hgT"][c * 128:(c + 1) * 128, :],
                       wait=wl)
            wf_done[b] = k.op("vector", "tensor_copy", whg[c][:], wf[b][:],
                              wait=[lv], inc=V)
        bhg_col, _ = C.colize("l2_bhg", es, W["L2_attn_b_hg"], 3072)
        bout_col, _ = C.colize("l2_bout", es, W["L2_attn_b_out"], 1024)
        h0_col, h0r = C.colize("l2_h0", es, W["L2_attn_h0"], 1536)
        ginit = es.enter_context(nc.sbuf_tensor("l2_ginit", [128, 12], F32))
        sv = k.op("scalar", "activation", ginit[:], h0_col[:], AF.Sigmoid,
                  wait=[h0r], inc=S)
        k.op("vector", "scalar_tensor_tensor", ginit[:], h0_col[:], 0.5,
             ginit[:], AL.add, AL.max, wait=[(S, sv)], inc=V)
        wout = [es.enter_context(nc.sbuf_tensor(f"l2_wo{c}", [128, 1024], BF16))
                for c in range(12)]
        for c in range(12):
            b = c % 2
            wl = [(V, wf_done[b])] if wf_done[b] else []
            lv = k.dma("sync", wf[b][:, 0:1024],
                       W["L2_W_outT"][c * 128:(c + 1) * 128, :], wait=wl)
            wf_done[b] = k.op("vector", "tensor_copy", wout[c][:],
                              wf[b][:, 0:1024], wait=[lv], inc=V)
        k.full_barrier()

        xnt = [[es.enter_context(nc.sbuf_tensor(f"l2_xnt{i}_{c}", [128, 512], BF16))
                for c in range(8)] for i in range(2)]
        gate = es.enter_context(nc.sbuf_tensor("l2_gate", [128, 2048], F32))
        hid = es.enter_context(nc.sbuf_tensor("l2_hid", [128, 2048], F32))
        sgm = es.enter_context(nc.sbuf_tensor("l2_sgm", [128, 2048], F32))
        hsc = es.enter_context(nc.sbuf_tensor("l2_hsc", [128, 2048], F32))
        hout = es.enter_context(nc.sbuf_tensor("l2_hout", [128, 2048], BF16))
        hmt = [es.enter_context(nc.sbuf_tensor(f"l2_hm{c}", [128, 512], BF16))
               for c in range(12)]
        sstg = [es.enter_context(nc.sbuf_tensor(f"l2_ss{i}", [128, 512], BF16))
                for i in range(2)]
        xn_used = [k.sems["T"].v, k.sems["T"].v]
        hout_store = None

        def mm_into(p, fbase, tt, b, nk=8, tiles=None, psfree=0):
            tiles = tiles if tiles is not None else whg
            for kc in range(nk):
                wl = []
                if kc == 0 and psfree:
                    wl = [(V, psfree)]
                k.op("tensor", "matmul", p[:, :],
                     tiles[kc][:, fbase:fbase + 128],
                     xnt[b][kc][:], start=(kc == 0), stop=(kc == nk - 1),
                     wait=wl, inc=(T if kc == nk - 1 else None))
            return k.sems["T"].v

        for sq_i in range(SEQ_PER_CORE):
            base_t = sq_i * SEQ
            for j in range(12):
                ps_free = k.sems["V"].v
                for tt in range(4):
                    b = tt % 2
                    gld = k.new_group("sync")
                    lv = None
                    for kc in range(8):
                        lv = k.dma("sync", xnt[b][kc][:],
                                   xnfm[kc * 128:(kc + 1) * 128,
                                        base_t + tt * 512:base_t + (tt + 1) * 512],
                                   gld,
                                   wait=[(T, xn_used[b])] if kc == 0 else [])
                    k.wait("tensor", lv[0], lv[1])
                    tv = mm_into(C.P[tt % 2], j * 128, tt, b, psfree=ps_free)
                    k.op("vector", "tensor_scalar_add",
                         hid[:, tt * 512:(tt + 1) * 512], C.P[tt % 2][:, :],
                         bhg_col[:, j:j + 1], wait=[(T, tv)])
                    tv = mm_into(C.P[2 + tt % 2], 1536 + j * 128, tt, b)
                    xn_used[b] = tv
                    ps_free = k.op("vector", "tensor_scalar_add",
                                   gate[:, tt * 512:(tt + 1) * 512],
                                   C.P[2 + tt % 2][:, :],
                                   bhg_col[:, 12 + j:13 + j], wait=[(T, tv)],
                                   inc=V)
                sv = k.op("scalar", "activation", sgm[:], gate[:], AF.Sigmoid,
                          wait=[(V, ps_free)], inc=S)
                k.op("vector", "tensor_scalar", gate[:], sgm[:], -1.0, 1.0,
                     AL.mult, AL.add, wait=[(S, sv)])
                sv = k.op("scalar", "activation", hsc[:], hid[:], AF.Sigmoid,
                          inc=S)
                k.op("vector", "scalar_tensor_tensor", hsc[:], hid[:], 0.5,
                     hsc[:], AL.add, AL.max, wait=[(S, sv)])
                k.op("vector", "tensor_mul", hsc[:], sgm[:], hsc[:])
                hm_v = k.op("vector", "tensor_tensor_scan", hout[:], gate[:],
                            hsc[:], ginit[:, j:j + 1], AL.mult, AL.add,
                            wait=[hout_store] if hout_store else [], inc=V)
                hout_store = k.dma("gpsimd",
                                   h2_d[j * 128:(j + 1) * 128, base_t:base_t + SEQ],
                                   hout[:], wait=[(V, hm_v)])
        k.full_barrier()
        psum_free = [k.sems["V"].v] * 2
        stg_done = [None, None]
        for tt in range(8):
            ghm = k.new_group("sync")
            hv = None
            for kc in range(12):
                wl = [(T, k.sems["T"].v)] if (tt and kc == 0) else []
                hv = k.dma("sync", hmt[kc][:],
                           h2_d[kc * 128:(kc + 1) * 128, tt * 512:(tt + 1) * 512],
                           ghm, wait=wl)
            k.wait("tensor", hv[0], hv[1])
            for fc in range(8):
                pi = fc % 2
                p = C.P[pi]
                for kc in range(12):
                    wl = [(V, psum_free[pi])] if kc == 0 else []
                    k.op("tensor", "matmul", p[:, :],
                         wout[kc][:, fc * 128:(fc + 1) * 128],
                         hmt[kc][:],
                         start=(kc == 0), stop=(kc == 11), wait=wl,
                         inc=(T if kc == 11 else None))
                sb = fc % 2
                wl = [(T, k.sems["T"].v)]
                if stg_done[sb]:
                    wl.append(stg_done[sb])
                psum_free[pi] = k.op("vector", "tensor_scalar_add", sstg[sb][:],
                                     p[:, :], bout_col[:, fc:fc + 1], wait=wl,
                                     inc=V)
                stg_done[sb] = k.dma(
                    "gpsimd",
                    s_d[fc * 128:(fc + 1) * 128, tt * 512:(tt + 1) * 512],
                    sstg[sb][:], wait=[(V, k.sems["V"].v)])
        k.full_barrier()
    if stop_after == "mixer":
        return

    with contextlib.ExitStack() as es:
        w1 = [es.enter_context(nc.sbuf_tensor(f"l2_w1_{c}", [128, 4096], BF16))
              for c in range(8)]
        w2 = [es.enter_context(nc.sbuf_tensor(f"l2_w2_{c}", [128, 4096], BF16))
              for c in range(8)]
        wf = [es.enter_context(nc.sbuf_tensor(f"l2_wfb{i}", [128, 2048], F32))
              for i in range(2)]
        xnt2 = [[es.enter_context(nc.sbuf_tensor(f"l2_xf{i}_{c}", [128, 512], BF16))
                 for c in range(8)] for i in range(2)]
        s1col, _ = C.colize("l2_s1", es, W["L2_ffn_w1_s"], 4096)
        s2col, _ = C.colize("l2_s2", es, W["L2_ffn_w2_s"], 4096)
        wf_done = [0, 0]
        for dst_tiles, wname in ((w1, "L2_w1T"), (w2, "L2_w2T")):
            for c in range(8):
                for hh in range(2):
                    b = (c * 2 + hh) % 2
                    wl = [(V, wf_done[b])] if wf_done[b] else []
                    lv = k.dma("sync", wf[b][:],
                               W[wname][c * 128:(c + 1) * 128,
                                        hh * 2048:(hh + 1) * 2048], wait=wl)
                    wf_done[b] = k.op("vector", "tensor_copy",
                                      dst_tiles[c][:, hh * 2048:(hh + 1) * 2048],
                                      wf[b][:], wait=[lv], inc=V)
        k.full_barrier()

        ut = es.enter_context(nc.sbuf_tensor("l2_ut", [128, 512], BF16))
        vt = es.enter_context(nc.sbuf_tensor("l2_vt", [128, 512], BF16))
        hstg = [es.enter_context(nc.sbuf_tensor(f"l2_hs{i}", [128, 512], BF16))
                for i in range(2)]
        psum_free = [0, 0]
        stg_d = [None, None]
        xn_used2 = [k.sems["T"].v, k.sems["T"].v]
        for tt in range(8):
            b = tt % 2
            gld = k.new_group("sync")
            lv = None
            for kc in range(8):
                lv = k.dma("sync", xnt2[b][kc][:],
                           xnfm[kc * 128:(kc + 1) * 128,
                                tt * 512:(tt + 1) * 512], gld,
                           wait=[(T, xn_used2[b])] if kc == 0 else [])
            k.wait("tensor", lv[0], lv[1])
            for fc in range(32):
                pi = fc % 2
                pu, pv = C.P[pi], C.P[2 + pi]
                for kc in range(8):
                    wl = [(V, psum_free[pi])] if kc == 0 else []
                    k.op("tensor", "matmul", pu[:, :],
                         w1[kc][:, fc * 128:(fc + 1) * 128],
                         xnt2[b][kc][:], start=(kc == 0),
                         stop=(kc == 7), wait=wl)
                for kc in range(8):
                    k.op("tensor", "matmul", pv[:, :],
                         w2[kc][:, fc * 128:(fc + 1) * 128],
                         xnt2[b][kc][:], start=(kc == 0),
                         stop=(kc == 7), inc=(T if kc == 7 else None))
                if fc == 31:
                    xn_used2[b] = k.sems["T"].v
                tv = k.sems["T"].v
                sv = k.op("scalar", "activation", ut[:], pu[:, :], AF.Silu,
                          scale=s1col[:, fc:fc + 1], wait=[(T, tv)], inc=S)
                k.op("vector", "tensor_scalar", vt[:], pv[:, :],
                     s2col[:, fc:fc + 1], None, AL.mult, wait=[(T, tv)])
                sb = fc % 2
                wl = [(S, sv)]
                if stg_d[sb]:
                    wl.append(stg_d[sb])
                psum_free[pi] = k.op("vector", "tensor_mul", hstg[sb][:], ut[:],
                                     vt[:], wait=wl, inc=V)
                stg_d[sb] = k.dma(
                    "gpsimd",
                    h1_d[fc * 128:(fc + 1) * 128, tt * 512:(tt + 1) * 512],
                    hstg[sb][:], wait=[(V, k.sems["V"].v)])
        k.full_barrier()
    if stop_after == "f1":
        return

    def big_layer(win_name, scol_name, src_d, dst_d, act_silu=False):
        with contextlib.ExitStack() as es:
            nm = win_name
            scol, _ = C.colize("l2c_" + scol_name, es, W[scol_name], 4096)
            wg = [es.enter_context(nc.sbuf_tensor(f"{nm}_wg{rc}", [128, 1024], BF16))
                  for rc in range(32)]
            ht2 = [[es.enter_context(nc.sbuf_tensor(f"{nm}_ht{i}_{rc}",
                                                    [128, 512], BF16))
                    for rc in range(32)] for i in range(2)]
            ht = ht2[0]
            ht_used = [0, 0]
            wfs = [es.enter_context(nc.sbuf_tensor(f"{nm}_wfs{i}", [128, 1024], F32))
                   for i in range(2)]
            hstg = [es.enter_context(nc.sbuf_tensor(f"{nm}_hg{i}", [128, 512], BF16))
                    for i in range(2)]
            psum_free = [0, 0]
            stg_d = [None, None]
            for grp in range(4):
                cv_done = [0, 0]
                for rc in range(32):
                    b = rc % 2
                    wl = [(V, cv_done[b])] if cv_done[b] else []
                    lv = k.dma("sync", wfs[b][:],
                               W[win_name][rc * 128:(rc + 1) * 128,
                                           grp * 1024:(grp + 1) * 1024],
                               wait=wl)
                    wl2 = [lv]
                    if grp and rc < 2:
                        wl2.append((T, k.sems["T"].v))
                    cv_done[b] = k.op("vector", "tensor_copy", wg[rc][:],
                                      wfs[b][:], wait=wl2, inc=V)
                wready = k.sems["V"].v
                for tt in range(8):
                    hb = (grp * 8 + tt) % 2
                    ht = ht2[hb]
                    gh = k.new_group("sync")
                    hv = None
                    for rc in range(32):
                        wl = [(T, ht_used[hb])] if (rc == 0 and ht_used[hb]) else []
                        hv = k.dma("sync", ht[rc][:],
                                   src_d[rc * 128:(rc + 1) * 128,
                                         tt * 512:(tt + 1) * 512], gh, wait=wl)
                    k.wait("tensor", hv[0], hv[1])
                    if tt == 0:
                        k.wait("tensor", V, wready)
                    for f2 in range(8):
                        pi = f2 % 2
                        p = C.P[pi]
                        for kc in range(32):
                            wl = []
                            if kc == 0 and psum_free[pi]:
                                wl = [(S, psum_free[pi])]
                            k.op("tensor", "matmul", p[:, :],
                                 wg[kc][:, f2 * 128:(f2 + 1) * 128], ht[kc][:],
                                 start=(kc == 0), stop=(kc == 31), wait=wl,
                                 inc=(T if kc == 31 else None))
                        ht_used[hb] = k.sems["T"].v
                        sb = f2 % 2
                        wl = [(T, k.sems["T"].v)]
                        if stg_d[sb]:
                            wl.append(stg_d[sb])
                        psum_free[pi] = k.op(
                            "scalar", "activation", hstg[sb][:], p[:, :],
                            AF.Gelu, scale=scol[:, grp * 8 + f2:grp * 8 + f2 + 1],
                            wait=wl, inc=S)
                        stg_d[sb] = k.dma(
                            "gpsimd",
                            dst_d[(grp * 8 + f2) * 128:(grp * 8 + f2 + 1) * 128,
                                  tt * 512:(tt + 1) * 512],
                            hstg[sb][:], wait=[(S, k.sems["S"].v)])
            k.full_barrier()

    big_layer("L2_h1T", "L2_ffn_h1_s", h1_d, h2_d)
    if stop_after == "f2":
        return
    big_layer("L2_h2T", "L2_ffn_h2_s", h2_d, h1_d)

    with contextlib.ExitStack() as es:
        s3col, _ = C.colize("l2_s3", es, W["L2_ffn_w3_s"], 1024)
        w3 = [es.enter_context(nc.sbuf_tensor(f"l2_w3_{rc}", [128, 1024], BF16))
              for rc in range(32)]
        wfs = [es.enter_context(nc.sbuf_tensor(f"l2_wf3{i}", [128, 1024], F32))
               for i in range(2)]
        ht = [es.enter_context(nc.sbuf_tensor(f"l2_ht3{rc}", [128, 512], BF16))
              for rc in range(32)]
        xsb = [es.enter_context(nc.sbuf_tensor(f"l2_xsb{i}", [128, 1024], F32))
               for i in range(4)]
        sfc = es.enter_context(nc.sbuf_tensor("l2_sfc", [128, 512], BF16))
        gt = es.enter_context(nc.sbuf_tensor("l2_gt", [128, 512], BF16))
        sg = es.enter_context(nc.sbuf_tensor("l2_sg", [128, 512], F32))
        cv_done = [0, 0]
        for rc in range(32):
            b = rc % 2
            wl = [(V, cv_done[b])] if cv_done[b] else []
            lv = k.dma("sync", wfs[b][:],
                       W["L2_w3T"][rc * 128:(rc + 1) * 128, :], wait=wl)
            cv_done[b] = k.op("vector", "tensor_copy", w3[rc][:], wfs[b][:],
                              wait=[lv], inc=V)
        k.full_barrier()
        psum_free = [0, 0]
        x_st = None
        for tt in range(8):
            gh = k.new_group("sync")
            hv = None
            for rc in range(32):
                wl = [(T, k.sems["T"].v)] if (tt and rc == 0) else []
                hv = k.dma("sync", ht[rc][:],
                           h1_d[rc * 128:(rc + 1) * 128, tt * 512:(tt + 1) * 512],
                           gh, wait=wl)
            gx = k.new_group("sync")
            xl = None
            for tb in range(4):
                wl = [x_st] if (x_st and tb == 0) else []
                xl = k.dma("sync", xsb[tb][:],
                           x_dram[tt * 512 + tb * 128:tt * 512 + (tb + 1) * 128, :],
                           gx, wait=wl)
            k.wait("tensor", hv[0], hv[1])
            k.wait("vector", xl[0], xl[1])
            for fc in range(8):
                pi = fc % 2
                p = C.P[pi]
                sl = k.dma("sync", sfc[:],
                           s_d[fc * 128:(fc + 1) * 128, tt * 512:(tt + 1) * 512],
                           wait=[(V, k.sems["V"].v)])
                for kc in range(32):
                    wl = []
                    if kc == 0 and psum_free[pi]:
                        wl = [(S, psum_free[pi])]
                    k.op("tensor", "matmul", p[:, :],
                         w3[kc][:, fc * 128:(fc + 1) * 128], ht[kc][:],
                         start=(kc == 0), stop=(kc == 31), wait=wl,
                         inc=(T if kc == 31 else None))
                psum_free[pi] = k.op("scalar", "activation", gt[:], p[:, :],
                                     AF.Silu, scale=s3col[:, fc:fc + 1],
                                     wait=[(T, k.sems["T"].v)], inc=S)
                mv = k.op("vector", "tensor_mul", sg[:], gt[:], sfc[:],
                          wait=[(S, k.sems["S"].v), sl], inc=V)
                for tb in range(4):
                    pt = C.P[4 + tb % 2]
                    tv = k.op("tensor", "transpose", pt[:, 0:128],
                              sg[:, tb * 128:(tb + 1) * 128],
                              C.sb["iden_f32"][:], wait=[(V, k.sems["V"].v)],
                              inc=T)
                    k.op("vector", "tensor_add",
                         xsb[tb][:, fc * 128:(fc + 1) * 128],
                         xsb[tb][:, fc * 128:(fc + 1) * 128], pt[:, 0:128],
                         wait=[(T, tv)], inc=V)
                k.wait("tensor", V, k.sems["V"].v)
            for tb in range(4):
                x_st = k.dma("gpsimd",
                             xout_dram[tt * 512 + tb * 128:tt * 512 + (tb + 1) * 128, :],
                             xsb[tb][:], wait=[(V, k.sems["V"].v)])
        k.full_barrier()


L3_WSPECS = {
    "L3_w1T": ([1024, 256], F32),
    "L3_w2T": ([256, 6144], F32),
    "L3_cw1T": ([2048, 256], F32),
    "L3_cw2T": ([256, 2048], F32),
    "L3_Wr": ([1024, 8], F32),
    "L3_WiT": ([8, 1024, 1024], F32),
    "L3_Wo": ([8, 1024, 1024], F32),
}


def emit_layer3(k: K, C: Ctx, W, x_dram, xout_dram, scratch, stop_after="all"):
    nc = k.nc
    T, V, S, P = k.sem("T"), k.sem("V"), k.sem("S"), k.sem("P")
    xnfm = scratch["xnfm"]
    s_d = scratch["s_fm"]
    ym_d = scratch["h1"]
    we_d = scratch["h2"]
    coeff_d = nc.dram_tensor("l3_coeff", [T_CORE, 8], F32, kind="Internal").ap()

    emit_ln(k, C, x_dram, xnfm)
    if stop_after == "ln":
        return

    FEPS = 1.1920929e-07

    with contextlib.ExitStack() as es:
        wr = es.enter_context(nc.sbuf_tensor("l3_wr", [128, 64], BF16))
        wrf = es.enter_context(nc.sbuf_tensor("l3_wrf", [128, 64], F32))
        xt = [[es.enter_context(nc.sbuf_tensor(f"l3_rx{i}_{c}", [128, 128], BF16))
               for c in range(8)] for i in range(2)]
        m1 = es.enter_context(nc.sbuf_tensor("l3_m1", [128, 1], F32))
        nm1 = es.enter_context(nc.sbuf_tensor("l3_nm1", [128, 1], F32))
        m2 = es.enter_context(nc.sbuf_tensor("l3_m2", [128, 1], F32))
        msk = es.enter_context(nc.sbuf_tensor("l3_msk", [128, 8], F32))
        eu = es.enter_context(nc.sbuf_tensor("l3_eu", [128, 8], F32))
        den = es.enter_context(nc.sbuf_tensor("l3_den", [128, 1], F32))
        rcp = es.enter_context(nc.sbuf_tensor("l3_rcp", [128, 1], F32))
        cof = es.enter_context(nc.sbuf_tensor("l3_cof", [128, 8], F32))
        gld = k.new_group("sync")
        lv = k.dma("sync", wrf[:].rearrange("p (c e) -> p c e", e=8),
                   W["L3_Wr"].rearrange("(c p) e -> c p e", p=128).transpose([1, 0, 2]),
                   gld)
        k.op("vector", "tensor_copy", wr[:], wrf[:], wait=[lv], inc=V)
        k.full_barrier()
        xn_used = [k.sems["T"].v, k.sems["T"].v]
        ps_free = [0, 0]
        for i in range(T_CORE // 128):
            b = i % 2
            gx = k.new_group("sync")
            lv = None
            for c in range(8):
                lv = k.dma("sync", xt[b][c][:],
                           xnfm[c * 128:(c + 1) * 128, i * 128:(i + 1) * 128],
                           gx, wait=[(T, xn_used[b])] if c == 0 else [])
            k.wait("tensor", lv[0], lv[1])
            p = C.P[b]
            for c in range(8):
                wl = [(V, ps_free[b])] if (c == 0 and ps_free[b]) else []
                k.op("tensor", "matmul", p[:, 0:8], xt[b][c][:],
                     wr[:, c * 8:(c + 1) * 8], start=(c == 0), stop=(c == 7),
                     wait=wl, inc=(T if c == 7 else None))
            tv = k.sems["T"].v
            xn_used[b] = tv
            k.op("vector", "reduce_max", m1[:], p[:, 0:8], AX.X, wait=[(T, tv)])
            k.op("vector", "reduce_max", nm1[:], p[:, 0:8], AX.X, negate=True)
            k.op("vector", "tensor_single_scalar", msk[:], p[:, 0:8], m1[:],
                 AL.is_ge)
            k.op("vector", "scalar_tensor_tensor", eu[:], msk[:], -1e30, p[:, 0:8],
                 AL.mult, AL.add)
            vv = k.op("vector", "reduce_max", m2[:], eu[:], AX.X, inc=V)
            sv = k.op("scalar", "activation", eu[:], p[:, 0:8], AF.Exp,
                      bias=nm1[:, 0:1], accum_out=den[:], wait=[(V, vv)], inc=S)
            k.op("vector", "reciprocal", rcp[:], den[:], wait=[(S, sv)])
            k.op("vector", "tensor_single_scalar", msk[:], p[:, 0:8], m2[:],
                 AL.is_ge)
            k.op("vector", "tensor_mul", cof[:], msk[:], eu[:])
            ps_free[b] = k.op("vector", "tensor_scalar_mul", cof[:], cof[:],
                              rcp[:], inc=V)
            k.dma("gpsimd", coeff_d[i * 128:(i + 1) * 128, :], cof[:],
                  wait=[(V, ps_free[b])])
        k.full_barrier()

    with contextlib.ExitStack() as es:
        C.load_tables(es, ["l3_cos", "l3_sin"])
        w1 = [es.enter_context(nc.sbuf_tensor(f"l3_w1_{c}", [128, 256], BF16))
              for c in range(8)]
        w2 = [es.enter_context(nc.sbuf_tensor(f"l3_w2_{c}", [128, 6144], BF16))
              for c in range(2)]
        cw1 = [es.enter_context(nc.sbuf_tensor(f"l3_cw1_{c}", [128, 256], BF16))
               for c in range(16)]
        cw2 = [es.enter_context(nc.sbuf_tensor(f"l3_cw2_{c}", [128, 2048], BF16))
               for c in range(2)]
        wfs = [es.enter_context(nc.sbuf_tensor(f"l3_wf{i}", [128, 1024], F32))
               for i in range(2)]
        cv_state = {"b": 0, "done": [0, 0]}

        def load_cast(dst_ap, src_ap):
            n = src_ap.shape[1]
            for o in range(0, n, 1024):
                w = min(1024, n - o)
                b = cv_state["b"]
                cv_state["b"] ^= 1
                wl = [(V, cv_state["done"][b])] if cv_state["done"][b] else []
                lv = k.dma("sync", wfs[b][:, 0:w], src_ap[:, o:o + w], wait=wl)
                cv_state["done"][b] = k.op("vector", "tensor_copy",
                                           dst_ap[:, o:o + w], wfs[b][:, 0:w],
                                           wait=[lv], inc=V)

        for c in range(8):
            load_cast(w1[c][:], W["L3_w1T"][c * 128:(c + 1) * 128, :])
        for c in range(2):
            load_cast(w2[c][:], W["L3_w2T"][c * 128:(c + 1) * 128, :])
        for c in range(16):
            load_cast(cw1[c][:], W["L3_cw1T"][c * 128:(c + 1) * 128, :])
        for c in range(2):
            load_cast(cw2[c][:], W["L3_cw2T"][c * 128:(c + 1) * 128, :])
        k.full_barrier()

        xnt = [[es.enter_context(nc.sbuf_tensor(f"l3_xnt{i}_{c}", [128, 512], BF16))
                for c in range(8)] for i in range(2)]
        rmsn = [es.enter_context(nc.sbuf_tensor(f"l3_rm{c}", [128, 512], BF16))
                for c in range(8)]
        hr = [es.enter_context(nc.sbuf_tensor(f"l3_hr{c}", [128, 2048], BF16))
              for c in range(2)]
        sqv = [es.enter_context(nc.sbuf_tensor(f"l3_sqv{i}", [128, 512], BF16))
               for i in range(2)]
        rrow = es.enter_context(nc.sbuf_tensor("l3_rrow", [1, 2048], F32))
        qf = es.enter_context(nc.sbuf_tensor("l3_qf", [128, 2048], F32))
        kf = es.enter_context(nc.sbuf_tensor("l3_kf", [128, 2048], F32))
        vb = es.enter_context(nc.sbuf_tensor("l3_vb", [128, 2048], BF16))
        skv = es.enter_context(nc.sbuf_tensor("l3_skv", [128, 2048], F32))
        cum = es.enter_context(nc.sbuf_tensor("l3_cum", [128, 2048], F32))
        ym = es.enter_context(nc.sbuf_tensor("l3_ym", [128, 2048], BF16))
        c1 = [es.enter_context(nc.sbuf_tensor(f"l3_c1_{c}", [128, 2048], BF16))
              for c in range(2)]
        ymt = xnt[0] + xnt[1]
        zz = es.enter_context(nc.sbuf_tensor("l3_zz", [128, 2048], F32))
        k.op("vector", "memset", zz[:], 1.0)
        sst = es.enter_context(nc.sbuf_tensor("l3_sst", [128, 512], BF16))
        sgg = es.enter_context(nc.sbuf_tensor("l3_sgg", [128, 512], BF16))
        xnc = es.enter_context(nc.sbuf_tensor("l3_xnc", [128, 512], BF16))
        uss = es.enter_context(nc.sbuf_tensor("l3_uss", [128, 512], F32))

        def rms_norm_fm(src_tiles, n_chunks, dst_tiles, dst_dtype_bf=True,
                        seq_base=0, width=2048, from_psum=False):
            for blk in range(width // 512):
                cs = blk * 512
                pS = C.P[4]
                tmv = [0, 0]
                for c in range(n_chunks):
                    sb2 = c % 2
                    swl = [(V, k.sems["V"].v)]
                    if tmv[sb2]:
                        swl.append((T, tmv[sb2]))
                    sv = k.op("scalar", "activation", sqv[sb2][:, :],
                              src_tiles[c][:, cs:cs + 512], AF.Square,
                              wait=swl, inc=S)
                    wl = [(S, sv)]
                    if c == 0:
                        wl.append((V, k.sems["V"].v))
                    tmv[sb2] = k.op("tensor", "matmul", pS[0:1, :],
                                    C.ones_col_bf[:, :], sqv[sb2][:, :],
                                    start=(c == 0), stop=(c == n_chunks - 1),
                                    wait=wl, inc=T)
                tv = k.sems["T"].v
                k.op("vector", "tensor_scalar", rrow[0:1, cs:cs + 512],
                     pS[0:1, :], 1.0 / (n_chunks * 128), FEPS, AL.mult, AL.add,
                     wait=[(T, tv)])
                k.op("scalar", "activation", rrow[0:1, cs:cs + 512],
                     rrow[0:1, cs:cs + 512], AF.Ln,
                     wait=[(V, k.sems["V"].v)])
                k.op("vector", "tensor_single_scalar", rrow[0:1, cs:cs + 512],
                     rrow[0:1, cs:cs + 512], -0.5, AL.mult,
                     wait=[(S, k.sems["S"].v)])
                sv = k.op("scalar", "activation", rrow[0:1, cs:cs + 512],
                          rrow[0:1, cs:cs + 512], AF.Exp,
                          wait=[(V, k.sems["V"].v)], inc=S)
                import os as _os
                if _os.environ.get("L3DBG2"):
                    dbg3 = nc.dram_tensor(f"dbg_rr{k.sems['T'].v}", [1, 2048], F32,
                                          kind="ExternalOutput").ap()
                    k.dma("gpsimd", dbg3[0:1, 0:512], rrow[0:1, cs:cs + 512],
                          wait=[(S, sv)])
                tv = k.op("tensor", "matmul", C.P[5][:, :], C.ones_row[0:1, :],
                          rrow[0:1, cs:cs + 512], start=True, stop=True,
                          wait=[(S, sv), (V, k.sems["V"].v)], inc=T)
                if _os.environ.get("L3DBG2") and not hasattr(k, "_dbg5"):
                    k._dbg5 = nc.dram_tensor("dbg_p5", [128, 512], F32,
                                             kind="ExternalOutput").ap()
                    vv2 = k.op("vector", "tensor_copy", cum[:, 0:512],
                               C.P[5][:, :], wait=[(T, tv)], inc=k.sem("V"))
                    k.dma("gpsimd", k._dbg5, cum[:, 0:512],
                          wait=[(k.sem("V"), vv2)])
                for c in range(n_chunks):
                    k.op("vector", "tensor_mul", dst_tiles[c][:, cs:cs + 512],
                         src_tiles[c][:, cs:cs + 512], C.P[5][:, :],
                         wait=[(T, tv)] if c == 0 else [], inc=V)

        for sq_i in range(SEQ_PER_CORE):
            base_t = sq_i * SEQ
            xn_used = [k.sems["T"].v, k.sems["T"].v]
            ps_free = k.sems["V"].v
            for tt in range(4):
                b = tt % 2
                gx = k.new_group("sync")
                lv = None
                for c in range(8):
                    lv = k.dma("sync", xnt[b][c][:],
                               xnfm[c * 128:(c + 1) * 128,
                                    base_t + tt * 512:base_t + (tt + 1) * 512],
                               gx, wait=[(T, xn_used[b]), (V, k.sems["V"].v)]
                               if c == 0 else [])
                k.wait("scalar", lv[0], lv[1])
                k.wait("vector", lv[0], lv[1])
                rms_norm_fm([xnt[b][c][:] for c in range(8)], 8,
                            [rmsn[c][:] for c in range(8)], width=512)
                import os as _os
                if _os.environ.get("L3DBG") and tt == 0 and sq_i == 0:
                    dbg2 = nc.dram_tensor("dbg_rms", [2, 128, 512], F32,
                                          kind="ExternalOutput").ap()
                    vv = k.sems["V"].v
                    k.dma("gpsimd", dbg2[0], rmsn[0][:], wait=[(V, vv)])
                    k.full_barrier()
                rms_done = k.sems["V"].v
                for rc in range(2):
                    p = C.P[rc]
                    for kc in range(8):
                        wl = [(V, max(ps_free, rms_done))] if kc == 0 else []
                        k.op("tensor", "matmul", p[:, :],
                             w1[kc][:, rc * 128:(rc + 1) * 128],
                             rmsn[kc][:], start=(kc == 0), stop=(kc == 7),
                             wait=wl, inc=(T if kc == 7 else None))
                    tv = k.sems["T"].v
                    if rc == 1:
                        xn_used[b] = tv
                    k.op("vector", "tensor_single_scalar", qf[:, 0:512], p[:, :],
                         0.0, AL.max, wait=[(T, tv)])
                    ps_free = k.op("vector", "tensor_mul",
                                   hr[rc][:, tt * 512:(tt + 1) * 512],
                                   qf[:, 0:512], qf[:, 0:512], inc=V)
                    if _os.environ.get("L3DBG") and tt == 0 and sq_i == 0 and rc == 0:
                        k.dma("gpsimd", dbg2[1], hr[0][:, 0:512].rearrange("p f -> p f"),
                              wait=[(V, ps_free)])
                        k.full_barrier()
            ym_store = None
            for h in range(16):
                ps_free2 = k.sems["V"].v
                for part, dst in ((0, qf), (1, kf), (2, vb)):
                    fbase = part * 2048 + h * 128
                    for tt in range(4):
                        p = C.P[2 + tt % 2]
                        for kc in range(2):
                            wl = [(V, ps_free2)] if kc == 0 else []
                            k.op("tensor", "matmul", p[:, :],
                                 w2[kc][:, fbase:fbase + 128],
                                 hr[kc][:, tt * 512:(tt + 1) * 512],
                                 start=(kc == 0), stop=(kc == 1), wait=wl,
                                 inc=(T if kc == 1 else None))
                        tv = k.sems["T"].v
                        ps_free2 = k.op("vector", "tensor_copy",
                                        dst[:, tt * 512:(tt + 1) * 512], p[:, :],
                                        wait=[(T, tv)], inc=V)
                import os as _os
                if _os.environ.get("L3DBG") and h == 0 and sq_i == 0:
                    dbg1 = nc.dram_tensor("dbg_qkv", [3, 128, 2048], F32,
                                          kind="ExternalOutput").ap()
                    dbg4 = nc.dram_tensor("dbg_hr", [2, 128, 2048], BF16,
                                          kind="ExternalOutput").ap()
                    dbg6 = nc.dram_tensor("dbg_w2", [2, 128, 256], BF16,
                                          kind="ExternalOutput").ap()
                    vv = k.sems["V"].v
                    k.dma("gpsimd", dbg1[0], qf[:], wait=[(V, vv)])
                    k.dma("gpsimd", dbg1[1], kf[:], wait=[(V, vv)])
                    for rc2 in range(2):
                        k.dma("gpsimd", dbg4[rc2], hr[rc2][:], wait=[(V, vv)])
                        k.dma("gpsimd", dbg6[rc2], w2[rc2][:, 0:256], wait=[(V, vv)])
                    k.full_barrier()
                rms_norm_fm([qf[:]], 1, [qf[:]], width=2048)
                rms_norm_fm([kf[:]], 1, [kf[:]], width=2048)
                if _os.environ.get("L3DBG") and h == 0 and sq_i == 0:
                    vv = k.sems["V"].v
                    k.dma("gpsimd", dbg1[2], qf[:], wait=[(V, vv)])
                    k.full_barrier()
                for z in (qf, kf):
                    cosv, sinv = C.sb["l3_cos"], C.sb["l3_sin"]
                    k.op("vector", "tensor_mul", skv[0:64, :], z[0:64, :],
                         cosv[0:64, :])
                    k.op("vector", "tensor_mul", cum[0:64, :], z[0:64, :],
                         sinv[0:64, :])
                    k.op("vector", "tensor_mul", skv[64:128, :], z[64:128, :],
                         sinv[64:128, :])
                    k.op("vector", "tensor_mul", cum[64:128, :], z[64:128, :],
                         cosv[64:128, :])
                    k.op("vector", "tensor_copy", z[0:64, :], skv[64:128, :])
                    k.op("vector", "tensor_add", z[0:64, :], z[0:64, :],
                         skv[0:64, :])
                    k.op("vector", "tensor_copy", z[64:128, :], cum[0:64, :])
                    k.op("vector", "tensor_sub", z[64:128, :], cum[64:128, :],
                         z[64:128, :])
                sv = k.op("scalar", "activation", kf[:], kf[:], AF.Sigmoid,
                          wait=[(V, k.sems["V"].v)], inc=S)
                k.op("vector", "tensor_mul", skv[:], kf[:], vb[:],
                     wait=[(S, sv)])
                k.op("vector", "tensor_tensor_scan", cum[:], zz[:], skv[:],
                     0.0, AL.mult, AL.add)
                mv = k.op("vector", "scalar_tensor_tensor", ym[:], qf[:], 0.0,
                          cum[:], AL.max, AL.mult,
                          wait=[ym_store] if ym_store else [], inc=V)
                ym_store = k.dma("gpsimd",
                                 ym_d[h * 128:(h + 1) * 128, base_t:base_t + SEQ],
                                 ym[:], wait=[(V, mv)])
            k.full_barrier()
            ps_free = 0
            for rc in range(2):
                for tt in range(4):
                    gy = k.new_group("sync")
                    lv = None
                    for c in range(16):
                        wl = [(T, k.sems["T"].v)] if c == 0 else []
                        lv = k.dma("sync", ymt[c][:],
                                   ym_d[c * 128:(c + 1) * 128,
                                        base_t + tt * 512:base_t + (tt + 1) * 512],
                                   gy, wait=wl)
                    k.wait("tensor", lv[0], lv[1])
                    p = C.P[tt % 2]
                    for kc in range(16):
                        wl = [(V, ps_free)] if (kc == 0 and ps_free) else []
                        k.op("tensor", "matmul", p[:, :],
                             cw1[kc][:, rc * 128:(rc + 1) * 128], ymt[kc][:],
                             start=(kc == 0), stop=(kc == 15), wait=wl,
                             inc=(T if kc == 15 else None))
                    ps_free = k.op("vector", "tensor_copy",
                                   c1[rc][:, tt * 512:(tt + 1) * 512],
                                   C.P[tt % 2][:, :], wait=[(T, k.sems["T"].v)],
                                   inc=V)
            s_store = None
            for tt in range(4):
                for f in range(8):
                    pu, pg = C.P[0 + (f % 2) * 2], C.P[1 + (f % 2) * 2]
                    for kc in range(2):
                        wl = [(V, k.sems["V"].v)] if kc == 0 else []
                        k.op("tensor", "matmul", pu[:, :],
                             cw2[kc][:, f * 128:(f + 1) * 128],
                             c1[kc][:, tt * 512:(tt + 1) * 512],
                             start=(kc == 0), stop=(kc == 1), wait=wl)
                    for kc in range(2):
                        k.op("tensor", "matmul", pg[:, :],
                             cw2[kc][:, 1024 + f * 128:1024 + (f + 1) * 128],
                             c1[kc][:, tt * 512:(tt + 1) * 512],
                             start=(kc == 0), stop=(kc == 1),
                             inc=(T if kc == 1 else None))
                    tv = k.sems["T"].v
                    xl = k.dma("sync", xnc[:],
                               xnfm[f * 128:(f + 1) * 128,
                                    base_t + tt * 512:base_t + (tt + 1) * 512],
                               wait=[(V, k.sems["V"].v)])
                    sv = k.op("scalar", "activation", sgg[:], pg[:, :], AF.Silu,
                              wait=[(T, tv)], inc=S)
                    k.op("vector", "tensor_mul", uss[:], pu[:, :], sgg[:],
                         wait=[(S, sv)])
                    wl = [xl]
                    if s_store:
                        wl.append(s_store)
                    mv = k.op("vector", "tensor_add", sst[:], xnc[:], uss[:],
                              wait=wl, inc=V)
                    s_store = k.dma("gpsimd",
                                    s_d[f * 128:(f + 1) * 128,
                                        base_t + tt * 512:base_t + (tt + 1) * 512],
                                    sst[:], wait=[(V, mv)])
        k.full_barrier()
    if stop_after == "mixer":
        return

    with contextlib.ExitStack() as es:
        wi = [es.enter_context(nc.sbuf_tensor(f"l3_wi{c}", [128, 1024], BF16))
              for c in range(8)]
        wo = [es.enter_context(nc.sbuf_tensor(f"l3_wo{c}", [128, 1024], BF16))
              for c in range(8)]
        wfs = [es.enter_context(nc.sbuf_tensor(f"l3_wfm{i}", [128, 1024], F32))
               for i in range(2)]
        stg = [es.enter_context(nc.sbuf_tensor(f"l3_wst{i}", [128, 512], BF16))
               for i in range(2)]
        cv_done = [0, 0]
        stg_d = [None, None]
        for e in range(8):
            for c in range(8):
                b = c % 2
                wl = [(V, cv_done[b])] if cv_done[b] else []
                if c < 2:
                    wl.append((T, k.sems["T"].v))
                lv = k.dma("sync", wfs[b][:], W["L3_WiT"][e, c * 128:(c + 1) * 128, :],
                           wait=wl)
                cv_done[b] = k.op("vector", "tensor_copy", wi[c][:], wfs[b][:],
                                  wait=[lv], inc=V)
            for c in range(8):
                b = c % 2
                wl = [(V, cv_done[b])]
                lv = k.dma("sync", wfs[b][:], W["L3_Wo"][e, c * 128:(c + 1) * 128, :],
                           wait=wl)
                cv_done[b] = k.op("vector", "tensor_copy", wo[c][:], wfs[b][:],
                                  wait=[lv], inc=V)
            wready = k.sems["V"].v
            k.wait("tensor", V, wready)
            for ic in range(8):
                for ob in range(2):
                    pi = ob
                    p = C.P[pi]
                    for hc in range(8):
                        wl = [(S, k.sems["S"].v)] if hc == 0 else []
                        k.op("tensor", "matmul", p[:, :],
                             wi[hc][:, ic * 128:(ic + 1) * 128],
                             wo[hc][:, ob * 512:(ob + 1) * 512],
                             start=(hc == 0), stop=(hc == 7), wait=wl,
                             inc=(T if hc == 7 else None))
                    sb = ob
                    wl = [(T, k.sems["T"].v)]
                    if stg_d[sb]:
                        wl.append(stg_d[sb])
                    sv = k.op("scalar", "copy", stg[sb][:], p[:, :], wait=wl,
                              inc=S)
                    rbase = (e % 4) * 1024 + ic * 128
                    cbase = (e // 4) * 1024 + ob * 512
                    stg_d[sb] = k.dma(
                        "gpsimd",
                        we_d[rbase:rbase + 128, cbase:cbase + 512],
                        stg[sb][:], wait=[(S, sv)])
        k.full_barrier()

    with contextlib.ExitStack() as es:
        xng = [es.enter_context(nc.sbuf_tensor(f"l3_xg{c}", [128, 1024], BF16))
               for c in range(8)]
        weg = [[es.enter_context(nc.sbuf_tensor(f"l3_we{i}_{c}", [128, 1024], BF16))
                for c in range(8)] for i in range(2)]
        gacc = [es.enter_context(nc.sbuf_tensor(f"l3_ga{tb}", [128, 1024], F32))
                for tb in range(8)]
        coft = [es.enter_context(nc.sbuf_tensor(f"l3_cf{tb}", [128, 8], F32))
                for tb in range(8)]
        gsil = es.enter_context(nc.sbuf_tensor("l3_gs", [128, 1024], BF16))
        ssl = es.enter_context(nc.sbuf_tensor("l3_ssl", [128, 1024], BF16))
        sgf = es.enter_context(nc.sbuf_tensor("l3_sgf", [128, 1024], F32))
        xsb = es.enter_context(nc.sbuf_tensor("l3_xsb", [128, 1024], F32))
        for tg in range(4):
            tbase = tg * 1024
            gx = k.new_group("sync")
            for tb in range(8):
                wl = [(T, k.sems["T"].v), (V, k.sems["V"].v)] if tb == 0 else []
                for c in range(8):
                    lv = k.dma("sync", xng[c][:, tb * 128:(tb + 1) * 128],
                               xnfm[c * 128:(c + 1) * 128,
                                    tbase + tb * 128:tbase + (tb + 1) * 128],
                               gx, wait=wl if c == 0 else [])
                cl = k.dma("sync", coft[tb][:],
                           coeff_d[tbase + tb * 128:tbase + (tb + 1) * 128, :], gx)
            k.wait("tensor", cl[0], cl[1])
            k.wait("vector", cl[0], cl[1])
            we_used = [k.sems["T"].v, k.sems["T"].v]
            for e in range(8):
                eb = e % 2
                gw = k.new_group("sync")
                wv = None
                for c in range(8):
                    rbase = (e % 4) * 1024 + c * 128
                    cbase = (e // 4) * 1024
                    wv = k.dma("sync", weg[eb][c][:],
                               we_d[rbase:rbase + 128, cbase:cbase + 1024],
                               gw, wait=[(T, we_used[eb])] if c == 0 else [])
                k.wait("tensor", wv[0], wv[1])
                for tb in range(8):
                    for ob in range(2):
                        p = C.P[2 * eb + ob]
                        for ic in range(8):
                            wl = [(V, k.sems["V"].v)] if ic == 0 else []
                            k.op("tensor", "matmul", p[:, :],
                                 xng[ic][:, tb * 128:(tb + 1) * 128],
                                 weg[eb][ic][:, ob * 512:(ob + 1) * 512],
                                 start=(ic == 0), stop=(ic == 7), wait=wl,
                                 inc=(T if ic == 7 else None))
                        tv = k.sems["T"].v
                        if tb == 7 and ob == 1:
                            we_used[eb] = tv
                        dst = gacc[tb][:, ob * 512:(ob + 1) * 512]
                        if e == 0:
                            k.op("vector", "tensor_scalar_mul", dst, p[:, :],
                                 coft[tb][:, 0:1],
                                 wait=[(T, tv), (S, k.sems["S"].v)], inc=V)
                        else:
                            k.op("vector", "scalar_tensor_tensor", dst, p[:, :],
                                 coft[tb][:, e:e + 1], dst, AL.mult, AL.add,
                                 wait=[(T, tv)], inc=V)
            x_st = None
            for tb in range(8):
                sv = k.op("scalar", "activation", gsil[:], gacc[tb][:], AF.Silu,
                          wait=[(V, k.sems["V"].v)], inc=S)
                gs = k.new_group("sync")
                sl = None
                for c in range(8):
                    sl = k.dma("sync", ssl[:, c * 128:(c + 1) * 128],
                               s_d[c * 128:(c + 1) * 128,
                                   tbase + tb * 128:tbase + (tb + 1) * 128],
                               gs, wait=[(V, k.sems["V"].v)] if c == 0 else [])
                xl = k.dma("sync", xsb[:],
                           x_dram[tbase + tb * 128:tbase + (tb + 1) * 128, :],
                           wait=[x_st] if x_st else [])
                k.wait("tensor", sl[0], sl[1])
                mvb = [0, 0]
                for c in range(8):
                    pb = c % 2
                    wl = [(V, mvb[pb])] if mvb[pb] else []
                    if c < 2:
                        wl += [(V, k.sems["V"].v), (S, sv)]
                    tv = k.op("tensor", "transpose", C.PBF[pb][:, 0:128],
                              ssl[:, c * 128:(c + 1) * 128], C.sb["iden_bf"][:],
                              wait=wl, inc=T)
                    mvb[pb] = k.op("vector", "tensor_mul",
                                   sgf[:, c * 128:(c + 1) * 128],
                                   gsil[:, c * 128:(c + 1) * 128],
                                   C.PBF[pb][:, 0:128],
                                   wait=[(T, tv)], inc=V)
                vv = k.op("vector", "tensor_add", xsb[:], xsb[:], sgf[:],
                          wait=[xl], inc=V)
                x_st = k.dma("gpsimd",
                             xout_dram[tbase + tb * 128:tbase + (tb + 1) * 128, :],
                             xsb[:], wait=[(V, vv)])
        k.full_barrier()


L1_WSPECS = {
    "L1_attn_Wq_down": ([1024, 1024], F32),
    "L1_attn_Wk_down": ([1024, 1024], F32),
    "L1_attn_Wv_down": ([1024, 1024], F32),
    "L1_attn_Wq_up_c": ([64, 32], F32),
    "L1_attn_Wq_up_e": ([64, 32], F32),
    "L1_attn_Wk_up_c": ([64, 32], F32),
    "L1_attn_Wk_up_e": ([64, 32], F32),
    "L1_attn_Wv_up": ([64, 64], F32),
    "L1_attn_Wc": ([1024, 1024], F32),
    "L1_Wc_r": ([1024, 128], F32),
    "L1_bc": ([128], F32),
    "L1_W1": ([128, 1024, 32], F32),
    "L1_W2T": ([128, 32, 1024], F32),
    "L1_B1T": ([32, 128], F32),
    "L1_B2": ([128, 1024], F32),
}


def emit_layer1(k: K, C: Ctx, W, x_dram, xout_dram, scratch, stop_after="all"):
    nc = k.nc
    T, V, S, P = k.sem("T"), k.sem("V"), k.sem("S"), k.sem("P")
    xnfm = scratch["xnfm"]
    xntok = scratch["xntok"]
    s_d = scratch["s_fm"]
    p_d = nc.dram_tensor("l1_p", [128, 128, 1024], BF16, kind="Internal").ap()
    r_d = nc.dram_tensor("l1_r", [128, 128, 1024], BF16, kind="Internal").ap()
    perm_d = nc.dram_tensor("l1_perm", [128, 128, 32], BF16, kind="Internal").ap()

    emit_ln(k, C, x_dram, xnfm, xntok)
    if stop_after == "ln":
        return

    with contextlib.ExitStack() as es:
        C.load_tables(es, ["l1_cos", "l1_sin"])
        wqd = [es.enter_context(nc.sbuf_tensor(f"l1_wqd{c}", [128, 1024], BF16))
               for c in range(8)]
        wkd = [es.enter_context(nc.sbuf_tensor(f"l1_wkd{c}", [128, 1024], BF16))
               for c in range(8)]
        wvd = [es.enter_context(nc.sbuf_tensor(f"l1_wvd{c}", [128, 1024], BF16))
               for c in range(8)]
        wc = [es.enter_context(nc.sbuf_tensor(f"l1_wc{c}", [128, 1024], BF16))
              for c in range(8)]
        wfs1 = es.enter_context(nc.sbuf_tensor("l1_wf0", [128, 1024], F32))
        ups_f = es.enter_context(nc.sbuf_tensor("l1_upsf", [64, 224], F32))
        upr_f = es.enter_context(nc.sbuf_tensor("l1_uprf", [64, 64], F32))
        ups = es.enter_context(nc.sbuf_tensor("l1_ups", [128, 288], BF16))
        cv_done = 0
        for tiles, name in ((wqd, "L1_attn_Wq_down"), (wkd, "L1_attn_Wk_down"),
                            (wvd, "L1_attn_Wv_down"), (wc, "L1_attn_Wc")):
            for c in range(8):
                wl = [(V, cv_done)] if cv_done else []
                lv = k.dma("sync", wfs1[:], W[name][c * 128:(c + 1) * 128, :],
                           wait=wl)
                cv_done = k.op("vector", "tensor_copy", tiles[c][:], wfs1[:],
                               wait=[lv], inc=V)
        gu = k.new_group("sync")
        lv = None
        for i, nm in enumerate(("L1_attn_Wq_up_c", "L1_attn_Wq_up_e",
                                "L1_attn_Wk_up_c", "L1_attn_Wk_up_e")):
            lv = k.dma("sync", ups_f[:, i * 32:(i + 1) * 32], W[nm], gu)
        lv = k.dma("sync", ups_f[:, 128:192], W["L1_attn_Wv_up"], gu)
        k.wait("vector", lv[0], lv[1])
        k.op("vector", "tensor_single_scalar", ups[0:64, 0:64], ups_f[:, 0:64],
             0.125, AL.mult)
        qe = ups_f[:, 32:64].rearrange("p (j two) -> p j two", two=2)
        qer = ups[0:64, 64:96].rearrange("p (j two) -> p j two", two=2)
        k.op("vector", "tensor_single_scalar", qer[:, :, 0], qe[:, :, 1],
             -0.125, AL.mult)
        k.op("vector", "tensor_single_scalar", qer[:, :, 1], qe[:, :, 0],
             0.125, AL.mult)
        k.op("vector", "tensor_copy", ups[0:64, 96:160], ups_f[:, 64:128])
        ke = ups_f[:, 96:128].rearrange("p (j two) -> p j two", two=2)
        ker = ups[0:64, 160:192].rearrange("p (j two) -> p j two", two=2)
        k.op("vector", "tensor_single_scalar", ker[:, :, 0], ke[:, :, 1],
             -1.0, AL.mult)
        k.op("vector", "tensor_copy", ker[:, :, 1], ke[:, :, 0])
        k.op("vector", "tensor_copy", ups[0:64, 192:256], ups_f[:, 128:192])
        k.op("vector", "tensor_copy", ups[64:128, 0:288], ups[0:64, 0:288])
        k.full_barrier()

        xnt = [[es.enter_context(nc.sbuf_tensor(f"l1_xnt{i}_{c}", [128, 512], BF16))
                for c in range(8)] for i in range(2)]
        ymix = [es.enter_context(nc.sbuf_tensor(f"l1_ym{c}", [128, 2048], BF16))
                for c in range(8)]
        qa = [es.enter_context(nc.sbuf_tensor(f"l1_qa{i}", [65, 2048], BF16))
              for i in range(2)]
        ka = [es.enter_context(nc.sbuf_tensor(f"l1_ka{i}", [65, 2048], BF16))
              for i in range(2)]
        vto = [es.enter_context(nc.sbuf_tensor(f"l1_v{i}", [128, 1040], BF16))
               for i in range(2)]
        qlat = es.enter_context(nc.sbuf_tensor("l1_qlat", [128, 2048], BF16))
        klat = es.enter_context(nc.sbuf_tensor("l1_klat", [128, 2048], BF16))
        vlat = es.enter_context(nc.sbuf_tensor("l1_vlat", [128, 2048], BF16))
        e1 = es.enter_context(nc.sbuf_tensor("l1_e1", [32, 2048], BF16))
        e2 = es.enter_context(nc.sbuf_tensor("l1_e2", [32, 2048], BF16))
        attT = [es.enter_context(nc.sbuf_tensor(f"l1_attT{i}", [128, 512], BF16))
                for i in range(2)]
        sq64 = es.enter_context(nc.sbuf_tensor("l1_sq64", [64, 2048], BF16))
        nrm = es.enter_context(nc.sbuf_tensor("l1_nrm", [1, 2048], F32))
        kmx = es.enter_context(nc.sbuf_tensor("l1_kmx", [1, 1], F32))
        rd = es.enter_context(nc.sbuf_tensor("l1_rd", [1, 512], F32))
        rb = es.enter_context(nc.sbuf_tensor("l1_rb", [64, 512], F32))
        sstg = [es.enter_context(nc.sbuf_tensor(f"l1_ss{i}", [128, 512], BF16))
                for i in range(2)]

        for sq_i in range(SEQ_PER_CORE):
            base_t = sq_i * SEQ
            for hp in range(8):
                xn_used = [k.sems["T"].v, k.sems["T"].v]
                psum_free = [k.sems["V"].v] * 3
                for tt in range(4):
                    b = tt % 2
                    gld = k.new_group("sync")
                    lv = None
                    for kc in range(8):
                        lv = k.dma("sync", xnt[b][kc][:],
                                   xnfm[kc * 128:(kc + 1) * 128,
                                        base_t + tt * 512:base_t + (tt + 1) * 512],
                                   gld,
                                   wait=[(T, xn_used[b])] if kc == 0 else [])
                    k.wait("tensor", lv[0], lv[1])
                    for pi, (tiles, dst) in enumerate(
                            ((wqd, qlat), (wkd, klat), (wvd, vlat))):
                        p = C.P[pi]
                        for kc in range(8):
                            wl = [(V, psum_free[pi])] if kc == 0 else []
                            k.op("tensor", "matmul", p[:, :],
                                 tiles[kc][:, hp * 128:(hp + 1) * 128],
                                 xnt[b][kc][:], start=(kc == 0), stop=(kc == 7),
                                 wait=wl, inc=(T if kc == 7 else None))
                        tv = k.sems["T"].v
                        if pi == 2:
                            xn_used[b] = tv
                        psum_free[pi] = k.op("vector", "tensor_copy",
                                             dst[:, tt * 512:(tt + 1) * 512],
                                             p[:, :], wait=[(T, tv)], inc=V)
                lat_done = k.sems["V"].v
                k.wait("tensor", V, lat_done)
                for hh in range(2):
                    lb = hh * 64
                    for (lat, dsta, ucol, ecol, rcol, ctab) in (
                            (qlat, qa[hh], 0, 32, 64, None),
                            (klat, ka[hh], 96, 128, 160, None)):
                        pc = C.P[0]
                        pe_ = C.P[1]
                        per = C.P[2]
                        for ts in range(4):
                            cs = ts * 512
                            wl = [(V, k.sems["V"].v)]
                            k.op("tensor", "matmul", pc[0:32, :],
                                 ups[lb:lb + 64, ucol:ucol + 32],
                                 lat[lb:lb + 64, cs:cs + 512], start=True,
                                 stop=True, wait=wl)
                            k.op("tensor", "matmul", pe_[0:32, :],
                                 ups[lb:lb + 64, ecol:ecol + 32],
                                 lat[lb:lb + 64, cs:cs + 512], start=True,
                                 stop=True)
                            tv = k.op("tensor", "matmul", per[0:32, :],
                                      ups[lb:lb + 64, rcol:rcol + 32],
                                      lat[lb:lb + 64, cs:cs + 512], start=True,
                                      stop=True, inc=T)
                            k.op("vector", "tensor_copy",
                                 dsta[0:32, cs:cs + 512], pc[0:32, :],
                                 wait=[(T, tv)])
                            k.op("vector", "tensor_copy", e1[:, cs:cs + 512],
                                 pe_[0:32, :])
                            k.op("vector", "tensor_copy", e2[:, cs:cs + 512],
                                 per[0:32, :], inc=V)
                        k.op("vector", "tensor_mul", e1[:], e1[:],
                             C.sb["l1_cos"][:])
                        k.op("vector", "tensor_mul", e2[:], e2[:],
                             C.sb["l1_sin"][:])
                        vv = k.op("vector", "tensor_add", e1[:], e1[:], e2[:],
                                  inc=V)
                        k.op("vector", "tensor_copy", dsta[32:64, :], e1[:],
                             inc=V)
                    vw = [0, 0]
                    for tb in range(16):
                        pv = C.P[3 + tb % 2]
                        wl = [(V, vw[tb % 2])] if vw[tb % 2] else []
                        tv = k.op("tensor", "matmul", pv[0:128, 0:64],
                                  vlat[lb:lb + 64, tb * 128:(tb + 1) * 128],
                                  ups[lb:lb + 64, 192:256], start=True,
                                  stop=True, wait=wl, inc=T)
                        vw[tb % 2] = k.op("vector", "tensor_copy",
                                          vto[hh][:, tb * 65:tb * 65 + 64],
                                          pv[0:128, 0:64], wait=[(T, tv)], inc=V)
                    k.op("vector", "memset",
                         vto[hh][:].rearrange("p (t c) -> p t c", c=65)[:, :, 64:65],
                         1.0, inc=V)
                for hh in range(2):
                    sv = k.op("scalar", "activation", sq64[:, :], ka[hh][0:64, :],
                              AF.Square,
                              wait=[(T, k.sems["T"].v), (V, k.sems["V"].v)], inc=S)
                    cpv = 0
                    for tt in range(4):
                        pst = C.P[3 + tt % 2]
                        wl = [(S, sv)]
                        if tt >= 2:
                            wl.append((V, cpv))
                        tv = k.op("tensor", "matmul", pst[0:1, :],
                                  C.ones_col_bf[0:64, :],
                                  sq64[:, tt * 512:(tt + 1) * 512], start=True,
                                  stop=True, wait=wl, inc=T)
                        cpv = k.op("vector", "tensor_copy",
                                   nrm[0:1, tt * 512:(tt + 1) * 512], pst[0:1, :],
                                   wait=[(T, tv)], inc=V)
                    kv = k.op("vector", "reduce_max", kmx[:], nrm[:], AX.X, inc=V)
                    sv = k.op("scalar", "activation", sq64[:, :], qa[hh][0:64, :],
                              AF.Square, wait=[(T, k.sems["T"].v)], inc=S)
                    for tt in range(4):
                        pst = C.P[3 + tt % 2]
                        wl = [(S, sv), (V, kv)]
                        if tt >= 2:
                            wl.append((V, cpv))
                        tv = k.op("tensor", "matmul", pst[0:1, :],
                                  C.ones_col_bf[0:64, :],
                                  sq64[:, tt * 512:(tt + 1) * 512], start=True,
                                  stop=True, wait=wl, inc=T)
                        cpv = k.op("vector", "tensor_copy",
                                   nrm[0:1, tt * 512:(tt + 1) * 512], pst[0:1, :],
                                   wait=[(T, tv)], inc=V)
                    vv = k.op("vector", "tensor_scalar_mul", nrm[:], nrm[:],
                              kmx[0:1, 0:1], inc=V)
                    k.op("vector", "tensor_single_scalar", nrm[:], nrm[:],
                         1e-30, AL.add, wait=[(V, vv)])
                    k.op("scalar", "activation", nrm[:], nrm[:], AF.Ln,
                         wait=[(V, k.sems["V"].v)])
                    k.op("vector", "tensor_single_scalar", nrm[:], nrm[:], 0.5,
                         AL.mult, wait=[(S, k.sems["S"].v)])
                    sv = k.op("scalar", "activation", nrm[:], nrm[:], AF.Exp,
                              wait=[(V, k.sems["V"].v)], inc=S)
                    k.op("vector", "tensor_scalar", qa[hh][64:65, :], nrm[0:1, :],
                         -1.0, -1.0, AL.mult, AL.add, wait=[(S, sv)])
                    k.op("vector", "memset", ka[hh][64:65, :], 1.0, inc=V)
                aug_ready = k.sems["V"].v
                k.wait("tensor", V, aug_ready)
                for hh in range(2):
                    qh, kh, vh = qa[hh], ka[hh], vto[hh]
                    y_ev = k.sems["V"].v
                    sc_ev = [k.sems["S"].v, k.sems["S"].v]
                    for Q in range(4):
                        for kt in range(4 * Q + 4):
                            pb = kt % 2
                            p = C.P[pb]
                            wl = [(S, sc_ev[pb])] if sc_ev[pb] else []
                            tv = k.op("tensor", "matmul", p[:, :],
                                      kh[0:65, kt * 128:(kt + 1) * 128],
                                      qh[0:65, Q * 512:(Q + 1) * 512],
                                      start=True, stop=True, wait=wl, inc=T)
                            sc_ev[pb] = k.op("scalar", "activation",
                                             attT[pb][:, :], p[:, :], AF.Exp,
                                             wait=[(T, tv)], inc=S)
                            r = kt - 4 * Q
                            src = (S, sc_ev[pb])
                            if r >= 0:
                                mv = k.op("vector", "tensor_tensor",
                                          attT[pb][:, :], attT[pb][:, :],
                                          C.sb[f"trimask{r}"][:, :], AL.mult,
                                          wait=[src], inc=V)
                                src = (V, mv)
                            wl2 = [src]
                            if kt == 0:
                                wl2.append((V, y_ev))
                            k.op("tensor", "matmul", C.P[2][0:65, :],
                                 vh[:, kt * 65:kt * 65 + 65], attT[pb][:, :],
                                 start=(kt == 0), stop=(kt == 4 * Q + 3),
                                 wait=wl2, inc=(T if kt == 4 * Q + 3 else None))
                        tv = k.sems["T"].v
                        rv = k.op("vector", "reciprocal", rd[:],
                                  C.P[2][64:65, :], wait=[(T, tv)], inc=V)
                        tv2 = k.op("tensor", "matmul", C.P[3][0:64, :],
                                   C.ones_row[0:1, 0:64], rd[:], start=True,
                                   stop=True, wait=[(V, rv)], inc=T)
                        cv = k.op("vector", "tensor_copy", rb[:], C.P[3][0:64, :],
                                  wait=[(T, tv2)], inc=V)
                        y_ev = k.op("vector", "tensor_mul",
                                    ymix[hp][hh * 64:hh * 64 + 64,
                                             Q * 512:(Q + 1) * 512],
                                    C.P[2][0:64, :], rb[:], inc=V)
                        k.wait("tensor", V, y_ev)
            psum_free = [k.sems["V"].v] * 2
            stg_done = [None, None]
            for tt in range(4):
                for fc in range(8):
                    pi = fc % 2
                    p = C.P[pi]
                    for kc in range(8):
                        wl = [(V, psum_free[pi])] if kc == 0 else []
                        k.op("tensor", "matmul", p[:, :],
                             wc[kc][:, fc * 128:(fc + 1) * 128],
                             ymix[kc][:, tt * 512:(tt + 1) * 512],
                             start=(kc == 0), stop=(kc == 7), wait=wl,
                             inc=(T if kc == 7 else None))
                    sb = fc % 2
                    wl = [(T, k.sems["T"].v)]
                    if stg_done[sb]:
                        wl.append(stg_done[sb])
                    psum_free[pi] = k.op("vector", "tensor_copy", sstg[sb][:],
                                         p[:, :], wait=wl, inc=V)
                    stg_done[sb] = k.dma(
                        "gpsimd",
                        s_d[fc * 128:(fc + 1) * 128,
                            base_t + tt * 512:base_t + (tt + 1) * 512],
                        sstg[sb][:], wait=[(V, k.sems["V"].v)])
        k.full_barrier()
    if stop_after == "mixer":
        return

    with contextlib.ExitStack() as es:
        C.load_tables(es, ["l1_tie"])
        wcr = [es.enter_context(nc.sbuf_tensor(f"l1_wcr{c}", [128, 128], BF16))
               for c in range(8)]
        wfs = [es.enter_context(nc.sbuf_tensor(f"l1_wfa{i}", [128, 128], F32))
               for i in range(2)]
        bcc, _ = C.colize("l1_bcc", es, W["L1_bc"], 128)
        cv_done = [0, 0]
        for c in range(8):
            b = c % 2
            wl = [(V, cv_done[b])] if cv_done[b] else []
            lv = k.dma("sync", wfs[b][:], W["L1_Wc_r"][c * 128:(c + 1) * 128, :],
                       wait=wl)
            cv_done[b] = k.op("vector", "tensor_copy", wcr[c][:], wfs[b][:],
                              wait=[lv], inc=V)
        xg = [[es.enter_context(nc.sbuf_tensor(f"l1_xg{i}_{c}", [128, 32], BF16))
               for c in range(8)] for i in range(2)]
        xtk = [es.enter_context(nc.sbuf_tensor(f"l1_xtk{i}", [32, 1024], BF16))
               for i in range(2)]
        lg = es.enter_context(nc.sbuf_tensor("l1_lg", [128, 32], F32))
        mrow = es.enter_context(nc.sbuf_tensor("l1_mrow", [128, 1], F32))
        permb = es.enter_context(nc.sbuf_tensor("l1_permb", [128, 32], BF16))
        permt = es.enter_context(nc.sbuf_tensor("l1_permt", [32, 128], BF16))
        pst = [es.enter_context(nc.sbuf_tensor(f"l1_pst{i}", [128, 512], BF16))
               for i in range(2)]
        k.full_barrier()
        xn_used = [k.sems["T"].v, k.sems["T"].v]
        xtk_used = [k.sems["T"].v, k.sems["T"].v]
        prev_write = [None, None]
        for g in range(128):
            b = g % 2
            gld = k.new_group("sync")
            lv = None
            for kc in range(8):
                lv = k.dma("sync", xg[b][kc][:],
                           xnfm[kc * 128:(kc + 1) * 128, g * 32:(g + 1) * 32],
                           gld, wait=[(T, xn_used[b])] if kc == 0 else [])
            xl = k.dma("sync", xtk[b][:],
                       xntok[g * 32:(g + 1) * 32, :], gld,
                       wait=[(T, xtk_used[b])])
            k.wait("tensor", lv[0], lv[1])
            p = C.P[b]
            for kc in range(8):
                wl = [(V, k.sems["V"].v)] if kc == 0 else []
                k.op("tensor", "matmul", p[:, 0:32], wcr[kc][:], xg[b][kc][:],
                     start=(kc == 0), stop=(kc == 7), wait=wl,
                     inc=(T if kc == 7 else None))
            tv = k.sems["T"].v
            xn_used[b] = tv
            k.op("vector", "scalar_tensor_tensor", lg[:], p[:, 0:32],
                 bcc[:, 0:1], C.sb["l1_tie"][:], AL.add, AL.add, wait=[(T, tv)])
            k.op("vector", "reduce_max", mrow[:], lg[:], AX.X)
            pv = k.op("vector", "tensor_single_scalar", permb[:], lg[:],
                      mrow[:, 0:1], AL.is_ge, inc=V)
            k.dma("gpsimd", perm_d[g], permb[:], wait=[(V, pv)])
            tv = k.op("tensor", "transpose", C.PBF[0][0:32, 0:128], permb[:],
                      C.sb["iden_bf"][:], wait=[(V, pv)], inc=T)
            pv2 = k.op("vector", "tensor_copy", permt[:], C.PBF[0][0:32, 0:128],
                       wait=[(T, tv)], inc=V)
            for ob in range(2):
                pp = C.P[2 + ob]
                tv = k.op("tensor", "matmul", pp[:, :], permt[:],
                          xtk[b][:, ob * 512:(ob + 1) * 512], start=True,
                          stop=True,
                          wait=[(V, pv2), xl] if ob == 0 else [], inc=T)
                xtk_used[b] = tv
                wl = [(T, tv)]
                if prev_write[ob]:
                    wl.append(prev_write[ob])
                vv = k.op("vector", "tensor_copy", pst[ob][:], pp[:, :],
                          wait=wl, inc=V)
                prev_write[ob] = k.dma("gpsimd",
                                       p_d[g, :, ob * 512:(ob + 1) * 512],
                                       pst[ob][:], wait=[(V, vv)])
        k.full_barrier()
    if stop_after == "ffn_a":
        return

    with contextlib.ExitStack() as es:
        pfm = [es.enter_context(nc.sbuf_tensor(f"l1_pfm{c}", [128, 128], BF16))
               for c in range(8)]
        w1t = [es.enter_context(nc.sbuf_tensor(f"l1_w1t{c}", [128, 32], BF16))
               for c in range(8)]
        w2t = es.enter_context(nc.sbuf_tensor("l1_w2t", [32, 1024], BF16))
        wfs = [es.enter_context(nc.sbuf_tensor(f"l1_wfb{i}", [128, 32], F32))
               for i in range(2)]
        w2f = es.enter_context(nc.sbuf_tensor("l1_w2f", [32, 1024], F32))
        b1t = es.enter_context(nc.sbuf_tensor("l1_b1t", [32, 128], F32))
        innt = es.enter_context(nc.sbuf_tensor("l1_innt", [32, 128], BF16))
        rst = [es.enter_context(nc.sbuf_tensor(f"l1_rst{i}", [128, 512], BF16))
               for i in range(2)]
        gb1 = k.new_group("sync")
        b1v = k.dma("sync", b1t[:], W["L1_B1T"], gb1)
        k.wait("vector", b1v[0], b1v[1])
        cv_done = [0, 0]
        w2_done = 0
        prev_write = [None, None]
        pfm_used = k.sems["T"].v
        for e in range(128):
            gp = k.new_group("sync")
            pv = None
            for c in range(8):
                pv = k.dma("sync", pfm[c][:],
                           p_d[:, e, c * 128:(c + 1) * 128], gp,
                           wait=[(T, pfm_used)] if c == 0 else [],
                           method="dma_start_transpose")
            gw = k.new_group("sync")
            wv = None
            for c in range(8):
                b = c % 2
                wl = [(V, cv_done[b])] if cv_done[b] else []
                wv = k.dma("sync", wfs[b][:], W["L1_W1"][e, c * 128:(c + 1) * 128, :],
                           gw, wait=wl)
                cv_done[b] = k.op("vector", "tensor_copy", w1t[c][:], wfs[b][:],
                                  wait=[wv], inc=V)
            w2l = k.dma("sync", w2f[:], W["L1_W2T"][e], wait=[(V, w2_done)]
                        if w2_done else [])
            w2_done = k.op("vector", "tensor_copy", w2t[:], w2f[:], wait=[w2l],
                           inc=V)
            k.wait("tensor", pv[0], pv[1])
            k.wait("tensor", V, k.sems["V"].v)
            p = C.P[0]
            for c in range(8):
                k.op("tensor", "matmul", p[0:32, 0:128], w1t[c][:], pfm[c][:],
                     start=(c == 0), stop=(c == 7),
                     inc=(T if c == 7 else None))
            tv = k.sems["T"].v
            pfm_used = tv
            k.op("vector", "tensor_scalar_add", innt[:], p[0:32, 0:128],
                 b1t[:, e:e + 1], wait=[(T, tv)])
            iv = k.op("vector", "tensor_relu", innt[:], innt[:], inc=V)
            for ob in range(2):
                pp = C.P[2 + ob]
                tv = k.op("tensor", "matmul", pp[:, :], innt[:],
                          w2t[:, ob * 512:(ob + 1) * 512], start=True, stop=True,
                          wait=[(V, iv)] if ob == 0 else [], inc=T)
                wl = [(T, tv)]
                if prev_write[ob]:
                    wl.append(prev_write[ob])
                vv = k.op("vector", "tensor_copy", rst[ob][:], pp[:, :],
                          wait=wl, inc=V)
                prev_write[ob] = k.dma("gpsimd",
                                       r_d[e, :, ob * 512:(ob + 1) * 512],
                                       rst[ob][:], wait=[(V, vv)])
        k.full_barrier()
    if stop_after == "ffn_b":
        return

    with contextlib.ExitStack() as es:
        b2t = [es.enter_context(nc.sbuf_tensor(f"l1_b2t{i}", [128, 1024], BF16))
               for i in range(1)]
        b2f = es.enter_context(nc.sbuf_tensor("l1_b2f", [128, 1024], F32))
        rcol = [es.enter_context(nc.sbuf_tensor(f"l1_rc{i}", [128, 1024], BF16))
                for i in range(2)]
        prmg = [es.enter_context(nc.sbuf_tensor(f"l1_pg{i}", [128, 32], BF16))
                for i in range(2)]
        sfc = es.enter_context(nc.sbuf_tensor("l1_sfc", [128, 32], BF16))
        gt = es.enter_context(nc.sbuf_tensor("l1_gt", [128, 32], BF16))
        sg = es.enter_context(nc.sbuf_tensor("l1_sg", [128, 32], F32))
        xsb = es.enter_context(nc.sbuf_tensor("l1_xsb", [32, 1024], F32))
        gb = k.new_group("sync")
        b2l = k.dma("sync", b2f[:], W["L1_B2"], gb)
        k.op("vector", "tensor_copy", b2t[0][:], b2f[:], wait=[b2l], inc=V)
        k.full_barrier()
        x_st = None
        rc_used = [k.sems["T"].v, k.sems["T"].v]
        pg_used = [k.sems["T"].v, k.sems["T"].v]
        for g in range(128):
            b = g % 2
            grp = k.new_group("sync")
            rv = k.dma("sync", rcol[b][:], r_d[:, g, :], grp,
                       wait=[(T, rc_used[b])])
            pgl = k.dma("sync", prmg[b][:], perm_d[g], grp,
                        wait=[(T, pg_used[b])])
            xl = k.dma("sync", xsb[:], x_dram[g * 32:(g + 1) * 32, :],
                       wait=[x_st] if x_st else [])
            k.wait("tensor", pgl[0], pgl[1])
            sgl = k.new_group("sync")
            sv_done = k.sems["V"].v
            for dc in range(8):
                p = C.P[dc % 2]
                sl = k.dma("sync", sfc[:],
                           s_d[dc * 128:(dc + 1) * 128, g * 32:(g + 1) * 32],
                           sgl, wait=[(V, sv_done)])
                k.op("tensor", "matmul", p[:, 0:32],
                     rcol[b][:, dc * 128:(dc + 1) * 128], prmg[b][:],
                     start=True, stop=False,
                     wait=[(V, k.sems["V"].v)] if dc < 2 else [])
                tv = k.op("tensor", "matmul", p[:, 0:32],
                          b2t[0][:, dc * 128:(dc + 1) * 128], prmg[b][:],
                          start=False, stop=True, inc=T)
                if dc >= 6:
                    rc_used[b] = tv
                    pg_used[b] = tv
                sv = k.op("scalar", "activation", gt[:], p[:, 0:32], AF.Silu,
                          wait=[(T, tv)], inc=S)
                mv = k.op("vector", "tensor_mul", sg[:], gt[:], sfc[:],
                          wait=[(S, sv), sl], inc=V)
                sv_done = mv
                pb2 = dc % 2
                tv2 = k.op("tensor", "transpose", C.P[4 + pb2][0:32, 0:128],
                           sg[:], C.sb["iden_f32"][:],
                           wait=[(V, mv)], inc=T)
                k.op("vector", "tensor_add", xsb[:, dc * 128:(dc + 1) * 128],
                     xsb[:, dc * 128:(dc + 1) * 128],
                     C.P[4 + pb2][0:32, 0:128],
                     wait=[(T, tv2), xl] if dc == 0 else [(T, tv2)], inc=V)
            x_st = k.dma("gpsimd", xout_dram[g * 32:(g + 1) * 32, :], xsb[:],
                         wait=[(V, k.sems["V"].v)])
        k.full_barrier()


def scratch_tensors(nc, which, debug=False):
    d = {}
    kind = "ExternalOutput" if debug else "Internal"
    mk = lambda n, sh, dt: nc.dram_tensor(n, sh, dt, kind=kind).ap()
    d["xnfm"] = mk("xnfm", [1024, 4096], BF16)
    d["xntok"] = mk("xntok", [4096, 1024], BF16)
    d["s_fm"] = mk("s_fm", [1024, 4096], BF16)
    if which in ("l0", "all"):
        d["h1"] = mk("h1", [4096, 4096], BF16)
        d["h2"] = mk("h2", [4096, 4096], BF16)
        d["w1eff"] = mk("w1eff", [4096, 4096], BF16)
    return d


def build_l2_program(stop_after="all", debug=False):
    nc = bass.Bass("TRN2", target_bir_lowering=False)
    k = K(nc)
    x_in = nc.dram_tensor("x_in", [T_CORE, DM], F32, kind="ExternalInput").ap()
    x_out = nc.dram_tensor("x_out", [T_CORE, DM], F32, kind="ExternalOutput").ap()
    C = Ctx(k, used=['iden_bf', 'iden_f32', 'trimask0', 'trimask1', 'trimask2', 'trimask3'])
    W = {n: nc.dram_tensor(n, sh, dt, kind="ExternalInput").ap()
         for n, (sh, dt) in L2_WSPECS.items()}
    scr = scratch_tensors(nc, "l0", debug=debug)
    emit_layer2(k, C, W, x_in, x_out, scr, stop_after=stop_after)
    k.finalize()
    return nc


def prep_l2_inputs(w):
    t = lambda a: np.ascontiguousarray(np.asarray(a, np.float32).T)
    c = lambda a: np.ascontiguousarray(np.asarray(a, np.float32))
    return {
        "L2_W_hgT": t(w["L2_attn_W_hg"]),
        "L2_attn_b_hg": c(w["L2_attn_b_hg"]),
        "L2_W_outT": t(w["L2_attn_W_out"]),
        "L2_attn_b_out": c(w["L2_attn_b_out"]),
        "L2_attn_h0": c(w["L2_attn_h0"]),
        "L2_w1T": t(w["L2_ffn_w1_w"]),
        "L2_ffn_w1_s": c(w["L2_ffn_w1_s"]),
        "L2_w2T": t(w["L2_ffn_w2_w"]),
        "L2_ffn_w2_s": c(w["L2_ffn_w2_s"]),
        "L2_h1T": t(w["L2_ffn_h1_w"]),
        "L2_ffn_h1_s": c(w["L2_ffn_h1_s"]),
        "L2_h2T": t(w["L2_ffn_h2_w"]),
        "L2_ffn_h2_s": c(w["L2_ffn_h2_s"]),
        "L2_w3T": t(w["L2_ffn_w3_w"]),
        "L2_ffn_w3_s": c(w["L2_ffn_w3_s"]),
    }


_L2_CACHE = {}


def run_layer2(x_full, inputs, stop_after="all", debug=False, cores=None):
    key = (stop_after, debug)
    if _L2_CACHE.get("key") != key:
        _L2_CACHE["nc"] = build_l2_program(stop_after, debug)
        _L2_CACHE["consts"] = host_consts()
        _L2_CACHE["key"] = key
    nc = _L2_CACHE["nc"]
    wmap = prep_l2_inputs(inputs)
    in_maps = []
    ncores = cores or N_CORES
    for core in range(ncores):
        m = {n: _L2_CACHE["consts"][n] for n in _L2_CACHE["nc"]._used_consts}
        m.update(wmap)
        m["x_in"] = np.ascontiguousarray(
            x_full[core * 2:(core + 1) * 2].reshape(T_CORE, DM), np.float32)
        in_maps.append(m)
    res = run_bass_kernel_spmd(nc, in_maps, core_ids=list(range(ncores)))
    if debug:
        return res
    return np.concatenate(
        [np.asarray(r["x_out"]).reshape(2, 2048, 1024) for r in res.results], 0)


def build_l3_program(stop_after="all", debug=False):
    nc = bass.Bass("TRN2", target_bir_lowering=False)
    k = K(nc)
    x_in = nc.dram_tensor("x_in", [T_CORE, DM], F32, kind="ExternalInput").ap()
    x_out = nc.dram_tensor("x_out", [T_CORE, DM], F32, kind="ExternalOutput").ap()
    C = Ctx(k, used=['iden_bf', 'iden_f32', 'trimask0', 'trimask1', 'trimask2', 'trimask3', 'l3_cos', 'l3_sin'])
    W = {n: nc.dram_tensor(n, sh, dt, kind="ExternalInput").ap()
         for n, (sh, dt) in L3_WSPECS.items()}
    scr = scratch_tensors(nc, "l0", debug=debug)
    emit_layer3(k, C, W, x_in, x_out, scr, stop_after=stop_after)
    k.finalize()
    return nc


def prep_l3_inputs(w):
    t = lambda a: np.ascontiguousarray(np.asarray(a, np.float32).T)
    c = lambda a: np.ascontiguousarray(np.asarray(a, np.float32))
    return {
        "L3_w1T": t(w["L3_attn_qkv_w1"]),
        "L3_w2T": t(w["L3_attn_qkv_w2"]),
        "L3_cw1T": t(w["L3_attn_cproj_w1"]),
        "L3_cw2T": t(w["L3_attn_cproj_w2"]),
        "L3_Wr": c(w["L3_ffn_Wr"]),
        "L3_WiT": np.ascontiguousarray(
            np.asarray(w["L3_ffn_Wi"], np.float32).transpose(0, 2, 1)),
        "L3_Wo": c(w["L3_ffn_Wo"]),
    }


_L3_CACHE = {}


def run_layer3(x_full, inputs, stop_after="all", debug=False, cores=None):
    key = (stop_after, debug)
    if _L3_CACHE.get("key") != key:
        _L3_CACHE["nc"] = build_l3_program(stop_after, debug)
        _L3_CACHE["consts"] = host_consts()
        _L3_CACHE["key"] = key
    nc = _L3_CACHE["nc"]
    wmap = prep_l3_inputs(inputs)
    in_maps = []
    ncores = cores or N_CORES
    for core in range(ncores):
        m = {n: _L3_CACHE["consts"][n] for n in _L3_CACHE["nc"]._used_consts}
        m.update(wmap)
        m["x_in"] = np.ascontiguousarray(
            x_full[core * 2:(core + 1) * 2].reshape(T_CORE, DM), np.float32)
        in_maps.append(m)
    res = run_bass_kernel_spmd(nc, in_maps, core_ids=list(range(ncores)))
    if debug:
        return res
    return np.concatenate(
        [np.asarray(r["x_out"]).reshape(2, 2048, 1024) for r in res.results], 0)


def build_l1_program(stop_after="all", debug=False):
    nc = bass.Bass("TRN2", target_bir_lowering=False)
    k = K(nc)
    x_in = nc.dram_tensor("x_in", [T_CORE, DM], F32, kind="ExternalInput").ap()
    x_out = nc.dram_tensor("x_out", [T_CORE, DM], F32, kind="ExternalOutput").ap()
    C = Ctx(k, used=['iden_bf', 'iden_f32', 'trimask0', 'trimask1', 'trimask2', 'trimask3', 'l1_cos', 'l1_sin', 'l1_tie'])
    W = {n: nc.dram_tensor(n, sh, dt, kind="ExternalInput").ap()
         for n, (sh, dt) in L1_WSPECS.items()}
    scr = scratch_tensors(nc, "l0", debug=debug)
    emit_layer1(k, C, W, x_in, x_out, scr, stop_after=stop_after)
    k.finalize()
    return nc


def prep_l1_inputs(w):
    c = lambda a: np.ascontiguousarray(np.asarray(a, np.float32))
    return {
        "L1_attn_Wq_down": c(w["L1_attn_Wq_down"]),
        "L1_attn_Wk_down": c(w["L1_attn_Wk_down"]),
        "L1_attn_Wv_down": c(w["L1_attn_Wv_down"]),
        "L1_attn_Wq_up_c": c(w["L1_attn_Wq_up_c"]),
        "L1_attn_Wq_up_e": c(w["L1_attn_Wq_up_e"]),
        "L1_attn_Wk_up_c": c(w["L1_attn_Wk_up_c"]),
        "L1_attn_Wk_up_e": c(w["L1_attn_Wk_up_e"]),
        "L1_attn_Wv_up": c(w["L1_attn_Wv_up"]),
        "L1_attn_Wc": c(w["L1_attn_Wc"]),
        "L1_Wc_r": c(w["L1_ffn_Wc"]).reshape(1024, 128),
        "L1_bc": c(w["L1_ffn_bc"]).reshape(128),
        "L1_W1": c(w["L1_ffn_W1"]).reshape(128, 1024, 32),
        "L1_W2T": np.ascontiguousarray(
            np.asarray(w["L1_ffn_W2"], np.float32).reshape(128, 1024, 32)
            .transpose(0, 2, 1)),
        "L1_B1T": np.ascontiguousarray(
            np.asarray(w["L1_ffn_b1"], np.float32).reshape(128, 32).T),
        "L1_B2": c(w["L1_ffn_b2"]).reshape(128, 1024),
    }


_L1_CACHE = {}


def run_layer1(x_full, inputs, stop_after="all", debug=False, cores=None):
    key = (stop_after, debug)
    if _L1_CACHE.get("key") != key:
        _L1_CACHE["nc"] = build_l1_program(stop_after, debug)
        _L1_CACHE["consts"] = host_consts()
        _L1_CACHE["key"] = key
    nc = _L1_CACHE["nc"]
    wmap = prep_l1_inputs(inputs)
    in_maps = []
    ncores = cores or N_CORES
    for core in range(ncores):
        m = {n: _L1_CACHE["consts"][n] for n in _L1_CACHE["nc"]._used_consts}
        m.update(wmap)
        m["x_in"] = np.ascontiguousarray(
            x_full[core * 2:(core + 1) * 2].reshape(T_CORE, DM), np.float32)
        in_maps.append(m)
    res = run_bass_kernel_spmd(nc, in_maps, core_ids=list(range(ncores)))
    if debug:
        return res
    return np.concatenate(
        [np.asarray(r["x_out"]).reshape(2, 2048, 1024) for r in res.results], 0)


def build_l0_program(stop_after="all", debug=False):
    nc = bass.Bass("TRN2", target_bir_lowering=False)
    k = K(nc)
    x_in = nc.dram_tensor("x_in", [T_CORE, DM], F32, kind="ExternalInput").ap()
    x_out = nc.dram_tensor("x_out", [T_CORE, DM], F32, kind="ExternalOutput").ap()
    C = Ctx(k, used=['iden_bf', 'iden_f32', 'trimask0', 'trimask1', 'trimask2', 'trimask3', 'l0_cos', 'l0_sin'])
    W = {n: nc.dram_tensor(n, sh, dt, kind="ExternalInput").ap()
         for n, (sh, dt) in L0_WSPECS.items()}
    scr = scratch_tensors(nc, "l0", debug=debug)
    emit_layer0(k, C, W, x_in, x_out, scr, stop_after=stop_after)
    k.finalize()
    return nc


def prep_l0_inputs(w):
    out = {}
    out["L0_attn_W_attn"] = np.ascontiguousarray(w["L0_attn_W_attn"], np.float32)
    out["L0_attn_b_attn"] = np.ascontiguousarray(w["L0_attn_b_attn"], np.float32)
    out["L0_attn_W_proj"] = np.ascontiguousarray(w["L0_attn_W_proj"], np.float32)
    out["L0_attn_b_proj"] = np.ascontiguousarray(w["L0_attn_b_proj"], np.float32)
    for n in ("L0_ffn_k0", "L0_ffn_k1", "L0_ffn_k2"):
        out[n] = np.ascontiguousarray(np.asarray(w[n])).view(np.uint8)
    for n in ("L0_ffn_s0", "L0_ffn_s1", "L0_ffn_s2"):
        out[n] = np.asarray(w[n], np.float32).reshape(1)
    return out


_L0_CACHE = {}


def run_layer0(x_full, inputs, stop_after="all", debug=False, cores=None):
    key = (stop_after, debug)
    if _L0_CACHE.get("key") != key:
        _L0_CACHE["nc"] = build_l0_program(stop_after, debug)
        _L0_CACHE["consts"] = host_consts()
        _L0_CACHE["key"] = key
    nc = _L0_CACHE["nc"]
    wmap = prep_l0_inputs(inputs)
    in_maps = []
    for core in range(N_CORES):
        m = {n: _L0_CACHE["consts"][n] for n in _L0_CACHE["nc"]._used_consts}
        m.update(wmap)
        m["x_in"] = np.ascontiguousarray(
            x_full[core * 2:(core + 1) * 2].reshape(T_CORE, DM), np.float32)
        in_maps.append(m)
    if cores is not None:
        in_maps = in_maps[:cores]
        res = run_bass_kernel_spmd(nc, in_maps, core_ids=list(range(cores)))
    else:
        res = run_bass_kernel_spmd(nc, in_maps, core_ids=list(range(N_CORES)))
    if debug:
        return res
    return np.concatenate(
        [np.asarray(r["x_out"]).reshape(2, 2048, 1024) for r in res.results], 0)


BASE_CONSTS = ["iden_bf", "iden_f32", "trimask0", "trimask1", "trimask2",
               "trimask3"]


def build_full_program():
    nc = bass.Bass("TRN2", target_bir_lowering=False)
    k = K(nc)
    x_in = nc.dram_tensor("x_in", [T_CORE, DM], F32, kind="ExternalInput").ap()
    x_out = nc.dram_tensor("x_out", [T_CORE, DM], F32, kind="ExternalOutput").ap()
    xa = nc.dram_tensor("x_a", [T_CORE, DM], F32, kind="Internal").ap()
    xb = nc.dram_tensor("x_b", [T_CORE, DM], F32, kind="Internal").ap()
    C = Ctx(k, used=BASE_CONSTS)
    W = {}
    for spec in (L0_WSPECS, L1_WSPECS, L2_WSPECS, L3_WSPECS):
        for n, (sh, dt) in spec.items():
            W[n] = nc.dram_tensor(n, sh, dt, kind="ExternalInput").ap()
    scr = scratch_tensors(nc, "l0")
    emit_layer0(k, C, W, x_in, xa, scr)
    emit_layer1(k, C, W, xa, xb, scr)
    emit_layer2(k, C, W, xb, xa, scr)
    emit_layer3(k, C, W, xa, x_out, scr)
    k.finalize()
    return nc


_FULL_CACHE = {}


def kernel_fused(inputs):
    if "nc" not in _FULL_CACHE:
        _FULL_CACHE["nc"] = build_full_program()
        _FULL_CACHE["consts"] = host_consts()
    nc = _FULL_CACHE["nc"]
    x = np.ascontiguousarray(np.asarray(inputs["L0_attn_x"]),
                             np.float32).reshape(16, 2048, 1024)
    wmap = {}
    wmap.update(prep_l0_inputs(inputs))
    wmap.update(prep_l1_inputs(inputs))
    wmap.update(prep_l2_inputs(inputs))
    wmap.update(prep_l3_inputs(inputs))
    consts = {n: _FULL_CACHE["consts"][n] for n in nc._used_consts}
    in_maps = []
    for core in range(N_CORES):
        m = dict(consts)
        m.update(wmap)
        m["x_in"] = np.ascontiguousarray(x[core * 2:(core + 1) * 2]
                                         .reshape(T_CORE, DM))
        in_maps.append(m)
    res = run_bass_kernel_spmd(nc, in_maps, core_ids=list(range(N_CORES)))
    return np.concatenate(
        [np.asarray(r["x_out"]).reshape(2, 2048, 1024) for r in res.results],
        0).astype(np.float32)


def kernel(**inputs):
    import os
    if os.environ.get("UNFUSED"):
        x = np.ascontiguousarray(np.asarray(inputs["L0_attn_x"]), np.float32)
        x = x.reshape(16, 2048, 1024)
        x = run_layer0(x, inputs)
        x = run_layer1(x, inputs)
        x = run_layer2(x, inputs)
        x = run_layer3(x, inputs)
        return x.astype(np.float32)
    return kernel_fused(inputs)


# version 04 (final)
# speedup vs baseline: 1.0632x; 1.0162x over previous
import contextlib
import math
import numpy as np
import ml_dtypes

import concourse.bass as bass
import concourse.mybir as mybir
from concourse.bass_utils import run_bass_kernel_spmd

AF = mybir.ActivationFunctionType
AL = mybir.AluOpType
AX = mybir.AxisListType
F32 = mybir.dt.float32
BF16 = mybir.dt.bfloat16
U8 = mybir.dt.uint8
bfdt = ml_dtypes.bfloat16

N_CORES = 8
SEQ = 2048
DM = 1024
SEQ_PER_CORE = 2
T_CORE = SEQ_PER_CORE * SEQ

ENGINES = ("sync", "tensor", "vector", "scalar", "gpsimd")


class Sem:
    def __init__(self, handle, name):
        self.h = handle
        self.name = name
        self.v = 0


class K:
    def __init__(self, nc):
        self.nc = nc
        self.q = {e: [] for e in ENGINES}
        self.sems = {}

    def sem(self, name) -> Sem:
        if name not in self.sems:
            self.sems[name] = Sem(self.nc.alloc_semaphore(name), name)
        return self.sems[name]

    SELF_SEM = {"vector": "V", "scalar": "S", "gpsimd": "P"}

    def op(self, eng, meth, *args, wait=(), inc=None, **kw):
        waits = [(s.h, v) for (s, v) in wait if v > 0]
        incs = []
        after = None
        auto = self.SELF_SEM.get(eng)
        if meth.startswith("dma_start"):
            auto = None
        if auto is not None:
            ssem = self.sem(auto)
            if ssem.v > 0:
                waits.append((ssem.h, ssem.v))
            ssem.v += 1
            incs.append((ssem.h, 1))
            after = ssem.v
        if inc is not None:
            s, kk = inc if isinstance(inc, tuple) else (inc, 1)
            if auto is not None and s is self.sem(auto):
                pass
            else:
                s.v += kk
                incs.append((s.h, kk))
                after = s.v

        use_drain = bool(incs) and not meth.startswith("dma_start") and (
            (eng == "tensor" and meth in ("matmul", "transpose")) or
            eng in ("vector", "scalar", "gpsimd"))

        def emit(e, meth=meth, args=args, kw=kw, waits=waits, incs=incs,
                 use_drain=use_drain):
            for h, v in waits:
                e.wait_ge(h, v)
            ins = getattr(e, meth)(*args, **kw)
            if use_drain:
                ins = e.drain()
            for h, kk in incs:
                ins.then_inc(h, kk)

        self.q[eng].append(emit)
        return after

    def wait(self, eng, sem, v):
        if v <= 0:
            return
        self.q[eng].append(lambda e, h=sem.h, v=v: e.wait_ge(h, v))

    POOL_SIZE = {"sync": 8, "gpsimd": 6, "scalar": 4}

    def _pool(self, eng):
        if not hasattr(self, "_dma_pools"):
            self._dma_pools = {}
        if eng not in self._dma_pools:
            n = self.POOL_SIZE[eng]
            self._dma_pools[eng] = {
                "sems": [self.sem(f"{eng}_dq{i}") for i in range(n)],
                "idx": 0}
        return self._dma_pools[eng]

    def new_group(self, eng):
        p = self._pool(eng)
        sem = p["sems"][p["idx"]]
        p["idx"] = (p["idx"] + 1) % len(p["sems"])
        return {"sem": sem, "throttle": sem.v, "fresh": True}

    def dma(self, eng, dst, src, g=None, wait=(), method="dma_start", **kw):
        if g is None:
            g = self.new_group(eng)
        wl = list(wait)
        if g["fresh"]:
            if g["throttle"] > 0:
                wl.append((g["sem"], g["throttle"]))
            g["fresh"] = False
        v = self.op(eng, method, wait=wl, inc=(g["sem"], 16), out=dst,
                    in_=src, **kw)
        return (g["sem"], v)

    def milestone(self, eng, sem, k=1):
        return self.op(eng, "nop", inc=(sem, k))

    def engv(self, eng):
        return self.sems[self.SELF_SEM.get(eng, "T")].v

    def full_barrier(self):
        T, V, S, P = self.sem("T"), self.sem("V"), self.sem("S"), self.sem("P")
        tv = self.milestone("tensor", T)
        vv = self.milestone("vector", V)
        sv = self.milestone("scalar", S)
        gv = self.milestone("gpsimd", P)
        targets = [(T, tv), (V, vv), (S, sv), (P, gv)]
        for name, sem in self.sems.items():
            if name.endswith(tuple("01234567")) and "_dq" in name and sem.v > 0:
                targets.append((sem, sem.v))
        for e in ENGINES:
            for sem, v in targets:
                self.wait(e, sem, v)

    def finalize(self):
        with self.nc.Block() as block:
            for name, reg in (("sync", block.sync), ("tensor", block.tensor),
                              ("vector", block.vector), ("scalar", block.scalar),
                              ("gpsimd", block.gpsimd)):
                ops = self.q[name]

                def run(e, ops=ops):
                    for f in ops:
                        f(e)

                reg(run)


def host_consts():
    c = {}
    c["iden_bf"] = np.eye(128, dtype=bfdt)
    c["iden_f32"] = np.eye(128, dtype=np.float32)
    for r in range(4):
        i = np.arange(128)[:, None] + r * 128
        j = np.arange(512)[None, :]
        c[f"trimask{r}"] = (i <= j).astype(bfdt)
    theta = 1.0 / (10000.0 ** (np.arange(0, 64, 2, dtype=np.float64) / 64))
    ang = np.arange(16, dtype=np.float64)[:, None] * theta[None, :]
    c["l0_cos"] = np.repeat(np.cos(ang).astype(np.float32).reshape(1, 512), 128, 0)
    c["l0_sin"] = np.repeat(np.sin(ang).astype(np.float32).reshape(1, 512), 128, 0)
    fr = (1.0 / 1024.0) ** np.linspace(0.0, 1.0, 32)
    fr = np.concatenate([fr, np.zeros(32)])
    th = np.arange(2048, dtype=np.float64)[None, :] * fr[:, None]
    cc = np.cos(th).astype(np.float32)
    ss = np.sin(th).astype(np.float32)
    c["l3_cos"] = np.concatenate([cc, cc], 0)
    c["l3_sin"] = np.concatenate([ss, ss], 0)
    invf = 1.0 / (10000.0 ** (np.arange(0, 32, 2, dtype=np.float64) / 32))
    d_ = np.arange(32)
    fr1 = invf[d_ % 16]
    th1 = np.arange(2048, dtype=np.float64)[None, :] * fr1[:, None]
    c["l1_cos"] = np.cos(th1).astype(np.float32)
    c["l1_sin"] = np.sin(th1).astype(np.float32)
    c["l1_tie"] = np.repeat(np.linspace(0.0, 1e-6, 32, dtype=np.float32
                                        ).reshape(1, 32), 128, 0)
    return c


CONST_SPECS = {
    "iden_bf": ([128, 128], BF16),
    "iden_f32": ([128, 128], F32),
    "trimask0": ([128, 512], BF16),
    "trimask1": ([128, 512], BF16),
    "trimask2": ([128, 512], BF16),
    "trimask3": ([128, 512], BF16),
    "l0_cos": ([128, 512], F32),
    "l0_sin": ([128, 512], F32),
    "l3_cos": ([128, 2048], F32),
    "l3_sin": ([128, 2048], F32),
    "l1_cos": ([32, 2048], F32),
    "l1_sin": ([32, 2048], F32),
    "l1_tie": ([128, 32], F32),
}


class Ctx:
    def __init__(self, k: K, used=None):
        self.k = k
        nc = k.nc
        self.dram = {}
        self.sb = {}
        g = k.new_group("sync")
        items = [(n, CONST_SPECS[n]) for n in (used or CONST_SPECS)]
        nc._used_consts = [n for n, _ in items]
        for n, (sh, dt) in items:
            self.dram[n] = nc.dram_tensor(n, sh, dt, kind="ExternalInput").ap()
            t = nc.alloc_sbuf_tensor("c_" + n, sh, dt)
            self.sb[n] = t
            ld = k.dma("sync", t[:], self.dram[n], g)
        self.ones_col = nc.alloc_sbuf_tensor("ones_col_f32", [128, 1], F32)
        self.ones_col_bf = nc.alloc_sbuf_tensor("ones_col_bf", [128, 1], BF16)
        self.ones_row = nc.alloc_sbuf_tensor("ones_row_f32", [1, 128], F32)
        k.wait("vector", ld[0], ld[1])
        k.op("vector", "memset", self.ones_col[:], 1.0)
        k.op("vector", "memset", self.ones_col_bf[:], 1.0)
        k.op("vector", "memset", self.ones_row[:], 1.0)
        self._immtile = nc.alloc_sbuf_tensor("imm_tile", [128, 8], F32)
        self._immvals = {}
        self.P = [nc.alloc_psum_tensor(f"P512_{i}", [128, 512], F32) for i in range(6)]
        self.PBF = [nc.alloc_psum_tensor(f"PBF_{i}", [128, 1024], BF16) for i in range(2)]
        for v in (1e-5, -0.5, 0.5, 1e-30, 2.0, -1.0, 1.1920929e-07):
            self.imm(v)
        k.full_barrier()

    def imm(self, val):
        if val not in self._immvals:
            i = len(self._immvals)
            assert i < 8
            self.k.op("vector", "memset", self._immtile[:, i:i + 1], float(val))
            self._immvals[val] = i
        i = self._immvals[val]
        return self._immtile[:, i:i + 1]

    def load_tables(self, es, names):
        k = self.k
        nc = k.nc
        g = k.new_group("sync")
        lv = None
        for n in names:
            sh, dt = CONST_SPECS[n]
            if n in self.sb and not isinstance(self.sb[n], tuple):
                continue
            t = es.enter_context(nc.sbuf_tensor("t_" + n + str(k.sems["T"].v),
                                                sh, dt))
            if n not in self.dram:
                self.dram[n] = nc.dram_tensor(n, sh, dt,
                                              kind="ExternalInput").ap()
                nc._used_consts.append(n)
            lv = k.dma("sync", t[:], self.dram[n], g)
            self.sb[n] = t
        if lv is not None:
            for e in ("vector", "scalar", "tensor", "gpsimd"):
                k.wait(e, lv[0], lv[1])

    def bcast_scalar(self, name, es, scal_sb_ap, ready, scale=1.0, add=0.0):
        k = self.k
        T, V = k.sem("T"), k.sem("V")
        out = es.enter_context(k.nc.sbuf_tensor(name, [128, 1], F32))
        S = k.sem("S")
        tv = k.op("tensor", "matmul", self.P[5][:, 0:1], self.ones_row[0:1, :],
                  scal_sb_ap, start=True, stop=True,
                  wait=[ready, (V, V.v), (S, S.v)], inc=T)
        vv = k.op("vector", "tensor_scalar", out[:], self.P[5][:, 0:1], scale, add,
                  AL.mult, AL.add, wait=[(T, tv)], inc=V)
        return out, (V, vv)

    def colize(self, name, es, vec_dram_ap, n_feat, rot_fn=None):
        k = self.k
        nc = k.nc
        DS, T, S, V, DG = (k.sem("DS"), k.sem("T"), k.sem("S"), k.sem("V"),
                           k.sem("DG"))
        nch = n_feat // 128
        if rot_fn is not None:
            row = es.enter_context(nc.sbuf_tensor(name + "_row", [1, n_feat], F32))
            lv = k.dma("sync", row[:], vec_dram_ap.rearrange("(a f) -> a f", a=1))
            row_ap, ready = rot_fn(row, lv)
            scr = nc.dram_tensor(name + "_scr", [n_feat], F32, kind="Internal").ap()
            sv = k.dma("gpsimd", scr.rearrange("(a f) -> a f", a=1), row_ap, wait=[ready])
            src, src_ready = scr, sv
        else:
            src, src_ready = vec_dram_ap, None
        rows = es.enter_context(nc.sbuf_tensor(name + "_rows", [nch, 128], F32))
        lv2 = k.dma("sync", rows[:], src.rearrange("(c p) -> c p", p=128),
                    wait=[src_ready] if src_ready else [])
        out = es.enter_context(nc.sbuf_tensor(name + "_cols", [128, nch], F32))
        tv = k.op("tensor", "transpose", self.P[5][:, 0:nch], rows[:],
                  self.sb["iden_f32"][0:nch, 0:nch],
                  wait=[lv2, (V, V.v), (S, k.sem("S").v)], inc=T)
        sv2 = k.op("scalar", "copy", out[:], self.P[5][:, 0:nch], wait=[(T, tv)],
                   inc=k.sem("S"))
        return out, (k.sem("S"), sv2)


def emit_ln(k: K, C: Ctx, x_dram, xnfm_dram, xntok_dram=None):
    nc = k.nc
    DS, DG = k.sem("DS"), k.sem("DG")
    T, V, S = k.sem("T"), k.sem("V"), k.sem("S")
    pfx = f"ln{len(k.q['vector'])}_"
    with contextlib.ExitStack() as es:
        xt = [es.enter_context(nc.sbuf_tensor(pfx + f"x{i}", [128, 1024], F32))
              for i in range(2)]
        xnb = [es.enter_context(nc.sbuf_tensor(pfx + f"xnb{i}", [128, 1024], BF16))
               for i in range(2)]
        stg = [es.enter_context(nc.sbuf_tensor(pfx + f"stg{i}", [128, 128], BF16))
               for i in range(2)]
        xc = es.enter_context(nc.sbuf_tensor(pfx + "xc", [128, 1024], F32))
        sq = es.enter_context(nc.sbuf_tensor(pfx + "sq", [128, 1024], F32))
        mu = es.enter_context(nc.sbuf_tensor(pfx + "mu", [128, 1], F32))
        ssum = es.enter_context(nc.sbuf_tensor(pfx + "ssum", [128, 1], F32))
        rstd = es.enter_context(nc.sbuf_tensor(pfx + "rstd", [128, 1], F32))
        x_done = [0, 0]
        xn_done_t = [0, 0]
        xn_done_dma = [None, None]
        stg_done = [None, None]
        tr_done = [0, 0]
        for i in range(T_CORE // 128):
            b = i % 2
            lv = k.dma("sync", xt[b][:], x_dram[i * 128:(i + 1) * 128, :],
                       wait=[(V, x_done[b])] if x_done[b] else [])
            k.wait("vector", lv[0], lv[1])
            k.op("vector", "reduce_sum", mu[:], xt[b][:], AX.X, negate=True)
            k.op("vector", "tensor_single_scalar", mu[:], mu[:], 1.0 / 1024.0, AL.mult)
            k.op("vector", "tensor_scalar_add", xc[:], xt[b][:], mu[:])
            x_done[b] = k.op("vector", "tensor_mul", sq[:], xc[:], xc[:], inc=V)
            k.op("vector", "reduce_sum", ssum[:], sq[:], AX.X)
            vv = k.op("vector", "tensor_single_scalar", ssum[:], ssum[:],
                      1.0 / 1024.0, AL.mult, inc=V)
            k.op("scalar", "activation", rstd[:], ssum[:], AF.Ln,
                 bias=C.imm(1e-5), wait=[(V, vv)])
            sv = k.op("scalar", "activation", rstd[:], rstd[:], AF.Exp,
                      scale=C.imm(-0.5), inc=S)
            wlist = [(S, sv)]
            if xn_done_t[b]:
                wlist.append((T, xn_done_t[b]))
            if xn_done_dma[b]:
                wlist.append(xn_done_dma[b])
            xnv = k.op("vector", "tensor_scalar_mul", xnb[b][:], xc[:], rstd[:],
                       wait=wlist, inc=V)
            if xntok_dram is not None:
                xn_done_dma[b] = k.dma("gpsimd",
                                       xntok_dram[i * 128:(i + 1) * 128, :],
                                       xnb[b][:], wait=[(V, xnv)])
            for cdx in range(8):
                pb = cdx % 2
                wl = [(V, xnv)]
                if tr_done[pb]:
                    wl.append((S, tr_done[pb]))
                tv = k.op("tensor", "transpose", C.PBF[pb][:, 0:128],
                          xnb[b][:, cdx * 128:(cdx + 1) * 128],
                          C.sb["iden_bf"][:], wait=wl, inc=T)
                xn_done_t[b] = tv
                sb = cdx % 2
                wl2 = [(T, tv)]
                if stg_done[sb]:
                    wl2.append(stg_done[sb])
                ev = k.op("scalar", "copy", stg[sb][:], C.PBF[pb][:, 0:128],
                          wait=wl2, inc=S)
                tr_done[pb] = ev
                stg_done[sb] = k.dma("gpsimd",
                                     xnfm_dram[cdx * 128:(cdx + 1) * 128,
                                               i * 128:(i + 1) * 128],
                                     stg[sb][:], wait=[(S, ev)])
    k.full_barrier()


L0_WSPECS = {
    "L0_attn_W_attn": ([1024, 3072], F32),
    "L0_attn_b_attn": ([3072], F32),
    "L0_attn_W_proj": ([1024, 1024], F32),
    "L0_attn_b_proj": ([1024], F32),
    "L0_ffn_k0": ([2048, 4096], U8),
    "L0_ffn_k1": ([4096, 4096], U8),
    "L0_ffn_k2": ([4096, 1024], U8),
    "L0_ffn_s0": ([1], F32),
    "L0_ffn_s1": ([1], F32),
    "L0_ffn_s2": ([1], F32),
}


def emit_layer0(k: K, C: Ctx, W, x_dram, xout_dram, scratch, stop_after="all"):
    nc = k.nc
    DS, DG, DA = k.sem("DS"), k.sem("DG"), k.sem("DA")
    T, V, S = k.sem("T"), k.sem("V"), k.sem("S")
    xnfm = scratch["xnfm"]
    s_d = scratch["s_fm"]
    h1_d = scratch["h1"]
    h2_d = scratch["h2"]
    w1e_d = scratch["w1eff"]

    emit_ln(k, C, x_dram, xnfm)
    if stop_after == "ln":
        return

    with contextlib.ExitStack() as es:
        C.load_tables(es, ["l0_cos", "l0_sin"])
        wrot = [es.enter_context(nc.sbuf_tensor(f"l0_wrot{c}", [128, 3072], BF16))
                for c in range(8)]
        wf = [es.enter_context(nc.sbuf_tensor(f"l0_wf{i}", [128, 1024], F32))
              for i in range(2)]
        st1 = es.enter_context(nc.sbuf_tensor("l0_st1", [128, 512], F32))
        st2 = es.enter_context(nc.sbuf_tensor("l0_st2", [128, 512], F32))
        st3 = es.enter_context(nc.sbuf_tensor("l0_st3", [128, 512], F32))
        st4 = es.enter_context(nc.sbuf_tensor("l0_st4", [128, 512], F32))

        def rot_part(E, O, dst_even, dst_odd, nrow, scale):
            cosv = C.sb["l0_cos"][0:nrow, :].rearrange("p (h j) -> p h j", j=32)
            sinv = C.sb["l0_sin"][0:nrow, :].rearrange("p (h j) -> p h j", j=32)
            t1 = st1[0:nrow, :].rearrange("p (h j) -> p h j", j=32)
            t2 = st2[0:nrow, :].rearrange("p (h j) -> p h j", j=32)
            t3 = st3[0:nrow, :].rearrange("p (h j) -> p h j", j=32)
            t4 = st4[0:nrow, :].rearrange("p (h j) -> p h j", j=32)
            k.op("vector", "tensor_tensor", t1, E, cosv, AL.mult)
            k.op("vector", "tensor_tensor", t2, O, sinv, AL.mult)
            k.op("vector", "tensor_tensor", t3, O, cosv, AL.mult)
            k.op("vector", "tensor_tensor", t4, E, sinv, AL.mult)
            k.op("vector", "tensor_sub", t1, t1, t2)
            k.op("vector", "tensor_add", t3, t3, t4)
            k.op("vector", "tensor_single_scalar", dst_even, t1, scale, AL.mult)
            return k.op("vector", "tensor_single_scalar", dst_odd, t3, scale,
                        AL.mult, inc=V)

        wf_done = [0, 0]
        for c in range(8):
            for part in range(3):
                b = (c * 3 + part) % 2
                base = part * 1024
                wl = [(V, wf_done[b])] if wf_done[b] else []
                lv = k.dma("sync", wf[b][:],
                           W["L0_attn_W_attn"][c * 128:(c + 1) * 128,
                                               base:base + 1024], wait=wl)
                k.wait("vector", lv[0], lv[1])
                if part == 2:
                    wf_done[b] = k.op("vector", "tensor_copy",
                                      wrot[c][:, 2048:3072], wf[b][:], inc=V)
                else:
                    src = wf[b][:].rearrange("p (h s j) -> p h s j", s=2, j=32)
                    dst = wrot[c][:, base:base + 1024].rearrange(
                        "p (h s j) -> p h s j", s=2, j=32)
                    scale = 0.125 if part == 0 else 1.0
                    wf_done[b] = rot_part(src[:, :, 0, :], src[:, :, 1, :],
                                          dst[:, :, 0, :], dst[:, :, 1, :],
                                          128, scale)

        def rot_b(row, ready):
            k.wait("vector", ready[0], ready[1])
            for base, scale in ((0, 0.125), (1024, 1.0)):
                src = row[0:1, base:base + 1024].rearrange(
                    "p (h s j) -> p h s j", s=2, j=32)
                rot_part(src[:, :, 0, :], src[:, :, 1, :],
                         src[:, :, 0, :], src[:, :, 1, :], 1, scale)
            rv = k.op("vector", "engine_nop", inc=V)
            return row[0:1, :], (V, rv)

        batt_col, _ = C.colize("l0_batt", es, W["L0_attn_b_attn"], 3072, rot_fn=rot_b)
        bproj_col, _ = C.colize("l0_bproj", es, W["L0_attn_b_proj"], 1024)

        wproj = [es.enter_context(nc.sbuf_tensor(f"l0_wpj{c}", [128, 1024], BF16))
                 for c in range(8)]
        wpf_done = k.sems["V"].v
        wpf_prev = [wpf_done, wpf_done]
        for c in range(8):
            b = c % 2
            wl = [(V, wpf_prev[b])]
            lv = k.dma("sync", wf[b][:], W["L0_attn_W_proj"][c * 128:(c + 1) * 128, :],
                       wait=wl)
            wpf_prev[b] = k.op("vector", "tensor_copy", wproj[c][:], wf[b][:],
                               wait=[lv], inc=V)
        k.full_barrier()
        if stop_after == "prep":
            return

        xnt = [[es.enter_context(nc.sbuf_tensor(f"l0_xnt{i}_{c}", [128, 512], BF16))
                for c in range(8)] for i in range(2)]
        ymix = [es.enter_context(nc.sbuf_tensor(f"l0_ym{c}", [128, 2048], BF16))
                for c in range(8)]
        qa = [es.enter_context(nc.sbuf_tensor(f"l0_qa{i}", [65, 2048], BF16))
              for i in range(2)]
        ka = [es.enter_context(nc.sbuf_tensor(f"l0_ka{i}", [65, 2048], BF16))
              for i in range(2)]
        vto = [es.enter_context(nc.sbuf_tensor(f"l0_v{i}", [128, 1040], BF16))
               for i in range(2)]
        vfm = es.enter_context(nc.sbuf_tensor("l0_vfm", [128, 2048], BF16))
        attT = [es.enter_context(nc.sbuf_tensor(f"l0_attT{i}", [128, 512], BF16))
                for i in range(2)]
        sq64 = es.enter_context(nc.sbuf_tensor("l0_sq64", [64, 2048], BF16))
        nrm = es.enter_context(nc.sbuf_tensor("l0_nrm", [1, 2048], F32))
        kmx = es.enter_context(nc.sbuf_tensor("l0_kmx", [1, 1], F32))
        rd = es.enter_context(nc.sbuf_tensor("l0_rd", [1, 512], F32))
        rb = es.enter_context(nc.sbuf_tensor("l0_rb", [64, 512], F32))
        sstg = [es.enter_context(nc.sbuf_tensor(f"l0_ss{i}", [128, 512], BF16))
                for i in range(2)]

        for sq_i in range(SEQ_PER_CORE):
            base_t = sq_i * SEQ
            for hp in range(8):
                xn_used = [k.sems["T"].v, k.sems["T"].v]
                psum_free = [k.sems["V"].v] * 3
                for tt in range(4):
                    b = tt % 2
                    lv = 0
                    for kc in range(8):
                        lv = k.dma("sync", xnt[b][kc][:],
                                   xnfm[kc * 128:(kc + 1) * 128,
                                        base_t + tt * 512:base_t + (tt + 1) * 512],
                                   wait=[(T, xn_used[b])] if kc == 0 else [])
                    k.wait("tensor", lv[0], lv[1])
                    for pi, fbase in ((0, hp * 128), (1, 1024 + hp * 128),
                                      (2, 2048 + hp * 128)):
                        p = C.P[pi]
                        for kc in range(8):
                            wl = [(V, psum_free[pi])] if kc == 0 else []
                            k.op("tensor", "matmul", p[:, :],
                                 wrot[kc][:, fbase:fbase + 128],
                                 xnt[b][kc][:], start=(kc == 0), stop=(kc == 7),
                                 wait=wl, inc=(T if kc == 7 else None))
                        tv = k.sems["T"].v
                        if pi == 2:
                            xn_used[b] = tv
                        cs = tt * 512
                        if pi < 2:
                            dste, dsto = (qa, ka)[pi][0], (qa, ka)[pi][1]
                            bc = hp if pi == 0 else 8 + hp
                            k.op("vector", "tensor_scalar_add",
                                 dste[0:64, cs:cs + 512], p[0:64, :],
                                 batt_col[0:64, bc:bc + 1], wait=[(T, tv)])
                            psum_free[pi] = k.op(
                                "vector", "tensor_scalar_add",
                                dsto[0:64, cs:cs + 512], p[64:128, :],
                                batt_col[64:128, bc:bc + 1], inc=V)
                        else:
                            psum_free[pi] = k.op(
                                "vector", "tensor_scalar_add", vfm[:, cs:cs + 512],
                                p[:, :], batt_col[:, 16 + hp:17 + hp],
                                wait=[(T, tv)], inc=V)
                k.op("vector", "memset",
                     vto[0][:].rearrange("p (t c) -> p t c", c=65)[:, :, 64:65], 1.0,
                     wait=[(T, k.sems["T"].v)])
                m1 = k.op("vector", "memset",
                          vto[1][:].rearrange("p (t c) -> p t c", c=65)[:, :, 64:65],
                          1.0, inc=V)
                tr_ev = [0, 0]
                for tb in range(16):
                    pb = tb % 2
                    wl = [(V, k.sems["V"].v)]
                    if tr_ev[pb]:
                        wl.append((S, tr_ev[pb]))
                    tv = k.op("tensor", "transpose", C.PBF[pb][:, 0:128],
                              vfm[:, tb * 128:(tb + 1) * 128], C.sb["iden_bf"][:],
                              wait=wl, inc=T)
                    k.op("scalar", "copy", vto[0][:, tb * 65:tb * 65 + 64],
                         C.PBF[pb][0:128, 0:64], wait=[(T, tv)])
                    tr_ev[pb] = k.op("scalar", "copy",
                                     vto[1][:, tb * 65:tb * 65 + 64],
                                     C.PBF[pb][0:128, 64:128], inc=S)
                for hh in range(2):
                    sv = k.op("scalar", "activation", sq64[:, :], ka[hh][0:64, :],
                              AF.Square,
                              wait=[(T, k.sems["T"].v), (V, k.sems["V"].v)], inc=S)
                    cpv = 0
                    for tt in range(4):
                        pst = C.P[3 + tt % 2]
                        wl = [(S, sv)]
                        if tt >= 2:
                            wl.append((V, cpv))
                        tv = k.op("tensor", "matmul", pst[0:1, :],
                                  C.ones_col_bf[0:64, :],
                                  sq64[:, tt * 512:(tt + 1) * 512], start=True,
                                  stop=True, wait=wl, inc=T)
                        cpv = k.op("vector", "tensor_copy",
                                   nrm[0:1, tt * 512:(tt + 1) * 512], pst[0:1, :],
                                   wait=[(T, tv)], inc=V)
                    kv = k.op("vector", "reduce_max", kmx[:], nrm[:], AX.X, inc=V)
                    sv = k.op("scalar", "activation", sq64[:, :], qa[hh][0:64, :],
                              AF.Square, wait=[(T, k.sems["T"].v)], inc=S)
                    for tt in range(4):
                        pst = C.P[3 + tt % 2]
                        wl = [(S, sv), (V, kv)]
                        if tt >= 2:
                            wl.append((V, cpv))
                        tv = k.op("tensor", "matmul", pst[0:1, :],
                                  C.ones_col_bf[0:64, :],
                                  sq64[:, tt * 512:(tt + 1) * 512], start=True,
                                  stop=True, wait=wl, inc=T)
                        cpv = k.op("vector", "tensor_copy",
                                   nrm[0:1, tt * 512:(tt + 1) * 512], pst[0:1, :],
                                   wait=[(T, tv)], inc=V)
                    vv = k.op("vector", "tensor_scalar_mul", nrm[:], nrm[:],
                              kmx[0:1, 0:1], inc=V)
                    k.op("vector", "tensor_single_scalar", nrm[:], nrm[:],
                         1e-30, AL.add, wait=[(V, vv)])
                    k.op("scalar", "activation", nrm[:], nrm[:], AF.Ln,
                         wait=[(V, k.sems["V"].v)])
                    k.op("vector", "tensor_single_scalar", nrm[:], nrm[:], 0.5,
                         AL.mult, wait=[(S, k.sems["S"].v)])
                    sv = k.op("scalar", "activation", nrm[:], nrm[:], AF.Exp,
                              wait=[(V, k.sems["V"].v)], inc=S)
                    k.op("vector", "tensor_scalar", qa[hh][64:65, :], nrm[0:1, :],
                         -1.0, -1.0, AL.mult, AL.add, wait=[(S, sv)])
                    k.op("vector", "memset", ka[hh][64:65, :], 1.0, inc=V)
                aug_ready = k.sems["V"].v
                k.wait("tensor", V, aug_ready)
                for hh in range(2):
                    qh, kh, vh = qa[hh], ka[hh], vto[hh]
                    y_ev = k.sems["V"].v
                    sc_ev = [k.sems["S"].v, k.sems["S"].v]
                    for Q in range(4):
                        for kt in range(4 * Q + 4):
                            pb = kt % 2
                            p = C.P[pb]
                            wl = [(S, sc_ev[pb])] if sc_ev[pb] else []
                            tv = k.op("tensor", "matmul", p[:, :],
                                      kh[0:65, kt * 128:(kt + 1) * 128],
                                      qh[0:65, Q * 512:(Q + 1) * 512],
                                      start=True, stop=True, wait=wl, inc=T)
                            sc_ev[pb] = k.op("scalar", "activation", attT[pb][:, :],
                                             p[:, :], AF.Exp, wait=[(T, tv)], inc=S)
                            r = kt - 4 * Q
                            src = (S, sc_ev[pb])
                            if r >= 0:
                                mv = k.op("vector", "tensor_tensor", attT[pb][:, :],
                                          attT[pb][:, :], C.sb[f"trimask{r}"][:, :],
                                          AL.mult, wait=[src], inc=V)
                                src = (V, mv)
                            wl2 = [src]
                            if kt == 0 and y_ev:
                                wl2.append((V, y_ev))
                            k.op("tensor", "matmul", C.P[2][0:65, :],
                                 vh[:, kt * 65:kt * 65 + 65], attT[pb][:, :],
                                 start=(kt == 0), stop=(kt == 4 * Q + 3),
                                 wait=wl2, inc=(T if kt == 4 * Q + 3 else None))
                        tv = k.sems["T"].v
                        rv = k.op("vector", "reciprocal", rd[:], C.P[2][64:65, :],
                                  wait=[(T, tv)], inc=V)
                        tv2 = k.op("tensor", "matmul", C.P[3][0:64, :],
                                   C.ones_row[0:1, 0:64], rd[:], start=True,
                                   stop=True, wait=[(V, rv)], inc=T)
                        cv = k.op("vector", "tensor_copy", rb[:], C.P[3][0:64, :],
                                  wait=[(T, tv2)], inc=V)
                        y_ev = k.op("vector", "tensor_mul",
                                    ymix[hp][hh * 64:hh * 64 + 64,
                                             Q * 512:(Q + 1) * 512],
                                    C.P[2][0:64, :], rb[:], inc=V)
                        k.wait("tensor", V, y_ev)
            psum_free = [k.sems["V"].v] * 2
            stg_done = [None, None]
            for tt in range(4):
                for fc in range(8):
                    pi = fc % 2
                    p = C.P[pi]
                    for kc in range(8):
                        wl = [(V, psum_free[pi])] if kc == 0 else []
                        k.op("tensor", "matmul", p[:, :],
                             wproj[kc][:, fc * 128:(fc + 1) * 128],
                             ymix[kc][:, tt * 512:(tt + 1) * 512],
                             start=(kc == 0), stop=(kc == 7), wait=wl,
                             inc=(T if kc == 7 else None))
                    sb = fc % 2
                    wl = [(T, k.sems["T"].v)]
                    if stg_done[sb]:
                        wl.append(stg_done[sb])
                    psum_free[pi] = k.op("vector", "tensor_scalar_add", sstg[sb][:],
                                         p[:, :], bproj_col[:, fc:fc + 1], wait=wl,
                                         inc=V)
                    stg_done[sb] = k.dma(
                        "gpsimd",
                        s_d[fc * 128:(fc + 1) * 128,
                            base_t + tt * 512:base_t + (tt + 1) * 512],
                        sstg[sb][:], wait=[(V, k.sems["V"].v)])
        k.full_barrier()
    if stop_after == "mixer":
        return

    with contextlib.ExitStack() as es:
        ssb = es.enter_context(nc.sbuf_tensor("l0_ssb", [1, 3], F32))
        l1 = k.dma("sync", ssb[0:1, 0:1], W["L0_ffn_s0"].rearrange("(a o) -> a o", a=1))
        l2 = k.dma("sync", ssb[0:1, 1:2], W["L0_ffn_s1"].rearrange("(a o) -> a o", a=1))
        l3 = k.dma("sync", ssb[0:1, 2:3], W["L0_ffn_s2"].rearrange("(a o) -> a o", a=1))
        s0c, r0 = C.bcast_scalar("l0_s0c", es, ssb[0:1, 0:1], l1, scale=2.0)
        s1c, r1 = C.bcast_scalar("l0_s1c", es, ssb[0:1, 1:2], l2, scale=2.0)
        s2c, r2 = C.bcast_scalar("l0_s2c", es, ssb[0:1, 2:3], l3, scale=2.0)
        k.wait("scalar", r2[0], r2[1])

        w0 = [es.enter_context(nc.sbuf_tensor(f"l0_w0_{c}", [128, 4096], BF16))
              for c in range(8)]
        u8a = [es.enter_context(nc.sbuf_tensor(f"l0_u8a{i}", [128, 4096], U8))
               for i in range(2)]
        u8b = [es.enter_context(nc.sbuf_tensor(f"l0_u8b{i}", [128, 4096], U8))
               for i in range(2)]
        cva = es.enter_context(nc.sbuf_tensor("l0_cva", [128, 4096], BF16))
        cvb = es.enter_context(nc.sbuf_tensor("l0_cvb", [128, 4096], BF16))
        xnf = [es.enter_context(nc.sbuf_tensor(f"l0_xnf{c}", [128, 4096], BF16))
               for c in range(8)]
        hstg = [es.enter_context(nc.sbuf_tensor(f"l0_hs{i}", [128, 512], BF16))
                for i in range(2)]
        ud = [0, 0]
        for c in range(8):
            b = c % 2
            wl = [(V, ud[b])] if ud[b] else []
            la = k.dma("sync", u8a[b][:], W["L0_ffn_k0"][c * 128:(c + 1) * 128, :],
                       wait=wl)
            lb = k.dma("sync", u8b[b][:],
                       W["L0_ffn_k0"][1024 + c * 128:1024 + (c + 1) * 128, :])
            k.op("scalar", "copy", cva[:], u8a[b][:],
                 wait=[la, lb, (V, k.sems["V"].v)])
            sv = k.op("scalar", "copy", cvb[:], u8b[b][:], inc=S)
            ud[b] = k.op("vector", "tensor_sub", w0[c][:], cva[:], cvb[:],
                         wait=[(S, sv)], inc=V)
        for c in range(8):
            k.dma("sync", xnf[c][:], xnfm[c * 128:(c + 1) * 128, :])
        k.full_barrier()

        psum_free = [0, 0]
        stg_d = [None, None]
        for tt in range(8):
            for fc in range(32):
                pi = fc % 2
                p = C.P[pi]
                for kc in range(8):
                    wl = [(S, psum_free[pi])] if kc == 0 else []
                    k.op("tensor", "matmul", p[:, :],
                         w0[kc][:, fc * 128:(fc + 1) * 128],
                         xnf[kc][:, tt * 512:(tt + 1) * 512], start=(kc == 0),
                         stop=(kc == 7), wait=wl, inc=(T if kc == 7 else None))
                sb = fc % 2
                wl = [(T, k.sems["T"].v)]
                if stg_d[sb]:
                    wl.append(stg_d[sb])
                psum_free[pi] = k.op("scalar", "activation", hstg[sb][:], p[:, :],
                                     AF.Relu, scale=s0c[:, 0:1], wait=wl, inc=S)
                stg_d[sb] = k.dma("gpsimd",
                                  h1_d[fc * 128:(fc + 1) * 128,
                                       tt * 512:(tt + 1) * 512],
                                  hstg[sb][:], wait=[(S, k.sems["S"].v)])
        ud = [0, 0]
        st_d = [None, None]
        w0_prep_done = k.sems["V"].v
        for rc in range(32):
            b = rc % 2
            wl = [(S, ud[b])] if ud[b] else []
            la = k.dma("sync", u8a[b][:], W["L0_ffn_k1"][rc * 128:(rc + 1) * 128, :],
                       wait=wl)
            wl2 = [la, (V, w0_prep_done)]
            if st_d[b]:
                wl2.append(st_d[b])
            ud[b] = k.op("scalar", "activation", cva[:] if b == 0 else cvb[:],
                         u8a[b][:], AF.Copy, scale=2.0, bias=-1.0,
                         wait=wl2, inc=S)
            st_d[b] = k.dma("gpsimd", w1e_d[rc * 128:(rc + 1) * 128, :],
                            cva[:] if b == 0 else cvb[:], wait=[(S, ud[b])])
        k.full_barrier()
    if stop_after == "f1":
        return

    with contextlib.ExitStack() as es:
        s1c = None
        ssb = es.enter_context(nc.sbuf_tensor("l0_ssb2", [1, 2], F32))
        l2 = k.dma("sync", ssb[0:1, 0:1], W["L0_ffn_s1"].rearrange("(a o) -> a o", a=1))
        l3 = k.dma("sync", ssb[0:1, 1:2], W["L0_ffn_s2"].rearrange("(a o) -> a o", a=1))
        s1c, r1 = C.bcast_scalar("l0_s1c2", es, ssb[0:1, 0:1], l2, scale=2.0)
        s2c, r2 = C.bcast_scalar("l0_s2c2", es, ssb[0:1, 1:2], l3, scale=2.0)
        k.wait("scalar", r2[0], r2[1])
        w1g = [es.enter_context(nc.sbuf_tensor(f"l0_w1g{rc}", [128, 1024], BF16))
               for rc in range(32)]
        h1t2 = [[es.enter_context(nc.sbuf_tensor(f"l0_h1t{i}_{rc}", [128, 512],
                                                 BF16)) for rc in range(32)]
                for i in range(2)]
        h1t = h1t2[0]
        h1t_used = [0, 0]
        hstg = [es.enter_context(nc.sbuf_tensor(f"l0_hs2{i}", [128, 512], BF16))
                for i in range(2)]
        import os
        F2G = int(os.environ.get("F2G", "4"))
        F2T = int(os.environ.get("F2T", "8"))
        F2F = int(os.environ.get("F2F", "8"))
        F2K = int(os.environ.get("F2K", "32"))
        psum_free = [0, 0]
        stg_d = [None, None]
        for grp in range(F2G):
            wv = 0
            for rc in range(32):
                wl = [(T, k.sems["T"].v)] if (grp and rc == 0) else []
                wv = k.dma("sync", w1g[rc][:],
                           w1e_d[rc * 128:(rc + 1) * 128,
                                 grp * 1024:(grp + 1) * 1024], wait=wl)
            for tt in range(F2T):
                hb = (grp * F2T + tt) % 2
                h1t = h1t2[hb]
                gh = k.new_group("sync")
                for rc in range(32):
                    wl = [(T, h1t_used[hb])] if (rc == 0 and h1t_used[hb]) else []
                    hv = k.dma("sync", h1t[rc][:],
                               h1_d[rc * 128:(rc + 1) * 128,
                                    tt * 512:(tt + 1) * 512], gh, wait=wl)
                k.wait("tensor", hv[0], hv[1])
                if tt == 0:
                    k.wait("tensor", wv[0], wv[1])
                if os.environ.get("F2B"):
                    k.full_barrier()
                if os.environ.get("F2D") and grp == 0 and tt == 0:
                    dbg_w = nc.dram_tensor("dbg_w1g", [32, 128, 1024], BF16,
                                           kind="ExternalOutput").ap()
                    dbg_h = nc.dram_tensor("dbg_h1t", [32, 128, 512], BF16,
                                           kind="ExternalOutput").ap()
                    for rc in range(32):
                        k.dma("gpsimd", dbg_w[rc], w1g[rc][:])
                        k.dma("gpsimd", dbg_h[rc], h1t[rc][:])
                    k.full_barrier()
                for f2 in range(F2F):
                    pi = f2 % 2
                    p = C.P[pi]
                    for kc in range(F2K):
                        wl = [(S, psum_free[pi])] if kc == 0 else []
                        k.op("tensor", "matmul", p[:, :],
                             w1g[kc][:, f2 * 128:(f2 + 1) * 128], h1t[kc][:],
                             start=(kc == 0), stop=(kc == F2K - 1), wait=wl,
                             inc=(T if kc == F2K - 1 else None))
                    h1t_used[hb] = k.sems["T"].v
                    sb = f2 % 2
                    wl = [(T, k.sems["T"].v)]
                    if stg_d[sb]:
                        wl.append(stg_d[sb])
                    F2E = int(os.environ.get("F2E", "3"))
                    if F2E == 9:
                        k.full_barrier()
                        psum_free[pi] = k.op("scalar", "copy", hstg[sb][:],
                                             p[:, :], inc=S)
                    elif F2E == 6:
                        psum_free[pi] = k.op("vector", "tensor_relu", hstg[sb][:],
                                             C.P[2][:, :], wait=wl, inc=k.sem("V"))
                    elif F2E == 7:
                        psum_free[pi] = k.op("vector", "tensor_relu", hstg[sb][:],
                                             hstg[sb][:], wait=wl, inc=k.sem("V"))
                    elif F2E == 4:
                        psum_free[pi] = k.op("vector", "tensor_relu", hstg[sb][:],
                                             p[:, :], wait=wl, inc=k.sem("V"))
                    elif F2E == 5:
                        psum_free[pi] = k.op("scalar", "copy", hstg[sb][:],
                                             p[:, :], wait=wl, inc=S)
                    elif F2E >= 1:
                        if F2E >= 2:
                            psum_free[pi] = k.op("scalar", "activation", hstg[sb][:],
                                                 p[:, :], AF.Relu, scale=s1c[:, 0:1],
                                                 wait=wl, inc=S)
                        else:
                            psum_free[pi] = k.op("scalar", "activation", hstg[sb][:],
                                                 p[:, :], AF.Relu,
                                                 wait=wl, inc=S)
                        if F2E >= 3:
                            stg_d[sb] = k.dma(
                                "gpsimd",
                                h2_d[(grp * 8 + f2) * 128:(grp * 8 + f2 + 1) * 128,
                                     tt * 512:(tt + 1) * 512],
                                hstg[sb][:], wait=[(S, k.sems["S"].v)])
        k.full_barrier()
        if stop_after == "f2":
            return

        u8t = [es.enter_context(nc.sbuf_tensor(f"l0_u8t{i}", [128, 1024], U8))
               for i in range(2)]
        ud2 = [0, 0]
        for rc in range(32):
            b = rc % 2
            wl = [(S, ud2[b])] if ud2[b] else []
            la = k.dma("sync", u8t[b][:],
                       W["L0_ffn_k2"][rc * 128:(rc + 1) * 128, 0:1024], wait=wl)
            ud2[b] = k.op("scalar", "activation", w1g[rc][:],
                          u8t[b][:], AF.Copy, scale=2.0,
                          bias=-1.0, wait=[la], inc=S)
        k.full_barrier()
        w2 = w1g
        xsb = [es.enter_context(nc.sbuf_tensor(f"l0_xsb{i}", [128, 1024], F32))
               for i in range(4)]
        sfc = es.enter_context(nc.sbuf_tensor("l0_sfc", [128, 512], BF16))
        gt = es.enter_context(nc.sbuf_tensor("l0_gt", [128, 512], BF16))
        sg = es.enter_context(nc.sbuf_tensor("l0_sg", [128, 512], F32))
        psum_free = [0, 0]
        x_st = None
        h1t = h1t2[0]
        for tt in range(8):
            gh = k.new_group("sync")
            for rc in range(32):
                wl = [(T, k.sems["T"].v)] if (rc == 0) else []
                hv = k.dma("sync", h1t[rc][:],
                           h2_d[rc * 128:(rc + 1) * 128, tt * 512:(tt + 1) * 512],
                           gh, wait=wl)
            gx = k.new_group("sync")
            for tb in range(4):
                wl = [x_st] if (x_st and tb == 0) else []
                xl = k.dma("sync", xsb[tb][:],
                           x_dram[tt * 512 + tb * 128:tt * 512 + (tb + 1) * 128, :],
                           gx, wait=wl)
            k.wait("tensor", hv[0], hv[1])
            k.wait("vector", xl[0], xl[1])
            for fc in range(8):
                pi = fc % 2
                p = C.P[pi]
                sl = k.dma("sync", sfc[:],
                           s_d[fc * 128:(fc + 1) * 128, tt * 512:(tt + 1) * 512],
                           wait=[(V, k.sems["V"].v)])
                for kc in range(32):
                    wl = [(S, psum_free[pi])] if kc == 0 else []
                    k.op("tensor", "matmul", p[:, :],
                         w2[kc][:, fc * 128:(fc + 1) * 128], h1t[kc][:],
                         start=(kc == 0), stop=(kc == 31), wait=wl,
                         inc=(T if kc == 31 else None))
                psum_free[pi] = k.op("scalar", "activation", gt[:], p[:, :],
                                     AF.Silu, scale=s2c[:, 0:1],
                                     wait=[(T, k.sems["T"].v)], inc=S)
                mv = k.op("vector", "tensor_mul", sg[:], gt[:], sfc[:],
                          wait=[(S, k.sems["S"].v), sl], inc=V)
                for tb in range(4):
                    pt = C.P[4 + tb % 2]
                    tv = k.op("tensor", "transpose", pt[:, 0:128],
                              sg[:, tb * 128:(tb + 1) * 128],
                              C.sb["iden_f32"][:], wait=[(V, k.sems["V"].v)],
                              inc=T)
                    k.op("vector", "tensor_add",
                         xsb[tb][:, fc * 128:(fc + 1) * 128],
                         xsb[tb][:, fc * 128:(fc + 1) * 128], pt[:, 0:128],
                         wait=[(T, tv)], inc=V)
                k.wait("tensor", V, k.sems["V"].v)
            for tb in range(4):
                x_st = k.dma("gpsimd",
                             xout_dram[tt * 512 + tb * 128:tt * 512 + (tb + 1) * 128, :],
                             xsb[tb][:], wait=[(V, k.sems["V"].v)])
        k.full_barrier()


L2_WSPECS = {
    "L2_W_hgT": ([1024, 3072], F32),
    "L2_attn_b_hg": ([3072], F32),
    "L2_W_outT": ([1536, 1024], F32),
    "L2_attn_b_out": ([1024], F32),
    "L2_attn_h0": ([1536], F32),
    "L2_w1T": ([1024, 4096], F32),
    "L2_ffn_w1_s": ([4096], F32),
    "L2_w2T": ([1024, 4096], F32),
    "L2_ffn_w2_s": ([4096], F32),
    "L2_h1T": ([4096, 4096], F32),
    "L2_ffn_h1_s": ([4096], F32),
    "L2_h2T": ([4096, 4096], F32),
    "L2_ffn_h2_s": ([4096], F32),
    "L2_w3T": ([4096, 1024], F32),
    "L2_ffn_w3_s": ([1024], F32),
}


def emit_layer2(k: K, C: Ctx, W, x_dram, xout_dram, scratch, stop_after="all"):
    nc = k.nc
    T, V, S = k.sem("T"), k.sem("V"), k.sem("S")
    xnfm = scratch["xnfm"]
    s_d = scratch["s_fm"]
    h1_d = scratch["h1"]
    h2_d = scratch["h2"]

    emit_ln(k, C, x_dram, xnfm)
    if stop_after == "ln":
        return

    with contextlib.ExitStack() as es:
        whg = [es.enter_context(nc.sbuf_tensor(f"l2_whg{c}", [128, 3072], BF16))
               for c in range(8)]
        wf = [es.enter_context(nc.sbuf_tensor(f"l2_wf{i}", [128, 3072], F32))
              for i in range(2)]
        wf_done = [0, 0]
        for c in range(8):
            b = c % 2
            wl = [(V, wf_done[b])] if wf_done[b] else []
            lv = k.dma("sync", wf[b][:], W["L2_W_hgT"][c * 128:(c + 1) * 128, :],
                       wait=wl)
            wf_done[b] = k.op("vector", "tensor_copy", whg[c][:], wf[b][:],
                              wait=[lv], inc=V)
        bhg_col, _ = C.colize("l2_bhg", es, W["L2_attn_b_hg"], 3072)
        bout_col, _ = C.colize("l2_bout", es, W["L2_attn_b_out"], 1024)
        h0_col, h0r = C.colize("l2_h0", es, W["L2_attn_h0"], 1536)
        ginit = es.enter_context(nc.sbuf_tensor("l2_ginit", [128, 12], F32))
        sv = k.op("scalar", "activation", ginit[:], h0_col[:], AF.Sigmoid,
                  wait=[h0r], inc=S)
        k.op("vector", "scalar_tensor_tensor", ginit[:], h0_col[:], 0.5,
             ginit[:], AL.add, AL.max, wait=[(S, sv)], inc=V)
        wout = [es.enter_context(nc.sbuf_tensor(f"l2_wo{c}", [128, 1024], BF16))
                for c in range(12)]
        for c in range(12):
            b = c % 2
            wl = [(V, wf_done[b])] if wf_done[b] else []
            lv = k.dma("sync", wf[b][:, 0:1024],
                       W["L2_W_outT"][c * 128:(c + 1) * 128, :], wait=wl)
            wf_done[b] = k.op("vector", "tensor_copy", wout[c][:],
                              wf[b][:, 0:1024], wait=[lv], inc=V)
        k.full_barrier()

        xnt = [[es.enter_context(nc.sbuf_tensor(f"l2_xnt{i}_{c}", [128, 512], BF16))
                for c in range(8)] for i in range(2)]
        gate = es.enter_context(nc.sbuf_tensor("l2_gate", [128, 2048], F32))
        hid = es.enter_context(nc.sbuf_tensor("l2_hid", [128, 2048], F32))
        sgm = es.enter_context(nc.sbuf_tensor("l2_sgm", [128, 2048], F32))
        hsc = es.enter_context(nc.sbuf_tensor("l2_hsc", [128, 2048], F32))
        hout = es.enter_context(nc.sbuf_tensor("l2_hout", [128, 2048], BF16))
        hmt = [es.enter_context(nc.sbuf_tensor(f"l2_hm{c}", [128, 512], BF16))
               for c in range(12)]
        sstg = [es.enter_context(nc.sbuf_tensor(f"l2_ss{i}", [128, 512], BF16))
                for i in range(2)]
        xn_used = [k.sems["T"].v, k.sems["T"].v]
        hout_store = None

        def mm_into(p, fbase, tt, b, nk=8, tiles=None, psfree=0):
            tiles = tiles if tiles is not None else whg
            for kc in range(nk):
                wl = []
                if kc == 0 and psfree:
                    wl = [(V, psfree)]
                k.op("tensor", "matmul", p[:, :],
                     tiles[kc][:, fbase:fbase + 128],
                     xnt[b][kc][:], start=(kc == 0), stop=(kc == nk - 1),
                     wait=wl, inc=(T if kc == nk - 1 else None))
            return k.sems["T"].v

        for sq_i in range(SEQ_PER_CORE):
            base_t = sq_i * SEQ
            for j in range(12):
                ps_free = k.sems["V"].v
                for tt in range(4):
                    b = tt % 2
                    gld = k.new_group("sync")
                    lv = None
                    for kc in range(8):
                        lv = k.dma("sync", xnt[b][kc][:],
                                   xnfm[kc * 128:(kc + 1) * 128,
                                        base_t + tt * 512:base_t + (tt + 1) * 512],
                                   gld,
                                   wait=[(T, xn_used[b])] if kc == 0 else [])
                    k.wait("tensor", lv[0], lv[1])
                    tv = mm_into(C.P[tt % 2], j * 128, tt, b, psfree=ps_free)
                    k.op("vector", "tensor_scalar_add",
                         hid[:, tt * 512:(tt + 1) * 512], C.P[tt % 2][:, :],
                         bhg_col[:, j:j + 1], wait=[(T, tv)])
                    tv = mm_into(C.P[2 + tt % 2], 1536 + j * 128, tt, b)
                    xn_used[b] = tv
                    ps_free = k.op("vector", "tensor_scalar_add",
                                   gate[:, tt * 512:(tt + 1) * 512],
                                   C.P[2 + tt % 2][:, :],
                                   bhg_col[:, 12 + j:13 + j], wait=[(T, tv)],
                                   inc=V)
                sv = k.op("scalar", "activation", sgm[:], gate[:], AF.Sigmoid,
                          wait=[(V, ps_free)], inc=S)
                k.op("vector", "tensor_scalar", gate[:], sgm[:], -1.0, 1.0,
                     AL.mult, AL.add, wait=[(S, sv)])
                sv = k.op("scalar", "activation", hsc[:], hid[:], AF.Sigmoid,
                          inc=S)
                k.op("vector", "scalar_tensor_tensor", hsc[:], hid[:], 0.5,
                     hsc[:], AL.add, AL.max, wait=[(S, sv)])
                k.op("vector", "tensor_mul", hsc[:], sgm[:], hsc[:])
                hm_v = k.op("vector", "tensor_tensor_scan", hout[:], gate[:],
                            hsc[:], ginit[:, j:j + 1], AL.mult, AL.add,
                            wait=[hout_store] if hout_store else [], inc=V)
                hout_store = k.dma("gpsimd",
                                   h2_d[j * 128:(j + 1) * 128, base_t:base_t + SEQ],
                                   hout[:], wait=[(V, hm_v)])
        k.full_barrier()
        psum_free = [k.sems["V"].v] * 2
        stg_done = [None, None]
        for tt in range(8):
            ghm = k.new_group("sync")
            hv = None
            for kc in range(12):
                wl = [(T, k.sems["T"].v)] if (tt and kc == 0) else []
                hv = k.dma("sync", hmt[kc][:],
                           h2_d[kc * 128:(kc + 1) * 128, tt * 512:(tt + 1) * 512],
                           ghm, wait=wl)
            k.wait("tensor", hv[0], hv[1])
            for fc in range(8):
                pi = fc % 2
                p = C.P[pi]
                for kc in range(12):
                    wl = [(V, psum_free[pi])] if kc == 0 else []
                    k.op("tensor", "matmul", p[:, :],
                         wout[kc][:, fc * 128:(fc + 1) * 128],
                         hmt[kc][:],
                         start=(kc == 0), stop=(kc == 11), wait=wl,
                         inc=(T if kc == 11 else None))
                sb = fc % 2
                wl = [(T, k.sems["T"].v)]
                if stg_done[sb]:
                    wl.append(stg_done[sb])
                psum_free[pi] = k.op("vector", "tensor_scalar_add", sstg[sb][:],
                                     p[:, :], bout_col[:, fc:fc + 1], wait=wl,
                                     inc=V)
                stg_done[sb] = k.dma(
                    "gpsimd",
                    s_d[fc * 128:(fc + 1) * 128, tt * 512:(tt + 1) * 512],
                    sstg[sb][:], wait=[(V, k.sems["V"].v)])
        k.full_barrier()
    if stop_after == "mixer":
        return

    with contextlib.ExitStack() as es:
        w1 = [es.enter_context(nc.sbuf_tensor(f"l2_w1_{c}", [128, 4096], BF16))
              for c in range(8)]
        w2 = [es.enter_context(nc.sbuf_tensor(f"l2_w2_{c}", [128, 4096], BF16))
              for c in range(8)]
        wf = [es.enter_context(nc.sbuf_tensor(f"l2_wfb{i}", [128, 2048], F32))
              for i in range(2)]
        xnt2 = [[es.enter_context(nc.sbuf_tensor(f"l2_xf{i}_{c}", [128, 512], BF16))
                 for c in range(8)] for i in range(2)]
        s1col, _ = C.colize("l2_s1", es, W["L2_ffn_w1_s"], 4096)
        s2col, _ = C.colize("l2_s2", es, W["L2_ffn_w2_s"], 4096)
        wf_done = [0, 0]
        for dst_tiles, wname in ((w1, "L2_w1T"), (w2, "L2_w2T")):
            for c in range(8):
                for hh in range(2):
                    b = (c * 2 + hh) % 2
                    wl = [(V, wf_done[b])] if wf_done[b] else []
                    lv = k.dma("sync", wf[b][:],
                               W[wname][c * 128:(c + 1) * 128,
                                        hh * 2048:(hh + 1) * 2048], wait=wl)
                    wf_done[b] = k.op("vector", "tensor_copy",
                                      dst_tiles[c][:, hh * 2048:(hh + 1) * 2048],
                                      wf[b][:], wait=[lv], inc=V)
        k.full_barrier()

        ut = es.enter_context(nc.sbuf_tensor("l2_ut", [128, 512], BF16))
        vt = es.enter_context(nc.sbuf_tensor("l2_vt", [128, 512], BF16))
        hstg = [es.enter_context(nc.sbuf_tensor(f"l2_hs{i}", [128, 512], BF16))
                for i in range(2)]
        psum_free = [0, 0]
        stg_d = [None, None]
        xn_used2 = [k.sems["T"].v, k.sems["T"].v]
        for tt in range(8):
            b = tt % 2
            gld = k.new_group("sync")
            lv = None
            for kc in range(8):
                lv = k.dma("sync", xnt2[b][kc][:],
                           xnfm[kc * 128:(kc + 1) * 128,
                                tt * 512:(tt + 1) * 512], gld,
                           wait=[(T, xn_used2[b])] if kc == 0 else [])
            k.wait("tensor", lv[0], lv[1])
            for fc in range(32):
                pi = fc % 2
                pu, pv = C.P[pi], C.P[2 + pi]
                for kc in range(8):
                    wl = [(V, psum_free[pi])] if kc == 0 else []
                    k.op("tensor", "matmul", pu[:, :],
                         w1[kc][:, fc * 128:(fc + 1) * 128],
                         xnt2[b][kc][:], start=(kc == 0),
                         stop=(kc == 7), wait=wl)
                for kc in range(8):
                    k.op("tensor", "matmul", pv[:, :],
                         w2[kc][:, fc * 128:(fc + 1) * 128],
                         xnt2[b][kc][:], start=(kc == 0),
                         stop=(kc == 7), inc=(T if kc == 7 else None))
                if fc == 31:
                    xn_used2[b] = k.sems["T"].v
                tv = k.sems["T"].v
                sv = k.op("scalar", "activation", ut[:], pu[:, :], AF.Silu,
                          scale=s1col[:, fc:fc + 1], wait=[(T, tv)], inc=S)
                k.op("vector", "tensor_scalar", vt[:], pv[:, :],
                     s2col[:, fc:fc + 1], None, AL.mult, wait=[(T, tv)])
                sb = fc % 2
                wl = [(S, sv)]
                if stg_d[sb]:
                    wl.append(stg_d[sb])
                psum_free[pi] = k.op("vector", "tensor_mul", hstg[sb][:], ut[:],
                                     vt[:], wait=wl, inc=V)
                stg_d[sb] = k.dma(
                    "gpsimd",
                    h1_d[fc * 128:(fc + 1) * 128, tt * 512:(tt + 1) * 512],
                    hstg[sb][:], wait=[(V, k.sems["V"].v)])
        k.full_barrier()
    if stop_after == "f1":
        return

    def big_layer(win_name, scol_name, src_d, dst_d, act_silu=False):
        with contextlib.ExitStack() as es:
            nm = win_name
            scol, _ = C.colize("l2c_" + scol_name, es, W[scol_name], 4096)
            wg = [es.enter_context(nc.sbuf_tensor(f"{nm}_wg{rc}", [128, 1024], BF16))
                  for rc in range(32)]
            ht2 = [[es.enter_context(nc.sbuf_tensor(f"{nm}_ht{i}_{rc}",
                                                    [128, 512], BF16))
                    for rc in range(32)] for i in range(2)]
            ht = ht2[0]
            ht_used = [0, 0]
            wfs = [es.enter_context(nc.sbuf_tensor(f"{nm}_wfs{i}", [128, 1024], F32))
                   for i in range(2)]
            hstg = [es.enter_context(nc.sbuf_tensor(f"{nm}_hg{i}", [128, 512], BF16))
                    for i in range(2)]
            psum_free = [0, 0]
            stg_d = [None, None]
            for grp in range(4):
                cv_done = [0, 0]
                for rc in range(32):
                    b = rc % 2
                    wl = [(V, cv_done[b])] if cv_done[b] else []
                    lv = k.dma("sync", wfs[b][:],
                               W[win_name][rc * 128:(rc + 1) * 128,
                                           grp * 1024:(grp + 1) * 1024],
                               wait=wl)
                    wl2 = [lv]
                    if grp and rc < 2:
                        wl2.append((T, k.sems["T"].v))
                    cv_done[b] = k.op("vector", "tensor_copy", wg[rc][:],
                                      wfs[b][:], wait=wl2, inc=V)
                wready = k.sems["V"].v
                for tt in range(8):
                    hb = (grp * 8 + tt) % 2
                    ht = ht2[hb]
                    gh = k.new_group("sync")
                    hv = None
                    for rc in range(32):
                        wl = [(T, ht_used[hb])] if (rc == 0 and ht_used[hb]) else []
                        hv = k.dma("sync", ht[rc][:],
                                   src_d[rc * 128:(rc + 1) * 128,
                                         tt * 512:(tt + 1) * 512], gh, wait=wl)
                    k.wait("tensor", hv[0], hv[1])
                    if tt == 0:
                        k.wait("tensor", V, wready)
                    for f2 in range(8):
                        pi = f2 % 2
                        p = C.P[pi]
                        for kc in range(32):
                            wl = []
                            if kc == 0 and psum_free[pi]:
                                wl = [(S, psum_free[pi])]
                            k.op("tensor", "matmul", p[:, :],
                                 wg[kc][:, f2 * 128:(f2 + 1) * 128], ht[kc][:],
                                 start=(kc == 0), stop=(kc == 31), wait=wl,
                                 inc=(T if kc == 31 else None))
                        ht_used[hb] = k.sems["T"].v
                        sb = f2 % 2
                        wl = [(T, k.sems["T"].v)]
                        if stg_d[sb]:
                            wl.append(stg_d[sb])
                        psum_free[pi] = k.op(
                            "scalar", "activation", hstg[sb][:], p[:, :],
                            AF.Gelu, scale=scol[:, grp * 8 + f2:grp * 8 + f2 + 1],
                            wait=wl, inc=S)
                        stg_d[sb] = k.dma(
                            "gpsimd",
                            dst_d[(grp * 8 + f2) * 128:(grp * 8 + f2 + 1) * 128,
                                  tt * 512:(tt + 1) * 512],
                            hstg[sb][:], wait=[(S, k.sems["S"].v)])
            k.full_barrier()

    big_layer("L2_h1T", "L2_ffn_h1_s", h1_d, h2_d)
    if stop_after == "f2":
        return
    big_layer("L2_h2T", "L2_ffn_h2_s", h2_d, h1_d)

    with contextlib.ExitStack() as es:
        s3col, _ = C.colize("l2_s3", es, W["L2_ffn_w3_s"], 1024)
        w3 = [es.enter_context(nc.sbuf_tensor(f"l2_w3_{rc}", [128, 1024], BF16))
              for rc in range(32)]
        wfs = [es.enter_context(nc.sbuf_tensor(f"l2_wf3{i}", [128, 1024], F32))
               for i in range(2)]
        ht = [es.enter_context(nc.sbuf_tensor(f"l2_ht3{rc}", [128, 512], BF16))
              for rc in range(32)]
        xsb = [es.enter_context(nc.sbuf_tensor(f"l2_xsb{i}", [128, 1024], F32))
               for i in range(4)]
        sfc = es.enter_context(nc.sbuf_tensor("l2_sfc", [128, 512], BF16))
        gt = es.enter_context(nc.sbuf_tensor("l2_gt", [128, 512], BF16))
        sg = es.enter_context(nc.sbuf_tensor("l2_sg", [128, 512], F32))
        cv_done = [0, 0]
        for rc in range(32):
            b = rc % 2
            wl = [(V, cv_done[b])] if cv_done[b] else []
            lv = k.dma("sync", wfs[b][:],
                       W["L2_w3T"][rc * 128:(rc + 1) * 128, :], wait=wl)
            cv_done[b] = k.op("vector", "tensor_copy", w3[rc][:], wfs[b][:],
                              wait=[lv], inc=V)
        k.full_barrier()
        psum_free = [0, 0]
        x_st = None
        for tt in range(8):
            gh = k.new_group("sync")
            hv = None
            for rc in range(32):
                wl = [(T, k.sems["T"].v)] if (tt and rc == 0) else []
                hv = k.dma("sync", ht[rc][:],
                           h1_d[rc * 128:(rc + 1) * 128, tt * 512:(tt + 1) * 512],
                           gh, wait=wl)
            gx = k.new_group("sync")
            xl = None
            for tb in range(4):
                wl = [x_st] if (x_st and tb == 0) else []
                xl = k.dma("sync", xsb[tb][:],
                           x_dram[tt * 512 + tb * 128:tt * 512 + (tb + 1) * 128, :],
                           gx, wait=wl)
            k.wait("tensor", hv[0], hv[1])
            k.wait("vector", xl[0], xl[1])
            for fc in range(8):
                pi = fc % 2
                p = C.P[pi]
                sl = k.dma("sync", sfc[:],
                           s_d[fc * 128:(fc + 1) * 128, tt * 512:(tt + 1) * 512],
                           wait=[(V, k.sems["V"].v)])
                for kc in range(32):
                    wl = []
                    if kc == 0 and psum_free[pi]:
                        wl = [(S, psum_free[pi])]
                    k.op("tensor", "matmul", p[:, :],
                         w3[kc][:, fc * 128:(fc + 1) * 128], ht[kc][:],
                         start=(kc == 0), stop=(kc == 31), wait=wl,
                         inc=(T if kc == 31 else None))
                psum_free[pi] = k.op("scalar", "activation", gt[:], p[:, :],
                                     AF.Silu, scale=s3col[:, fc:fc + 1],
                                     wait=[(T, k.sems["T"].v)], inc=S)
                mv = k.op("vector", "tensor_mul", sg[:], gt[:], sfc[:],
                          wait=[(S, k.sems["S"].v), sl], inc=V)
                for tb in range(4):
                    pt = C.P[4 + tb % 2]
                    tv = k.op("tensor", "transpose", pt[:, 0:128],
                              sg[:, tb * 128:(tb + 1) * 128],
                              C.sb["iden_f32"][:], wait=[(V, k.sems["V"].v)],
                              inc=T)
                    k.op("vector", "tensor_add",
                         xsb[tb][:, fc * 128:(fc + 1) * 128],
                         xsb[tb][:, fc * 128:(fc + 1) * 128], pt[:, 0:128],
                         wait=[(T, tv)], inc=V)
                k.wait("tensor", V, k.sems["V"].v)
            for tb in range(4):
                x_st = k.dma("gpsimd",
                             xout_dram[tt * 512 + tb * 128:tt * 512 + (tb + 1) * 128, :],
                             xsb[tb][:], wait=[(V, k.sems["V"].v)])
        k.full_barrier()


L3_WSPECS = {
    "L3_w1T": ([1024, 256], F32),
    "L3_w2T": ([256, 6144], F32),
    "L3_cw1T": ([2048, 256], F32),
    "L3_cw2T": ([256, 2048], F32),
    "L3_Wr": ([1024, 8], F32),
    "L3_WiT": ([8, 1024, 1024], F32),
    "L3_Wo": ([8, 1024, 1024], F32),
}


def emit_layer3(k: K, C: Ctx, W, x_dram, xout_dram, scratch, stop_after="all"):
    nc = k.nc
    T, V, S, P = k.sem("T"), k.sem("V"), k.sem("S"), k.sem("P")
    xnfm = scratch["xnfm"]
    s_d = scratch["s_fm"]
    ym_d = scratch["h1"]
    we_d = scratch["h2"]
    coeff_d = nc.dram_tensor("l3_coeff", [T_CORE, 8], F32, kind="Internal").ap()

    emit_ln(k, C, x_dram, xnfm)
    if stop_after == "ln":
        return

    FEPS = 1.1920929e-07

    with contextlib.ExitStack() as es:
        wr = es.enter_context(nc.sbuf_tensor("l3_wr", [128, 64], BF16))
        wrf = es.enter_context(nc.sbuf_tensor("l3_wrf", [128, 64], F32))
        xt = [[es.enter_context(nc.sbuf_tensor(f"l3_rx{i}_{c}", [128, 128], BF16))
               for c in range(8)] for i in range(2)]
        m1 = es.enter_context(nc.sbuf_tensor("l3_m1", [128, 1], F32))
        nm1 = es.enter_context(nc.sbuf_tensor("l3_nm1", [128, 1], F32))
        m2 = es.enter_context(nc.sbuf_tensor("l3_m2", [128, 1], F32))
        msk = es.enter_context(nc.sbuf_tensor("l3_msk", [128, 8], F32))
        eu = es.enter_context(nc.sbuf_tensor("l3_eu", [128, 8], F32))
        den = es.enter_context(nc.sbuf_tensor("l3_den", [128, 1], F32))
        rcp = es.enter_context(nc.sbuf_tensor("l3_rcp", [128, 1], F32))
        cof = es.enter_context(nc.sbuf_tensor("l3_cof", [128, 8], F32))
        gld = k.new_group("sync")
        lv = k.dma("sync", wrf[:].rearrange("p (c e) -> p c e", e=8),
                   W["L3_Wr"].rearrange("(c p) e -> c p e", p=128).transpose([1, 0, 2]),
                   gld)
        k.op("vector", "tensor_copy", wr[:], wrf[:], wait=[lv], inc=V)
        k.full_barrier()
        xn_used = [k.sems["T"].v, k.sems["T"].v]
        ps_free = [0, 0]
        for i in range(T_CORE // 128):
            b = i % 2
            gx = k.new_group("sync")
            lv = None
            for c in range(8):
                lv = k.dma("sync", xt[b][c][:],
                           xnfm[c * 128:(c + 1) * 128, i * 128:(i + 1) * 128],
                           gx, wait=[(T, xn_used[b])] if c == 0 else [])
            k.wait("tensor", lv[0], lv[1])
            p = C.P[b]
            for c in range(8):
                wl = [(V, ps_free[b])] if (c == 0 and ps_free[b]) else []
                k.op("tensor", "matmul", p[:, 0:8], xt[b][c][:],
                     wr[:, c * 8:(c + 1) * 8], start=(c == 0), stop=(c == 7),
                     wait=wl, inc=(T if c == 7 else None))
            tv = k.sems["T"].v
            xn_used[b] = tv
            k.op("vector", "reduce_max", m1[:], p[:, 0:8], AX.X, wait=[(T, tv)])
            k.op("vector", "reduce_max", nm1[:], p[:, 0:8], AX.X, negate=True)
            k.op("vector", "tensor_single_scalar", msk[:], p[:, 0:8], m1[:],
                 AL.is_ge)
            k.op("vector", "scalar_tensor_tensor", eu[:], msk[:], -1e30, p[:, 0:8],
                 AL.mult, AL.add)
            vv = k.op("vector", "reduce_max", m2[:], eu[:], AX.X, inc=V)
            sv = k.op("scalar", "activation", eu[:], p[:, 0:8], AF.Exp,
                      bias=nm1[:, 0:1], accum_out=den[:], wait=[(V, vv)], inc=S)
            k.op("vector", "reciprocal", rcp[:], den[:], wait=[(S, sv)])
            k.op("vector", "tensor_single_scalar", msk[:], p[:, 0:8], m2[:],
                 AL.is_ge)
            k.op("vector", "tensor_mul", cof[:], msk[:], eu[:])
            ps_free[b] = k.op("vector", "tensor_scalar_mul", cof[:], cof[:],
                              rcp[:], inc=V)
            k.dma("gpsimd", coeff_d[i * 128:(i + 1) * 128, :], cof[:],
                  wait=[(V, ps_free[b])])
        k.full_barrier()

    with contextlib.ExitStack() as es:
        C.load_tables(es, ["l3_cos", "l3_sin"])
        w1 = [es.enter_context(nc.sbuf_tensor(f"l3_w1_{c}", [128, 256], BF16))
              for c in range(8)]
        w2 = [es.enter_context(nc.sbuf_tensor(f"l3_w2_{c}", [128, 6144], BF16))
              for c in range(2)]
        cw1 = [es.enter_context(nc.sbuf_tensor(f"l3_cw1_{c}", [128, 256], BF16))
               for c in range(16)]
        cw2 = [es.enter_context(nc.sbuf_tensor(f"l3_cw2_{c}", [128, 2048], BF16))
               for c in range(2)]
        wfs = [es.enter_context(nc.sbuf_tensor(f"l3_wf{i}", [128, 1024], F32))
               for i in range(2)]
        cv_state = {"b": 0, "done": [0, 0]}

        def load_cast(dst_ap, src_ap):
            n = src_ap.shape[1]
            for o in range(0, n, 1024):
                w = min(1024, n - o)
                b = cv_state["b"]
                cv_state["b"] ^= 1
                wl = [(V, cv_state["done"][b])] if cv_state["done"][b] else []
                lv = k.dma("sync", wfs[b][:, 0:w], src_ap[:, o:o + w], wait=wl)
                cv_state["done"][b] = k.op("vector", "tensor_copy",
                                           dst_ap[:, o:o + w], wfs[b][:, 0:w],
                                           wait=[lv], inc=V)

        for c in range(8):
            load_cast(w1[c][:], W["L3_w1T"][c * 128:(c + 1) * 128, :])
        for c in range(2):
            load_cast(w2[c][:], W["L3_w2T"][c * 128:(c + 1) * 128, :])
        for c in range(16):
            load_cast(cw1[c][:], W["L3_cw1T"][c * 128:(c + 1) * 128, :])
        for c in range(2):
            load_cast(cw2[c][:], W["L3_cw2T"][c * 128:(c + 1) * 128, :])
        k.full_barrier()

        xnt = [[es.enter_context(nc.sbuf_tensor(f"l3_xnt{i}_{c}", [128, 512], BF16))
                for c in range(8)] for i in range(2)]
        rmsn = [es.enter_context(nc.sbuf_tensor(f"l3_rm{c}", [128, 512], BF16))
                for c in range(8)]
        hr = [es.enter_context(nc.sbuf_tensor(f"l3_hr{c}", [128, 2048], BF16))
              for c in range(2)]
        sqv = [es.enter_context(nc.sbuf_tensor(f"l3_sqv{i}", [128, 512], BF16))
               for i in range(2)]
        rrow = es.enter_context(nc.sbuf_tensor("l3_rrow", [1, 2048], F32))
        qf = es.enter_context(nc.sbuf_tensor("l3_qf", [128, 2048], F32))
        kf = es.enter_context(nc.sbuf_tensor("l3_kf", [128, 2048], F32))
        vb = es.enter_context(nc.sbuf_tensor("l3_vb", [128, 2048], BF16))
        skv = es.enter_context(nc.sbuf_tensor("l3_skv", [128, 2048], F32))
        cum = es.enter_context(nc.sbuf_tensor("l3_cum", [128, 2048], F32))
        ym = es.enter_context(nc.sbuf_tensor("l3_ym", [128, 2048], BF16))
        c1 = [es.enter_context(nc.sbuf_tensor(f"l3_c1_{c}", [128, 2048], BF16))
              for c in range(2)]
        ymt = xnt[0] + xnt[1]
        zz = es.enter_context(nc.sbuf_tensor("l3_zz", [128, 2048], F32))
        k.op("vector", "memset", zz[:], 1.0)
        sst = es.enter_context(nc.sbuf_tensor("l3_sst", [128, 512], BF16))
        sgg = es.enter_context(nc.sbuf_tensor("l3_sgg", [128, 512], BF16))
        xnc = es.enter_context(nc.sbuf_tensor("l3_xnc", [128, 512], BF16))
        uss = es.enter_context(nc.sbuf_tensor("l3_uss", [128, 512], F32))

        def rms_norm_fm(src_tiles, n_chunks, dst_tiles, dst_dtype_bf=True,
                        seq_base=0, width=2048, from_psum=False):
            for blk in range(width // 512):
                cs = blk * 512
                pS = C.P[4]
                tmv = [0, 0]
                for c in range(n_chunks):
                    sb2 = c % 2
                    swl = [(V, k.sems["V"].v)]
                    if tmv[sb2]:
                        swl.append((T, tmv[sb2]))
                    sv = k.op("scalar", "activation", sqv[sb2][:, :],
                              src_tiles[c][:, cs:cs + 512], AF.Square,
                              wait=swl, inc=S)
                    wl = [(S, sv)]
                    if c == 0:
                        wl.append((V, k.sems["V"].v))
                    tmv[sb2] = k.op("tensor", "matmul", pS[0:1, :],
                                    C.ones_col_bf[:, :], sqv[sb2][:, :],
                                    start=(c == 0), stop=(c == n_chunks - 1),
                                    wait=wl, inc=T)
                tv = k.sems["T"].v
                k.op("vector", "tensor_scalar", rrow[0:1, cs:cs + 512],
                     pS[0:1, :], 1.0 / (n_chunks * 128), FEPS, AL.mult, AL.add,
                     wait=[(T, tv)])
                k.op("scalar", "activation", rrow[0:1, cs:cs + 512],
                     rrow[0:1, cs:cs + 512], AF.Ln,
                     wait=[(V, k.sems["V"].v)])
                k.op("vector", "tensor_single_scalar", rrow[0:1, cs:cs + 512],
                     rrow[0:1, cs:cs + 512], -0.5, AL.mult,
                     wait=[(S, k.sems["S"].v)])
                sv = k.op("scalar", "activation", rrow[0:1, cs:cs + 512],
                          rrow[0:1, cs:cs + 512], AF.Exp,
                          wait=[(V, k.sems["V"].v)], inc=S)
                import os as _os
                if _os.environ.get("L3DBG2"):
                    dbg3 = nc.dram_tensor(f"dbg_rr{k.sems['T'].v}", [1, 2048], F32,
                                          kind="ExternalOutput").ap()
                    k.dma("gpsimd", dbg3[0:1, 0:512], rrow[0:1, cs:cs + 512],
                          wait=[(S, sv)])
                tv = k.op("tensor", "matmul", C.P[5][:, :], C.ones_row[0:1, :],
                          rrow[0:1, cs:cs + 512], start=True, stop=True,
                          wait=[(S, sv), (V, k.sems["V"].v)], inc=T)
                if _os.environ.get("L3DBG2") and not hasattr(k, "_dbg5"):
                    k._dbg5 = nc.dram_tensor("dbg_p5", [128, 512], F32,
                                             kind="ExternalOutput").ap()
                    vv2 = k.op("vector", "tensor_copy", cum[:, 0:512],
                               C.P[5][:, :], wait=[(T, tv)], inc=k.sem("V"))
                    k.dma("gpsimd", k._dbg5, cum[:, 0:512],
                          wait=[(k.sem("V"), vv2)])
                for c in range(n_chunks):
                    k.op("vector", "tensor_mul", dst_tiles[c][:, cs:cs + 512],
                         src_tiles[c][:, cs:cs + 512], C.P[5][:, :],
                         wait=[(T, tv)] if c == 0 else [], inc=V)

        for sq_i in range(SEQ_PER_CORE):
            base_t = sq_i * SEQ
            xn_used = [k.sems["T"].v, k.sems["T"].v]
            ps_free = k.sems["V"].v
            for tt in range(4):
                b = tt % 2
                gx = k.new_group("sync")
                lv = None
                for c in range(8):
                    lv = k.dma("sync", xnt[b][c][:],
                               xnfm[c * 128:(c + 1) * 128,
                                    base_t + tt * 512:base_t + (tt + 1) * 512],
                               gx, wait=[(T, xn_used[b]), (V, k.sems["V"].v)]
                               if c == 0 else [])
                k.wait("scalar", lv[0], lv[1])
                k.wait("vector", lv[0], lv[1])
                rms_norm_fm([xnt[b][c][:] for c in range(8)], 8,
                            [rmsn[c][:] for c in range(8)], width=512)
                import os as _os
                if _os.environ.get("L3DBG") and tt == 0 and sq_i == 0:
                    dbg2 = nc.dram_tensor("dbg_rms", [2, 128, 512], F32,
                                          kind="ExternalOutput").ap()
                    vv = k.sems["V"].v
                    k.dma("gpsimd", dbg2[0], rmsn[0][:], wait=[(V, vv)])
                    k.full_barrier()
                rms_done = k.sems["V"].v
                for rc in range(2):
                    p = C.P[rc]
                    for kc in range(8):
                        wl = [(V, max(ps_free, rms_done))] if kc == 0 else []
                        k.op("tensor", "matmul", p[:, :],
                             w1[kc][:, rc * 128:(rc + 1) * 128],
                             rmsn[kc][:], start=(kc == 0), stop=(kc == 7),
                             wait=wl, inc=(T if kc == 7 else None))
                    tv = k.sems["T"].v
                    if rc == 1:
                        xn_used[b] = tv
                    k.op("vector", "tensor_single_scalar", qf[:, 0:512], p[:, :],
                         0.0, AL.max, wait=[(T, tv)])
                    ps_free = k.op("vector", "tensor_mul",
                                   hr[rc][:, tt * 512:(tt + 1) * 512],
                                   qf[:, 0:512], qf[:, 0:512], inc=V)
                    if _os.environ.get("L3DBG") and tt == 0 and sq_i == 0 and rc == 0:
                        k.dma("gpsimd", dbg2[1], hr[0][:, 0:512].rearrange("p f -> p f"),
                              wait=[(V, ps_free)])
                        k.full_barrier()
            ym_store = None
            for h in range(16):
                ps_free2 = k.sems["V"].v
                for part, dst in ((0, qf), (1, kf), (2, vb)):
                    fbase = part * 2048 + h * 128
                    for tt in range(4):
                        p = C.P[2 + tt % 2]
                        for kc in range(2):
                            wl = [(V, ps_free2)] if kc == 0 else []
                            k.op("tensor", "matmul", p[:, :],
                                 w2[kc][:, fbase:fbase + 128],
                                 hr[kc][:, tt * 512:(tt + 1) * 512],
                                 start=(kc == 0), stop=(kc == 1), wait=wl,
                                 inc=(T if kc == 1 else None))
                        tv = k.sems["T"].v
                        ps_free2 = k.op("vector", "tensor_copy",
                                        dst[:, tt * 512:(tt + 1) * 512], p[:, :],
                                        wait=[(T, tv)], inc=V)
                import os as _os
                if _os.environ.get("L3DBG") and h == 0 and sq_i == 0:
                    dbg1 = nc.dram_tensor("dbg_qkv", [3, 128, 2048], F32,
                                          kind="ExternalOutput").ap()
                    dbg4 = nc.dram_tensor("dbg_hr", [2, 128, 2048], BF16,
                                          kind="ExternalOutput").ap()
                    dbg6 = nc.dram_tensor("dbg_w2", [2, 128, 256], BF16,
                                          kind="ExternalOutput").ap()
                    vv = k.sems["V"].v
                    k.dma("gpsimd", dbg1[0], qf[:], wait=[(V, vv)])
                    k.dma("gpsimd", dbg1[1], kf[:], wait=[(V, vv)])
                    for rc2 in range(2):
                        k.dma("gpsimd", dbg4[rc2], hr[rc2][:], wait=[(V, vv)])
                        k.dma("gpsimd", dbg6[rc2], w2[rc2][:, 0:256], wait=[(V, vv)])
                    k.full_barrier()
                rms_norm_fm([qf[:]], 1, [qf[:]], width=2048)
                rms_norm_fm([kf[:]], 1, [kf[:]], width=2048)
                if _os.environ.get("L3DBG") and h == 0 and sq_i == 0:
                    vv = k.sems["V"].v
                    k.dma("gpsimd", dbg1[2], qf[:], wait=[(V, vv)])
                    k.full_barrier()
                for z in (qf, kf):
                    cosv, sinv = C.sb["l3_cos"], C.sb["l3_sin"]
                    k.op("vector", "tensor_mul", skv[0:64, :], z[0:64, :],
                         cosv[0:64, :])
                    k.op("vector", "tensor_mul", cum[0:64, :], z[0:64, :],
                         sinv[0:64, :])
                    k.op("vector", "tensor_mul", skv[64:128, :], z[64:128, :],
                         sinv[64:128, :])
                    k.op("vector", "tensor_mul", cum[64:128, :], z[64:128, :],
                         cosv[64:128, :])
                    k.op("vector", "tensor_copy", z[0:64, :], skv[64:128, :])
                    k.op("vector", "tensor_add", z[0:64, :], z[0:64, :],
                         skv[0:64, :])
                    k.op("vector", "tensor_copy", z[64:128, :], cum[0:64, :])
                    k.op("vector", "tensor_sub", z[64:128, :], cum[64:128, :],
                         z[64:128, :])
                sv = k.op("scalar", "activation", kf[:], kf[:], AF.Sigmoid,
                          wait=[(V, k.sems["V"].v)], inc=S)
                k.op("vector", "tensor_mul", skv[:], kf[:], vb[:],
                     wait=[(S, sv)])
                k.op("vector", "tensor_tensor_scan", cum[:], zz[:], skv[:],
                     0.0, AL.mult, AL.add)
                mv = k.op("vector", "scalar_tensor_tensor", ym[:], qf[:], 0.0,
                          cum[:], AL.max, AL.mult,
                          wait=[ym_store] if ym_store else [], inc=V)
                ym_store = k.dma("gpsimd",
                                 ym_d[h * 128:(h + 1) * 128, base_t:base_t + SEQ],
                                 ym[:], wait=[(V, mv)])
            k.full_barrier()
            ps_free = 0
            for rc in range(2):
                for tt in range(4):
                    gy = k.new_group("sync")
                    lv = None
                    for c in range(16):
                        wl = [(T, k.sems["T"].v)] if c == 0 else []
                        lv = k.dma("sync", ymt[c][:],
                                   ym_d[c * 128:(c + 1) * 128,
                                        base_t + tt * 512:base_t + (tt + 1) * 512],
                                   gy, wait=wl)
                    k.wait("tensor", lv[0], lv[1])
                    p = C.P[tt % 2]
                    for kc in range(16):
                        wl = [(V, ps_free)] if (kc == 0 and ps_free) else []
                        k.op("tensor", "matmul", p[:, :],
                             cw1[kc][:, rc * 128:(rc + 1) * 128], ymt[kc][:],
                             start=(kc == 0), stop=(kc == 15), wait=wl,
                             inc=(T if kc == 15 else None))
                    ps_free = k.op("vector", "tensor_copy",
                                   c1[rc][:, tt * 512:(tt + 1) * 512],
                                   C.P[tt % 2][:, :], wait=[(T, k.sems["T"].v)],
                                   inc=V)
            s_store = None
            for tt in range(4):
                for f in range(8):
                    pu, pg = C.P[0 + (f % 2) * 2], C.P[1 + (f % 2) * 2]
                    for kc in range(2):
                        wl = [(V, k.sems["V"].v)] if kc == 0 else []
                        k.op("tensor", "matmul", pu[:, :],
                             cw2[kc][:, f * 128:(f + 1) * 128],
                             c1[kc][:, tt * 512:(tt + 1) * 512],
                             start=(kc == 0), stop=(kc == 1), wait=wl)
                    for kc in range(2):
                        k.op("tensor", "matmul", pg[:, :],
                             cw2[kc][:, 1024 + f * 128:1024 + (f + 1) * 128],
                             c1[kc][:, tt * 512:(tt + 1) * 512],
                             start=(kc == 0), stop=(kc == 1),
                             inc=(T if kc == 1 else None))
                    tv = k.sems["T"].v
                    xl = k.dma("sync", xnc[:],
                               xnfm[f * 128:(f + 1) * 128,
                                    base_t + tt * 512:base_t + (tt + 1) * 512],
                               wait=[(V, k.sems["V"].v)])
                    sv = k.op("scalar", "activation", sgg[:], pg[:, :], AF.Silu,
                              wait=[(T, tv)], inc=S)
                    k.op("vector", "tensor_mul", uss[:], pu[:, :], sgg[:],
                         wait=[(S, sv)])
                    wl = [xl]
                    if s_store:
                        wl.append(s_store)
                    mv = k.op("vector", "tensor_add", sst[:], xnc[:], uss[:],
                              wait=wl, inc=V)
                    s_store = k.dma("gpsimd",
                                    s_d[f * 128:(f + 1) * 128,
                                        base_t + tt * 512:base_t + (tt + 1) * 512],
                                    sst[:], wait=[(V, mv)])
        k.full_barrier()
    if stop_after == "mixer":
        return

    with contextlib.ExitStack() as es:
        wi = [es.enter_context(nc.sbuf_tensor(f"l3_wi{c}", [128, 1024], BF16))
              for c in range(8)]
        wo = [es.enter_context(nc.sbuf_tensor(f"l3_wo{c}", [128, 1024], BF16))
              for c in range(8)]
        wfs = [es.enter_context(nc.sbuf_tensor(f"l3_wfm{i}", [128, 1024], F32))
               for i in range(2)]
        stg = [es.enter_context(nc.sbuf_tensor(f"l3_wst{i}", [128, 512], BF16))
               for i in range(2)]
        cv_done = [0, 0]
        stg_d = [None, None]
        for e in range(8):
            for c in range(8):
                b = c % 2
                wl = [(V, cv_done[b])] if cv_done[b] else []
                if c < 2:
                    wl.append((T, k.sems["T"].v))
                lv = k.dma("sync", wfs[b][:], W["L3_WiT"][e, c * 128:(c + 1) * 128, :],
                           wait=wl)
                cv_done[b] = k.op("vector", "tensor_copy", wi[c][:], wfs[b][:],
                                  wait=[lv], inc=V)
            for c in range(8):
                b = c % 2
                wl = [(V, cv_done[b])]
                lv = k.dma("sync", wfs[b][:], W["L3_Wo"][e, c * 128:(c + 1) * 128, :],
                           wait=wl)
                cv_done[b] = k.op("vector", "tensor_copy", wo[c][:], wfs[b][:],
                                  wait=[lv], inc=V)
            wready = k.sems["V"].v
            k.wait("tensor", V, wready)
            for ic in range(8):
                for ob in range(2):
                    pi = ob
                    p = C.P[pi]
                    for hc in range(8):
                        wl = [(S, k.sems["S"].v)] if hc == 0 else []
                        k.op("tensor", "matmul", p[:, :],
                             wi[hc][:, ic * 128:(ic + 1) * 128],
                             wo[hc][:, ob * 512:(ob + 1) * 512],
                             start=(hc == 0), stop=(hc == 7), wait=wl,
                             inc=(T if hc == 7 else None))
                    sb = ob
                    wl = [(T, k.sems["T"].v)]
                    if stg_d[sb]:
                        wl.append(stg_d[sb])
                    sv = k.op("scalar", "copy", stg[sb][:], p[:, :], wait=wl,
                              inc=S)
                    rbase = (e % 4) * 1024 + ic * 128
                    cbase = (e // 4) * 1024 + ob * 512
                    stg_d[sb] = k.dma(
                        "gpsimd",
                        we_d[rbase:rbase + 128, cbase:cbase + 512],
                        stg[sb][:], wait=[(S, sv)])
        k.full_barrier()

    with contextlib.ExitStack() as es:
        xng = [es.enter_context(nc.sbuf_tensor(f"l3_xg{c}", [128, 1024], BF16))
               for c in range(8)]
        weg = [[es.enter_context(nc.sbuf_tensor(f"l3_we{i}_{c}", [128, 1024], BF16))
                for c in range(8)] for i in range(2)]
        gacc = [es.enter_context(nc.sbuf_tensor(f"l3_ga{tb}", [128, 1024], F32))
                for tb in range(8)]
        coft = [es.enter_context(nc.sbuf_tensor(f"l3_cf{tb}", [128, 8], F32))
                for tb in range(8)]
        gsil = es.enter_context(nc.sbuf_tensor("l3_gs", [128, 1024], BF16))
        ssl = es.enter_context(nc.sbuf_tensor("l3_ssl", [128, 1024], BF16))
        sgf = es.enter_context(nc.sbuf_tensor("l3_sgf", [128, 1024], F32))
        xsb = es.enter_context(nc.sbuf_tensor("l3_xsb", [128, 1024], F32))
        for tg in range(4):
            tbase = tg * 1024
            gx = k.new_group("sync")
            for tb in range(8):
                wl = [(T, k.sems["T"].v), (V, k.sems["V"].v)] if tb == 0 else []
                for c in range(8):
                    lv = k.dma("sync", xng[c][:, tb * 128:(tb + 1) * 128],
                               xnfm[c * 128:(c + 1) * 128,
                                    tbase + tb * 128:tbase + (tb + 1) * 128],
                               gx, wait=wl if c == 0 else [])
                cl = k.dma("sync", coft[tb][:],
                           coeff_d[tbase + tb * 128:tbase + (tb + 1) * 128, :], gx)
            k.wait("tensor", cl[0], cl[1])
            k.wait("vector", cl[0], cl[1])
            we_used = [k.sems["T"].v, k.sems["T"].v]
            for e in range(8):
                eb = e % 2
                gw = k.new_group("sync")
                wv = None
                for c in range(8):
                    rbase = (e % 4) * 1024 + c * 128
                    cbase = (e // 4) * 1024
                    wv = k.dma("sync", weg[eb][c][:],
                               we_d[rbase:rbase + 128, cbase:cbase + 1024],
                               gw, wait=[(T, we_used[eb])] if c == 0 else [])
                k.wait("tensor", wv[0], wv[1])
                for tb in range(8):
                    for ob in range(2):
                        p = C.P[2 * eb + ob]
                        for ic in range(8):
                            wl = [(V, k.sems["V"].v)] if ic == 0 else []
                            k.op("tensor", "matmul", p[:, :],
                                 xng[ic][:, tb * 128:(tb + 1) * 128],
                                 weg[eb][ic][:, ob * 512:(ob + 1) * 512],
                                 start=(ic == 0), stop=(ic == 7), wait=wl,
                                 inc=(T if ic == 7 else None))
                        tv = k.sems["T"].v
                        if tb == 7 and ob == 1:
                            we_used[eb] = tv
                        dst = gacc[tb][:, ob * 512:(ob + 1) * 512]
                        if e == 0:
                            k.op("vector", "tensor_scalar_mul", dst, p[:, :],
                                 coft[tb][:, 0:1],
                                 wait=[(T, tv), (S, k.sems["S"].v)], inc=V)
                        else:
                            k.op("vector", "scalar_tensor_tensor", dst, p[:, :],
                                 coft[tb][:, e:e + 1], dst, AL.mult, AL.add,
                                 wait=[(T, tv)], inc=V)
            x_st = None
            for tb in range(8):
                sv = k.op("scalar", "activation", gsil[:], gacc[tb][:], AF.Silu,
                          wait=[(V, k.sems["V"].v)], inc=S)
                gs = k.new_group("sync")
                sl = None
                for c in range(8):
                    sl = k.dma("sync", ssl[:, c * 128:(c + 1) * 128],
                               s_d[c * 128:(c + 1) * 128,
                                   tbase + tb * 128:tbase + (tb + 1) * 128],
                               gs, wait=[(V, k.sems["V"].v)] if c == 0 else [])
                xl = k.dma("sync", xsb[:],
                           x_dram[tbase + tb * 128:tbase + (tb + 1) * 128, :],
                           wait=[x_st] if x_st else [])
                k.wait("tensor", sl[0], sl[1])
                mvb = [0, 0]
                for c in range(8):
                    pb = c % 2
                    wl = [(V, mvb[pb])] if mvb[pb] else []
                    if c < 2:
                        wl += [(V, k.sems["V"].v), (S, sv)]
                    tv = k.op("tensor", "transpose", C.PBF[pb][:, 0:128],
                              ssl[:, c * 128:(c + 1) * 128], C.sb["iden_bf"][:],
                              wait=wl, inc=T)
                    mvb[pb] = k.op("vector", "tensor_mul",
                                   sgf[:, c * 128:(c + 1) * 128],
                                   gsil[:, c * 128:(c + 1) * 128],
                                   C.PBF[pb][:, 0:128],
                                   wait=[(T, tv)], inc=V)
                vv = k.op("vector", "tensor_add", xsb[:], xsb[:], sgf[:],
                          wait=[xl], inc=V)
                x_st = k.dma("gpsimd",
                             xout_dram[tbase + tb * 128:tbase + (tb + 1) * 128, :],
                             xsb[:], wait=[(V, vv)])
        k.full_barrier()


L1_WSPECS = {
    "L1_attn_Wq_down": ([1024, 1024], F32),
    "L1_attn_Wk_down": ([1024, 1024], F32),
    "L1_attn_Wv_down": ([1024, 1024], F32),
    "L1_attn_Wq_up_c": ([64, 32], F32),
    "L1_attn_Wq_up_e": ([64, 32], F32),
    "L1_attn_Wk_up_c": ([64, 32], F32),
    "L1_attn_Wk_up_e": ([64, 32], F32),
    "L1_attn_Wv_up": ([64, 64], F32),
    "L1_attn_Wc": ([1024, 1024], F32),
    "L1_Wc_r": ([1024, 128], F32),
    "L1_bc": ([128], F32),
    "L1_W1": ([128, 1024, 32], F32),
    "L1_W2T": ([128, 32, 1024], F32),
    "L1_B1T": ([32, 128], F32),
    "L1_B2": ([128, 1024], F32),
}


def emit_layer1(k: K, C: Ctx, W, x_dram, xout_dram, scratch, stop_after="all"):
    nc = k.nc
    T, V, S, P = k.sem("T"), k.sem("V"), k.sem("S"), k.sem("P")
    xnfm = scratch["xnfm"]
    xntok = scratch["xntok"]
    s_d = scratch["s_fm"]
    p_d = nc.dram_tensor("l1_p", [128, 128, 1024], BF16, kind="Internal").ap()
    r_d = nc.dram_tensor("l1_r", [128, 128, 1024], BF16, kind="Internal").ap()
    perm_d = nc.dram_tensor("l1_perm", [128, 128, 32], BF16, kind="Internal").ap()

    emit_ln(k, C, x_dram, xnfm, xntok)
    if stop_after == "ln":
        return

    with contextlib.ExitStack() as es:
        C.load_tables(es, ["l1_cos", "l1_sin"])
        wqd = [es.enter_context(nc.sbuf_tensor(f"l1_wqd{c}", [128, 1024], BF16))
               for c in range(8)]
        wkd = [es.enter_context(nc.sbuf_tensor(f"l1_wkd{c}", [128, 1024], BF16))
               for c in range(8)]
        wvd = [es.enter_context(nc.sbuf_tensor(f"l1_wvd{c}", [128, 1024], BF16))
               for c in range(8)]
        wc = [es.enter_context(nc.sbuf_tensor(f"l1_wc{c}", [128, 1024], BF16))
              for c in range(8)]
        wfs1 = es.enter_context(nc.sbuf_tensor("l1_wf0", [128, 1024], F32))
        ups_f = es.enter_context(nc.sbuf_tensor("l1_upsf", [64, 224], F32))
        upr_f = es.enter_context(nc.sbuf_tensor("l1_uprf", [64, 64], F32))
        ups = es.enter_context(nc.sbuf_tensor("l1_ups", [128, 288], BF16))
        cv_done = 0
        for tiles, name in ((wqd, "L1_attn_Wq_down"), (wkd, "L1_attn_Wk_down"),
                            (wvd, "L1_attn_Wv_down"), (wc, "L1_attn_Wc")):
            for c in range(8):
                wl = [(V, cv_done)] if cv_done else []
                lv = k.dma("sync", wfs1[:], W[name][c * 128:(c + 1) * 128, :],
                           wait=wl)
                cv_done = k.op("vector", "tensor_copy", tiles[c][:], wfs1[:],
                               wait=[lv], inc=V)
        gu = k.new_group("sync")
        lv = None
        for i, nm in enumerate(("L1_attn_Wq_up_c", "L1_attn_Wq_up_e",
                                "L1_attn_Wk_up_c", "L1_attn_Wk_up_e")):
            lv = k.dma("sync", ups_f[:, i * 32:(i + 1) * 32], W[nm], gu)
        lv = k.dma("sync", ups_f[:, 128:192], W["L1_attn_Wv_up"], gu)
        k.wait("vector", lv[0], lv[1])
        k.op("vector", "tensor_single_scalar", ups[0:64, 0:64], ups_f[:, 0:64],
             0.125, AL.mult)
        qe = ups_f[:, 32:64].rearrange("p (j two) -> p j two", two=2)
        qer = ups[0:64, 64:96].rearrange("p (j two) -> p j two", two=2)
        k.op("vector", "tensor_single_scalar", qer[:, :, 0], qe[:, :, 1],
             -0.125, AL.mult)
        k.op("vector", "tensor_single_scalar", qer[:, :, 1], qe[:, :, 0],
             0.125, AL.mult)
        k.op("vector", "tensor_copy", ups[0:64, 96:160], ups_f[:, 64:128])
        ke = ups_f[:, 96:128].rearrange("p (j two) -> p j two", two=2)
        ker = ups[0:64, 160:192].rearrange("p (j two) -> p j two", two=2)
        k.op("vector", "tensor_single_scalar", ker[:, :, 0], ke[:, :, 1],
             -1.0, AL.mult)
        k.op("vector", "tensor_copy", ker[:, :, 1], ke[:, :, 0])
        k.op("vector", "tensor_copy", ups[0:64, 192:256], ups_f[:, 128:192])
        k.op("vector", "tensor_copy", ups[64:128, 0:288], ups[0:64, 0:288])
        k.full_barrier()

        xnt = [[es.enter_context(nc.sbuf_tensor(f"l1_xnt{i}_{c}", [128, 512], BF16))
                for c in range(8)] for i in range(2)]
        ymix = [es.enter_context(nc.sbuf_tensor(f"l1_ym{c}", [128, 2048], BF16))
                for c in range(8)]
        qa = [es.enter_context(nc.sbuf_tensor(f"l1_qa{i}", [65, 2048], BF16))
              for i in range(2)]
        ka = [es.enter_context(nc.sbuf_tensor(f"l1_ka{i}", [65, 2048], BF16))
              for i in range(2)]
        vto = [es.enter_context(nc.sbuf_tensor(f"l1_v{i}", [128, 1040], BF16))
               for i in range(2)]
        qlat = es.enter_context(nc.sbuf_tensor("l1_qlat", [128, 2048], BF16))
        klat = es.enter_context(nc.sbuf_tensor("l1_klat", [128, 2048], BF16))
        vlat = es.enter_context(nc.sbuf_tensor("l1_vlat", [128, 2048], BF16))
        e1 = es.enter_context(nc.sbuf_tensor("l1_e1", [32, 2048], BF16))
        e2 = es.enter_context(nc.sbuf_tensor("l1_e2", [32, 2048], BF16))
        attT = [es.enter_context(nc.sbuf_tensor(f"l1_attT{i}", [128, 512], BF16))
                for i in range(2)]
        sq64 = es.enter_context(nc.sbuf_tensor("l1_sq64", [64, 2048], BF16))
        nrm = es.enter_context(nc.sbuf_tensor("l1_nrm", [1, 2048], F32))
        kmx = es.enter_context(nc.sbuf_tensor("l1_kmx", [1, 1], F32))
        rd = es.enter_context(nc.sbuf_tensor("l1_rd", [1, 512], F32))
        rb = es.enter_context(nc.sbuf_tensor("l1_rb", [64, 512], F32))
        sstg = [es.enter_context(nc.sbuf_tensor(f"l1_ss{i}", [128, 512], BF16))
                for i in range(2)]

        for sq_i in range(SEQ_PER_CORE):
            base_t = sq_i * SEQ
            for hp in range(8):
                xn_used = [k.sems["T"].v, k.sems["T"].v]
                psum_free = [k.sems["V"].v] * 3
                for tt in range(4):
                    b = tt % 2
                    gld = k.new_group("sync")
                    lv = None
                    for kc in range(8):
                        lv = k.dma("sync", xnt[b][kc][:],
                                   xnfm[kc * 128:(kc + 1) * 128,
                                        base_t + tt * 512:base_t + (tt + 1) * 512],
                                   gld,
                                   wait=[(T, xn_used[b])] if kc == 0 else [])
                    k.wait("tensor", lv[0], lv[1])
                    for pi, (tiles, dst) in enumerate(
                            ((wqd, qlat), (wkd, klat), (wvd, vlat))):
                        p = C.P[pi]
                        for kc in range(8):
                            wl = [(V, psum_free[pi])] if kc == 0 else []
                            k.op("tensor", "matmul", p[:, :],
                                 tiles[kc][:, hp * 128:(hp + 1) * 128],
                                 xnt[b][kc][:], start=(kc == 0), stop=(kc == 7),
                                 wait=wl, inc=(T if kc == 7 else None))
                        tv = k.sems["T"].v
                        if pi == 2:
                            xn_used[b] = tv
                        psum_free[pi] = k.op("vector", "tensor_copy",
                                             dst[:, tt * 512:(tt + 1) * 512],
                                             p[:, :], wait=[(T, tv)], inc=V)
                lat_done = k.sems["V"].v
                k.wait("tensor", V, lat_done)
                for hh in range(2):
                    lb = hh * 64
                    for (lat, dsta, ucol, ecol, rcol, ctab) in (
                            (qlat, qa[hh], 0, 32, 64, None),
                            (klat, ka[hh], 96, 128, 160, None)):
                        pc = C.P[0]
                        pe_ = C.P[1]
                        per = C.P[2]
                        for ts in range(4):
                            cs = ts * 512
                            wl = [(V, k.sems["V"].v)]
                            k.op("tensor", "matmul", pc[0:32, :],
                                 ups[lb:lb + 64, ucol:ucol + 32],
                                 lat[lb:lb + 64, cs:cs + 512], start=True,
                                 stop=True, wait=wl)
                            k.op("tensor", "matmul", pe_[0:32, :],
                                 ups[lb:lb + 64, ecol:ecol + 32],
                                 lat[lb:lb + 64, cs:cs + 512], start=True,
                                 stop=True)
                            tv = k.op("tensor", "matmul", per[0:32, :],
                                      ups[lb:lb + 64, rcol:rcol + 32],
                                      lat[lb:lb + 64, cs:cs + 512], start=True,
                                      stop=True, inc=T)
                            k.op("vector", "tensor_copy",
                                 dsta[0:32, cs:cs + 512], pc[0:32, :],
                                 wait=[(T, tv)])
                            k.op("vector", "tensor_copy", e1[:, cs:cs + 512],
                                 pe_[0:32, :])
                            k.op("vector", "tensor_copy", e2[:, cs:cs + 512],
                                 per[0:32, :], inc=V)
                        k.op("vector", "tensor_mul", e1[:], e1[:],
                             C.sb["l1_cos"][:])
                        k.op("vector", "tensor_mul", e2[:], e2[:],
                             C.sb["l1_sin"][:])
                        vv = k.op("vector", "tensor_add", e1[:], e1[:], e2[:],
                                  inc=V)
                        k.op("vector", "tensor_copy", dsta[32:64, :], e1[:],
                             inc=V)
                    vw = [0, 0]
                    for tb in range(16):
                        pv = C.P[3 + tb % 2]
                        wl = [(V, vw[tb % 2])] if vw[tb % 2] else []
                        tv = k.op("tensor", "matmul", pv[0:128, 0:64],
                                  vlat[lb:lb + 64, tb * 128:(tb + 1) * 128],
                                  ups[lb:lb + 64, 192:256], start=True,
                                  stop=True, wait=wl, inc=T)
                        vw[tb % 2] = k.op("vector", "tensor_copy",
                                          vto[hh][:, tb * 65:tb * 65 + 64],
                                          pv[0:128, 0:64], wait=[(T, tv)], inc=V)
                    k.op("vector", "memset",
                         vto[hh][:].rearrange("p (t c) -> p t c", c=65)[:, :, 64:65],
                         1.0, inc=V)
                for hh in range(2):
                    sv = k.op("scalar", "activation", sq64[:, :], ka[hh][0:64, :],
                              AF.Square,
                              wait=[(T, k.sems["T"].v), (V, k.sems["V"].v)], inc=S)
                    cpv = 0
                    for tt in range(4):
                        pst = C.P[3 + tt % 2]
                        wl = [(S, sv)]
                        if tt >= 2:
                            wl.append((V, cpv))
                        tv = k.op("tensor", "matmul", pst[0:1, :],
                                  C.ones_col_bf[0:64, :],
                                  sq64[:, tt * 512:(tt + 1) * 512], start=True,
                                  stop=True, wait=wl, inc=T)
                        cpv = k.op("vector", "tensor_copy",
                                   nrm[0:1, tt * 512:(tt + 1) * 512], pst[0:1, :],
                                   wait=[(T, tv)], inc=V)
                    kv = k.op("vector", "reduce_max", kmx[:], nrm[:], AX.X, inc=V)
                    sv = k.op("scalar", "activation", sq64[:, :], qa[hh][0:64, :],
                              AF.Square, wait=[(T, k.sems["T"].v)], inc=S)
                    for tt in range(4):
                        pst = C.P[3 + tt % 2]
                        wl = [(S, sv), (V, kv)]
                        if tt >= 2:
                            wl.append((V, cpv))
                        tv = k.op("tensor", "matmul", pst[0:1, :],
                                  C.ones_col_bf[0:64, :],
                                  sq64[:, tt * 512:(tt + 1) * 512], start=True,
                                  stop=True, wait=wl, inc=T)
                        cpv = k.op("vector", "tensor_copy",
                                   nrm[0:1, tt * 512:(tt + 1) * 512], pst[0:1, :],
                                   wait=[(T, tv)], inc=V)
                    vv = k.op("vector", "tensor_scalar_mul", nrm[:], nrm[:],
                              kmx[0:1, 0:1], inc=V)
                    k.op("vector", "tensor_single_scalar", nrm[:], nrm[:],
                         1e-30, AL.add, wait=[(V, vv)])
                    k.op("scalar", "activation", nrm[:], nrm[:], AF.Ln,
                         wait=[(V, k.sems["V"].v)])
                    k.op("vector", "tensor_single_scalar", nrm[:], nrm[:], 0.5,
                         AL.mult, wait=[(S, k.sems["S"].v)])
                    sv = k.op("scalar", "activation", nrm[:], nrm[:], AF.Exp,
                              wait=[(V, k.sems["V"].v)], inc=S)
                    k.op("vector", "tensor_scalar", qa[hh][64:65, :], nrm[0:1, :],
                         -1.0, -1.0, AL.mult, AL.add, wait=[(S, sv)])
                    k.op("vector", "memset", ka[hh][64:65, :], 1.0, inc=V)
                aug_ready = k.sems["V"].v
                k.wait("tensor", V, aug_ready)
                for hh in range(2):
                    qh, kh, vh = qa[hh], ka[hh], vto[hh]
                    y_ev = k.sems["V"].v
                    sc_ev = [k.sems["S"].v, k.sems["S"].v]
                    for Q in range(4):
                        for kt in range(4 * Q + 4):
                            pb = kt % 2
                            p = C.P[pb]
                            wl = [(S, sc_ev[pb])] if sc_ev[pb] else []
                            tv = k.op("tensor", "matmul", p[:, :],
                                      kh[0:65, kt * 128:(kt + 1) * 128],
                                      qh[0:65, Q * 512:(Q + 1) * 512],
                                      start=True, stop=True, wait=wl, inc=T)
                            sc_ev[pb] = k.op("scalar", "activation",
                                             attT[pb][:, :], p[:, :], AF.Exp,
                                             wait=[(T, tv)], inc=S)
                            r = kt - 4 * Q
                            src = (S, sc_ev[pb])
                            if r >= 0:
                                mv = k.op("vector", "tensor_tensor",
                                          attT[pb][:, :], attT[pb][:, :],
                                          C.sb[f"trimask{r}"][:, :], AL.mult,
                                          wait=[src], inc=V)
                                src = (V, mv)
                            wl2 = [src]
                            if kt == 0:
                                wl2.append((V, y_ev))
                            k.op("tensor", "matmul", C.P[2][0:65, :],
                                 vh[:, kt * 65:kt * 65 + 65], attT[pb][:, :],
                                 start=(kt == 0), stop=(kt == 4 * Q + 3),
                                 wait=wl2, inc=(T if kt == 4 * Q + 3 else None))
                        tv = k.sems["T"].v
                        rv = k.op("vector", "reciprocal", rd[:],
                                  C.P[2][64:65, :], wait=[(T, tv)], inc=V)
                        tv2 = k.op("tensor", "matmul", C.P[3][0:64, :],
                                   C.ones_row[0:1, 0:64], rd[:], start=True,
                                   stop=True, wait=[(V, rv)], inc=T)
                        cv = k.op("vector", "tensor_copy", rb[:], C.P[3][0:64, :],
                                  wait=[(T, tv2)], inc=V)
                        y_ev = k.op("vector", "tensor_mul",
                                    ymix[hp][hh * 64:hh * 64 + 64,
                                             Q * 512:(Q + 1) * 512],
                                    C.P[2][0:64, :], rb[:], inc=V)
                        k.wait("tensor", V, y_ev)
            psum_free = [k.sems["V"].v] * 2
            stg_done = [None, None]
            for tt in range(4):
                for fc in range(8):
                    pi = fc % 2
                    p = C.P[pi]
                    for kc in range(8):
                        wl = [(V, psum_free[pi])] if kc == 0 else []
                        k.op("tensor", "matmul", p[:, :],
                             wc[kc][:, fc * 128:(fc + 1) * 128],
                             ymix[kc][:, tt * 512:(tt + 1) * 512],
                             start=(kc == 0), stop=(kc == 7), wait=wl,
                             inc=(T if kc == 7 else None))
                    sb = fc % 2
                    wl = [(T, k.sems["T"].v)]
                    if stg_done[sb]:
                        wl.append(stg_done[sb])
                    psum_free[pi] = k.op("vector", "tensor_copy", sstg[sb][:],
                                         p[:, :], wait=wl, inc=V)
                    stg_done[sb] = k.dma(
                        "gpsimd",
                        s_d[fc * 128:(fc + 1) * 128,
                            base_t + tt * 512:base_t + (tt + 1) * 512],
                        sstg[sb][:], wait=[(V, k.sems["V"].v)])
        k.full_barrier()
    if stop_after == "mixer":
        return

    with contextlib.ExitStack() as es:
        C.load_tables(es, ["l1_tie"])
        wcr = [es.enter_context(nc.sbuf_tensor(f"l1_wcr{c}", [128, 128], BF16))
               for c in range(8)]
        wfs = [es.enter_context(nc.sbuf_tensor(f"l1_wfa{i}", [128, 128], F32))
               for i in range(2)]
        bcc, _ = C.colize("l1_bcc", es, W["L1_bc"], 128)
        cv_done = [0, 0]
        for c in range(8):
            b = c % 2
            wl = [(V, cv_done[b])] if cv_done[b] else []
            lv = k.dma("sync", wfs[b][:], W["L1_Wc_r"][c * 128:(c + 1) * 128, :],
                       wait=wl)
            cv_done[b] = k.op("vector", "tensor_copy", wcr[c][:], wfs[b][:],
                              wait=[lv], inc=V)
        xg = [[es.enter_context(nc.sbuf_tensor(f"l1_xg{i}_{c}", [128, 32], BF16))
               for c in range(8)] for i in range(2)]
        xtk = [es.enter_context(nc.sbuf_tensor(f"l1_xtk{i}", [32, 1024], BF16))
               for i in range(2)]
        lg = es.enter_context(nc.sbuf_tensor("l1_lg", [128, 32], F32))
        mrow = es.enter_context(nc.sbuf_tensor("l1_mrow", [128, 1], F32))
        permb = es.enter_context(nc.sbuf_tensor("l1_permb", [128, 32], BF16))
        permt = es.enter_context(nc.sbuf_tensor("l1_permt", [32, 128], BF16))
        pst = [es.enter_context(nc.sbuf_tensor(f"l1_pst{i}", [128, 512], BF16))
               for i in range(2)]
        k.full_barrier()
        xn_used = [k.sems["T"].v, k.sems["T"].v]
        xtk_used = [k.sems["T"].v, k.sems["T"].v]
        prev_write = [None, None]
        for g in range(128):
            b = g % 2
            gld = k.new_group("sync")
            lv = None
            for kc in range(8):
                lv = k.dma("sync", xg[b][kc][:],
                           xnfm[kc * 128:(kc + 1) * 128, g * 32:(g + 1) * 32],
                           gld, wait=[(T, xn_used[b])] if kc == 0 else [])
            xl = k.dma("sync", xtk[b][:],
                       xntok[g * 32:(g + 1) * 32, :], gld,
                       wait=[(T, xtk_used[b])])
            k.wait("tensor", lv[0], lv[1])
            p = C.P[b]
            for kc in range(8):
                wl = [(V, k.sems["V"].v)] if kc == 0 else []
                k.op("tensor", "matmul", p[:, 0:32], wcr[kc][:], xg[b][kc][:],
                     start=(kc == 0), stop=(kc == 7), wait=wl,
                     inc=(T if kc == 7 else None))
            tv = k.sems["T"].v
            xn_used[b] = tv
            k.op("vector", "scalar_tensor_tensor", lg[:], p[:, 0:32],
                 bcc[:, 0:1], C.sb["l1_tie"][:], AL.add, AL.add, wait=[(T, tv)])
            k.op("vector", "reduce_max", mrow[:], lg[:], AX.X)
            pv = k.op("vector", "tensor_single_scalar", permb[:], lg[:],
                      mrow[:, 0:1], AL.is_ge, inc=V)
            k.dma("gpsimd", perm_d[g], permb[:], wait=[(V, pv)])
            tv = k.op("tensor", "transpose", C.PBF[0][0:32, 0:128], permb[:],
                      C.sb["iden_bf"][:], wait=[(V, pv)], inc=T)
            pv2 = k.op("vector", "tensor_copy", permt[:], C.PBF[0][0:32, 0:128],
                       wait=[(T, tv)], inc=V)
            for ob in range(2):
                pp = C.P[2 + ob]
                tv = k.op("tensor", "matmul", pp[:, :], permt[:],
                          xtk[b][:, ob * 512:(ob + 1) * 512], start=True,
                          stop=True,
                          wait=[(V, pv2), xl] if ob == 0 else [], inc=T)
                xtk_used[b] = tv
                wl = [(T, tv)]
                if prev_write[ob]:
                    wl.append(prev_write[ob])
                vv = k.op("vector", "tensor_copy", pst[ob][:], pp[:, :],
                          wait=wl, inc=V)
                prev_write[ob] = k.dma("gpsimd",
                                       p_d[g, :, ob * 512:(ob + 1) * 512],
                                       pst[ob][:], wait=[(V, vv)])
        k.full_barrier()
    if stop_after == "ffn_a":
        return

    with contextlib.ExitStack() as es:
        pfm2 = [[es.enter_context(nc.sbuf_tensor(f"l1_pfm{i}_{c}", [128, 128],
                                                 BF16)) for c in range(8)]
                for i in range(2)]
        w1t2 = [[es.enter_context(nc.sbuf_tensor(f"l1_w1t{i}_{c}", [128, 32],
                                                 BF16)) for c in range(8)]
                for i in range(2)]
        w2t2 = [es.enter_context(nc.sbuf_tensor(f"l1_w2t{i}", [32, 1024], BF16))
                for i in range(2)]
        wfs = [es.enter_context(nc.sbuf_tensor(f"l1_wfb{i}", [128, 32], F32))
               for i in range(2)]
        w2f = es.enter_context(nc.sbuf_tensor("l1_w2f", [32, 1024], F32))
        b1t = es.enter_context(nc.sbuf_tensor("l1_b1t", [32, 128], F32))
        innt = es.enter_context(nc.sbuf_tensor("l1_innt", [32, 128], BF16))
        rst = [es.enter_context(nc.sbuf_tensor(f"l1_rst{i}", [128, 512], BF16))
               for i in range(2)]
        gb1 = k.new_group("sync")
        b1v = k.dma("sync", b1t[:], W["L1_B1T"], gb1)
        k.wait("vector", b1v[0], b1v[1])
        cv_done = [0, 0]
        w2_used = [0, 0]
        prev_write = [None, None]
        pfm_used = [0, 0]
        cast_done = [0, 0]
        load_done = [None, None]
        for e in range(128):
            eb = e % 2
            pfm, w1t, w2t = pfm2[eb], w1t2[eb], w2t2[eb]
            gp = k.new_group("sync")
            pv = None
            for c in range(8):
                pv = k.dma("sync", pfm[c][:],
                           p_d[:, e, c * 128:(c + 1) * 128], gp,
                           wait=[(T, pfm_used[eb])] if (c == 0 and pfm_used[eb])
                           else [],
                           method="dma_start_transpose")
            gw = k.new_group("sync")
            wv = None
            for c in range(8):
                b = c % 2
                wl = [(V, cv_done[b])] if cv_done[b] else []
                wv = k.dma("sync", wfs[b][:], W["L1_W1"][e, c * 128:(c + 1) * 128, :],
                           gw, wait=wl)
                cwl = [wv]
                if c == 0 and pfm_used[eb]:
                    cwl.append((T, pfm_used[eb]))
                cv_done[b] = k.op("vector", "tensor_copy", w1t[c][:], wfs[b][:],
                                  wait=cwl, inc=V)
            w2l = k.dma("sync", w2f[:], W["L1_W2T"][e],
                        wait=[(V, cast_done[eb])] if cast_done[eb] else [])
            w2wl = [w2l]
            if w2_used[eb]:
                w2wl.append((T, w2_used[eb]))
            cast_done[eb] = k.op("vector", "tensor_copy", w2t[:], w2f[:],
                                 wait=w2wl, inc=V)
            load_done[eb] = pv
            k.wait("tensor", pv[0], pv[1])
            k.wait("tensor", V, cast_done[eb])
            p = C.P[0]
            for c in range(8):
                k.op("tensor", "matmul", p[0:32, 0:128], w1t[c][:], pfm[c][:],
                     start=(c == 0), stop=(c == 7),
                     wait=[(V, k.sems["V"].v)] if e and c == 0 else [],
                     inc=(T if c == 7 else None))
            tv = k.sems["T"].v
            pfm_used[eb] = tv
            k.op("vector", "tensor_scalar_add", innt[:], p[0:32, 0:128],
                 b1t[:, e:e + 1], wait=[(T, tv)])
            iv = k.op("vector", "tensor_relu", innt[:], innt[:], inc=V)
            for ob in range(2):
                pp = C.P[2 + ob]
                tv = k.op("tensor", "matmul", pp[:, :], innt[:],
                          w2t[:, ob * 512:(ob + 1) * 512], start=True, stop=True,
                          wait=[(V, iv)] if ob == 0 else [], inc=T)
                if ob == 1:
                    w2_used[eb] = tv
                wl = [(T, tv)]
                if prev_write[ob]:
                    wl.append(prev_write[ob])
                vv = k.op("vector", "tensor_copy", rst[ob][:], pp[:, :],
                          wait=wl, inc=V)
                prev_write[ob] = k.dma("gpsimd",
                                       r_d[e, :, ob * 512:(ob + 1) * 512],
                                       rst[ob][:], wait=[(V, vv)])
        k.full_barrier()
    if stop_after == "ffn_b":
        return

    with contextlib.ExitStack() as es:
        b2t = [es.enter_context(nc.sbuf_tensor(f"l1_b2t{i}", [128, 1024], BF16))
               for i in range(1)]
        b2f = es.enter_context(nc.sbuf_tensor("l1_b2f", [128, 1024], F32))
        rcol = [es.enter_context(nc.sbuf_tensor(f"l1_rc{i}", [128, 1024], BF16))
                for i in range(2)]
        prmg = [es.enter_context(nc.sbuf_tensor(f"l1_pg{i}", [128, 32], BF16))
                for i in range(2)]
        sfc = es.enter_context(nc.sbuf_tensor("l1_sfc", [128, 32], BF16))
        gt = es.enter_context(nc.sbuf_tensor("l1_gt", [128, 32], BF16))
        sg = es.enter_context(nc.sbuf_tensor("l1_sg", [128, 32], F32))
        xsb = es.enter_context(nc.sbuf_tensor("l1_xsb", [32, 1024], F32))
        gb = k.new_group("sync")
        b2l = k.dma("sync", b2f[:], W["L1_B2"], gb)
        k.op("vector", "tensor_copy", b2t[0][:], b2f[:], wait=[b2l], inc=V)
        k.full_barrier()
        x_st = None
        rc_used = [k.sems["T"].v, k.sems["T"].v]
        pg_used = [k.sems["T"].v, k.sems["T"].v]
        for g in range(128):
            b = g % 2
            grp = k.new_group("sync")
            rv = k.dma("sync", rcol[b][:], r_d[:, g, :], grp,
                       wait=[(T, rc_used[b])])
            pgl = k.dma("sync", prmg[b][:], perm_d[g], grp,
                        wait=[(T, pg_used[b])])
            xl = k.dma("sync", xsb[:], x_dram[g * 32:(g + 1) * 32, :],
                       wait=[x_st] if x_st else [])
            k.wait("tensor", pgl[0], pgl[1])
            sgl = k.new_group("sync")
            sv_done = k.sems["V"].v
            for dc in range(8):
                p = C.P[dc % 2]
                sl = k.dma("sync", sfc[:],
                           s_d[dc * 128:(dc + 1) * 128, g * 32:(g + 1) * 32],
                           sgl, wait=[(V, sv_done)])
                k.op("tensor", "matmul", p[:, 0:32],
                     rcol[b][:, dc * 128:(dc + 1) * 128], prmg[b][:],
                     start=True, stop=False,
                     wait=[(V, k.sems["V"].v)] if dc < 2 else [])
                tv = k.op("tensor", "matmul", p[:, 0:32],
                          b2t[0][:, dc * 128:(dc + 1) * 128], prmg[b][:],
                          start=False, stop=True, inc=T)
                if dc >= 6:
                    rc_used[b] = tv
                    pg_used[b] = tv
                sv = k.op("scalar", "activation", gt[:], p[:, 0:32], AF.Silu,
                          wait=[(T, tv)], inc=S)
                mv = k.op("vector", "tensor_mul", sg[:], gt[:], sfc[:],
                          wait=[(S, sv), sl], inc=V)
                sv_done = mv
                pb2 = dc % 2
                tv2 = k.op("tensor", "transpose", C.P[4 + pb2][0:32, 0:128],
                           sg[:], C.sb["iden_f32"][:],
                           wait=[(V, mv)], inc=T)
                k.op("vector", "tensor_add", xsb[:, dc * 128:(dc + 1) * 128],
                     xsb[:, dc * 128:(dc + 1) * 128],
                     C.P[4 + pb2][0:32, 0:128],
                     wait=[(T, tv2), xl] if dc == 0 else [(T, tv2)], inc=V)
            x_st = k.dma("gpsimd", xout_dram[g * 32:(g + 1) * 32, :], xsb[:],
                         wait=[(V, k.sems["V"].v)])
        k.full_barrier()


def scratch_tensors(nc, which, debug=False):
    d = {}
    kind = "ExternalOutput" if debug else "Internal"
    mk = lambda n, sh, dt: nc.dram_tensor(n, sh, dt, kind=kind).ap()
    d["xnfm"] = mk("xnfm", [1024, 4096], BF16)
    d["xntok"] = mk("xntok", [4096, 1024], BF16)
    d["s_fm"] = mk("s_fm", [1024, 4096], BF16)
    if which in ("l0", "all"):
        d["h1"] = mk("h1", [4096, 4096], BF16)
        d["h2"] = mk("h2", [4096, 4096], BF16)
        d["w1eff"] = mk("w1eff", [4096, 4096], BF16)
    return d


def build_l2_program(stop_after="all", debug=False):
    nc = bass.Bass("TRN2", target_bir_lowering=False)
    k = K(nc)
    x_in = nc.dram_tensor("x_in", [T_CORE, DM], F32, kind="ExternalInput").ap()
    x_out = nc.dram_tensor("x_out", [T_CORE, DM], F32, kind="ExternalOutput").ap()
    C = Ctx(k, used=['iden_bf', 'iden_f32', 'trimask0', 'trimask1', 'trimask2', 'trimask3'])
    W = {n: nc.dram_tensor(n, sh, dt, kind="ExternalInput").ap()
         for n, (sh, dt) in L2_WSPECS.items()}
    scr = scratch_tensors(nc, "l0", debug=debug)
    emit_layer2(k, C, W, x_in, x_out, scr, stop_after=stop_after)
    k.finalize()
    return nc


def prep_l2_inputs(w):
    t = lambda a: np.ascontiguousarray(np.asarray(a, np.float32).T)
    c = lambda a: np.ascontiguousarray(np.asarray(a, np.float32))
    return {
        "L2_W_hgT": t(w["L2_attn_W_hg"]),
        "L2_attn_b_hg": c(w["L2_attn_b_hg"]),
        "L2_W_outT": t(w["L2_attn_W_out"]),
        "L2_attn_b_out": c(w["L2_attn_b_out"]),
        "L2_attn_h0": c(w["L2_attn_h0"]),
        "L2_w1T": t(w["L2_ffn_w1_w"]),
        "L2_ffn_w1_s": c(w["L2_ffn_w1_s"]),
        "L2_w2T": t(w["L2_ffn_w2_w"]),
        "L2_ffn_w2_s": c(w["L2_ffn_w2_s"]),
        "L2_h1T": t(w["L2_ffn_h1_w"]),
        "L2_ffn_h1_s": c(w["L2_ffn_h1_s"]),
        "L2_h2T": t(w["L2_ffn_h2_w"]),
        "L2_ffn_h2_s": c(w["L2_ffn_h2_s"]),
        "L2_w3T": t(w["L2_ffn_w3_w"]),
        "L2_ffn_w3_s": c(w["L2_ffn_w3_s"]),
    }


_L2_CACHE = {}


def run_layer2(x_full, inputs, stop_after="all", debug=False, cores=None):
    key = (stop_after, debug)
    if _L2_CACHE.get("key") != key:
        _L2_CACHE["nc"] = build_l2_program(stop_after, debug)
        _L2_CACHE["consts"] = host_consts()
        _L2_CACHE["key"] = key
    nc = _L2_CACHE["nc"]
    wmap = prep_l2_inputs(inputs)
    in_maps = []
    ncores = cores or N_CORES
    for core in range(ncores):
        m = {n: _L2_CACHE["consts"][n] for n in _L2_CACHE["nc"]._used_consts}
        m.update(wmap)
        m["x_in"] = np.ascontiguousarray(
            x_full[core * 2:(core + 1) * 2].reshape(T_CORE, DM), np.float32)
        in_maps.append(m)
    res = run_bass_kernel_spmd(nc, in_maps, core_ids=list(range(ncores)))
    if debug:
        return res
    return np.concatenate(
        [np.asarray(r["x_out"]).reshape(2, 2048, 1024) for r in res.results], 0)


def build_l3_program(stop_after="all", debug=False):
    nc = bass.Bass("TRN2", target_bir_lowering=False)
    k = K(nc)
    x_in = nc.dram_tensor("x_in", [T_CORE, DM], F32, kind="ExternalInput").ap()
    x_out = nc.dram_tensor("x_out", [T_CORE, DM], F32, kind="ExternalOutput").ap()
    C = Ctx(k, used=['iden_bf', 'iden_f32', 'trimask0', 'trimask1', 'trimask2', 'trimask3', 'l3_cos', 'l3_sin'])
    W = {n: nc.dram_tensor(n, sh, dt, kind="ExternalInput").ap()
         for n, (sh, dt) in L3_WSPECS.items()}
    scr = scratch_tensors(nc, "l0", debug=debug)
    emit_layer3(k, C, W, x_in, x_out, scr, stop_after=stop_after)
    k.finalize()
    return nc


def prep_l3_inputs(w):
    t = lambda a: np.ascontiguousarray(np.asarray(a, np.float32).T)
    c = lambda a: np.ascontiguousarray(np.asarray(a, np.float32))
    return {
        "L3_w1T": t(w["L3_attn_qkv_w1"]),
        "L3_w2T": t(w["L3_attn_qkv_w2"]),
        "L3_cw1T": t(w["L3_attn_cproj_w1"]),
        "L3_cw2T": t(w["L3_attn_cproj_w2"]),
        "L3_Wr": c(w["L3_ffn_Wr"]),
        "L3_WiT": np.ascontiguousarray(
            np.asarray(w["L3_ffn_Wi"], np.float32).transpose(0, 2, 1)),
        "L3_Wo": c(w["L3_ffn_Wo"]),
    }


_L3_CACHE = {}


def run_layer3(x_full, inputs, stop_after="all", debug=False, cores=None):
    key = (stop_after, debug)
    if _L3_CACHE.get("key") != key:
        _L3_CACHE["nc"] = build_l3_program(stop_after, debug)
        _L3_CACHE["consts"] = host_consts()
        _L3_CACHE["key"] = key
    nc = _L3_CACHE["nc"]
    wmap = prep_l3_inputs(inputs)
    in_maps = []
    ncores = cores or N_CORES
    for core in range(ncores):
        m = {n: _L3_CACHE["consts"][n] for n in _L3_CACHE["nc"]._used_consts}
        m.update(wmap)
        m["x_in"] = np.ascontiguousarray(
            x_full[core * 2:(core + 1) * 2].reshape(T_CORE, DM), np.float32)
        in_maps.append(m)
    res = run_bass_kernel_spmd(nc, in_maps, core_ids=list(range(ncores)))
    if debug:
        return res
    return np.concatenate(
        [np.asarray(r["x_out"]).reshape(2, 2048, 1024) for r in res.results], 0)


def build_l1_program(stop_after="all", debug=False):
    nc = bass.Bass("TRN2", target_bir_lowering=False)
    k = K(nc)
    x_in = nc.dram_tensor("x_in", [T_CORE, DM], F32, kind="ExternalInput").ap()
    x_out = nc.dram_tensor("x_out", [T_CORE, DM], F32, kind="ExternalOutput").ap()
    C = Ctx(k, used=['iden_bf', 'iden_f32', 'trimask0', 'trimask1', 'trimask2', 'trimask3', 'l1_cos', 'l1_sin', 'l1_tie'])
    W = {n: nc.dram_tensor(n, sh, dt, kind="ExternalInput").ap()
         for n, (sh, dt) in L1_WSPECS.items()}
    scr = scratch_tensors(nc, "l0", debug=debug)
    emit_layer1(k, C, W, x_in, x_out, scr, stop_after=stop_after)
    k.finalize()
    return nc


def prep_l1_inputs(w):
    c = lambda a: np.ascontiguousarray(np.asarray(a, np.float32))
    return {
        "L1_attn_Wq_down": c(w["L1_attn_Wq_down"]),
        "L1_attn_Wk_down": c(w["L1_attn_Wk_down"]),
        "L1_attn_Wv_down": c(w["L1_attn_Wv_down"]),
        "L1_attn_Wq_up_c": c(w["L1_attn_Wq_up_c"]),
        "L1_attn_Wq_up_e": c(w["L1_attn_Wq_up_e"]),
        "L1_attn_Wk_up_c": c(w["L1_attn_Wk_up_c"]),
        "L1_attn_Wk_up_e": c(w["L1_attn_Wk_up_e"]),
        "L1_attn_Wv_up": c(w["L1_attn_Wv_up"]),
        "L1_attn_Wc": c(w["L1_attn_Wc"]),
        "L1_Wc_r": c(w["L1_ffn_Wc"]).reshape(1024, 128),
        "L1_bc": c(w["L1_ffn_bc"]).reshape(128),
        "L1_W1": c(w["L1_ffn_W1"]).reshape(128, 1024, 32),
        "L1_W2T": np.ascontiguousarray(
            np.asarray(w["L1_ffn_W2"], np.float32).reshape(128, 1024, 32)
            .transpose(0, 2, 1)),
        "L1_B1T": np.ascontiguousarray(
            np.asarray(w["L1_ffn_b1"], np.float32).reshape(128, 32).T),
        "L1_B2": c(w["L1_ffn_b2"]).reshape(128, 1024),
    }


_L1_CACHE = {}


def run_layer1(x_full, inputs, stop_after="all", debug=False, cores=None):
    key = (stop_after, debug)
    if _L1_CACHE.get("key") != key:
        _L1_CACHE["nc"] = build_l1_program(stop_after, debug)
        _L1_CACHE["consts"] = host_consts()
        _L1_CACHE["key"] = key
    nc = _L1_CACHE["nc"]
    wmap = prep_l1_inputs(inputs)
    in_maps = []
    ncores = cores or N_CORES
    for core in range(ncores):
        m = {n: _L1_CACHE["consts"][n] for n in _L1_CACHE["nc"]._used_consts}
        m.update(wmap)
        m["x_in"] = np.ascontiguousarray(
            x_full[core * 2:(core + 1) * 2].reshape(T_CORE, DM), np.float32)
        in_maps.append(m)
    res = run_bass_kernel_spmd(nc, in_maps, core_ids=list(range(ncores)))
    if debug:
        return res
    return np.concatenate(
        [np.asarray(r["x_out"]).reshape(2, 2048, 1024) for r in res.results], 0)


def build_l0_program(stop_after="all", debug=False):
    nc = bass.Bass("TRN2", target_bir_lowering=False)
    k = K(nc)
    x_in = nc.dram_tensor("x_in", [T_CORE, DM], F32, kind="ExternalInput").ap()
    x_out = nc.dram_tensor("x_out", [T_CORE, DM], F32, kind="ExternalOutput").ap()
    C = Ctx(k, used=['iden_bf', 'iden_f32', 'trimask0', 'trimask1', 'trimask2', 'trimask3', 'l0_cos', 'l0_sin'])
    W = {n: nc.dram_tensor(n, sh, dt, kind="ExternalInput").ap()
         for n, (sh, dt) in L0_WSPECS.items()}
    scr = scratch_tensors(nc, "l0", debug=debug)
    emit_layer0(k, C, W, x_in, x_out, scr, stop_after=stop_after)
    k.finalize()
    return nc


def prep_l0_inputs(w):
    out = {}
    out["L0_attn_W_attn"] = np.ascontiguousarray(w["L0_attn_W_attn"], np.float32)
    out["L0_attn_b_attn"] = np.ascontiguousarray(w["L0_attn_b_attn"], np.float32)
    out["L0_attn_W_proj"] = np.ascontiguousarray(w["L0_attn_W_proj"], np.float32)
    out["L0_attn_b_proj"] = np.ascontiguousarray(w["L0_attn_b_proj"], np.float32)
    for n in ("L0_ffn_k0", "L0_ffn_k1", "L0_ffn_k2"):
        out[n] = np.ascontiguousarray(np.asarray(w[n])).view(np.uint8)
    for n in ("L0_ffn_s0", "L0_ffn_s1", "L0_ffn_s2"):
        out[n] = np.asarray(w[n], np.float32).reshape(1)
    return out


_L0_CACHE = {}


def run_layer0(x_full, inputs, stop_after="all", debug=False, cores=None):
    key = (stop_after, debug)
    if _L0_CACHE.get("key") != key:
        _L0_CACHE["nc"] = build_l0_program(stop_after, debug)
        _L0_CACHE["consts"] = host_consts()
        _L0_CACHE["key"] = key
    nc = _L0_CACHE["nc"]
    wmap = prep_l0_inputs(inputs)
    in_maps = []
    for core in range(N_CORES):
        m = {n: _L0_CACHE["consts"][n] for n in _L0_CACHE["nc"]._used_consts}
        m.update(wmap)
        m["x_in"] = np.ascontiguousarray(
            x_full[core * 2:(core + 1) * 2].reshape(T_CORE, DM), np.float32)
        in_maps.append(m)
    if cores is not None:
        in_maps = in_maps[:cores]
        res = run_bass_kernel_spmd(nc, in_maps, core_ids=list(range(cores)))
    else:
        res = run_bass_kernel_spmd(nc, in_maps, core_ids=list(range(N_CORES)))
    if debug:
        return res
    return np.concatenate(
        [np.asarray(r["x_out"]).reshape(2, 2048, 1024) for r in res.results], 0)


BASE_CONSTS = ["iden_bf", "iden_f32", "trimask0", "trimask1", "trimask2",
               "trimask3"]


def build_full_program():
    nc = bass.Bass("TRN2", target_bir_lowering=False)
    k = K(nc)
    x_in = nc.dram_tensor("x_in", [T_CORE, DM], F32, kind="ExternalInput").ap()
    x_out = nc.dram_tensor("x_out", [T_CORE, DM], F32, kind="ExternalOutput").ap()
    xa = nc.dram_tensor("x_a", [T_CORE, DM], F32, kind="Internal").ap()
    xb = nc.dram_tensor("x_b", [T_CORE, DM], F32, kind="Internal").ap()
    C = Ctx(k, used=BASE_CONSTS)
    W = {}
    for spec in (L0_WSPECS, L1_WSPECS, L2_WSPECS, L3_WSPECS):
        for n, (sh, dt) in spec.items():
            W[n] = nc.dram_tensor(n, sh, dt, kind="ExternalInput").ap()
    scr = scratch_tensors(nc, "l0")
    emit_layer0(k, C, W, x_in, xa, scr)
    emit_layer1(k, C, W, xa, xb, scr)
    emit_layer2(k, C, W, xb, xa, scr)
    emit_layer3(k, C, W, xa, x_out, scr)
    k.finalize()
    return nc


_FULL_CACHE = {}


def kernel_fused(inputs):
    if "nc" not in _FULL_CACHE:
        _FULL_CACHE["nc"] = build_full_program()
        _FULL_CACHE["consts"] = host_consts()
    nc = _FULL_CACHE["nc"]
    x = np.ascontiguousarray(np.asarray(inputs["L0_attn_x"]),
                             np.float32).reshape(16, 2048, 1024)
    wmap = {}
    wmap.update(prep_l0_inputs(inputs))
    wmap.update(prep_l1_inputs(inputs))
    wmap.update(prep_l2_inputs(inputs))
    wmap.update(prep_l3_inputs(inputs))
    consts = {n: _FULL_CACHE["consts"][n] for n in nc._used_consts}
    in_maps = []
    for core in range(N_CORES):
        m = dict(consts)
        m.update(wmap)
        m["x_in"] = np.ascontiguousarray(x[core * 2:(core + 1) * 2]
                                         .reshape(T_CORE, DM))
        in_maps.append(m)
    res = run_bass_kernel_spmd(nc, in_maps, core_ids=list(range(N_CORES)))
    return np.concatenate(
        [np.asarray(r["x_out"]).reshape(2, 2048, 1024) for r in res.results],
        0).astype(np.float32)


def kernel(**inputs):
    import os
    if os.environ.get("UNFUSED"):
        x = np.ascontiguousarray(np.asarray(inputs["L0_attn_x"]), np.float32)
        x = x.reshape(16, 2048, 1024)
        x = run_layer0(x, inputs)
        x = run_layer1(x, inputs)
        x = run_layer2(x, inputs)
        x = run_layer3(x, inputs)
        return x.astype(np.float32)
    return kernel_fused(inputs)
